# version 07
# baseline (speedup 1.0000x reference)
import numpy as np
from contextlib import ExitStack, contextmanager

import concourse.bass as bass
import concourse.mybir as mybir
from concourse.bass_utils import run_bass_kernel_spmd

F32 = mybir.dt.float32
BF16 = mybir.dt.bfloat16
U32 = mybir.dt.uint32
AF = mybir.ActivationFunctionType
ALU = mybir.AluOpType
AX = mybir.AxisListType.X

D = 1024
NT = 2048
NTILE = 16
H = 16
DH = 64
NCORES = 8
LN_EPS = 1e-5
GLA_EPS = 1e-6

ENGINES = ("sync", "tensor", "vector", "scalar", "gpsimd")


class Ctr:
    __slots__ = ("h", "n", "name")

    def __init__(self, h, name):
        self.h = h
        self.n = 0
        self.name = name


class Prog:
    def __init__(self):
        self.nc = bass.Bass("TRN2", target_bir_lowering=False, debug=False)
        self.stack = ExitStack()
        self.root = self.stack
        self._sems = {}

    def sem(self, name):
        if name not in self._sems:
            h = self.root.enter_context(self.nc.semaphore(name))
            self._sems[name] = Ctr(h, name)
        return self._sems[name]

    def _uname(self, name):
        if not hasattr(self, "_names"):
            self._names = {}
        n = self._names.get(name, 0)
        self._names[name] = n + 1
        return name if n == 0 else f"{name}__{n}"

    def sbuf(self, name, shape, dtype):
        return self.stack.enter_context(
            self.nc.sbuf_tensor(self._uname(name), shape, dtype))

    def psum(self, name, shape, dtype):
        return self.stack.enter_context(
            self.nc.psum_tensor(self._uname(name), shape, dtype))

    @contextmanager
    def scope(self):
        outer = self.stack
        self.stack = ExitStack()
        try:
            yield
        finally:
            self.stack.close()
            self.stack = outer


class Emit:

    def __init__(self, p):
        self.p = p
        self.pend = {e: [] for e in ENGINES}
        self.eng_ctr = {e: p.sem("c_" + e) for e in ENGINES if e != "sync"}
        self.dma_sems = {}
        self.last_w = {}
        self.readers = {}
        self.hw = {e: {} for e in ENGINES}

    def dma_sem(self, eng, key):
        k = (eng, key)
        if k not in self.dma_sems:
            self.dma_sems[k] = self.p.sem(f"d{len(self.dma_sems)}_{key[:12]}")
        return self.dma_sems[k]

    def op(self, eng, fn, reads=(), writes=(), dma=False, n_dma=1, self_inc=0,
           dkey=None):
        if dma or self_inc:
            if dkey is None:
                dkey = writes[0] if writes else (reads[0] if reads else "anon")
            ctr = self.dma_sem(eng, dkey)
        else:
            ctr = self.eng_ctr[eng]
        waits = {}

        def need(dep):
            if dep is None:
                return
            c, n = dep
            if n <= 0 or self.hw[eng].get(c, 0) >= n:
                return
            waits[c] = max(waits.get(c, 0), n)

        for b in reads:
            need(self.last_w.get(b))
        for b in writes:
            need(self.last_w.get(b))
            for c, n in self.readers.get(b, {}).items():
                need((c, n))

        for c, n in waits.items():
            self.hw[eng][c] = max(self.hw[eng].get(c, 0), n)
        wl = [(c.h, n) for c, n in waits.items()]

        if self_inc:
            ctr.n += self_inc
        else:
            ctr.n += 16 * n_dma if dma else 1
        my_cnt = ctr.n
        inc_k = 16 if (dma or self_inc) else 1

        def closure(e, fn=fn, wl=wl, ctr=ctr, self_inc=self_inc, inc_k=inc_k):
            for (sh, n) in wl:
                e.wait_ge(sh, n)
            last = fn(e)
            if self_inc:
                return
            assert last is not None, "op fn must return its last instruction(s)"
            if not isinstance(last, (list, tuple)):
                last = [last]
            if inc_k == 16:
                for ins in last:
                    ins.then_inc(ctr.h, inc_k)
            else:
                last[-1].then_inc(ctr.h, 1)

        self.pend[eng].append(closure)
        for b in reads:
            self.readers.setdefault(b, {})[ctr] = my_cnt
        for b in writes:
            self.last_w[b] = (ctr, my_cnt)
            self.readers[b] = {}
        return (ctr, my_cnt)

    def barrier(self, exclude=()):
        snap = [(c, c.n) for c in
                list(self.eng_ctr.values()) + list(self.dma_sems.values())
                if c.n > 0 and c not in exclude]
        for eng in ENGINES:
            def cl(e, snap=snap):
                for c, n in snap:
                    e.wait_ge(c.h, n)
            self.pend[eng].append(cl)
            for c, n in snap:
                self.hw[eng][c] = max(self.hw[eng].get(c, 0), n)

    def flush(self):
        with self.p.nc.Block() as block:
            for e in ENGINES:
                ops = self.pend[e]
                if not ops:
                    continue

                def run_all(eng, ops=ops):
                    for cl in ops:
                        cl(eng)
                getattr(block, e)(run_all)
            final = [(c.h, c.n) for c in
                     list(self.eng_ctr.values()) + list(self.dma_sems.values()) if c.n > 0]

            def tail(eng):
                for (sh, n) in final:
                    eng.wait_ge(sh, n)
            block.sync(tail)
        self.pend = {e: [] for e in ENGINES}


def make_const(e, g_eng, ap, key, kind, sq=128):
    if kind == "ones":
        e.op(g_eng, lambda g: g.memset(ap, 1.0), writes=(key,))
        return
    fill0, cmp = (0.0, ALU.not_equal) if kind == "ident" else (1.0, ALU.is_gt)
    if kind == "ident":
        e.op(g_eng, lambda g: g.memset(ap, 0.0), writes=(key,))
        e.op(g_eng, lambda g: g.affine_select(out=ap, in_=ap, compare_op=ALU.not_equal,
                                              fill=1.0, base=0, pattern=[[-1, sq]],
                                              channel_multiplier=1),
             reads=(key,), writes=(key,))
    else:
        e.op(g_eng, lambda g: g.memset(ap, 1.0), writes=(key,))
        e.op(g_eng, lambda g: g.affine_select(out=ap, in_=ap, compare_op=ALU.is_gt,
                                              fill=0.0, base=0, pattern=[[-1, sq]],
                                              channel_multiplier=1),
             reads=(key,), writes=(key,))


def emit_ln_tiles(e, p, x_d, u_fm, bufs, n_tiles=NTILE, stat_out=None, ukey="u",
                  src_key=None):
    SY, TE, VE, SC = "sync", "tensor", "vector", "scalar"
    xt, stat, ubf, psT, ident_bf = (bufs["xt"], bufs["stat"], bufs["ubf"],
                                    bufs["psT"], bufs["ident_bf"])
    for i in range(n_tiles):
        xb, sb, ub = xt[i % 2], stat[i % 2], ubf[i % 2]
        xk, sk, uk = f"xt{i % 2}", f"stat{i % 2}", f"ubf{i % 2}"
        e.op(SY, lambda s, i=i, xb=xb: s.dma_start(
            out=xb[:], in_=x_d[i * 128:(i + 1) * 128, :]), writes=(xk,), dma=True,
            reads=(() if src_key is None else tuple(src_key)))
        e.op(VE, lambda v, xb=xb, sb=sb: v.bn_stats(out=sb[:, 0:6], in_=xb[:, 0:512]),
             reads=(xk,), writes=(sk,))
        e.op(VE, lambda v, xb=xb, sb=sb: v.bn_stats(out=sb[:, 6:12], in_=xb[:, 512:1024]),
             reads=(xk,), writes=(sk,))
        e.op(VE, lambda v, sb=sb: v.bn_aggr(out=sb[:, 12:14], in_=sb[:, 0:12]),
             reads=(sk,), writes=(sk,))
        e.op(VE, lambda v, sb=sb: v.tensor_scalar_add(sb[:, 13:14], sb[:, 13:14],
                                                      float(LN_EPS)),
             reads=(sk,), writes=(sk,))
        e.op(SC, lambda s, sb=sb: s.activation(out=sb[:, 14:15], in_=sb[:, 13:14],
                                               func=AF.Ln),
             reads=(sk,), writes=(sk,))
        e.op(SC, lambda s, sb=sb: s.activation(out=sb[:, 15:16], in_=sb[:, 14:15],
                                               func=AF.Exp, scale=-0.5),
             reads=(sk,), writes=(sk,))
        e.op(VE, lambda v, xb=xb, sb=sb, ub=ub: v.tensor_scalar(
            out=ub[:], in0=xb[:], scalar1=sb[:, 12:13], scalar2=sb[:, 15:16],
            op0=ALU.subtract, op1=ALU.mult), reads=(xk, sk), writes=(uk,))
        if stat_out is not None:
            e.op(VE, lambda v, sb=sb, i=i: v.tensor_copy(
                out=stat_out[:, 2 * i:2 * i + 2],
                in_=sb[:].rearrange("p (a b) -> p a b", b=1)[:, (12, 15), :].squeeze(2)
                if False else sb[:, 12:16:3]),
                reads=(sk,), writes=("lnstat_" + ukey,))

        def fn_tr(t, ub=ub):
            last = None
            for dc in range(8):
                last = t.transpose(psT[:, dc * 128:(dc + 1) * 128],
                                   ub[:, dc * 128:(dc + 1) * 128], ident_bf[:])
            return last
        e.op(TE, fn_tr, reads=(uk, "identbf"), writes=("psT",))
        e.op(VE, lambda v, i=i: v.tensor_copy(
            out=u_fm[:, :, i * 128:(i + 1) * 128],
            in_=psT[:].rearrange("p (a b) -> p a b", b=128)),
            reads=("psT",), writes=("ufm_" + ukey,))


def emit_proj_fm(e, dst, dk, u_fm, wsel, wkeys, psB, elu_bufs=None,
                 n_tg=4, psb_key="psB", ukey="ufm_u"):
    TE, VE, SC = "tensor", "vector", "scalar"
    for oc in range(8):
        for tg in range(n_tg):
            def fn_mm(t, oc=oc, tg=tg):
                last = None
                for dc in range(8):
                    last = t.matmul(psB[:], wsel(dc, oc),
                                    u_fm[:, dc, tg * 512:(tg + 1) * 512],
                                    start=(dc == 0), stop=(dc == 7))
                return last
            e.op(TE, fn_mm, reads=wkeys + (ukey,), writes=(psb_key,))
            if elu_bufs is not None:
                elu_sc, elu_sc2 = elu_bufs
                e.op(VE, lambda v: v.tensor_scalar(out=elu_sc[:], in0=psB[:], scalar1=0.0,
                                                   scalar2=None, op0=ALU.min),
                     reads=(psb_key,), writes=("elusc",))
                e.op(SC, lambda s: s.activation(out=elu_sc2[:], in_=elu_sc[:], func=AF.Exp),
                     reads=("elusc",), writes=("elusc2",))
                e.op(VE, lambda v, oc=oc, tg=tg: v.scalar_tensor_tensor(
                    out=dst[:, oc, tg * 512:(tg + 1) * 512], in0=psB[:], scalar=0.0,
                    in1=elu_sc2[:], op0=ALU.max, op1=ALU.add),
                    reads=(psb_key, "elusc2"), writes=(dk,))
            else:
                e.op(VE, lambda v, oc=oc, tg=tg: v.tensor_copy(
                    out=dst[:, oc, tg * 512:(tg + 1) * 512], in_=psB[:]),
                    reads=(psb_key,), writes=(dk,))


def emit_v_tm(e, v_tm, vk, u_fm, udk, wsel, wkeys, psB, psb_key="psB"):
    TE, VE = "tensor", "vector"
    for ti in range(NTILE):
        for ng in range(2):
            def fn_mm(t, ti=ti, ng=ng):
                last = None
                for dc in range(8):
                    last = t.matmul(psB[:], u_fm[:, dc, ti * 128:(ti + 1) * 128],
                                    wsel(dc, ng), start=(dc == 0), stop=(dc == 7))
                return last
            e.op(TE, fn_mm, reads=wkeys + ("ufm_" + udk,), writes=(psb_key,))
            e.op(VE, lambda v, ti=ti, ng=ng: v.tensor_copy(
                out=v_tm[:, ti].rearrange("p (h x) -> p h x", x=65)[:, ng * 8:(ng + 1) * 8, 0:64],
                in_=psB[:].rearrange("p (h d) -> p h d", d=64)),
                reads=(psb_key,), writes=(vk,))


def emit_p1(p, e, C, io, car):
    x_d, wq_d, wk_d, wv_d = io["x"], io["Wq"], io["Wk"], io["Wv"]
    wa_d, ba_d = io["Wa"], io["ba"]
    chs_d, cha_d, stage1 = io["chunkS"], io["chunkA"], io["stage1"]
    q_fm, k_fm, v_tm, lna_tm = car

    xt = [p.sbuf(f"xt{i}", [128, D], F32) for i in range(2)]
    stat = [p.sbuf(f"stat{i}", [128, 16], F32) for i in range(2)]
    ubf = [p.sbuf(f"ubf{i}", [128, D], BF16) for i in range(2)]
    u_fm = p.sbuf("ufm", [128, 8, NT], BF16)
    lna_fm = p.sbuf("lnafm", [16, NT], F32)
    a_fm = p.sbuf("afm", [16, NT], F32)
    wmat = p.sbuf("wmat", [128, 16, D], BF16)
    wa_sb = p.sbuf("wasb", [128, 8 * H], BF16)
    ba2_sb = p.sbuf("ba2sb", [16, 1], F32)
    elu_sc = p.sbuf("elusc", [128, 512], F32)
    elu_sc2 = p.sbuf("elusc2", [128, 512], F32)
    ident_bf, ident_f, msl = C["identbf"], C["identf"], C["msl"]
    ones_col, ones_row = C["onescolf"], C["onesrow"]
    k_tm = p.sbuf("ktm", [128, D], BF16)
    khat = p.sbuf("khat", [128, D], BF16)
    escale = p.sbuf("escale", [128, H], BF16)
    arow = p.sbuf("arow", [1, H], F32)
    s_run = p.sbuf("srun", [128, 520], F32)
    s_stash = p.sbuf("sstash", [128, 520], F32)

    psB = p.psum("psB", [128, 512], F32)
    psT = p.psum("psT", [128, 1024], BF16)
    psS = p.psum("psS", [128, 1024], F32)
    psR = p.psum("psR", [128, 128], F32)

    SY, TE, VE, SC, GP = ENGINES

    e.op(GP, lambda g: g.memset(
        v_tm[:].rearrange("p h (x y) -> p h x y", y=65)[:, :, :, 64:65], 1.0),
        writes=("vtm_ones",))

    def load_w(dst_half, w_d, tag):
        def fn(g):
            return [g.dma_start(out=wmat[:, dst_half * 8 + dc, :],
                                in_=w_d[dc * 128:(dc + 1) * 128, :]) for dc in range(8)]
        e.op(GP, fn, writes=(tag,), dma=True, n_dma=8)

    load_w(0, wk_d, "wmatA")
    load_w(1, wq_d, "wmatB")

    def fn_wa(g):
        ins = [g.dma_start(out=wa_sb[:, dc * H:(dc + 1) * H],
                           in_=wa_d[dc * 128:(dc + 1) * 128, :]) for dc in range(8)]
        ins.append(g.dma_start(out=ba2_sb[:], in_=ba_d[:].rearrange("(h x) -> h x", x=1)))
        return ins
    e.op(GP, fn_wa, writes=("wasb", "ba2"), dma=True, n_dma=9)
    e.op(VE, lambda v: v.tensor_scalar_add(ba2_sb[:], ba2_sb[:], 2.0),
         reads=("ba2",), writes=("ba2",))

    emit_ln_tiles(e, p, x_d, u_fm,
                  dict(xt=xt, stat=stat, ubf=ubf, psT=psT, ident_bf=ident_bf), ukey="u")

    emit_proj_fm(e, k_fm, "kfm", u_fm,
                 lambda dc, oc: wmat[:, dc, oc * 128:(oc + 1) * 128], ("wmatA",),
                 psB, elu_bufs=(elu_sc, elu_sc2))
    load_w(0, wv_d, "wmatA")
    emit_proj_fm(e, q_fm, "qfm", u_fm,
                 lambda dc, oc: wmat[:, 8 + dc, oc * 128:(oc + 1) * 128], ("wmatB",),
                 psB, elu_bufs=(elu_sc, elu_sc2))

    emit_v_tm(e, v_tm, "vtm", u_fm, "u",
              lambda dc, ng: wmat[:, dc, ng * 512:(ng + 1) * 512], ("wmatA",), psB)

    for tg in range(4):
        def fn_amm(t, tg=tg):
            last = None
            for dc in range(8):
                last = t.matmul(psB[0:16, :], wa_sb[:, dc * H:(dc + 1) * H],
                                u_fm[:, dc, tg * 512:(tg + 1) * 512],
                                start=(dc == 0), stop=(dc == 7))
            return last
        e.op(TE, fn_amm, reads=("wasb", "ufm_u"), writes=("psB",))
        e.op(SC, lambda s, tg=tg: s.activation(
            out=a_fm[:, tg * 512:(tg + 1) * 512], in_=psB[0:16, :], func=AF.Sigmoid,
            bias=ba2_sb[:], scale=1.0), reads=("psB", "ba2"), writes=("afm",))
    e.op(SC, lambda s: s.activation(out=lna_fm[:], in_=a_fm[:], func=AF.Ln),
         reads=("afm",), writes=("lnafm",))

    for grp in range(4):
        def fn_tr(t, grp=grp):
            last = None
            for cc in range(4):
                c = grp * 4 + cc
                last = t.transpose(psR[:, cc * 16:(cc + 1) * 16],
                                   lna_fm[:, c * 128:(c + 1) * 128], ident_f[0:16, 0:16])
            return last
        e.op(TE, fn_tr, reads=("lnafm", "identf"), writes=("psR",))
        e.op(VE, lambda v, grp=grp: v.tensor_copy(
            out=lna_tm[:, grp * 64:(grp + 1) * 64], in_=psR[:, 0:64]),
            reads=("psR",), writes=("lnatm",))

    e.op(VE, lambda v: v.memset(s_run[:], 0.0), writes=("srun",))
    for c in range(16):
        lna_c = lna_tm[:, c * 16:(c + 1) * 16]

        def fn_rev(t, lna_c=lna_c):
            t.matmul(psR[:, 0:16], msl[:], lna_c, start=True, stop=True)
            return t.matmul(psR[0:1, 16:32], ones_col[:], lna_c, start=True, stop=True)
        e.op(TE, fn_rev, reads=("lnatm", "msl", "onescol"), writes=("psR",))
        e.op(SC, lambda s: s.activation(out=escale[:], in_=psR[:, 0:16], func=AF.Exp),
             reads=("psR",), writes=("escale",))
        e.op(SC, lambda s: s.activation(out=arow[:], in_=psR[0:1, 16:32], func=AF.Exp),
             reads=("psR",), writes=("arow",))

        def fn_ktr(t, c=c):
            last = None
            for dc in range(8):
                last = t.transpose(psT[:, dc * 128:(dc + 1) * 128],
                                   k_fm[:, dc, c * 128:(c + 1) * 128], ident_bf[:])
            return last
        e.op(TE, fn_ktr, reads=("kfm", "identbf"), writes=("psT",))
        e.op(VE, lambda v: v.tensor_copy(out=k_tm[:], in_=psT[:]),
             reads=("psT",), writes=("ktm",))
        e.op(VE, lambda v: v.tensor_tensor(
            out=khat[:].rearrange("p (h d) -> p h d", d=64),
            in0=k_tm[:].rearrange("p (h d) -> p h d", d=64),
            in1=escale[:].unsqueeze(2).broadcast_to([128, 16, 64]), op=ALU.mult),
            reads=("ktm", "escale"), writes=("khat",))

        def fn_sadd(t, c=c):
            last = None
            for h in range(16):
                pb = 64 * (h % 2)
                j = h // 2
                off = (j // 4) * 512 + (j % 4) * 65
                last = t.matmul(psS[pb:pb + 64, off:off + 65],
                                khat[:, h * 64:(h + 1) * 64],
                                v_tm[:, c, h * 65:(h + 1) * 65],
                                start=True, stop=True)
            return last
        e.op(TE, fn_sadd, reads=("khat", "vtm", "vtm_ones"), writes=("psS",))

        def fn_abc(t):
            t.matmul(psR[0:64, 32:40], ones_row[0:1, 0:64], arow[:, 0:8],
                     start=True, stop=True)
            return t.matmul(psR[64:128, 32:40], ones_row[0:1, 0:64], arow[:, 8:16],
                            start=True, stop=True)
        e.op(TE, fn_abc, reads=("arow", "onesrow"), writes=("psR",))

        psS_v = psS[:].rearrange("p (b s) -> p b s", s=512)[:, :, 0:260]\
            .rearrange("p b (j x) -> p b j x", x=65)
        e.op(VE, lambda v, psS_v=psS_v: v.tensor_copy(
            out=s_stash[:].rearrange("p (b j x) -> p b j x", b=2, j=4), in_=psS_v),
             reads=("psS",), writes=("sstash",))
        e.op(VE, lambda v: v.tensor_tensor(
            out=s_run[:].rearrange("p (j x) -> p j x", x=65),
            in0=s_run[:].rearrange("p (j x) -> p j x", x=65),
            in1=psR[0:128, 32:40].unsqueeze(2).broadcast_to([128, 8, 65]),
            op=ALU.mult), reads=("srun", "psR"), writes=("srun",))
        e.op(VE, lambda v, psS_v=psS_v: v.tensor_tensor(
            out=s_run[:].rearrange("p (b j x) -> p b j x", b=2, j=4),
            in0=s_run[:].rearrange("p (b j x) -> p b j x", b=2, j=4),
            in1=psS_v, op=ALU.add),
             reads=("srun", "psS"), writes=("srun",))

        e.op(SY, lambda s, c=c: s.dma_start(out=chs_d[c], in_=s_stash[:]),
             reads=("sstash",), dma=True)
        e.op(SY, lambda s, c=c: s.dma_start(out=cha_d[c:c + 1, :], in_=arow[:]),
             reads=("arow",), dma=True)

    def fn_ac(t):
        last = None
        for c in range(16):
            last = t.matmul(psR[0:1, 48:64], ones_col[:], lna_tm[:, c * 16:(c + 1) * 16],
                            start=(c == 0), stop=(c == 15))
        return last
    e.op(TE, fn_ac, reads=("lnatm", "onescol"), writes=("psR",))
    e.op(SC, lambda s: s.activation(out=arow[:], in_=psR[0:1, 48:64], func=AF.Exp),
         reads=("psR",), writes=("arow",))

    z16 = p.sbuf("z16", [128, 16], F32)
    e.op(GP, lambda g: g.memset(z16[:], 0.0), writes=("z16",))
    e.op(VE, lambda v: v.tensor_copy(out=z16[0:1, :], in_=arow[:]),
         reads=("arow", "z16"), writes=("z16",))
    e.op(SY, lambda s: s.dma_start(out=stage1[:, 0:520], in_=s_run[:]),
         reads=("srun",), writes=("stage1",), dma=True, dkey="stage1")
    e.op(SY, lambda s: s.dma_start(out=stage1[:, 520:536], in_=z16[:]),
         reads=("z16",), writes=("stage1",), dma=True, dkey="stage1")


def emit_p2(p, e, C, io, car, upto=16, lvl=99):
    x_d, wo_d = io["x"], io["AWo"]
    chs_d, cha_d = io["chunkS"], io["chunkA"]
    gath1, pm_d = io["gath1"], io["pm"]
    x1_d, stage2 = io["x1"], io["stage2"]
    q_fm, k_fm, v_tm, lna_tm = car

    sprev = p.sbuf("sprev", [128, 3, 536], F32)
    aprev = p.sbuf("aprev", [1, 3 * H], F32)
    pm_sb = p.sbuf("pmsb", [1, 8], F32)
    pmb = p.sbuf("pmb", [128, 8], F32)
    chs = [p.sbuf(f"chs{i}", [128, 520], F32) for i in range(2)]
    cha = p.sbuf("cha", [1, 16 * H], F32)
    wo_sb = p.sbuf("wosb", [128, 8, D], BF16)
    s_run = p.sbuf("srun", [128, 520], F32)
    s_run_bf = p.sbuf("srunbf", [128, 520], BF16)
    ident_bf, ident_f = C["identbf"], C["identf"]
    mle, msl, mneg = C["mle"], C["msl"], C["mneg"]
    ones_row = C["onesrow"]
    lexp_tm = p.sbuf("lexptm", [128, H], F32)
    msl_sc = p.sbuf("mslsc", [128, 256], F32)
    dpair = p.sbuf("dpair", [128, 256], BF16)
    apair = p.sbuf("apair", [128, 256], BF16)
    apair_v = apair[:].rearrange("p (b s) -> p b s", s=128)
    cr_sb = p.sbuf("crsb", [128, 130], F32)
    n_sb = p.sbuf("nsb", [128, 16, 65], F32)
    dr_sb = p.sbuf("drsb", [128, H], F32)
    obuf = p.sbuf("obuf", [128, D], BF16)
    attn_fm = p.sbuf("attnfm", [128, 8, 128], BF16)
    xt = [p.sbuf(f"xt{i}", [128, D], F32) for i in range(2)]
    x1t = p.sbuf("x1t", [128, D], F32)
    u1t = p.sbuf("u1t", [128, D], F32)
    stat = p.sbuf("stat0", [128, 16], F32)

    psBD = p.psum("psBD", [128, 1024], F32)
    psN = p.psum("psN", [128, 1536], F32)
    psY = p.psum("psY", [128, 512], F32)
    psT = p.psum("psT", [128, 1024], BF16)
    psR = p.psum("psR", [128, 512], F32)

    SY, TE, VE, SC, GP = ENGINES

    def fn_ldprev(s):
        return [s.dma_start(out=sprev[:, k, :], in_=gath1[k * 128:(k + 1) * 128, :])
                for k in range(3)]
    e.op(SY, fn_ldprev, reads=("gath1",), writes=("sprev",), dma=True, n_dma=3)
    e.op(SY, lambda s: s.dma_start(out=pm_sb[:], in_=pm_d[:]),
         writes=("pm",), dma=True, dkey="pmld")
    e.op(SY, lambda s: s.dma_start(out=cha[:], in_=cha_d[:].rearrange("k h -> (k h)").rearrange("(o n) -> o n", o=1)), writes=("cha",), dma=True)
    e.op(GP, lambda g: g.dma_start(out=wo_sb[:],
                                   in_=wo_d[:].rearrange("(a p) b -> p a b", p=128)),
         writes=("wosb",), dma=True)

    def emit_abc(src_row, tag, extra=()):
        def fn(t):
            t.matmul(psR[0:64, 16:24], ones_row[0:1, 0:64], src_row[:, 0:16:2],
                     start=True, stop=True)
            return t.matmul(psR[64:128, 16:24], ones_row[0:1, 0:64], src_row[:, 1:16:2],
                            start=True, stop=True)
        e.op(TE, fn, reads=(tag, "onesrow") + tuple(extra), writes=("psR2k",))

    e.op(TE, lambda t: t.matmul(psR[:, 32:40], ones_row[0:1, :], pm_sb[0:1, :],
                                start=True, stop=True),
         reads=("pm", "onesrow"), writes=("psR2k",))
    e.op(VE, lambda v: v.tensor_copy(out=pmb[:], in_=psR[:, 32:40]),
         reads=("psR2k",), writes=("pmb",))
    def fn_aprime(v):
        return [v.tensor_scalar(out=aprev[0:1, k * 16:(k + 1) * 16],
                                in0=sprev[0:1, k, 520:536],
                                scalar1=pm_sb[0:1, k:k + 1],
                                scalar2=pm_sb[0:1, 4 + k:5 + k],
                                op0=ALU.mult, op1=ALU.add) for k in range(3)]
    e.op(VE, fn_aprime, reads=("sprev", "pm"), writes=("aprev",))
    e.op(VE, lambda v: v.tensor_scalar(out=s_run[:], in0=sprev[:, 0, 0:520],
                                       scalar1=pmb[:, 0:1], scalar2=None, op0=ALU.mult),
         reads=("sprev", "pmb"), writes=("srun",))
    for k in range(1, 3):
        emit_abc(aprev[0:1, k * 16:(k + 1) * 16], "aprev")
        e.op(VE, lambda v: v.tensor_tensor(
            out=s_run[:].rearrange("p (j x) -> p j x", x=65),
            in0=s_run[:].rearrange("p (j x) -> p j x", x=65),
            in1=psR[0:128, 16:24].unsqueeze(2).broadcast_to([128, 8, 65]),
            op=ALU.mult), reads=("srun", "psR2k"), writes=("srun",))
        e.op(VE, lambda v, k=k: v.scalar_tensor_tensor(
            out=s_run[:], in0=sprev[:, k, 0:520], scalar=pmb[:, k:k + 1],
            in1=s_run[:], op0=ALU.mult, op1=ALU.add),
             reads=("srun", "sprev", "pmb"), writes=("srun",))

    for c in range(upto):
        lna_c = lna_tm[:, c * 16:(c + 1) * 16]
        if lvl >= 0.2:
            e.op(SY, lambda s, c=c: s.dma_start(out=chs[c % 2][:], in_=chs_d[c]),
                 writes=(f"chs{c % 2}",), dma=True)

        if lvl >= 0.4:
            e.op(VE, lambda v: v.tensor_copy(out=s_run_bf[:], in_=s_run[:]),
                 reads=("srun",), writes=("srunbf",))

        if lvl >= 0.6:
            def fn_l(t, lna_c=lna_c):
                return t.matmul(psR[:, 0:16], mle[:], lna_c, start=True, stop=True)
            e.op(TE, fn_l, reads=("lnatm", "mle"), writes=("psR2k",))
        if 0.7 <= lvl < 0.72:
            e.op(VE, lambda v: v.tensor_copy(out=lexp_tm[:], in_=psR[:, 0:16]),
                 reads=("psR2k",), writes=("lexptm",))
        if 0.72 <= lvl < 0.74:
            e.op(SC, lambda s: s.copy(out=lexp_tm[:], in_=psR[:, 0:16]),
                 reads=("psR2k",), writes=("lexptm",))
        if 0.74 <= lvl < 0.76:
            e.op(SC, lambda s: s.activation(out=lexp_tm[:], in_=s_run[:, 0:16],
                                            func=AF.Exp),
                 reads=("srun",), writes=("lexptm",))
        if lvl >= 0.8:
            e.op(SC, lambda s: s.activation(out=lexp_tm[:], in_=psR[:, 0:16], func=AF.Exp),
                 reads=("psR2k",), writes=("lexptm",))

        for hp in range(8 if lvl >= 2 else 0):
            def fn_msc(v, hp=hp, lna_c=lna_c):
                return [v.tensor_scalar(out=msl_sc[:, j * 128:(j + 1) * 128], in0=msl[:],
                                        scalar1=lna_c[:, 2 * hp + j:2 * hp + j + 1],
                                        scalar2=None, op0=ALU.mult) for j in range(2)]
            e.op(VE, fn_msc, reads=("msl", "lnatm"), writes=("mslsc",))

            def fn_darg(t, hp=hp):
                last = []
                for j in range(2):
                    t.matmul(psBD[:, 256 + j * 512:384 + j * 512],
                             msl_sc[:, j * 128:(j + 1) * 128], mle[:],
                             start=True, stop=False)
                    last.append(t.matmul(psBD[:, 256 + j * 512:384 + j * 512],
                                         ident_f[:], mneg[:], start=False, stop=True))
                return last[-1]
            e.op(TE, fn_darg, reads=("mslsc", "mle", "mneg", "identf"), writes=("psBD",))

            def fn_dexp(s, hp=hp):
                return [s.activation(out=dpair[:, j * 128:(j + 1) * 128],
                                     in_=psBD[:, 256 + j * 512:384 + j * 512], func=AF.Exp)
                        for j in range(2)]
            e.op(SC, fn_dexp, reads=("psBD",), writes=("dpair",))

            def fn_qk(t, c=c, hp=hp):
                last = []
                for s in range(2):
                    pb = 64 * s
                    last.append(t.matmul(
                        psBD[:, s * 512:s * 512 + 128],
                        k_fm[pb:pb + 64, hp, c * 128:(c + 1) * 128],
                        q_fm[pb:pb + 64, hp, c * 128:(c + 1) * 128],
                        start=True, stop=True))
                return last
            e.op(TE, fn_qk, reads=("kfm", "qfm"), writes=("psBD",))
            psBD_qk = psBD[:].rearrange("p (b s) -> p b s", s=512)[:, :, 0:128]
            dpair_v = dpair[:].rearrange("p (b s) -> p b s", s=128)
            e.op(VE, lambda v: v.tensor_tensor(out=apair_v, in0=psBD_qk,
                                               in1=dpair_v, op=ALU.mult),
                 reads=("psBD", "dpair"), writes=("apair",))

            if lvl < 3:
                continue
            def fn_n(t, c=c, hp=hp):
                last = []
                for s in range(2):
                    h = 2 * hp + s
                    t.matmul(psN[:, s * 65:(s + 1) * 65], apair[:, s * 128:(s + 1) * 128],
                             v_tm[:, c, h * 65:(h + 1) * 65], start=True, stop=True)
                    pb = 64 * s
                    last.append(t.matmul(
                        psN[:, 512 * (s + 1):512 * (s + 1) + 65],
                        q_fm[pb:pb + 64, hp, c * 128:(c + 1) * 128],
                        s_run_bf[pb:pb + 64, hp * 65:(hp + 1) * 65],
                        start=True, stop=True))
                return last
            e.op(TE, fn_n, reads=("apair", "vtm", "qfm", "srunbf"), writes=("psN",))

            def fn_cr(s):
                return [s.copy(out=cr_sb[:, j * 65:(j + 1) * 65],
                               in_=psN[:, 512 * (j + 1):512 * (j + 1) + 65])
                        for j in range(2)]
            e.op(SC, fn_cr, reads=("psN",), writes=("crsb",))

            def fn_comb(v, hp=hp):
                return [v.scalar_tensor_tensor(
                    out=n_sb[:, 2 * hp + s, :], in0=cr_sb[:, s * 65:(s + 1) * 65],
                    scalar=lexp_tm[:, 2 * hp + s:2 * hp + s + 1],
                    in1=psN[:, s * 65:(s + 1) * 65], op0=ALU.mult, op1=ALU.add)
                    for s in range(2)]
            e.op(VE, fn_comb, reads=("crsb", "lexptm", "psN"), writes=("nsb",))

        if lvl < 4:
            continue
        e.op(VE, lambda v: v.tensor_scalar_add(dr_sb[:], n_sb[:, :, 64], float(GLA_EPS)),
             reads=("nsb",), writes=("drsb",))
        e.op(VE, lambda v: v.reciprocal(out=dr_sb[:], in_=dr_sb[:]),
             reads=("drsb",), writes=("drsb",))
        e.op(VE, lambda v: v.tensor_tensor(
            out=obuf[:].rearrange("p (h d) -> p h d", d=64),
            in0=n_sb[:, :, 0:64],
            in1=dr_sb[:].unsqueeze(2).broadcast_to([128, 16, 64]), op=ALU.mult),
            reads=("nsb", "drsb"), writes=("obuf",))

        emit_abc(cha[0:1, c * 16:(c + 1) * 16], "cha")
        e.op(VE, lambda v: v.tensor_tensor(
            out=s_run[:].rearrange("p (j x) -> p j x", x=65),
            in0=s_run[:].rearrange("p (j x) -> p j x", x=65),
            in1=psR[0:128, 16:24].unsqueeze(2).broadcast_to([128, 8, 65]),
            op=ALU.mult), reads=("srun", "psR2k"), writes=("srun",))
        e.op(VE, lambda v, c=c: v.tensor_tensor(out=s_run[:], in0=s_run[:],
                                                in1=chs[c % 2][:], op=ALU.add),
             reads=("srun", f"chs{c % 2}"), writes=("srun",))

        def fn_tr(t):
            last = None
            for dc in range(8):
                last = t.transpose(psT[:, dc * 128:(dc + 1) * 128],
                                   obuf[:, dc * 128:(dc + 1) * 128], ident_bf[:])
            return last
        e.op(TE, fn_tr, reads=("obuf", "identbf"), writes=("psT",))
        e.op(VE, lambda v: v.tensor_copy(
            out=attn_fm[:], in_=psT[:].rearrange("p (a b) -> p a b", b=128)),
            reads=("psT",), writes=("attnfm",))

        e.op(SY, lambda s, c=c: s.dma_start(out=xt[c % 2][:],
                                            in_=x_d[c * 128:(c + 1) * 128, :]),
             writes=(f"xt{c % 2}",), dma=True)

        for ng in range(2):
            def fn_wo(t, ng=ng):
                last = None
                for pc in range(8):
                    last = t.matmul(psY[:], attn_fm[:, pc, :],
                                    wo_sb[:, pc, ng * 512:(ng + 1) * 512],
                                    start=(pc == 0), stop=(pc == 7))
                return last
            e.op(TE, fn_wo, reads=("attnfm", "wosb"), writes=("psY",))
            e.op(VE, lambda v, c=c, ng=ng: v.tensor_tensor(
                out=x1t[:, ng * 512:(ng + 1) * 512],
                in0=xt[c % 2][:, ng * 512:(ng + 1) * 512],
                in1=psY[:], op=ALU.add),
                reads=(f"xt{c % 2}", "psY"), writes=("x1t",))
        e.op(SY, lambda s, c=c: s.dma_start(out=x1_d[c * 128:(c + 1) * 128, :],
                                            in_=x1t[:]),
             reads=("x1t",), dma=True, dkey="x1out")

        if c == 15:
            e.op(VE, lambda v: v.bn_stats(out=stat[:, 0:6], in_=x1t[:, 0:512]),
                 reads=("x1t",), writes=("stat0",))
            e.op(VE, lambda v: v.bn_stats(out=stat[:, 6:12], in_=x1t[:, 512:1024]),
                 reads=("x1t",), writes=("stat0",))
            e.op(VE, lambda v: v.bn_aggr(out=stat[:, 12:14], in_=stat[:, 0:12]),
                 reads=("stat0",), writes=("stat0",))
            e.op(VE, lambda v: v.tensor_scalar_add(stat[:, 13:14], stat[:, 13:14],
                                                   float(LN_EPS)),
                 reads=("stat0",), writes=("stat0",))
            e.op(SC, lambda s: s.activation(out=stat[:, 14:15], in_=stat[:, 13:14],
                                            func=AF.Ln), reads=("stat0",), writes=("stat0",))
            e.op(SC, lambda s: s.activation(out=stat[:, 15:16], in_=stat[:, 14:15],
                                            func=AF.Exp, scale=-0.5),
                 reads=("stat0",), writes=("stat0",))
            e.op(VE, lambda v: v.tensor_scalar(out=u1t[:], in0=x1t[:],
                                               scalar1=stat[:, 12:13],
                                               scalar2=stat[:, 15:16],
                                               op0=ALU.subtract, op1=ALU.mult),
                 reads=("x1t", "stat0"), writes=("u1t",))
            e.op(SY, lambda s: s.dma_start(out=stage2[:], in_=u1t[127:128, :]),
                 reads=("u1t",), writes=("stage2",), dma=True, dkey="u1out")


def emit_p3(p, e, C, io):
    x1_d, gath2 = io["x1"], io["gath2"]
    wp1_d, bp1_d, wp2_d, bp2_d = io["Wp1"], io["bp1"], io["Wp2"], io["bp2"]
    w1_d, b1_d, w2_d, b2_d = io["W1"], io["b1"], io["W2"], io["b2"]
    wo_d, bo_d = io["FWo"], io["bo"]
    x2_d, stage3 = io["x2"], io["stage3"]

    xt = [p.sbuf(f"xt{i}", [128, D], F32) for i in range(2)]
    stat = [p.sbuf(f"stat{i}", [128, 16], F32) for i in range(2)]
    ubf = [p.sbuf(f"ubf{i}", [128, D], BF16) for i in range(2)]
    u_fm = p.sbuf("ufm", [128, 8, NT], BF16)
    lnstat = p.sbuf("lnstat", [128, 2 * NTILE], F32)
    lrow = p.sbuf("lrow", [1, D], F32)
    wp1_sb = p.sbuf("wp1sb", [128, 8, 128], BF16)
    wp2_sb = p.sbuf("wp2sb", [128, 8], BF16)
    bp1_sb = p.sbuf("bp1sb", [1, 128], F32)
    bp2_sb = p.sbuf("bp2sb", [1, 8], F32)
    glog = p.sbuf("glog", [1, 8], F32)
    gmax = p.sbuf("gmax", [1, 8], F32)
    gsel = p.sbuf("gsel", [1, 8], U32)
    w1_sb = p.sbuf("w1sb", [128, 8, 2048], BF16)
    w2big = p.sbuf("w2big", [128, 16, 1024], BF16)
    w2_sb = w2big[:]
    b1r = p.sbuf("b1r", [1, 2048], F32)
    b2r = p.sbuf("b2r", [1, D], BF16)
    bor = p.sbuf("bor", [1, D], BF16)
    b1c = p.sbuf("b1c", [128, 16], F32)
    wo_sb = p.sbuf("wosb", [128, 8, D], BF16)
    ident_bf, ident_f = C["identbf"], C["identf"]
    ones_row, ones_row_bf = C["onesrow"], C["onesrowbf"]
    h_sb = p.sbuf("hsb", [128, 512], BF16)
    r_sb = p.sbuf("rsb", [128, D], BF16)
    r_fm = p.sbuf("rfm", [128, 8, 128], BF16)
    x2t = p.sbuf("x2t", [128, D], F32)
    u1t = p.sbuf("u1t", [128, D], F32)
    u2_fm = u_fm

    psB = p.psum("psB", [128, 512], F32)
    psT = p.psum("psT", [128, 1024], BF16)
    psH = p.psum("psH", [128, 1024], F32)
    psR2 = p.psum("psR2", [128, 2048], F32)

    SY, TE, VE, SC, GP = ENGINES

    e.op(SY, lambda s: s.dma_start(out=lrow[:], in_=gath2[3:4, :]),
         reads=("gath2",), writes=("lrow",), dma=True)
    e.op(GP, lambda g: g.dma_start(out=wp1_sb[:],
                                   in_=wp1_d[:].rearrange("(a p) f -> p a f", p=128)),
         writes=("wp1",), dma=True)
    e.op(GP, lambda g: g.dma_start(out=wp2_sb[:], in_=wp2_d[:]), writes=("wp2",), dma=True)
    e.op(SY, lambda s: s.dma_start(out=bp1_sb[:],
                                   in_=bp1_d[:].rearrange("(o n) -> o n", o=1)),
         writes=("bp1",), dma=True)
    e.op(SY, lambda s: s.dma_start(out=bp2_sb[:],
                                   in_=bp2_d[:].rearrange("(o n) -> o n", o=1)),
         writes=("bp2",), dma=True)

    def fn_lvt(t):
        last = None
        for dc in range(8):
            last = t.transpose(psB[:, dc:dc + 1], lrow[0:1, dc * 128:(dc + 1) * 128],
                               ident_f[0:1, 0:1])
        return last
    e.op(TE, fn_lvt, reads=("lrow", "identf"), writes=("psB",))
    lv_fm = p.sbuf("lvfm", [128, 8], BF16)
    e.op(VE, lambda v: v.tensor_copy(out=lv_fm[:], in_=psB[:, 0:8]),
         reads=("psB",), writes=("lvfm",))
    def fn_bp1(t):
        return t.transpose(psB[:, 8:9], bp1_sb[:], ident_f[0:1, 0:1])
    e.op(TE, fn_bp1, reads=("bp1", "identf"), writes=("psB",))
    bp1c = p.sbuf("bp1c", [128, 1], F32)
    e.op(VE, lambda v: v.tensor_copy(out=bp1c[:], in_=psB[:, 8:9]),
         reads=("psB",), writes=("bp1c",))

    def fn_z1(t):
        last = None
        for dc in range(8):
            last = t.matmul(psH[:, 0:1], wp1_sb[:, dc, :], lv_fm[:, dc:dc + 1],
                            start=(dc == 0), stop=(dc == 7))
        return last
    e.op(TE, fn_z1, reads=("wp1", "lvfm"), writes=("psH0",))
    z1g = p.sbuf("z1g", [128, 1], BF16)
    e.op(SC, lambda s: s.activation(out=z1g[:], in_=psH[:, 0:1], func=AF.Gelu,
                                    bias=bp1c[:], scale=1.0),
         reads=("psH0", "bp1c"), writes=("z1g",))
    def fn_z2(t):
        return t.matmul(psH[:, 4:12][0:1, :], z1g[:], wp2_sb[:], start=True, stop=True)
    e.op(TE, fn_z2, reads=("z1g", "wp2"), writes=("psH0",))
    e.op(VE, lambda v: v.tensor_add(out=glog[:], in0=psH[0:1, 4:12], in1=bp2_sb[:]),
         reads=("psH0", "bp2"), writes=("glog",))
    e.op(VE, lambda v: v.max(out=gmax[:], in_=glog[:]), reads=("glog",), writes=("gmax",))
    e.op(VE, lambda v: v.max_index(out=gsel[:], in_max=gmax[:], in_values=glog[:]),
         reads=("gmax", "glog"), writes=("gsel",))

    def fn_sel_dma(g):
        reg = g.alloc_register("selreg")
        g.load(reg, gsel[0:1, 0:1])
        dsem = e.dma_sem(GP, "expertw")
        for ex in range(8):
            with g.If_eq(reg, ex):
                g.dma_start(out=w1_sb[:],
                            in_=w1_d[ex].rearrange("(a p) f -> p a f", p=128)
                            ).then_inc(dsem.h, 16)
                g.dma_start(out=w2_sb,
                            in_=w2_d[ex].rearrange("(a p) f -> p a f", p=128)
                            ).then_inc(dsem.h, 16)
                g.dma_start(out=b1r[:],
                            in_=b1_d[ex].rearrange("(o n) -> o n", o=1)
                            ).then_inc(dsem.h, 16)
                g.dma_start(out=b2r[:],
                            in_=b2_d[ex].rearrange("(o n) -> o n", o=1)
                            ).then_inc(dsem.h, 16)
        return None
    e.op(GP, fn_sel_dma, reads=("gsel",), writes=("w1", "w2", "b1r", "b2r"),
         self_inc=64, dkey="expertw")

    for grp in range(2):
        def fn_b1(t, grp=grp):
            last = None
            for j in range(8):
                fc = grp * 8 + j
                last = t.transpose(psB[:, 16 + j:17 + j],
                                   b1r[0:1, fc * 128:(fc + 1) * 128], ident_f[0:1, 0:1])
            return last
        e.op(TE, fn_b1, reads=("b1r", "identf"), writes=("psB",))
        e.op(VE, lambda v, grp=grp: v.tensor_copy(out=b1c[:, grp * 8:(grp + 1) * 8],
                                                  in_=psB[:, 16:24]),
             reads=("psB",), writes=("b1c",))

    e.op(GP, lambda g: g.dma_start(out=wo_sb[:],
                                   in_=wo_d[:].rearrange("(a p) b -> p a b", p=128)),
         writes=("wosb",), dma=True)
    e.op(GP, lambda g: g.dma_start(out=bor[:], in_=bo_d[:].rearrange("(o n) -> o n", o=1)),
         writes=("bor",), dma=True)

    emit_ln_tiles(e, p, x1_d, u_fm,
                  dict(xt=xt, stat=stat, ubf=ubf, psT=psT, ident_bf=ident_bf),
                  stat_out=lnstat, ukey="u")

    for tp in range(8):
        for fc in range(16):
            par = fc % 2
            hps = psH[:, par * 512:par * 512 + 256]
            hsv = h_sb[:, par * 256:(par + 1) * 256]

            def fn_h(t, tp=tp, fc=fc, hps=hps):
                last = None
                for dc in range(8):
                    last = t.matmul(hps, w1_sb[:, dc, fc * 128:(fc + 1) * 128],
                                    u_fm[:, dc, tp * 256:(tp + 1) * 256],
                                    start=(dc == 0), stop=(dc == 7))
                return last
            e.op(TE, fn_h, reads=("w1", "ufm_u"), writes=(f"psH{par}",))
            e.op(SC, lambda s, fc=fc, hps=hps, hsv=hsv: s.activation(
                out=hsv, in_=hps, func=AF.Gelu,
                bias=b1c[:, fc:fc + 1], scale=1.0),
                 reads=(f"psH{par}", "b1c"), writes=(f"hsb{par}",))

            def fn_r(t, tp=tp, fc=fc, hsv=hsv):
                last = []
                for ti in range(2):
                    for ng in range(2):
                        last.append(t.matmul(
                            psR2[:, ti * 1024 + ng * 512:ti * 1024 + (ng + 1) * 512],
                            hsv[:, ti * 128:(ti + 1) * 128],
                            w2_sb[:, fc, ng * 512:(ng + 1) * 512],
                            start=(fc == 0), stop=False))
                return last[-1]
            e.op(TE, fn_r, reads=(f"hsb{par}", "w2"), writes=("psR2",))
        def fn_rb(t, tp=tp):
            last = []
            for ti in range(2):
                for ng in range(2):
                    last.append(t.matmul(
                        psR2[:, ti * 1024 + ng * 512:ti * 1024 + (ng + 1) * 512],
                        ones_row_bf[:], b2r[0:1, ng * 512:(ng + 1) * 512],
                        start=False, stop=True))
            return last[-1]
        e.op(TE, fn_rb, reads=("onesrowbf", "b2r"), writes=("psR2",))

        for ti in range(2):
            e.op(VE, lambda v, ti=ti: v.tensor_copy(out=r_sb[:],
                                                    in_=psR2[:, ti * 1024:(ti + 1) * 1024]),
                 reads=("psR2",), writes=("rsb",))

            def fn_tr(t):
                last = None
                for dc in range(8):
                    last = t.transpose(psT[:, dc * 128:(dc + 1) * 128],
                                       r_sb[:, dc * 128:(dc + 1) * 128], ident_bf[:])
                return last
            e.op(TE, fn_tr, reads=("rsb", "identbf"), writes=("psT",))
            e.op(VE, lambda v: v.tensor_copy(
                out=r_fm[:], in_=psT[:].rearrange("p (a b) -> p a b", b=128)),
                reads=("psT",), writes=("rfm",))

            i = tp * 2 + ti
            e.op(SY, lambda s, i=i: s.dma_start(out=xt[i % 2][:],
                                                in_=x1_d[i * 128:(i + 1) * 128, :]),
                 writes=(f"xt{i % 2}",), dma=True)
            for ng in range(2):
                def fn_o(t, ng=ng):
                    last = None
                    for pc in range(8):
                        last = t.matmul(psH[:, 0:512], r_fm[:, pc, :],
                                        wo_sb[:, pc, ng * 512:(ng + 1) * 512],
                                        start=(pc == 0), stop=False)
                    return t.matmul(psH[:, 0:512], ones_row_bf[:],
                                    bor[0:1, ng * 512:(ng + 1) * 512],
                                    start=False, stop=True)
                e.op(TE, fn_o, reads=("rfm", "wosb", "onesrowbf", "bor"), writes=("psH0",))
                e.op(VE, lambda v, i=i, ng=ng: v.tensor_scalar(
                    out=u1t[:, ng * 512:(ng + 1) * 512],
                    in0=xt[i % 2][:, ng * 512:(ng + 1) * 512],
                    scalar1=lnstat[:, 2 * i:2 * i + 1], scalar2=lnstat[:, 2 * i + 1:2 * i + 2],
                    op0=ALU.subtract, op1=ALU.mult),
                    reads=(f"xt{i % 2}", "lnstat_u"), writes=("u1t",))
                e.op(VE, lambda v, i=i, ng=ng: v.tensor_add(
                    out=u1t[:, ng * 512:(ng + 1) * 512],
                    in0=u1t[:, ng * 512:(ng + 1) * 512],
                    in1=xt[i % 2][:, ng * 512:(ng + 1) * 512]),
                    reads=(f"xt{i % 2}", "u1t"), writes=("u1t",))
                e.op(VE, lambda v, ng=ng: v.scalar_tensor_tensor(
                    out=x2t[:, ng * 512:(ng + 1) * 512], in0=psH[:, 0:512], scalar=0.3,
                    in1=u1t[:, ng * 512:(ng + 1) * 512], op0=ALU.mult, op1=ALU.add),
                    reads=("psH0", "u1t"), writes=("x2t",))
            e.op(SY, lambda s, i=i: s.dma_start(out=x2_d[i * 128:(i + 1) * 128, :],
                                                in_=x2t[:]),
                 reads=("x2t",), writes=(f"x2o{i}",), dma=True, dkey="x2out")

    e.op(VE, lambda v: v.engine_nop(), reads=(), writes=("ufm_u",))
    emit_ln_tiles(e, p, x2_d, u2_fm,
                  dict(xt=xt, stat=stat, ubf=ubf, psT=psT, ident_bf=ident_bf), ukey="u2",
                  src_key=[f"x2o{i}" for i in range(16)])
    e.op(SY, lambda s: s.dma_start(out=stage3[:],
                                   in_=u2_fm[:].rearrange("p a t -> p (a t)")),
         reads=("ufm_u2",), writes=("stage3",), dma=True, dkey="u2out")


def emit_p4(p, e, C, io):
    gath3 = io["gath3"]
    wq_d, wk_d, wv_d = io["WqT"], io["WkT"], io["WvT"]
    stage4 = io["stage4"]

    wq_sb = p.sbuf("wqsb", [128, 8, 256], BF16)
    wk_sb = p.sbuf("wksb", [128, 8, 256], BF16)
    wv_sb = p.sbuf("wvsb", [128, 8, 256], BF16)
    u2r = [p.sbuf(f"u2r{i}", [128, 8, NT], BF16) for i in range(2)]
    q_fm = p.sbuf("qfm", [128, 2, 4 * NT], BF16)
    k_fm = p.sbuf("kfm", [128, 2, 4 * NT], BF16)
    v_tm = p.sbuf("vtm", [128, 64, 260], BF16)
    pt_sb = p.sbuf("ptsb", [128, 2048], BF16)
    sq_sc = p.sbuf("sqsc", [128, 2048], BF16)
    bmax = p.sbuf("bmax", [1, 64], F32)
    negb2 = p.sbuf("negb2", [128, 2], F32)
    linv = p.sbuf("linv", [1, 512], F32)
    linvb = p.sbuf("linvb", [64, 512], F32)
    ot = [p.sbuf(f"ot{i}", [128, 512], BF16) for i in range(2)]
    nac = [[p.sbuf(f"nac{i}{s}", [65, 512], F32) for s in range(2)]
           for i in range(2)]
    ones_row, ones_col = C["onesrow"], C["onescolbf"]

    psS = p.psum("psS", [128, 2048], F32)
    psA = [p.psum("psA0", [128, 512], F32), p.psum("psA1", [128, 512], F32)]
    psB = p.psum("psB", [128, 512], F32)

    SY, TE, VE, SC, GP = ENGINES

    e.op(GP, lambda g: g.memset(
        v_tm[:].rearrange("p t (h x) -> p t h x", x=65)[:, :, :, 64:65], 1.0),
        writes=("vtm_ones",))
    e.op(GP, lambda g: g.memset(bmax[:], 0.0),
         writes=("bmax_q", "bmax_k", "bmax_t", "bmax_s"))

    for w_d, w_sb, tag in ((wq_d, wq_sb, "wq"), (wk_d, wk_sb, "wk"),
                           (wv_d, wv_sb, "wv")):
        e.op(GP, lambda g, w_d=w_d, w_sb=w_sb: g.dma_start(
            out=w_sb[:], in_=w_d.rearrange("(a p) f -> p a f", p=128)),
            writes=(tag,), dma=True, dkey=tag)

    for r in range(4):
        ub, ukk = u2r[r % 2], f"u2r{r % 2}"
        e.op(SY, lambda s, r=r, ub=ub: s.dma_start(
            out=ub[:],
            in_=gath3[r * 128:(r + 1) * 128, :].rearrange("p (a t) -> p a t", a=8)),
             reads=("gath3",), writes=(ukk,), dma=True, dkey=ukk)
        for dst, dk, tag in ((q_fm, "qfm", "wq"), (k_fm, "kfm", "wk")):
            w_sb = {"wq": wq_sb, "wk": wk_sb}[tag]
            for oc in range(2):
                for tg in range(4):
                    def fn_mm(t, oc=oc, tg=tg, ub=ub, w_sb=w_sb):
                        last = None
                        for dc in range(8):
                            last = t.matmul(psB[:], w_sb[:, dc, oc * 128:(oc + 1) * 128],
                                            ub[:, dc, tg * 512:(tg + 1) * 512],
                                            start=(dc == 0), stop=(dc == 7))
                        return last
                    e.op(TE, fn_mm, reads=(tag, ukk), writes=("psB",))
                    e.op(VE, lambda v, oc=oc, tg=tg, r=r, dst=dst, dk=dk: v.tensor_copy(
                        out=dst[:, oc, r * NT + tg * 512:r * NT + (tg + 1) * 512],
                        in_=psB[:]),
                        reads=("psB",), writes=(dk,))
        for ti in range(16):
            def fn_vm(t, ti=ti, ub=ub):
                last = None
                for dc in range(8):
                    last = t.matmul(psB[:, 0:256], ub[:, dc, ti * 128:(ti + 1) * 128],
                                    wv_sb[:, dc, :], start=(dc == 0), stop=(dc == 7))
                return last
            e.op(TE, fn_vm, reads=("wv", ukk), writes=("psB",))
            e.op(VE, lambda v, ti=ti, r=r: v.tensor_copy(
                out=v_tm[:, r * 16 + ti].rearrange("p (h x) -> p h x", x=65)[:, :, 0:64],
                in_=psB[:, 0:256].rearrange("p (h d) -> p h d", d=64)),
                reads=("psB",), writes=("vtm",))

    def emit_norms(src_ap, src_keys, out_off, tag):
        e.op(GP, lambda g: g.tensor_mul(sq_sc[:], src_ap, src_ap),
             reads=src_keys, writes=("sqsc",))

        def fn_nq(t):
            last = []
            for s in range(2):
                for g in range(4):
                    last.append(t.matmul(
                        psS[0:1, s * 1024 + (g % 2) * 512:s * 1024 + (g % 2) * 512 + 512],
                        ones_col[64 * s:64 * s + 64, :],
                        sq_sc[64 * s:64 * s + 64, g * 512:(g + 1) * 512],
                        start=(g < 2), stop=(g >= 2)))
            return last
        e.op(TE, fn_nq, reads=("sqsc", "onescol"), writes=("psS0", "psS1"))

        def fn_red(v):
            v.reduce_max(bmax[:, 30:31], psS[0:1, 0:1024], axis=AX)
            return v.reduce_max(bmax[:, 31:32], psS[0:1, 1024:2048], axis=AX)
        e.op(VE, fn_red, reads=("psS0", "psS1"), writes=("bmax_t",))
        e.op(VE, lambda v: v.tensor_max(bmax[:, out_off:out_off + 2],
                                        bmax[:, out_off:out_off + 2], bmax[:, 30:32]),
             reads=("bmax_t", tag), writes=(tag,))

    for hp in range(2):
        for r in range(4):
            emit_norms(q_fm[:, hp, r * NT:(r + 1) * NT], ("qfm",), 2 * hp, "bmax_q")
            emit_norms(k_fm[:, hp, r * NT:(r + 1) * NT], ("kfm",), 8 + 2 * hp, "bmax_k")

    for hp in range(2):
        e.op(VE, lambda v, hp=hp: v.tensor_mul(bmax[:, 24:26], bmax[:, 2 * hp:2 * hp + 2],
                                               bmax[:, 8 + 2 * hp:8 + 2 * hp + 2]),
             reads=("bmax_q", "bmax_k"), writes=("bmax_s",))
        e.op(VE, lambda v: v.tensor_max(bmax[:, 26:27], bmax[:, 24:25], bmax[:, 25:26]),
             reads=("bmax_s",), writes=("bmax_s",))
        e.op(SC, lambda s: s.activation(out=bmax[:, 27:28], in_=bmax[:, 26:27],
                                        func=AF.Sqrt), reads=("bmax_s",), writes=("bmax_s",))
        e.op(VE, lambda v: v.tensor_scalar(out=bmax[:, 28:29], in0=bmax[:, 27:28],
                                           scalar1=-0.1275, scalar2=None, op0=ALU.mult),
             reads=("bmax_s",), writes=("bmax_s",))
        e.op(TE, lambda t: t.matmul(psB[:, 0:1], ones_row[:], bmax[0:1, 28:29],
                                    start=True, stop=True),
             reads=("bmax_s", "onesrow"), writes=("psB",))
        e.op(SC, lambda s, hp=hp: s.copy(out=negb2[:, hp:hp + 1], in_=psB[:, 0:1]),
             reads=("psB",), writes=("negb",))

    dmneg = p.sbuf("dmneg", [128, 4, 512], BF16)
    identp4 = C["identbf"]

    def fn_dm(g):
        last = None
        for k in range(4):
            g.memset(dmneg[:, k, :], 0.0)
            last = g.affine_select(
                out=dmneg[:, k, :], in_=dmneg[:, k, :], compare_op=ALU.is_ge,
                fill=-1e30, base=-128 * k, pattern=[[1, 512]],
                channel_multiplier=-1)
        return last
    e.op(GP, fn_dm, writes=("dmneg",))

    for hp in range(2):
        for qt in range(16):
            ntt = 4 * (qt + 1)

            def emit_s(tt):
                h = tt % 2
                diag = tt >= 4 * qt

                def fn_s2(t, hp=hp, qt=qt, tt=tt, h=h, diag=diag):
                    last = []
                    for s in range(2):
                        seg = psS[:, h * 1024 + s * 512:h * 1024 + (s + 1) * 512]
                        mm = t.matmul(seg,
                                      k_fm[64 * s:64 * s + 64, hp, tt * 128:(tt + 1) * 128],
                                      q_fm[64 * s:64 * s + 64, hp, qt * 512:(qt + 1) * 512],
                                      start=True, stop=not diag)
                        if diag:
                            k = tt - 4 * qt
                            mm = t.matmul(seg, identp4[:], dmneg[:, k, :],
                                          start=False, stop=True)
                        last.append(mm)
                    return last
                e.op(TE, fn_s2, reads=("kfm", "qfm", "identbf", "dmneg"),
                     writes=(f"psS{h}",))
                e.op(SC, lambda s, tt=tt, h=h, hp=hp: s.activation(
                    out=pt_sb[:, h * 1024:(h + 1) * 1024],
                    in_=psS[:, h * 1024:(h + 1) * 1024], func=AF.Exp,
                    bias=negb2[:, hp:hp + 1], scale=0.125),
                    reads=(f"psS{h}", "negb"), writes=(f"pt{h}",))

            def emit_pv(tt):
                h = tt % 2

                def fn_pv(t, hp=hp, tt=tt, h=h, first=(tt == 0),
                          stop=(tt == ntt - 1)):
                    out = []
                    for s in range(2):
                        lh = 2 * hp + s
                        out.append(t.matmul(
                            psA[s][0:65, :],
                            v_tm[:, tt, lh * 65:(lh + 1) * 65],
                            pt_sb[:, h * 1024 + s * 512:h * 1024 + (s + 1) * 512],
                            start=first, stop=stop, skip_group_check=True))
                    return out
                e.op(TE, fn_pv, reads=("vtm", "vtm_ones", f"pt{h}"),
                     writes=("psA0", "psA1"))

            emit_s(0)
            for tt in range(1, ntt):
                emit_s(tt)
                emit_pv(tt - 1)
            emit_pv(ntt - 1)

            par = qt % 2
            ob, obk = ot[par], f"ot{par}"
            for s in range(2):
                e.op(VE, lambda v, s=s, par=par: v.tensor_copy(
                    out=nac[par][s][:], in_=psA[s][0:65, :]),
                    reads=(f"psA{s}",), writes=(f"nac{par}{s}",))
            for s in range(2):
                e.op(VE, lambda v, s=s, par=par: v.tensor_scalar(
                    out=linv[:], in0=nac[par][s][64:65, :], scalar1=1e-35,
                    scalar2=None, op0=ALU.max),
                    reads=(f"nac{par}{s}",), writes=("linv",))
                e.op(VE, lambda v: v.reciprocal(out=linv[:], in_=linv[:]),
                     reads=("linv",), writes=("linv",))
                e.op(TE, lambda t: t.matmul(psB[0:64, :], ones_row[0:1, 0:64], linv[:],
                                            start=True, stop=True),
                     reads=("linv", "onesrow"), writes=("psB",))
                e.op(SC, lambda s_: s_.copy(out=linvb[:], in_=psB[0:64, :]),
                     reads=("psB",), writes=("linvb",))
                e.op(VE, lambda v, s=s, ob=ob, obk=obk, par=par: v.tensor_tensor(
                    out=ob[64 * s:64 * s + 64, :],
                    in0=nac[par][s][0:64, :], in1=linvb[:], op=ALU.mult),
                    reads=(f"nac{par}{s}", "linvb"), writes=(obk,))
            e.op(SY, lambda s, hp=hp, qt=qt, ob=ob, obk=obk: s.dma_start(
                out=stage4[:, hp * 8192 + qt * 512:hp * 8192 + (qt + 1) * 512],
                in_=ob[:]),
                reads=(obk,), writes=("stage4",), dma=True, dkey="oout")


def emit_p5(p, e, C, io, lvl=99):
    x2_d, gath4, cpos_d = io["x2"], io["gath4"], io["cpos"]
    wot_d, gw_d = io["WoT"], io["gate_w"]
    w1_d, w2_d, w3_d = io["m_w1"], io["m_w2"], io["m_w3"]
    sw1_d, sw2_d, sw3_d = io["sw1"], io["sw2"], io["sw3"]
    x4_d, x3_d = io["x4"], io["x3"]

    xt = [p.sbuf(f"xt{i}", [128, D], F32) for i in range(2)]
    stat = [p.sbuf(f"stat{i}", [128, 16], F32) for i in range(2)]
    ubf = [p.sbuf(f"ubf{i}", [128, D], BF16) for i in range(2)]
    u_fm = p.sbuf("ufm", [128, 8, NT], BF16)
    o_sb = u_fm
    cp_sb = p.sbuf("cpsb", [1, 1], U32)
    ident_bf = C["identbf"]

    psH1 = p.psum("psH1", [128, 1024], F32)
    psH3 = p.psum("psH3", [128, 1024], F32)
    psY = [p.psum("psY0", [128, 512], F32), p.psum("psY1", [128, 512], F32)]
    psT = p.psum("psT", [128, 1024], BF16)
    psZ = p.psum("psZ", [128, 512], F32)

    SY, TE, VE, SC, GP = ENGINES

    e.op(SY, lambda s: s.dma_start(out=cp_sb[:], in_=cpos_d[:]),
         writes=("cpsb",), dma=True, dkey="cpld")
    ogat = e.dma_sem(GP, "ogat")

    def fn_ogather(g):
        reg = g.alloc_register("posreg")
        g.load(reg, cp_sb[0:1, 0:1])
        for j in range(4):
            with g.If_eq(reg, j):
                for gsrc in range(4):
                    for ch in range(2):
                        g.dma_start(
                            out=o_sb[:, 2 * gsrc + ch, :],
                            in_=gath4[gsrc * 128:(gsrc + 1) * 128,
                                      ch * 8192 + j * NT:ch * 8192 + (j + 1) * NT]
                        ).then_inc(ogat.h, 16)
        return None
    e.op(GP, fn_ogather, reads=("gath4", "cpsb"), writes=("ufm_u",),
         self_inc=128, dkey="ogat")

    with p.scope():
        wot_sb = p.sbuf("wotsb", [128, 8, D], BF16)
        x3t = p.sbuf("x3t", [128, D], F32)
        x4t = p.sbuf("x4t", [128, D], F32)
        e.op(GP, lambda g: g.dma_start(out=wot_sb[:],
                                       in_=wot_d[:].rearrange("(a p) b -> p a b", p=128)),
             writes=("wot",), dma=True, dkey="wotload")
        x3b = [x3t, x4t]
        for i in range(NTILE):
            e.op(SY, lambda s, i=i: s.dma_start(out=xt[i % 2][:],
                                                in_=x2_d[i * 128:(i + 1) * 128, :]),
                 writes=(f"xt{i % 2}",), dma=True)
            xb = x3b[i % 2]
            xbk = ("x3t", "x4t")[i % 2]
            for ng in range(2):
                def fn_y(t, i=i, ng=ng, wot_sb=wot_sb):
                    last = None
                    for mc in range(8):
                        last = t.matmul(psY[0][:], o_sb[:, mc, i * 128:(i + 1) * 128],
                                        wot_sb[:, mc, ng * 512:(ng + 1) * 512],
                                        start=(mc == 0), stop=(mc == 7))
                    return last
                e.op(TE, fn_y, reads=("ufm_u", "wot"), writes=("psY0",))
                e.op(VE, lambda v, i=i, ng=ng, xb=xb: v.tensor_add(
                    out=xb[:, ng * 512:(ng + 1) * 512],
                    in0=xt[i % 2][:, ng * 512:(ng + 1) * 512], in1=psY[0][:]),
                    reads=(f"xt{i % 2}", "psY0"), writes=(xbk,))
            e.op(SY, lambda s, i=i, xb=xb: s.dma_start(
                out=x3_d[i * 128:(i + 1) * 128, :], in_=xb[:]),
                 reads=(xbk,), writes=(f"x3o{i}",), dma=True, dkey=f"x3out{i % 2}")
    e.barrier()

    y_acc = p.sbuf("yacc", [128, 16, D], F32)
    w1h = [p.sbuf(f"w1h{i}", [128, 8, D], BF16) for i in range(2)]
    w3h = [p.sbuf(f"w3h{i}", [128, 8, D], BF16) for i in range(2)]
    w2h_t = p.sbuf("w2h", [128, 8, D], BF16)
    w2h = [w2h_t, w2h_t]
    act_buf = p.sbuf("actbuf", [128, 8, 512], BF16)
    s1_sb = p.sbuf("s1sb", [128, 1024], BF16)
    gwt_sb = p.sbuf("gwtsb", [128, 8, 8], BF16)
    comb = p.sbuf("comb", [128, 16, 9], F32)
    gz = p.sbuf("gz", [128, 8], F32)
    gm = p.sbuf("gm", [128, 8], F32)
    negm1 = p.sbuf("negm1", [128, 1], F32)
    gsc = p.sbuf("gsc", [128, 8], F32)
    gw_sb = act_buf[:].rearrange("p a b -> p (a b)")[0:8, 0:1024]
    e.op(GP, lambda g: g.memset(y_acc[:], 0.0), writes=("yacc",))

    if lvl < 2:
        return
    emit_ln_tiles(e, p, x3_d, u_fm,
                  dict(xt=xt, stat=stat, ubf=ubf, psT=psT, ident_bf=ident_bf), ukey="u",
                  src_key=[f"x3o{i}" for i in range(16)])

    if lvl < 3:
        return
    e.op(GP, lambda g: g.dma_start(out=gw_sb, in_=gw_d[:]), writes=("gw",), dma=True)
    for dc in range(8):
        e.op(TE, lambda t, dc=dc: t.transpose(psZ[:, 0:8].bitcast(BF16)[:, 0:8],
                                              gw_sb[:, dc * 128:(dc + 1) * 128],
                                              ident_bf[0:8, 0:8]),
             reads=("gw", "identbf"), writes=("psZ",))
        e.op(VE, lambda v, dc=dc: v.tensor_copy(out=gwt_sb[:, dc, :],
                                                in_=psZ[:, 0:8].bitcast(BF16)[:, 0:8]),
             reads=("psZ",), writes=("gwt",))
    for ti in range(NTILE):
        def fn_z(t, ti=ti):
            last = None
            for dc in range(8):
                last = t.matmul(psZ[:, 16:24], u_fm[:, dc, ti * 128:(ti + 1) * 128],
                                gwt_sb[:, dc, :], start=(dc == 0), stop=(dc == 7))
            return last
        e.op(TE, fn_z, reads=("ufm_u", "gwt"), writes=("psZ",))
        e.op(VE, lambda v: v.tensor_copy(out=gz[:], in_=psZ[:, 16:24]),
             reads=("psZ",), writes=("gz",))
        e.op(VE, lambda v: v.max(out=gm[:], in_=gz[:]), reads=("gz",), writes=("gm",))
        e.op(VE, lambda v: v.tensor_scalar(out=negm1[:], in0=gm[:, 0:1], scalar1=-1.0,
                                           scalar2=None, op0=ALU.mult),
             reads=("gm",), writes=("negm1",))
        e.op(SC, lambda s: s.activation(out=gsc[:], in_=gz[:], func=AF.Exp,
                                        bias=negm1[:], scale=1.0),
             reads=("gz", "negm1"), writes=("gsc",))
        e.op(SC, lambda s: s.activation(out=negm1[:], in_=gm[:, 1:2], func=AF.Exp,
                                        bias=negm1[:], scale=1.0),
             reads=("gm", "negm1"), writes=("negm1b",))
        e.op(VE, lambda v: v.tensor_scalar_add(negm1[:], negm1[:], 1.0),
             reads=("negm1b",), writes=("negm1b",))
        e.op(VE, lambda v: v.reciprocal(out=negm1[:], in_=negm1[:]),
             reads=("negm1b",), writes=("negm1b",))
        e.op(VE, lambda v: v.tensor_scalar(out=gsc[:], in0=gsc[:], scalar1=negm1[:],
                                           scalar2=None, op0=ALU.mult),
             reads=("gsc", "negm1b"), writes=("gsc",))
        e.op(VE, lambda v: v.tensor_scalar(out=gm[:], in0=gz[:], scalar1=gm[:, 1:2],
                                           scalar2=None, op0=ALU.is_ge),
             reads=("gz", "gm"), writes=("gmask",))
        e.op(VE, lambda v, ti=ti: v.tensor_mul(comb[:, ti, 0:8], gsc[:], gm[:]),
             reads=("gsc", "gmask"), writes=("comb",))
        e.op(VE, lambda v, ti=ti: v.memset(comb[:, ti, 8:9], 1.0),
             reads=(), writes=("comb",))

    e.op(VE, lambda v: v.engine_nop(), reads=(), writes=("gw",))

    if lvl < 4:
        return
    for eh in range(18):
        ex, half = eh // 2, eh % 2
        wb1, wb3, wb2 = w1h[eh % 2], w3h[eh % 2], w2h[eh % 2]
        wk1, wk3, wk2 = f"w1h{eh % 2}", f"w3h{eh % 2}", "w2h"
        if ex < 8:
            s1d = w1_d[ex][:, half * 1024:(half + 1) * 1024]
            s3d = w3_d[ex][:, half * 1024:(half + 1) * 1024]
            s2d = w2_d[ex][half * 1024:(half + 1) * 1024, :]
        else:
            s1d = sw1_d[0][:, half * 1024:(half + 1) * 1024]
            s3d = sw3_d[0][:, half * 1024:(half + 1) * 1024]
            s2d = sw2_d[0][half * 1024:(half + 1) * 1024, :]
        e.op(GP, lambda g, s1d=s1d, wb1=wb1: g.dma_start(
            out=wb1[:], in_=s1d.rearrange("(a p) f -> p a f", p=128)),
            writes=(wk1,), dma=True, dkey=wk1)
        e.op(GP, lambda g, s3d=s3d, wb3=wb3: g.dma_start(
            out=wb3[:], in_=s3d.rearrange("(a p) f -> p a f", p=128)),
            writes=(wk3,), dma=True, dkey=wk3)
        e.op(GP, lambda g, s2d=s2d, wb2=wb2: g.dma_start(
            out=wb2[:], in_=s2d.rearrange("(a p) f -> p a f", p=128)),
            writes=(wk2,), dma=True, dkey=wk2)

        for tq in range(4):
            for fc in range(8):
                par = fc % 2
                h1 = psH1[:, par * 512:(par + 1) * 512]
                h3 = psH3[:, par * 512:(par + 1) * 512]
                s1v = s1_sb[:, par * 512:(par + 1) * 512]

                def fn_h(t, tq=tq, fc=fc, wb1=wb1, wb3=wb3, h1=h1, h3=h3):
                    last = []
                    for dc in range(8):
                        t.matmul(h1, wb1[:, dc, fc * 128:(fc + 1) * 128],
                                 u_fm[:, dc, tq * 512:(tq + 1) * 512],
                                 start=(dc == 0), stop=(dc == 7))
                    for dc in range(8):
                        last.append(t.matmul(h3, wb3[:, dc, fc * 128:(fc + 1) * 128],
                                             u_fm[:, dc, tq * 512:(tq + 1) * 512],
                                             start=(dc == 0), stop=(dc == 7)))
                    return last[-1]
                e.op(TE, fn_h, reads=(wk1, wk3, "ufm_u"),
                     writes=(f"psH1{par}", f"psH3{par}"))
                e.op(SC, lambda s, h1=h1, s1v=s1v: s.activation(
                    out=s1v, in_=h1, func=AF.Silu),
                     reads=(f"psH1{par}",), writes=(f"s1sb{par}",))
                e.op(VE, lambda v, fc=fc, h3=h3, s1v=s1v: v.tensor_mul(
                    act_buf[:, fc, :], s1v, h3),
                    reads=(f"s1sb{par}", f"psH3{par}"), writes=("actbuf",))
            for ti in range(4):
                gi = tq * 4 + ti
                for ng in range(2):
                    def fn_y(t, ti=ti, ng=ng, wb2=wb2):
                        last = None
                        for fc in range(8):
                            last = t.matmul(psY[ng][:], act_buf[:, fc, ti * 128:(ti + 1) * 128],
                                            wb2[:, fc, ng * 512:(ng + 1) * 512],
                                            start=(fc == 0), stop=(fc == 7))
                        return last
                    e.op(TE, fn_y, reads=("actbuf", wk2), writes=(f"psY{ng}",))
                    e.op(VE, lambda v, gi=gi, ng=ng, ex=ex: v.scalar_tensor_tensor(
                        out=y_acc[:, gi, ng * 512:(ng + 1) * 512], in0=psY[ng][:],
                        scalar=comb[:, gi, ex:ex + 1],
                        in1=y_acc[:, gi, ng * 512:(ng + 1) * 512],
                        op0=ALU.mult, op1=ALU.add),
                        reads=(f"psY{ng}", "comb", "yacc"), writes=("yacc",))

    for i in range(NTILE):
        e.op(SY, lambda s, i=i: s.dma_start(out=xt[0][:], in_=x3_d[i * 128:(i + 1) * 128, :]),
             reads=(f"x3o{i}",), writes=("xt0",), dma=True, dkey="x3ld")
        e.op(VE, lambda v, i=i: v.tensor_add(out=xt[1][:], in0=xt[0][:], in1=y_acc[:, i, :]),
             reads=("xt0", "yacc"), writes=("xt1",))
        e.op(SY, lambda s, i=i: s.dma_start(out=x4_d[i * 128:(i + 1) * 128, :], in_=xt[1][:]),
             reads=("xt1",), dma=True, dkey="x4out")


def shard_x(x_full):
    out = []
    for c in range(NCORES):
        b, j = c // 4, c % 4
        out.append(np.ascontiguousarray(x_full[b, j * NT:(j + 1) * NT, :]))
    return out


RG2x4 = [[0, 1, 2, 3], [4, 5, 6, 7]]
MEGA_UPTO = 5
P2_KW = {}
P5_KW = {}
MEGA_DBG = False
MEGA_NOCC = False



def build_mega():
    p = Prog()
    nc = p.nc
    e = Emit(p)
    SY, TE, VE, SC, GP = ENGINES

    ext_in = {
        "x": ([NT, D], F32),
        "Wq": ([D, D], F32), "Wk": ([D, D], F32), "Wv": ([D, D], F32),
        "Wa": ([D, H], F32), "ba": ([H], F32), "AWo": ([D, D], F32),
        "Wp1": ([D, 128], F32), "bp1": ([128], F32),
        "Wp2": ([128, 8], F32), "bp2": ([8], F32),
        "W1": ([8, D, 2048], F32), "b1": ([8, 2048], F32),
        "W2": ([8, 2048, D], F32), "b2": ([8, D], F32),
        "FWo": ([D, D], F32), "bo": ([D], F32),
        "WqT": ([D, 256], F32), "WkT": ([D, 256], F32), "WvT": ([D, 256], F32),
        "WoT": ([D, D], F32),
        "gate_w": ([8, D], F32),
        "m_w1": ([8, D, 2048], F32), "m_w2": ([8, 2048, D], F32),
        "m_w3": ([8, D, 2048], F32),
        "sw1": ([1, D, 2048], F32), "sw2": ([1, 2048, D], F32),
        "sw3": ([1, D, 2048], F32),
        "pm": ([1, 8], F32), "cpos": ([1, 1], U32),
    }
    io = {k: nc.dram_tensor(k, sh, dt, kind="ExternalInput").ap()
          for k, (sh, dt) in ext_in.items()}
    io["x4"] = nc.dram_tensor("x4", [NT, D], F32, kind="ExternalOutput").ap()
    intern = {
        "chunkS": ([16, 128, 520], F32), "chunkA": ([16, H], F32),
        "stage1": ([128, 536], F32), "gath1": ([512, 536], F32),
        "stage2": ([1, D], F32), "gath2": ([4, D], F32),
        "stage3": ([128, 8 * NT], BF16), "gath3": ([512, 8 * NT], BF16),
        "stage4": ([128, 8 * NT], BF16), "gath4": ([512, 8 * NT], BF16),
    }
    for k, (sh, dt) in intern.items():
        io[k] = nc.dram_tensor(k, sh, dt, kind="Internal").ap()
    xkind = "ExternalOutput" if MEGA_DBG else "Internal"
    for k in ("x1", "x2", "x3"):
        io[k] = nc.dram_tensor(k, [NT, D], F32, kind=xkind).ap()

    C = {
        "identbf": p.sbuf("c_identbf", [128, 128], BF16),
        "identf": p.sbuf("c_identf", [128, 128], F32),
        "msl": p.sbuf("c_msl", [128, 128], F32),
        "mle": p.sbuf("c_mle", [128, 128], F32),
        "mneg": p.sbuf("c_mneg", [128, 128], F32),
        "onesrow": p.sbuf("c_onesrow", [1, 128], F32),
        "onesrowbf": p.sbuf("c_onesrowbf", [1, 128], BF16),
        "onescolf": p.sbuf("c_onescolf", [128, 1], F32),
        "onescolbf": p.sbuf("c_onescolbf", [128, 1], BF16),
    }
    make_const(e, GP, C["identbf"][:], "identbf", "ident")
    make_const(e, GP, C["identf"][:], "identf", "ident")
    make_const(e, GP, C["msl"][:], "msl", "strict_lower")
    make_const(e, GP, C["onesrow"][:], "onesrow", "ones")
    make_const(e, GP, C["onesrowbf"][:], "onesrowbf", "ones")
    make_const(e, GP, C["onescolf"][:], "onescol", "ones")
    make_const(e, GP, C["onescolbf"][:], "onescolbf", "ones")
    e.op(GP, lambda g: g.memset(C["mle"][:], 1.0), writes=("mle",))
    e.op(GP, lambda g: g.affine_select(out=C["mle"][:], in_=C["mle"][:],
                                       compare_op=ALU.is_ge, fill=0.0, base=0,
                                       pattern=[[1, 128]], channel_multiplier=-1),
         reads=("mle",), writes=("mle",))
    e.op(GP, lambda g: g.memset(C["mneg"][:], 0.0), writes=("mneg",))
    e.op(GP, lambda g: g.affine_select(out=C["mneg"][:], in_=C["mneg"][:],
                                       compare_op=ALU.is_ge, fill=-1e30, base=0,
                                       pattern=[[1, 128]], channel_multiplier=-1),
         reads=("mneg",), writes=("mneg",))

    def collective(name, src_key, dst_key, in_ap, out_ap):
        if MEGA_NOCC:
            return ()
        ctr = e.dma_sem(GP, name)

        def fn(g):
            return g.collective_compute(
                "AllGather", mybir.AluOpType.bypass, replica_groups=RG2x4,
                ins=[in_ap.opt()], outs=[out_ap.opt()]).then_inc(ctr.h)
        e.op(GP, fn, reads=(src_key,), writes=(dst_key,), self_inc=1, dkey=name)
        return (ctr,)

    x1 = x2 = x3 = x4 = ()
    with p.scope():
        car = (p.sbuf("qfm", [128, 8, NT], BF16),
               p.sbuf("kfm", [128, 8, NT], BF16),
               p.sbuf("vtm", [128, 16, 1040], BF16),
               p.sbuf("lnatm", [128, 16 * H], F32))
        with p.scope():
            emit_p1(p, e, C, io, car)
        x1 = collective("cc1", "stage1", "gath1", io["stage1"], io["gath1"])
        e.barrier(exclude=x1)
        if MEGA_UPTO >= 2:
            with p.scope():
                emit_p2(p, e, C, io, car, **P2_KW)
            x2 = collective("cc2", "stage2", "gath2", io["stage2"], io["gath2"])
    e.barrier(exclude=x1 + x2)
    if MEGA_UPTO >= 3:
        with p.scope():
            emit_p3(p, e, C, io)
            x3 = collective("cc3", "stage3", "gath3", io["stage3"], io["gath3"])
        e.barrier(exclude=x3)
    if MEGA_UPTO >= 4:
        with p.scope():
            emit_p4(p, e, C, io)
            x4 = collective("cc4", "stage4", "gath4", io["stage4"], io["gath4"])
        e.barrier(exclude=x4)
    if MEGA_UPTO >= 5:
        with p.scope():
            emit_p5(p, e, C, io, **P5_KW)
    e.flush()
    return p


_PROG_CACHE = {}


def _prog(name, builder):
    if name not in _PROG_CACHE:
        _PROG_CACHE[name] = builder()
    return _PROG_CACHE[name]


def kernel(**inputs):
    f32 = lambda a: np.ascontiguousarray(a, dtype=np.float32)
    x = f32(inputs["L0_attn_x"])
    shards = shard_x(x)
    p = _prog("mega", build_mega)

    def wslice(w, g):
        return np.ascontiguousarray(w[256 * g:256 * (g + 1), :].T)

    com = dict(
        Wq=f32(inputs["L0_attn_Wq"]), Wk=f32(inputs["L0_attn_Wk"]),
        Wv=f32(inputs["L0_attn_Wv"]), Wa=f32(inputs["L0_attn_Wa"]),
        ba=f32(inputs["L0_attn_ba"]), AWo=f32(inputs["L0_attn_Wo"]),
        Wp1=f32(inputs["L0_ffn_Wp1"]), bp1=f32(inputs["L0_ffn_bp1"]),
        Wp2=f32(inputs["L0_ffn_Wp2"]), bp2=f32(inputs["L0_ffn_bp2"]),
        W1=f32(inputs["L0_ffn_W1"]), b1=f32(inputs["L0_ffn_b1"]),
        W2=f32(inputs["L0_ffn_W2"]), b2=f32(inputs["L0_ffn_b2"]),
        FWo=f32(inputs["L0_ffn_Wo"]), bo=f32(inputs["L0_ffn_bo"]),
        WoT=np.ascontiguousarray(f32(inputs["L1_attn_Wo"]).T),
        gate_w=f32(inputs["L1_ffn_gate_w"]),
        m_w1=f32(inputs["L1_ffn_w1"]), m_w2=f32(inputs["L1_ffn_w2"]),
        m_w3=f32(inputs["L1_ffn_w3"]),
        sw1=f32(inputs["L1_ffn_sw1"]), sw2=f32(inputs["L1_ffn_sw2"]),
        sw3=f32(inputs["L1_ffn_sw3"]),
    )
    wq1, wk1, wv1 = (f32(inputs["L1_attn_Wq"]), f32(inputs["L1_attn_Wk"]),
                     f32(inputs["L1_attn_Wv"]))
    maps = []
    for c in range(NCORES):
        g = c % 4
        pmv = np.zeros((1, 8), np.float32)
        for k in range(3):
            pmv[0, k] = 1.0 if k < g else 0.0
            pmv[0, 4 + k] = 1.0 - pmv[0, k]
        maps.append(dict(x=shards[c],
                         WqT=wslice(wq1, g), WkT=wslice(wk1, g), WvT=wslice(wv1, g),
                         pm=pmv, cpos=np.array([[g]], np.uint32), **com))
    res = run_bass_kernel_spmd(p.nc, maps, core_ids=list(range(NCORES))).results
    out = np.empty((2, 8192, D), np.float32)
    for c in range(NCORES):
        b, jp = c // 4, c % 4
        out[b, jp * NT:(jp + 1) * NT, :] = res[c]["x4"]
    return out


# version 08
# speedup vs baseline: 1.0205x; 1.0205x over previous
import numpy as np
from contextlib import ExitStack, contextmanager

import concourse.bass as bass
import concourse.mybir as mybir
from concourse.bass_utils import run_bass_kernel_spmd

F32 = mybir.dt.float32
BF16 = mybir.dt.bfloat16
U32 = mybir.dt.uint32
AF = mybir.ActivationFunctionType
ALU = mybir.AluOpType
AX = mybir.AxisListType.X

D = 1024
NT = 2048
NTILE = 16
H = 16
DH = 64
NCORES = 8
LN_EPS = 1e-5
GLA_EPS = 1e-6

ENGINES = ("sync", "tensor", "vector", "scalar", "gpsimd")


class Ctr:
    __slots__ = ("h", "n", "name")

    def __init__(self, h, name):
        self.h = h
        self.n = 0
        self.name = name


class Prog:
    def __init__(self):
        self.nc = bass.Bass("TRN2", target_bir_lowering=False, debug=False)
        self.stack = ExitStack()
        self.root = self.stack
        self._sems = {}

    def sem(self, name):
        if name not in self._sems:
            h = self.root.enter_context(self.nc.semaphore(name))
            self._sems[name] = Ctr(h, name)
        return self._sems[name]

    def _uname(self, name):
        if not hasattr(self, "_names"):
            self._names = {}
        n = self._names.get(name, 0)
        self._names[name] = n + 1
        return name if n == 0 else f"{name}__{n}"

    def sbuf(self, name, shape, dtype):
        return self.stack.enter_context(
            self.nc.sbuf_tensor(self._uname(name), shape, dtype))

    def psum(self, name, shape, dtype):
        return self.stack.enter_context(
            self.nc.psum_tensor(self._uname(name), shape, dtype))

    @contextmanager
    def scope(self):
        outer = self.stack
        self.stack = ExitStack()
        try:
            yield
        finally:
            self.stack.close()
            self.stack = outer


class Emit:

    def __init__(self, p):
        self.p = p
        self.pend = {e: [] for e in ENGINES}
        self.eng_ctr = {e: p.sem("c_" + e) for e in ENGINES if e != "sync"}
        self.dma_sems = {}
        self.last_w = {}
        self.readers = {}
        self.hw = {e: {} for e in ENGINES}

    def dma_sem(self, eng, key):
        k = (eng, key)
        if k not in self.dma_sems:
            self.dma_sems[k] = self.p.sem(f"d{len(self.dma_sems)}_{key[:12]}")
        return self.dma_sems[k]

    def op(self, eng, fn, reads=(), writes=(), dma=False, n_dma=1, self_inc=0,
           dkey=None):
        if dma or self_inc:
            if dkey is None:
                dkey = writes[0] if writes else (reads[0] if reads else "anon")
            ctr = self.dma_sem(eng, dkey)
        else:
            ctr = self.eng_ctr[eng]
        waits = {}

        def need(dep):
            if dep is None:
                return
            c, n = dep
            if n <= 0 or self.hw[eng].get(c, 0) >= n:
                return
            waits[c] = max(waits.get(c, 0), n)

        for b in reads:
            need(self.last_w.get(b))
        for b in writes:
            need(self.last_w.get(b))
            for c, n in self.readers.get(b, {}).items():
                need((c, n))

        for c, n in waits.items():
            self.hw[eng][c] = max(self.hw[eng].get(c, 0), n)
        wl = [(c.h, n) for c, n in waits.items()]

        if self_inc:
            ctr.n += self_inc
        else:
            ctr.n += 16 * n_dma if dma else 1
        my_cnt = ctr.n
        inc_k = 16 if (dma or self_inc) else 1

        def closure(e, fn=fn, wl=wl, ctr=ctr, self_inc=self_inc, inc_k=inc_k):
            for (sh, n) in wl:
                e.wait_ge(sh, n)
            last = fn(e)
            if self_inc:
                return
            assert last is not None, "op fn must return its last instruction(s)"
            if not isinstance(last, (list, tuple)):
                last = [last]
            if inc_k == 16:
                for ins in last:
                    ins.then_inc(ctr.h, inc_k)
            else:
                last[-1].then_inc(ctr.h, 1)

        self.pend[eng].append(closure)
        for b in reads:
            self.readers.setdefault(b, {})[ctr] = my_cnt
        for b in writes:
            self.last_w[b] = (ctr, my_cnt)
            self.readers[b] = {}
        return (ctr, my_cnt)

    def barrier(self, exclude=()):
        snap = [(c, c.n) for c in
                list(self.eng_ctr.values()) + list(self.dma_sems.values())
                if c.n > 0 and c not in exclude]
        for eng in ENGINES:
            def cl(e, snap=snap):
                for c, n in snap:
                    e.wait_ge(c.h, n)
            self.pend[eng].append(cl)
            for c, n in snap:
                self.hw[eng][c] = max(self.hw[eng].get(c, 0), n)

    def flush(self):
        with self.p.nc.Block() as block:
            for e in ENGINES:
                ops = self.pend[e]
                if not ops:
                    continue

                def run_all(eng, ops=ops):
                    for cl in ops:
                        cl(eng)
                getattr(block, e)(run_all)
            final = [(c.h, c.n) for c in
                     list(self.eng_ctr.values()) + list(self.dma_sems.values()) if c.n > 0]

            def tail(eng):
                for (sh, n) in final:
                    eng.wait_ge(sh, n)
            block.sync(tail)
        self.pend = {e: [] for e in ENGINES}


def make_const(e, g_eng, ap, key, kind, sq=128):
    if kind == "ones":
        e.op(g_eng, lambda g: g.memset(ap, 1.0), writes=(key,))
        return
    fill0, cmp = (0.0, ALU.not_equal) if kind == "ident" else (1.0, ALU.is_gt)
    if kind == "ident":
        e.op(g_eng, lambda g: g.memset(ap, 0.0), writes=(key,))
        e.op(g_eng, lambda g: g.affine_select(out=ap, in_=ap, compare_op=ALU.not_equal,
                                              fill=1.0, base=0, pattern=[[-1, sq]],
                                              channel_multiplier=1),
             reads=(key,), writes=(key,))
    else:
        e.op(g_eng, lambda g: g.memset(ap, 1.0), writes=(key,))
        e.op(g_eng, lambda g: g.affine_select(out=ap, in_=ap, compare_op=ALU.is_gt,
                                              fill=0.0, base=0, pattern=[[-1, sq]],
                                              channel_multiplier=1),
             reads=(key,), writes=(key,))


def emit_ln_tiles(e, p, x_d, u_fm, bufs, n_tiles=NTILE, stat_out=None, ukey="u",
                  src_key=None):
    SY, TE, VE, SC = "sync", "tensor", "vector", "scalar"
    xt, stat, ubf, psT, ident_bf = (bufs["xt"], bufs["stat"], bufs["ubf"],
                                    bufs["psT"], bufs["ident_bf"])
    for i in range(n_tiles):
        xb, sb, ub = xt[i % 2], stat[i % 2], ubf[i % 2]
        xk, sk, uk = f"xt{i % 2}", f"stat{i % 2}", f"ubf{i % 2}"
        e.op(SY, lambda s, i=i, xb=xb: s.dma_start(
            out=xb[:], in_=x_d[i * 128:(i + 1) * 128, :]), writes=(xk,), dma=True,
            reads=(() if src_key is None else tuple(src_key)))
        e.op(VE, lambda v, xb=xb, sb=sb: v.bn_stats(out=sb[:, 0:6], in_=xb[:, 0:512]),
             reads=(xk,), writes=(sk,))
        e.op(VE, lambda v, xb=xb, sb=sb: v.bn_stats(out=sb[:, 6:12], in_=xb[:, 512:1024]),
             reads=(xk,), writes=(sk,))
        e.op(VE, lambda v, sb=sb: v.bn_aggr(out=sb[:, 12:14], in_=sb[:, 0:12]),
             reads=(sk,), writes=(sk,))
        e.op(VE, lambda v, sb=sb: v.tensor_scalar_add(sb[:, 13:14], sb[:, 13:14],
                                                      float(LN_EPS)),
             reads=(sk,), writes=(sk,))
        e.op(SC, lambda s, sb=sb: s.activation(out=sb[:, 14:15], in_=sb[:, 13:14],
                                               func=AF.Ln),
             reads=(sk,), writes=(sk,))
        e.op(SC, lambda s, sb=sb: s.activation(out=sb[:, 15:16], in_=sb[:, 14:15],
                                               func=AF.Exp, scale=-0.5),
             reads=(sk,), writes=(sk,))
        e.op(VE, lambda v, xb=xb, sb=sb, ub=ub: v.tensor_scalar(
            out=ub[:], in0=xb[:], scalar1=sb[:, 12:13], scalar2=sb[:, 15:16],
            op0=ALU.subtract, op1=ALU.mult), reads=(xk, sk), writes=(uk,))
        if stat_out is not None:
            e.op(VE, lambda v, sb=sb, i=i: v.tensor_copy(
                out=stat_out[:, 2 * i:2 * i + 2],
                in_=sb[:].rearrange("p (a b) -> p a b", b=1)[:, (12, 15), :].squeeze(2)
                if False else sb[:, 12:16:3]),
                reads=(sk,), writes=("lnstat_" + ukey,))

        def fn_tr(t, ub=ub):
            last = None
            for dc in range(8):
                last = t.transpose(psT[:, dc * 128:(dc + 1) * 128],
                                   ub[:, dc * 128:(dc + 1) * 128], ident_bf[:])
            return last
        e.op(TE, fn_tr, reads=(uk, "identbf"), writes=("psT",))
        e.op(VE, lambda v, i=i: v.tensor_copy(
            out=u_fm[:, :, i * 128:(i + 1) * 128],
            in_=psT[:].rearrange("p (a b) -> p a b", b=128)),
            reads=("psT",), writes=("ufm_" + ukey,))


def emit_proj_fm(e, dst, dk, u_fm, wsel, wkeys, psB, elu_bufs=None,
                 n_tg=4, psb_key="psB", ukey="ufm_u"):
    TE, VE, SC = "tensor", "vector", "scalar"
    for oc in range(8):
        for tg in range(n_tg):
            def fn_mm(t, oc=oc, tg=tg):
                last = None
                for dc in range(8):
                    last = t.matmul(psB[:], wsel(dc, oc),
                                    u_fm[:, dc, tg * 512:(tg + 1) * 512],
                                    start=(dc == 0), stop=(dc == 7))
                return last
            e.op(TE, fn_mm, reads=wkeys + (ukey,), writes=(psb_key,))
            if elu_bufs is not None:
                elu_sc, elu_sc2 = elu_bufs
                e.op(VE, lambda v: v.tensor_scalar(out=elu_sc[:], in0=psB[:], scalar1=0.0,
                                                   scalar2=None, op0=ALU.min),
                     reads=(psb_key,), writes=("elusc",))
                e.op(SC, lambda s: s.activation(out=elu_sc2[:], in_=elu_sc[:], func=AF.Exp),
                     reads=("elusc",), writes=("elusc2",))
                e.op(VE, lambda v, oc=oc, tg=tg: v.scalar_tensor_tensor(
                    out=dst[:, oc, tg * 512:(tg + 1) * 512], in0=psB[:], scalar=0.0,
                    in1=elu_sc2[:], op0=ALU.max, op1=ALU.add),
                    reads=(psb_key, "elusc2"), writes=(dk,))
            else:
                e.op(VE, lambda v, oc=oc, tg=tg: v.tensor_copy(
                    out=dst[:, oc, tg * 512:(tg + 1) * 512], in_=psB[:]),
                    reads=(psb_key,), writes=(dk,))


def emit_v_tm(e, v_tm, vk, u_fm, udk, wsel, wkeys, psB, psb_key="psB"):
    TE, VE = "tensor", "vector"
    for ti in range(NTILE):
        for ng in range(2):
            def fn_mm(t, ti=ti, ng=ng):
                last = None
                for dc in range(8):
                    last = t.matmul(psB[:], u_fm[:, dc, ti * 128:(ti + 1) * 128],
                                    wsel(dc, ng), start=(dc == 0), stop=(dc == 7))
                return last
            e.op(TE, fn_mm, reads=wkeys + ("ufm_" + udk,), writes=(psb_key,))
            e.op(VE, lambda v, ti=ti, ng=ng: v.tensor_copy(
                out=v_tm[:, ti].rearrange("p (h x) -> p h x", x=65)[:, ng * 8:(ng + 1) * 8, 0:64],
                in_=psB[:].rearrange("p (h d) -> p h d", d=64)),
                reads=(psb_key,), writes=(vk,))


def emit_p1(p, e, C, io, car):
    x_d, wq_d, wk_d, wv_d = io["x"], io["Wq"], io["Wk"], io["Wv"]
    wa_d, ba_d = io["Wa"], io["ba"]
    chs_d, cha_d, stage1 = io["chunkS"], io["chunkA"], io["stage1"]
    q_fm, k_fm, v_tm, lna_tm = car

    xt = [p.sbuf(f"xt{i}", [128, D], F32) for i in range(2)]
    stat = [p.sbuf(f"stat{i}", [128, 16], F32) for i in range(2)]
    ubf = [p.sbuf(f"ubf{i}", [128, D], BF16) for i in range(2)]
    u_fm = p.sbuf("ufm", [128, 8, NT], BF16)
    lna_fm = p.sbuf("lnafm", [16, NT], F32)
    a_fm = p.sbuf("afm", [16, NT], F32)
    wmat = p.sbuf("wmat", [128, 16, D], BF16)
    wa_sb = p.sbuf("wasb", [128, 8 * H], BF16)
    ba2_sb = p.sbuf("ba2sb", [16, 1], F32)
    elu_sc = p.sbuf("elusc", [128, 512], F32)
    elu_sc2 = p.sbuf("elusc2", [128, 512], F32)
    ident_bf, ident_f, msl = C["identbf"], C["identf"], C["msl"]
    ones_col, ones_row = C["onescolf"], C["onesrow"]
    k_tm = p.sbuf("ktm", [128, D], BF16)
    khat = p.sbuf("khat", [128, D], BF16)
    escale = p.sbuf("escale", [128, H], BF16)
    arow = p.sbuf("arow", [1, H], F32)
    s_run = p.sbuf("srun", [128, 520], F32)
    s_stash = p.sbuf("sstash", [128, 520], F32)

    psB = p.psum("psB", [128, 512], F32)
    psT = p.psum("psT", [128, 1024], BF16)
    psS = p.psum("psS", [128, 1024], F32)
    psR = p.psum("psR", [128, 128], F32)

    SY, TE, VE, SC, GP = ENGINES

    e.op(GP, lambda g: g.memset(
        v_tm[:].rearrange("p h (x y) -> p h x y", y=65)[:, :, :, 64:65], 1.0),
        writes=("vtm_ones",))

    def load_w(dst_half, w_d, tag):
        def fn(g):
            return [g.dma_start(out=wmat[:, dst_half * 8 + dc, :],
                                in_=w_d[dc * 128:(dc + 1) * 128, :]) for dc in range(8)]
        e.op(GP, fn, writes=(tag,), dma=True, n_dma=8)

    load_w(0, wk_d, "wmatA")
    load_w(1, wq_d, "wmatB")

    def fn_wa(g):
        ins = [g.dma_start(out=wa_sb[:, dc * H:(dc + 1) * H],
                           in_=wa_d[dc * 128:(dc + 1) * 128, :]) for dc in range(8)]
        ins.append(g.dma_start(out=ba2_sb[:], in_=ba_d[:].rearrange("(h x) -> h x", x=1)))
        return ins
    e.op(GP, fn_wa, writes=("wasb", "ba2"), dma=True, n_dma=9)
    e.op(VE, lambda v: v.tensor_scalar_add(ba2_sb[:], ba2_sb[:], 2.0),
         reads=("ba2",), writes=("ba2",))

    emit_ln_tiles(e, p, x_d, u_fm,
                  dict(xt=xt, stat=stat, ubf=ubf, psT=psT, ident_bf=ident_bf), ukey="u")

    emit_proj_fm(e, k_fm, "kfm", u_fm,
                 lambda dc, oc: wmat[:, dc, oc * 128:(oc + 1) * 128], ("wmatA",),
                 psB, elu_bufs=(elu_sc, elu_sc2))
    load_w(0, wv_d, "wmatA")
    emit_proj_fm(e, q_fm, "qfm", u_fm,
                 lambda dc, oc: wmat[:, 8 + dc, oc * 128:(oc + 1) * 128], ("wmatB",),
                 psB, elu_bufs=(elu_sc, elu_sc2))

    emit_v_tm(e, v_tm, "vtm", u_fm, "u",
              lambda dc, ng: wmat[:, dc, ng * 512:(ng + 1) * 512], ("wmatA",), psB)

    for tg in range(4):
        def fn_amm(t, tg=tg):
            last = None
            for dc in range(8):
                last = t.matmul(psB[0:16, :], wa_sb[:, dc * H:(dc + 1) * H],
                                u_fm[:, dc, tg * 512:(tg + 1) * 512],
                                start=(dc == 0), stop=(dc == 7))
            return last
        e.op(TE, fn_amm, reads=("wasb", "ufm_u"), writes=("psB",))
        e.op(SC, lambda s, tg=tg: s.activation(
            out=a_fm[:, tg * 512:(tg + 1) * 512], in_=psB[0:16, :], func=AF.Sigmoid,
            bias=ba2_sb[:], scale=1.0), reads=("psB", "ba2"), writes=("afm",))
    e.op(SC, lambda s: s.activation(out=lna_fm[:], in_=a_fm[:], func=AF.Ln),
         reads=("afm",), writes=("lnafm",))

    for grp in range(4):
        def fn_tr(t, grp=grp):
            last = None
            for cc in range(4):
                c = grp * 4 + cc
                last = t.transpose(psR[:, cc * 16:(cc + 1) * 16],
                                   lna_fm[:, c * 128:(c + 1) * 128], ident_f[0:16, 0:16])
            return last
        e.op(TE, fn_tr, reads=("lnafm", "identf"), writes=("psR",))
        e.op(VE, lambda v, grp=grp: v.tensor_copy(
            out=lna_tm[:, grp * 64:(grp + 1) * 64], in_=psR[:, 0:64]),
            reads=("psR",), writes=("lnatm",))

    e.op(VE, lambda v: v.memset(s_run[:], 0.0), writes=("srun",))
    for c in range(16):
        lna_c = lna_tm[:, c * 16:(c + 1) * 16]

        def fn_rev(t, lna_c=lna_c):
            t.matmul(psR[:, 0:16], msl[:], lna_c, start=True, stop=True)
            return t.matmul(psR[0:1, 16:32], ones_col[:], lna_c, start=True, stop=True)
        e.op(TE, fn_rev, reads=("lnatm", "msl", "onescol"), writes=("psR",))
        e.op(SC, lambda s: s.activation(out=escale[:], in_=psR[:, 0:16], func=AF.Exp),
             reads=("psR",), writes=("escale",))
        e.op(SC, lambda s: s.activation(out=arow[:], in_=psR[0:1, 16:32], func=AF.Exp),
             reads=("psR",), writes=("arow",))

        def fn_ktr(t, c=c):
            last = None
            for dc in range(8):
                last = t.transpose(psT[:, dc * 128:(dc + 1) * 128],
                                   k_fm[:, dc, c * 128:(c + 1) * 128], ident_bf[:])
            return last
        e.op(TE, fn_ktr, reads=("kfm", "identbf"), writes=("psT",))
        e.op(VE, lambda v: v.tensor_copy(out=k_tm[:], in_=psT[:]),
             reads=("psT",), writes=("ktm",))
        e.op(VE, lambda v: v.tensor_tensor(
            out=khat[:].rearrange("p (h d) -> p h d", d=64),
            in0=k_tm[:].rearrange("p (h d) -> p h d", d=64),
            in1=escale[:].unsqueeze(2).broadcast_to([128, 16, 64]), op=ALU.mult),
            reads=("ktm", "escale"), writes=("khat",))

        def fn_sadd(t, c=c):
            last = None
            for h in range(16):
                pb = 64 * (h % 2)
                j = h // 2
                off = (j // 4) * 512 + (j % 4) * 65
                last = t.matmul(psS[pb:pb + 64, off:off + 65],
                                khat[:, h * 64:(h + 1) * 64],
                                v_tm[:, c, h * 65:(h + 1) * 65],
                                start=True, stop=True)
            return last
        e.op(TE, fn_sadd, reads=("khat", "vtm", "vtm_ones"), writes=("psS",))

        def fn_abc(t):
            t.matmul(psR[0:64, 32:40], ones_row[0:1, 0:64], arow[:, 0:8],
                     start=True, stop=True)
            return t.matmul(psR[64:128, 32:40], ones_row[0:1, 0:64], arow[:, 8:16],
                            start=True, stop=True)
        e.op(TE, fn_abc, reads=("arow", "onesrow"), writes=("psR",))

        psS_v = psS[:].rearrange("p (b s) -> p b s", s=512)[:, :, 0:260]\
            .rearrange("p b (j x) -> p b j x", x=65)
        e.op(VE, lambda v, psS_v=psS_v: v.tensor_copy(
            out=s_stash[:].rearrange("p (b j x) -> p b j x", b=2, j=4), in_=psS_v),
             reads=("psS",), writes=("sstash",))
        e.op(VE, lambda v: v.tensor_tensor(
            out=s_run[:].rearrange("p (j x) -> p j x", x=65),
            in0=s_run[:].rearrange("p (j x) -> p j x", x=65),
            in1=psR[0:128, 32:40].unsqueeze(2).broadcast_to([128, 8, 65]),
            op=ALU.mult), reads=("srun", "psR"), writes=("srun",))
        e.op(VE, lambda v, psS_v=psS_v: v.tensor_tensor(
            out=s_run[:].rearrange("p (b j x) -> p b j x", b=2, j=4),
            in0=s_run[:].rearrange("p (b j x) -> p b j x", b=2, j=4),
            in1=psS_v, op=ALU.add),
             reads=("srun", "psS"), writes=("srun",))

        e.op(SY, lambda s, c=c: s.dma_start(out=chs_d[c], in_=s_stash[:]),
             reads=("sstash",), dma=True)
        e.op(SY, lambda s, c=c: s.dma_start(out=cha_d[c:c + 1, :], in_=arow[:]),
             reads=("arow",), dma=True)

    def fn_ac(t):
        last = None
        for c in range(16):
            last = t.matmul(psR[0:1, 48:64], ones_col[:], lna_tm[:, c * 16:(c + 1) * 16],
                            start=(c == 0), stop=(c == 15))
        return last
    e.op(TE, fn_ac, reads=("lnatm", "onescol"), writes=("psR",))
    e.op(SC, lambda s: s.activation(out=arow[:], in_=psR[0:1, 48:64], func=AF.Exp),
         reads=("psR",), writes=("arow",))

    z16 = p.sbuf("z16", [128, 16], F32)
    e.op(GP, lambda g: g.memset(z16[:], 0.0), writes=("z16",))
    e.op(VE, lambda v: v.tensor_copy(out=z16[0:1, :], in_=arow[:]),
         reads=("arow", "z16"), writes=("z16",))
    e.op(SY, lambda s: s.dma_start(out=stage1[:, 0:520], in_=s_run[:]),
         reads=("srun",), writes=("stage1",), dma=True, dkey="stage1")
    e.op(SY, lambda s: s.dma_start(out=stage1[:, 520:536], in_=z16[:]),
         reads=("z16",), writes=("stage1",), dma=True, dkey="stage1")


def emit_p2(p, e, C, io, car, upto=16, lvl=99):
    x_d, wo_d = io["x"], io["AWo"]
    chs_d, cha_d = io["chunkS"], io["chunkA"]
    gath1, pm_d = io["gath1"], io["pm"]
    x1_d, stage2 = io["x1"], io["stage2"]
    q_fm, k_fm, v_tm, lna_tm = car

    sprev = p.sbuf("sprev", [128, 3, 536], F32)
    aprev = p.sbuf("aprev", [1, 3 * H], F32)
    pm_sb = p.sbuf("pmsb", [1, 8], F32)
    pmb = p.sbuf("pmb", [128, 8], F32)
    chs = [p.sbuf(f"chs{i}", [128, 520], F32) for i in range(2)]
    cha = p.sbuf("cha", [1, 16 * H], F32)
    wo_sb = p.sbuf("wosb", [128, 8, D], BF16)
    s_run = p.sbuf("srun", [128, 520], F32)
    s_run_bf = p.sbuf("srunbf", [128, 520], BF16)
    ident_bf, ident_f = C["identbf"], C["identf"]
    mle, msl, mneg = C["mle"], C["msl"], C["mneg"]
    ones_row = C["onesrow"]
    lexp_tm = p.sbuf("lexptm", [128, H], F32)
    msl_sc = p.sbuf("mslsc", [128, 256], F32)
    dpair = p.sbuf("dpair", [128, 256], BF16)
    apair = p.sbuf("apair", [128, 256], BF16)
    apair_v = apair[:].rearrange("p (b s) -> p b s", s=128)
    cr_sb = p.sbuf("crsb", [128, 130], F32)
    n_sb = p.sbuf("nsb", [128, 16, 65], F32)
    dr_sb = p.sbuf("drsb", [128, H], F32)
    obuf = p.sbuf("obuf", [128, D], BF16)
    attn_fm = p.sbuf("attnfm", [128, 8, 128], BF16)
    xt = [p.sbuf(f"xt{i}", [128, D], F32) for i in range(2)]
    x1t = p.sbuf("x1t", [128, D], F32)
    u1t = p.sbuf("u1t", [128, D], F32)
    stat = p.sbuf("stat0", [128, 16], F32)

    psBD = p.psum("psBD", [128, 1024], F32)
    psN = p.psum("psN", [128, 1536], F32)
    psY = p.psum("psY", [128, 512], F32)
    psT = p.psum("psT", [128, 1024], BF16)
    psR = p.psum("psR", [128, 512], F32)

    SY, TE, VE, SC, GP = ENGINES

    def fn_ldprev(s):
        return [s.dma_start(out=sprev[:, k, :], in_=gath1[k * 128:(k + 1) * 128, :])
                for k in range(3)]
    e.op(SY, fn_ldprev, reads=("gath1",), writes=("sprev",), dma=True, n_dma=3)
    e.op(SY, lambda s: s.dma_start(out=pm_sb[:], in_=pm_d[:]),
         writes=("pm",), dma=True, dkey="pmld")
    e.op(SY, lambda s: s.dma_start(out=cha[:], in_=cha_d[:].rearrange("k h -> (k h)").rearrange("(o n) -> o n", o=1)), writes=("cha",), dma=True)
    e.op(GP, lambda g: g.dma_start(out=wo_sb[:],
                                   in_=wo_d[:].rearrange("(a p) b -> p a b", p=128)),
         writes=("wosb",), dma=True)

    def emit_abc(src_row, tag, extra=()):
        def fn(t):
            t.matmul(psR[0:64, 16:24], ones_row[0:1, 0:64], src_row[:, 0:16:2],
                     start=True, stop=True)
            return t.matmul(psR[64:128, 16:24], ones_row[0:1, 0:64], src_row[:, 1:16:2],
                            start=True, stop=True)
        e.op(TE, fn, reads=(tag, "onesrow") + tuple(extra), writes=("psR2k",))

    e.op(TE, lambda t: t.matmul(psR[:, 32:40], ones_row[0:1, :], pm_sb[0:1, :],
                                start=True, stop=True),
         reads=("pm", "onesrow"), writes=("psR2k",))
    e.op(VE, lambda v: v.tensor_copy(out=pmb[:], in_=psR[:, 32:40]),
         reads=("psR2k",), writes=("pmb",))
    def fn_aprime(v):
        return [v.tensor_scalar(out=aprev[0:1, k * 16:(k + 1) * 16],
                                in0=sprev[0:1, k, 520:536],
                                scalar1=pm_sb[0:1, k:k + 1],
                                scalar2=pm_sb[0:1, 4 + k:5 + k],
                                op0=ALU.mult, op1=ALU.add) for k in range(3)]
    e.op(VE, fn_aprime, reads=("sprev", "pm"), writes=("aprev",))
    e.op(VE, lambda v: v.tensor_scalar(out=s_run[:], in0=sprev[:, 0, 0:520],
                                       scalar1=pmb[:, 0:1], scalar2=None, op0=ALU.mult),
         reads=("sprev", "pmb"), writes=("srun",))
    for k in range(1, 3):
        emit_abc(aprev[0:1, k * 16:(k + 1) * 16], "aprev")
        e.op(VE, lambda v: v.tensor_tensor(
            out=s_run[:].rearrange("p (j x) -> p j x", x=65),
            in0=s_run[:].rearrange("p (j x) -> p j x", x=65),
            in1=psR[0:128, 16:24].unsqueeze(2).broadcast_to([128, 8, 65]),
            op=ALU.mult), reads=("srun", "psR2k"), writes=("srun",))
        e.op(VE, lambda v, k=k: v.scalar_tensor_tensor(
            out=s_run[:], in0=sprev[:, k, 0:520], scalar=pmb[:, k:k + 1],
            in1=s_run[:], op0=ALU.mult, op1=ALU.add),
             reads=("srun", "sprev", "pmb"), writes=("srun",))

    for c in range(upto):
        lna_c = lna_tm[:, c * 16:(c + 1) * 16]
        if lvl >= 0.2:
            e.op(SY, lambda s, c=c: s.dma_start(out=chs[c % 2][:], in_=chs_d[c]),
                 writes=(f"chs{c % 2}",), dma=True)

        if lvl >= 0.4:
            e.op(VE, lambda v: v.tensor_copy(out=s_run_bf[:], in_=s_run[:]),
                 reads=("srun",), writes=("srunbf",))

        if lvl >= 0.6:
            def fn_l(t, lna_c=lna_c):
                return t.matmul(psR[:, 0:16], mle[:], lna_c, start=True, stop=True)
            e.op(TE, fn_l, reads=("lnatm", "mle"), writes=("psR2k",))
        if 0.7 <= lvl < 0.72:
            e.op(VE, lambda v: v.tensor_copy(out=lexp_tm[:], in_=psR[:, 0:16]),
                 reads=("psR2k",), writes=("lexptm",))
        if 0.72 <= lvl < 0.74:
            e.op(SC, lambda s: s.copy(out=lexp_tm[:], in_=psR[:, 0:16]),
                 reads=("psR2k",), writes=("lexptm",))
        if 0.74 <= lvl < 0.76:
            e.op(SC, lambda s: s.activation(out=lexp_tm[:], in_=s_run[:, 0:16],
                                            func=AF.Exp),
                 reads=("srun",), writes=("lexptm",))
        if lvl >= 0.8:
            e.op(SC, lambda s: s.activation(out=lexp_tm[:], in_=psR[:, 0:16], func=AF.Exp),
                 reads=("psR2k",), writes=("lexptm",))

        for hp in range(8 if lvl >= 2 else 0):
            def fn_msc(v, hp=hp, lna_c=lna_c):
                return [v.tensor_scalar(out=msl_sc[:, j * 128:(j + 1) * 128], in0=msl[:],
                                        scalar1=lna_c[:, 2 * hp + j:2 * hp + j + 1],
                                        scalar2=None, op0=ALU.mult) for j in range(2)]
            e.op(VE, fn_msc, reads=("msl", "lnatm"), writes=("mslsc",))

            def fn_darg(t, hp=hp):
                last = []
                for j in range(2):
                    t.matmul(psBD[:, 256 + j * 512:384 + j * 512],
                             msl_sc[:, j * 128:(j + 1) * 128], mle[:],
                             start=True, stop=False)
                    last.append(t.matmul(psBD[:, 256 + j * 512:384 + j * 512],
                                         ident_f[:], mneg[:], start=False, stop=True))
                return last[-1]
            e.op(TE, fn_darg, reads=("mslsc", "mle", "mneg", "identf"), writes=("psBD",))

            def fn_dexp(s, hp=hp):
                return [s.activation(out=dpair[:, j * 128:(j + 1) * 128],
                                     in_=psBD[:, 256 + j * 512:384 + j * 512], func=AF.Exp)
                        for j in range(2)]
            e.op(SC, fn_dexp, reads=("psBD",), writes=("dpair",))

            def fn_qk(t, c=c, hp=hp):
                last = []
                for s in range(2):
                    pb = 64 * s
                    last.append(t.matmul(
                        psBD[:, s * 512:s * 512 + 128],
                        k_fm[pb:pb + 64, hp, c * 128:(c + 1) * 128],
                        q_fm[pb:pb + 64, hp, c * 128:(c + 1) * 128],
                        start=True, stop=True))
                return last
            e.op(TE, fn_qk, reads=("kfm", "qfm"), writes=("psBD",))
            psBD_qk = psBD[:].rearrange("p (b s) -> p b s", s=512)[:, :, 0:128]
            dpair_v = dpair[:].rearrange("p (b s) -> p b s", s=128)
            e.op(VE, lambda v: v.tensor_tensor(out=apair_v, in0=psBD_qk,
                                               in1=dpair_v, op=ALU.mult),
                 reads=("psBD", "dpair"), writes=("apair",))

            if lvl < 3:
                continue
            def fn_n(t, c=c, hp=hp):
                last = []
                for s in range(2):
                    h = 2 * hp + s
                    t.matmul(psN[:, s * 65:(s + 1) * 65], apair[:, s * 128:(s + 1) * 128],
                             v_tm[:, c, h * 65:(h + 1) * 65], start=True, stop=True)
                    pb = 64 * s
                    last.append(t.matmul(
                        psN[:, 512 * (s + 1):512 * (s + 1) + 65],
                        q_fm[pb:pb + 64, hp, c * 128:(c + 1) * 128],
                        s_run_bf[pb:pb + 64, hp * 65:(hp + 1) * 65],
                        start=True, stop=True))
                return last
            e.op(TE, fn_n, reads=("apair", "vtm", "qfm", "srunbf"), writes=("psN",))

            def fn_cr(s):
                return [s.copy(out=cr_sb[:, j * 65:(j + 1) * 65],
                               in_=psN[:, 512 * (j + 1):512 * (j + 1) + 65])
                        for j in range(2)]
            e.op(SC, fn_cr, reads=("psN",), writes=("crsb",))

            def fn_comb(v, hp=hp):
                return [v.scalar_tensor_tensor(
                    out=n_sb[:, 2 * hp + s, :], in0=cr_sb[:, s * 65:(s + 1) * 65],
                    scalar=lexp_tm[:, 2 * hp + s:2 * hp + s + 1],
                    in1=psN[:, s * 65:(s + 1) * 65], op0=ALU.mult, op1=ALU.add)
                    for s in range(2)]
            e.op(VE, fn_comb, reads=("crsb", "lexptm", "psN"), writes=("nsb",))

        if lvl < 4:
            continue
        e.op(VE, lambda v: v.tensor_scalar_add(dr_sb[:], n_sb[:, :, 64], float(GLA_EPS)),
             reads=("nsb",), writes=("drsb",))
        e.op(VE, lambda v: v.reciprocal(out=dr_sb[:], in_=dr_sb[:]),
             reads=("drsb",), writes=("drsb",))
        e.op(VE, lambda v: v.tensor_tensor(
            out=obuf[:].rearrange("p (h d) -> p h d", d=64),
            in0=n_sb[:, :, 0:64],
            in1=dr_sb[:].unsqueeze(2).broadcast_to([128, 16, 64]), op=ALU.mult),
            reads=("nsb", "drsb"), writes=("obuf",))

        emit_abc(cha[0:1, c * 16:(c + 1) * 16], "cha")
        e.op(VE, lambda v: v.tensor_tensor(
            out=s_run[:].rearrange("p (j x) -> p j x", x=65),
            in0=s_run[:].rearrange("p (j x) -> p j x", x=65),
            in1=psR[0:128, 16:24].unsqueeze(2).broadcast_to([128, 8, 65]),
            op=ALU.mult), reads=("srun", "psR2k"), writes=("srun",))
        e.op(VE, lambda v, c=c: v.tensor_tensor(out=s_run[:], in0=s_run[:],
                                                in1=chs[c % 2][:], op=ALU.add),
             reads=("srun", f"chs{c % 2}"), writes=("srun",))

        def fn_tr(t):
            last = None
            for dc in range(8):
                last = t.transpose(psT[:, dc * 128:(dc + 1) * 128],
                                   obuf[:, dc * 128:(dc + 1) * 128], ident_bf[:])
            return last
        e.op(TE, fn_tr, reads=("obuf", "identbf"), writes=("psT",))
        e.op(VE, lambda v: v.tensor_copy(
            out=attn_fm[:], in_=psT[:].rearrange("p (a b) -> p a b", b=128)),
            reads=("psT",), writes=("attnfm",))

        e.op(SY, lambda s, c=c: s.dma_start(out=xt[c % 2][:],
                                            in_=x_d[c * 128:(c + 1) * 128, :]),
             writes=(f"xt{c % 2}",), dma=True)

        for ng in range(2):
            def fn_wo(t, ng=ng):
                last = None
                for pc in range(8):
                    last = t.matmul(psY[:], attn_fm[:, pc, :],
                                    wo_sb[:, pc, ng * 512:(ng + 1) * 512],
                                    start=(pc == 0), stop=(pc == 7))
                return last
            e.op(TE, fn_wo, reads=("attnfm", "wosb"), writes=("psY",))
            e.op(VE, lambda v, c=c, ng=ng: v.tensor_tensor(
                out=x1t[:, ng * 512:(ng + 1) * 512],
                in0=xt[c % 2][:, ng * 512:(ng + 1) * 512],
                in1=psY[:], op=ALU.add),
                reads=(f"xt{c % 2}", "psY"), writes=("x1t",))
        e.op(SY, lambda s, c=c: s.dma_start(out=x1_d[c * 128:(c + 1) * 128, :],
                                            in_=x1t[:]),
             reads=("x1t",), dma=True, dkey="x1out")

        if c == 15:
            e.op(VE, lambda v: v.bn_stats(out=stat[:, 0:6], in_=x1t[:, 0:512]),
                 reads=("x1t",), writes=("stat0",))
            e.op(VE, lambda v: v.bn_stats(out=stat[:, 6:12], in_=x1t[:, 512:1024]),
                 reads=("x1t",), writes=("stat0",))
            e.op(VE, lambda v: v.bn_aggr(out=stat[:, 12:14], in_=stat[:, 0:12]),
                 reads=("stat0",), writes=("stat0",))
            e.op(VE, lambda v: v.tensor_scalar_add(stat[:, 13:14], stat[:, 13:14],
                                                   float(LN_EPS)),
                 reads=("stat0",), writes=("stat0",))
            e.op(SC, lambda s: s.activation(out=stat[:, 14:15], in_=stat[:, 13:14],
                                            func=AF.Ln), reads=("stat0",), writes=("stat0",))
            e.op(SC, lambda s: s.activation(out=stat[:, 15:16], in_=stat[:, 14:15],
                                            func=AF.Exp, scale=-0.5),
                 reads=("stat0",), writes=("stat0",))
            e.op(VE, lambda v: v.tensor_scalar(out=u1t[:], in0=x1t[:],
                                               scalar1=stat[:, 12:13],
                                               scalar2=stat[:, 15:16],
                                               op0=ALU.subtract, op1=ALU.mult),
                 reads=("x1t", "stat0"), writes=("u1t",))
            e.op(SY, lambda s: s.dma_start(out=stage2[:], in_=u1t[127:128, :]),
                 reads=("u1t",), writes=("stage2",), dma=True, dkey="u1out")


def emit_p3(p, e, C, io):
    x1_d, gath2 = io["x1"], io["gath2"]
    wp1_d, bp1_d, wp2_d, bp2_d = io["Wp1"], io["bp1"], io["Wp2"], io["bp2"]
    w1_d, b1_d, w2_d, b2_d = io["W1"], io["b1"], io["W2"], io["b2"]
    wo_d, bo_d = io["FWo"], io["bo"]
    x2_d, stage3 = io["x2"], io["stage3"]

    xt = [p.sbuf(f"xt{i}", [128, D], F32) for i in range(2)]
    stat = [p.sbuf(f"stat{i}", [128, 16], F32) for i in range(2)]
    ubf = [p.sbuf(f"ubf{i}", [128, D], BF16) for i in range(2)]
    u_fm = p.sbuf("ufm", [128, 8, NT], BF16)
    lnstat = p.sbuf("lnstat", [128, 2 * NTILE], F32)
    lrow = p.sbuf("lrow", [1, D], F32)
    wp1_sb = p.sbuf("wp1sb", [128, 8, 128], BF16)
    wp2_sb = p.sbuf("wp2sb", [128, 8], BF16)
    bp1_sb = p.sbuf("bp1sb", [1, 128], F32)
    bp2_sb = p.sbuf("bp2sb", [1, 8], F32)
    glog = p.sbuf("glog", [1, 8], F32)
    gmax = p.sbuf("gmax", [1, 8], F32)
    gsel = p.sbuf("gsel", [1, 8], U32)
    w1_sb = p.sbuf("w1sb", [128, 8, 2048], BF16)
    w2big = p.sbuf("w2big", [128, 16, 1024], BF16)
    w2_sb = w2big[:]
    b1r = p.sbuf("b1r", [1, 2048], F32)
    b2r = p.sbuf("b2r", [1, D], BF16)
    bor = p.sbuf("bor", [1, D], BF16)
    b1c = p.sbuf("b1c", [128, 16], F32)
    wo_sb = p.sbuf("wosb", [128, 8, D], BF16)
    ident_bf, ident_f = C["identbf"], C["identf"]
    ones_row, ones_row_bf = C["onesrow"], C["onesrowbf"]
    h_sb = p.sbuf("hsb", [128, 512], BF16)
    r_sb = p.sbuf("rsb", [128, D], BF16)
    r_fm = p.sbuf("rfm", [128, 8, 128], BF16)
    x2t = p.sbuf("x2t", [128, D], F32)
    u1t = p.sbuf("u1t", [128, D], F32)
    u2_fm = u_fm

    psB = p.psum("psB", [128, 512], F32)
    psT = p.psum("psT", [128, 1024], BF16)
    psH = p.psum("psH", [128, 1024], F32)
    psR2 = p.psum("psR2", [128, 2048], F32)

    SY, TE, VE, SC, GP = ENGINES

    e.op(SY, lambda s: s.dma_start(out=lrow[:], in_=gath2[3:4, :]),
         reads=("gath2",), writes=("lrow",), dma=True)
    e.op(GP, lambda g: g.dma_start(out=wp1_sb[:],
                                   in_=wp1_d[:].rearrange("(a p) f -> p a f", p=128)),
         writes=("wp1",), dma=True)
    e.op(GP, lambda g: g.dma_start(out=wp2_sb[:], in_=wp2_d[:]), writes=("wp2",), dma=True)
    e.op(SY, lambda s: s.dma_start(out=bp1_sb[:],
                                   in_=bp1_d[:].rearrange("(o n) -> o n", o=1)),
         writes=("bp1",), dma=True)
    e.op(SY, lambda s: s.dma_start(out=bp2_sb[:],
                                   in_=bp2_d[:].rearrange("(o n) -> o n", o=1)),
         writes=("bp2",), dma=True)

    def fn_lvt(t):
        last = None
        for dc in range(8):
            last = t.transpose(psB[:, dc:dc + 1], lrow[0:1, dc * 128:(dc + 1) * 128],
                               ident_f[0:1, 0:1])
        return last
    e.op(TE, fn_lvt, reads=("lrow", "identf"), writes=("psB",))
    lv_fm = p.sbuf("lvfm", [128, 8], BF16)
    e.op(VE, lambda v: v.tensor_copy(out=lv_fm[:], in_=psB[:, 0:8]),
         reads=("psB",), writes=("lvfm",))
    def fn_bp1(t):
        return t.transpose(psB[:, 8:9], bp1_sb[:], ident_f[0:1, 0:1])
    e.op(TE, fn_bp1, reads=("bp1", "identf"), writes=("psB",))
    bp1c = p.sbuf("bp1c", [128, 1], F32)
    e.op(VE, lambda v: v.tensor_copy(out=bp1c[:], in_=psB[:, 8:9]),
         reads=("psB",), writes=("bp1c",))

    def fn_z1(t):
        last = None
        for dc in range(8):
            last = t.matmul(psH[:, 0:1], wp1_sb[:, dc, :], lv_fm[:, dc:dc + 1],
                            start=(dc == 0), stop=(dc == 7))
        return last
    e.op(TE, fn_z1, reads=("wp1", "lvfm"), writes=("psH0",))
    z1g = p.sbuf("z1g", [128, 1], BF16)
    e.op(SC, lambda s: s.activation(out=z1g[:], in_=psH[:, 0:1], func=AF.Gelu,
                                    bias=bp1c[:], scale=1.0),
         reads=("psH0", "bp1c"), writes=("z1g",))
    def fn_z2(t):
        return t.matmul(psH[:, 4:12][0:1, :], z1g[:], wp2_sb[:], start=True, stop=True)
    e.op(TE, fn_z2, reads=("z1g", "wp2"), writes=("psH0",))
    e.op(VE, lambda v: v.tensor_add(out=glog[:], in0=psH[0:1, 4:12], in1=bp2_sb[:]),
         reads=("psH0", "bp2"), writes=("glog",))
    e.op(VE, lambda v: v.max(out=gmax[:], in_=glog[:]), reads=("glog",), writes=("gmax",))
    e.op(VE, lambda v: v.max_index(out=gsel[:], in_max=gmax[:], in_values=glog[:]),
         reads=("gmax", "glog"), writes=("gsel",))

    def fn_sel_dma(g):
        reg = g.alloc_register("selreg")
        g.load(reg, gsel[0:1, 0:1])
        dsem = e.dma_sem(GP, "expertw")
        for ex in range(8):
            with g.If_eq(reg, ex):
                g.dma_start(out=w1_sb[:],
                            in_=w1_d[ex].rearrange("(a p) f -> p a f", p=128)
                            ).then_inc(dsem.h, 16)
                g.dma_start(out=w2_sb,
                            in_=w2_d[ex].rearrange("(a p) f -> p a f", p=128)
                            ).then_inc(dsem.h, 16)
                g.dma_start(out=b1r[:],
                            in_=b1_d[ex].rearrange("(o n) -> o n", o=1)
                            ).then_inc(dsem.h, 16)
                g.dma_start(out=b2r[:],
                            in_=b2_d[ex].rearrange("(o n) -> o n", o=1)
                            ).then_inc(dsem.h, 16)
        return None
    e.op(GP, fn_sel_dma, reads=("gsel",), writes=("w1", "w2", "b1r", "b2r"),
         self_inc=64, dkey="expertw")

    for grp in range(2):
        def fn_b1(t, grp=grp):
            last = None
            for j in range(8):
                fc = grp * 8 + j
                last = t.transpose(psB[:, 16 + j:17 + j],
                                   b1r[0:1, fc * 128:(fc + 1) * 128], ident_f[0:1, 0:1])
            return last
        e.op(TE, fn_b1, reads=("b1r", "identf"), writes=("psB",))
        e.op(VE, lambda v, grp=grp: v.tensor_copy(out=b1c[:, grp * 8:(grp + 1) * 8],
                                                  in_=psB[:, 16:24]),
             reads=("psB",), writes=("b1c",))

    e.op(GP, lambda g: g.dma_start(out=wo_sb[:],
                                   in_=wo_d[:].rearrange("(a p) b -> p a b", p=128)),
         writes=("wosb",), dma=True)
    e.op(GP, lambda g: g.dma_start(out=bor[:], in_=bo_d[:].rearrange("(o n) -> o n", o=1)),
         writes=("bor",), dma=True)

    emit_ln_tiles(e, p, x1_d, u_fm,
                  dict(xt=xt, stat=stat, ubf=ubf, psT=psT, ident_bf=ident_bf),
                  stat_out=lnstat, ukey="u")

    for tp in range(8):
        for fc in range(16):
            par = fc % 2
            hps = psH[:, par * 512:par * 512 + 256]
            hsv = h_sb[:, par * 256:(par + 1) * 256]

            def fn_h(t, tp=tp, fc=fc, hps=hps):
                last = None
                for dc in range(8):
                    last = t.matmul(hps, w1_sb[:, dc, fc * 128:(fc + 1) * 128],
                                    u_fm[:, dc, tp * 256:(tp + 1) * 256],
                                    start=(dc == 0), stop=(dc == 7))
                return last
            e.op(TE, fn_h, reads=("w1", "ufm_u"), writes=(f"psH{par}",))
            e.op(SC, lambda s, fc=fc, hps=hps, hsv=hsv: s.activation(
                out=hsv, in_=hps, func=AF.Gelu,
                bias=b1c[:, fc:fc + 1], scale=1.0),
                 reads=(f"psH{par}", "b1c"), writes=(f"hsb{par}",))

            def fn_r(t, tp=tp, fc=fc, hsv=hsv):
                last = []
                for ti in range(2):
                    for ng in range(2):
                        last.append(t.matmul(
                            psR2[:, ti * 1024 + ng * 512:ti * 1024 + (ng + 1) * 512],
                            hsv[:, ti * 128:(ti + 1) * 128],
                            w2_sb[:, fc, ng * 512:(ng + 1) * 512],
                            start=(fc == 0), stop=False))
                return last[-1]
            e.op(TE, fn_r, reads=(f"hsb{par}", "w2"), writes=("psR2",))
        def fn_rb(t, tp=tp):
            last = []
            for ti in range(2):
                for ng in range(2):
                    last.append(t.matmul(
                        psR2[:, ti * 1024 + ng * 512:ti * 1024 + (ng + 1) * 512],
                        ones_row_bf[:], b2r[0:1, ng * 512:(ng + 1) * 512],
                        start=False, stop=True))
            return last[-1]
        e.op(TE, fn_rb, reads=("onesrowbf", "b2r"), writes=("psR2",))

        for ti in range(2):
            e.op(VE, lambda v, ti=ti: v.tensor_copy(out=r_sb[:],
                                                    in_=psR2[:, ti * 1024:(ti + 1) * 1024]),
                 reads=("psR2",), writes=("rsb",))

            def fn_tr(t):
                last = None
                for dc in range(8):
                    last = t.transpose(psT[:, dc * 128:(dc + 1) * 128],
                                       r_sb[:, dc * 128:(dc + 1) * 128], ident_bf[:])
                return last
            e.op(TE, fn_tr, reads=("rsb", "identbf"), writes=("psT",))
            e.op(VE, lambda v: v.tensor_copy(
                out=r_fm[:], in_=psT[:].rearrange("p (a b) -> p a b", b=128)),
                reads=("psT",), writes=("rfm",))

            i = tp * 2 + ti
            e.op(SY, lambda s, i=i: s.dma_start(out=xt[i % 2][:],
                                                in_=x1_d[i * 128:(i + 1) * 128, :]),
                 writes=(f"xt{i % 2}",), dma=True)
            for ng in range(2):
                def fn_o(t, ng=ng):
                    last = None
                    for pc in range(8):
                        last = t.matmul(psH[:, 0:512], r_fm[:, pc, :],
                                        wo_sb[:, pc, ng * 512:(ng + 1) * 512],
                                        start=(pc == 0), stop=False)
                    return t.matmul(psH[:, 0:512], ones_row_bf[:],
                                    bor[0:1, ng * 512:(ng + 1) * 512],
                                    start=False, stop=True)
                e.op(TE, fn_o, reads=("rfm", "wosb", "onesrowbf", "bor"), writes=("psH0",))
                e.op(VE, lambda v, i=i, ng=ng: v.tensor_scalar(
                    out=u1t[:, ng * 512:(ng + 1) * 512],
                    in0=xt[i % 2][:, ng * 512:(ng + 1) * 512],
                    scalar1=lnstat[:, 2 * i:2 * i + 1], scalar2=lnstat[:, 2 * i + 1:2 * i + 2],
                    op0=ALU.subtract, op1=ALU.mult),
                    reads=(f"xt{i % 2}", "lnstat_u"), writes=("u1t",))
                e.op(VE, lambda v, i=i, ng=ng: v.tensor_add(
                    out=u1t[:, ng * 512:(ng + 1) * 512],
                    in0=u1t[:, ng * 512:(ng + 1) * 512],
                    in1=xt[i % 2][:, ng * 512:(ng + 1) * 512]),
                    reads=(f"xt{i % 2}", "u1t"), writes=("u1t",))
                e.op(VE, lambda v, ng=ng: v.scalar_tensor_tensor(
                    out=x2t[:, ng * 512:(ng + 1) * 512], in0=psH[:, 0:512], scalar=0.3,
                    in1=u1t[:, ng * 512:(ng + 1) * 512], op0=ALU.mult, op1=ALU.add),
                    reads=("psH0", "u1t"), writes=("x2t",))
            e.op(SY, lambda s, i=i: s.dma_start(out=x2_d[i * 128:(i + 1) * 128, :],
                                                in_=x2t[:]),
                 reads=("x2t",), writes=(f"x2o{i}",), dma=True, dkey="x2out")

    e.op(VE, lambda v: v.engine_nop(), reads=(), writes=("ufm_u",))
    emit_ln_tiles(e, p, x2_d, u2_fm,
                  dict(xt=xt, stat=stat, ubf=ubf, psT=psT, ident_bf=ident_bf), ukey="u2",
                  src_key=[f"x2o{i}" for i in range(16)])
    e.op(SY, lambda s: s.dma_start(out=stage3[:],
                                   in_=u2_fm[:].rearrange("p a t -> p (a t)")),
         reads=("ufm_u2",), writes=("stage3",), dma=True, dkey="u2out")


def emit_p4(p, e, C, io):
    gath3 = io["gath3"]
    wq_d, wk_d, wv_d = io["WqT"], io["WkT"], io["WvT"]
    stage4 = io["stage4"]

    wq_sb = p.sbuf("wqsb", [128, 8, 256], BF16)
    wk_sb = p.sbuf("wksb", [128, 8, 256], BF16)
    wv_sb = p.sbuf("wvsb", [128, 8, 256], BF16)
    u2r = [p.sbuf(f"u2r{i}", [128, 8, NT], BF16) for i in range(2)]
    q_fm = p.sbuf("qfm", [128, 2, 4 * NT], BF16)
    k_fm = p.sbuf("kfm", [128, 2, 4 * NT], BF16)
    v_tm = p.sbuf("vtm", [128, 64, 260], BF16)
    pt_sb = p.sbuf("ptsb", [128, 2048], BF16)
    sq_sc = p.sbuf("sqsc", [128, 2048], BF16)
    bmax = p.sbuf("bmax", [1, 64], F32)
    negb2 = p.sbuf("negb2", [128, 2], F32)
    linvb = p.sbuf("linvb", [64, 512], F32)
    ot = [p.sbuf(f"ot{i}", [128, 512], BF16) for i in range(2)]
    nac = [[p.sbuf(f"nac{i}{s}", [65, 512], F32) for s in range(2)]
           for i in range(2)]
    linv2 = [[p.sbuf(f"linv{i}{s}", [1, 512], F32) for s in range(2)]
             for i in range(2)]
    ones_row, ones_col = C["onesrow"], C["onescolbf"]

    psS = p.psum("psS", [128, 2048], F32)
    psA = [p.psum("psA0", [128, 512], F32), p.psum("psA1", [128, 512], F32)]
    psB = p.psum("psB", [128, 512], F32)

    SY, TE, VE, SC, GP = ENGINES

    e.op(GP, lambda g: g.memset(
        v_tm[:].rearrange("p t (h x) -> p t h x", x=65)[:, :, :, 64:65], 1.0),
        writes=("vtm_ones",))
    e.op(GP, lambda g: g.memset(bmax[:], 0.0),
         writes=("bmax_q", "bmax_k", "bmax_t", "bmax_s"))

    for w_d, w_sb, tag in ((wq_d, wq_sb, "wq"), (wk_d, wk_sb, "wk"),
                           (wv_d, wv_sb, "wv")):
        e.op(GP, lambda g, w_d=w_d, w_sb=w_sb: g.dma_start(
            out=w_sb[:], in_=w_d.rearrange("(a p) f -> p a f", p=128)),
            writes=(tag,), dma=True, dkey=tag)

    for r in range(4):
        ub, ukk = u2r[r % 2], f"u2r{r % 2}"
        e.op(SY, lambda s, r=r, ub=ub: s.dma_start(
            out=ub[:],
            in_=gath3[r * 128:(r + 1) * 128, :].rearrange("p (a t) -> p a t", a=8)),
             reads=("gath3",), writes=(ukk,), dma=True, dkey=ukk)
        for dst, dk, tag in ((q_fm, "qfm", "wq"), (k_fm, "kfm", "wk")):
            w_sb = {"wq": wq_sb, "wk": wk_sb}[tag]
            for oc in range(2):
                for tg in range(4):
                    def fn_mm(t, oc=oc, tg=tg, ub=ub, w_sb=w_sb):
                        last = None
                        for dc in range(8):
                            last = t.matmul(psB[:], w_sb[:, dc, oc * 128:(oc + 1) * 128],
                                            ub[:, dc, tg * 512:(tg + 1) * 512],
                                            start=(dc == 0), stop=(dc == 7))
                        return last
                    e.op(TE, fn_mm, reads=(tag, ukk), writes=("psB",))
                    e.op(VE, lambda v, oc=oc, tg=tg, r=r, dst=dst, dk=dk: v.tensor_copy(
                        out=dst[:, oc, r * NT + tg * 512:r * NT + (tg + 1) * 512],
                        in_=psB[:]),
                        reads=("psB",), writes=(dk,))
        for ti in range(16):
            def fn_vm(t, ti=ti, ub=ub):
                last = None
                for dc in range(8):
                    last = t.matmul(psB[:, 0:256], ub[:, dc, ti * 128:(ti + 1) * 128],
                                    wv_sb[:, dc, :], start=(dc == 0), stop=(dc == 7))
                return last
            e.op(TE, fn_vm, reads=("wv", ukk), writes=("psB",))
            e.op(VE, lambda v, ti=ti, r=r: v.tensor_copy(
                out=v_tm[:, r * 16 + ti].rearrange("p (h x) -> p h x", x=65)[:, :, 0:64],
                in_=psB[:, 0:256].rearrange("p (h d) -> p h d", d=64)),
                reads=("psB",), writes=("vtm",))

    def emit_norms(src_ap, src_keys, out_off, tag):
        e.op(GP, lambda g: g.tensor_mul(sq_sc[:], src_ap, src_ap),
             reads=src_keys, writes=("sqsc",))

        def fn_nq(t):
            last = []
            for s in range(2):
                for g in range(4):
                    last.append(t.matmul(
                        psS[0:1, s * 1024 + (g % 2) * 512:s * 1024 + (g % 2) * 512 + 512],
                        ones_col[64 * s:64 * s + 64, :],
                        sq_sc[64 * s:64 * s + 64, g * 512:(g + 1) * 512],
                        start=(g < 2), stop=(g >= 2)))
            return last
        e.op(TE, fn_nq, reads=("sqsc", "onescol"), writes=("psS0", "psS1"))

        def fn_red(v):
            v.reduce_max(bmax[:, 30:31], psS[0:1, 0:1024], axis=AX)
            return v.reduce_max(bmax[:, 31:32], psS[0:1, 1024:2048], axis=AX)
        e.op(VE, fn_red, reads=("psS0", "psS1"), writes=("bmax_t",))
        e.op(VE, lambda v: v.tensor_max(bmax[:, out_off:out_off + 2],
                                        bmax[:, out_off:out_off + 2], bmax[:, 30:32]),
             reads=("bmax_t", tag), writes=(tag,))

    for hp in range(2):
        for r in range(4):
            emit_norms(q_fm[:, hp, r * NT:(r + 1) * NT], ("qfm",), 2 * hp, "bmax_q")
            emit_norms(k_fm[:, hp, r * NT:(r + 1) * NT], ("kfm",), 8 + 2 * hp, "bmax_k")

    for hp in range(2):
        e.op(VE, lambda v, hp=hp: v.tensor_mul(bmax[:, 24:26], bmax[:, 2 * hp:2 * hp + 2],
                                               bmax[:, 8 + 2 * hp:8 + 2 * hp + 2]),
             reads=("bmax_q", "bmax_k"), writes=("bmax_s",))
        e.op(VE, lambda v: v.tensor_max(bmax[:, 26:27], bmax[:, 24:25], bmax[:, 25:26]),
             reads=("bmax_s",), writes=("bmax_s",))
        e.op(SC, lambda s: s.activation(out=bmax[:, 27:28], in_=bmax[:, 26:27],
                                        func=AF.Sqrt), reads=("bmax_s",), writes=("bmax_s",))
        e.op(VE, lambda v: v.tensor_scalar(out=bmax[:, 28:29], in0=bmax[:, 27:28],
                                           scalar1=-0.1275, scalar2=None, op0=ALU.mult),
             reads=("bmax_s",), writes=("bmax_s",))
        e.op(TE, lambda t: t.matmul(psB[:, 0:1], ones_row[:], bmax[0:1, 28:29],
                                    start=True, stop=True),
             reads=("bmax_s", "onesrow"), writes=("psB",))
        e.op(SC, lambda s, hp=hp: s.copy(out=negb2[:, hp:hp + 1], in_=psB[:, 0:1]),
             reads=("psB",), writes=("negb",))

    dmneg = p.sbuf("dmneg", [128, 4, 512], BF16)
    identp4 = C["identbf"]

    def fn_dm(g):
        last = None
        for k in range(4):
            g.memset(dmneg[:, k, :], 0.0)
            last = g.affine_select(
                out=dmneg[:, k, :], in_=dmneg[:, k, :], compare_op=ALU.is_ge,
                fill=-1e30, base=-128 * k, pattern=[[1, 512]],
                channel_multiplier=-1)
        return last
    e.op(GP, fn_dm, writes=("dmneg",))

    pending_div = []
    for hp in range(2):
        for qt in range(16):
            ntt = 4 * (qt + 1)

            def emit_s(tt):
                h = tt % 2
                diag = tt >= 4 * qt

                def fn_s2(t, hp=hp, qt=qt, tt=tt, h=h, diag=diag):
                    last = []
                    for s in range(2):
                        seg = psS[:, h * 1024 + s * 512:h * 1024 + (s + 1) * 512]
                        mm = t.matmul(seg,
                                      k_fm[64 * s:64 * s + 64, hp, tt * 128:(tt + 1) * 128],
                                      q_fm[64 * s:64 * s + 64, hp, qt * 512:(qt + 1) * 512],
                                      start=True, stop=not diag)
                        if diag:
                            k = tt - 4 * qt
                            mm = t.matmul(seg, identp4[:], dmneg[:, k, :],
                                          start=False, stop=True)
                        last.append(mm)
                    return last
                e.op(TE, fn_s2, reads=("kfm", "qfm", "identbf", "dmneg"),
                     writes=(f"psS{h}",))
                e.op(SC, lambda s, tt=tt, h=h, hp=hp: s.activation(
                    out=pt_sb[:, h * 1024:(h + 1) * 1024],
                    in_=psS[:, h * 1024:(h + 1) * 1024], func=AF.Exp,
                    bias=negb2[:, hp:hp + 1], scale=0.125),
                    reads=(f"psS{h}", "negb"), writes=(f"pt{h}",))

            def emit_pv(tt):
                h = tt % 2

                def fn_pv(t, hp=hp, tt=tt, h=h, first=(tt == 0),
                          stop=(tt == ntt - 1)):
                    out = []
                    for s in range(2):
                        lh = 2 * hp + s
                        out.append(t.matmul(
                            psA[s][0:65, :],
                            v_tm[:, tt, lh * 65:(lh + 1) * 65],
                            pt_sb[:, h * 1024 + s * 512:h * 1024 + (s + 1) * 512],
                            start=first, stop=stop, skip_group_check=True))
                    return out
                e.op(TE, fn_pv, reads=("vtm", "vtm_ones", f"pt{h}"),
                     writes=("psA0", "psA1"))

            emit_s(0)
            if pending_div:
                pending_div.pop(0)()
            for tt in range(1, ntt):
                emit_s(tt)
                emit_pv(tt - 1)
            emit_pv(ntt - 1)

            par = qt % 2
            for s in range(2):
                e.op(VE, lambda v, s=s, par=par: v.tensor_copy(
                    out=nac[par][s][:], in_=psA[s][0:65, :]),
                    reads=(f"psA{s}",), writes=(f"nac{par}{s}",))
            for s in range(2):
                e.op(VE, lambda v, s=s, par=par: v.tensor_scalar(
                    out=linv2[par][s][:], in0=nac[par][s][64:65, :], scalar1=1e-35,
                    scalar2=None, op0=ALU.max),
                    reads=(f"nac{par}{s}",), writes=(f"linv{par}{s}",))
                e.op(VE, lambda v, s=s, par=par: v.reciprocal(
                    out=linv2[par][s][:], in_=linv2[par][s][:]),
                     reads=(f"linv{par}{s}",), writes=(f"linv{par}{s}",))

            def div_b(hp=hp, qt=qt, par=par):
                ob, obk = ot[par], f"ot{par}"
                for s in range(2):
                    e.op(TE, lambda t, s=s, par=par: t.matmul(
                        psB[0:64, :], ones_row[0:1, 0:64], linv2[par][s][:],
                        start=True, stop=True),
                         reads=(f"linv{par}{s}", "onesrow"), writes=("psB",))
                    e.op(SC, lambda s_: s_.copy(out=linvb[:], in_=psB[0:64, :]),
                         reads=("psB",), writes=("linvb",))
                    e.op(VE, lambda v, s=s, ob=ob, obk=obk, par=par: v.tensor_tensor(
                        out=ob[64 * s:64 * s + 64, :],
                        in0=nac[par][s][0:64, :], in1=linvb[:], op=ALU.mult),
                        reads=(f"nac{par}{s}", "linvb"), writes=(obk,))
                e.op(SY, lambda s, hp=hp, qt=qt, ob=ob, obk=obk: s.dma_start(
                    out=stage4[:, hp * 8192 + qt * 512:hp * 8192 + (qt + 1) * 512],
                    in_=ob[:]),
                    reads=(obk,), writes=("stage4",), dma=True, dkey="oout")
            pending_div.append(div_b)
    while pending_div:
        pending_div.pop(0)()


def emit_p5(p, e, C, io, lvl=99):
    x2_d, gath4, cpos_d = io["x2"], io["gath4"], io["cpos"]
    wot_d, gw_d = io["WoT"], io["gate_w"]
    w1_d, w2_d, w3_d = io["m_w1"], io["m_w2"], io["m_w3"]
    sw1_d, sw2_d, sw3_d = io["sw1"], io["sw2"], io["sw3"]
    x4_d, x3_d = io["x4"], io["x3"]

    xt = [p.sbuf(f"xt{i}", [128, D], F32) for i in range(2)]
    stat = [p.sbuf(f"stat{i}", [128, 16], F32) for i in range(2)]
    ubf = [p.sbuf(f"ubf{i}", [128, D], BF16) for i in range(2)]
    u_fm = p.sbuf("ufm", [128, 8, NT], BF16)
    o_sb = u_fm
    cp_sb = p.sbuf("cpsb", [1, 1], U32)
    ident_bf = C["identbf"]

    psH1 = p.psum("psH1", [128, 1024], F32)
    psH3 = p.psum("psH3", [128, 1024], F32)
    psY = [p.psum("psY0", [128, 512], F32), p.psum("psY1", [128, 512], F32)]
    psT = p.psum("psT", [128, 1024], BF16)
    psZ = p.psum("psZ", [128, 512], F32)

    SY, TE, VE, SC, GP = ENGINES

    e.op(SY, lambda s: s.dma_start(out=cp_sb[:], in_=cpos_d[:]),
         writes=("cpsb",), dma=True, dkey="cpld")
    ogat = e.dma_sem(GP, "ogat")

    def fn_ogather(g):
        reg = g.alloc_register("posreg")
        g.load(reg, cp_sb[0:1, 0:1])
        for j in range(4):
            with g.If_eq(reg, j):
                for gsrc in range(4):
                    for ch in range(2):
                        g.dma_start(
                            out=o_sb[:, 2 * gsrc + ch, :],
                            in_=gath4[gsrc * 128:(gsrc + 1) * 128,
                                      ch * 8192 + j * NT:ch * 8192 + (j + 1) * NT]
                        ).then_inc(ogat.h, 16)
        return None
    e.op(GP, fn_ogather, reads=("gath4", "cpsb"), writes=("ufm_u",),
         self_inc=128, dkey="ogat")

    with p.scope():
        wot_sb = p.sbuf("wotsb", [128, 8, D], BF16)
        x3t = p.sbuf("x3t", [128, D], F32)
        x4t = p.sbuf("x4t", [128, D], F32)
        e.op(GP, lambda g: g.dma_start(out=wot_sb[:],
                                       in_=wot_d[:].rearrange("(a p) b -> p a b", p=128)),
             writes=("wot",), dma=True, dkey="wotload")
        x3b = [x3t, x4t]
        for i in range(NTILE):
            e.op(SY, lambda s, i=i: s.dma_start(out=xt[i % 2][:],
                                                in_=x2_d[i * 128:(i + 1) * 128, :]),
                 writes=(f"xt{i % 2}",), dma=True)
            xb = x3b[i % 2]
            xbk = ("x3t", "x4t")[i % 2]
            for ng in range(2):
                def fn_y(t, i=i, ng=ng, wot_sb=wot_sb):
                    last = None
                    for mc in range(8):
                        last = t.matmul(psY[0][:], o_sb[:, mc, i * 128:(i + 1) * 128],
                                        wot_sb[:, mc, ng * 512:(ng + 1) * 512],
                                        start=(mc == 0), stop=(mc == 7))
                    return last
                e.op(TE, fn_y, reads=("ufm_u", "wot"), writes=("psY0",))
                e.op(VE, lambda v, i=i, ng=ng, xb=xb: v.tensor_add(
                    out=xb[:, ng * 512:(ng + 1) * 512],
                    in0=xt[i % 2][:, ng * 512:(ng + 1) * 512], in1=psY[0][:]),
                    reads=(f"xt{i % 2}", "psY0"), writes=(xbk,))
            e.op(SY, lambda s, i=i, xb=xb: s.dma_start(
                out=x3_d[i * 128:(i + 1) * 128, :], in_=xb[:]),
                 reads=(xbk,), writes=(f"x3o{i}",), dma=True, dkey=f"x3out{i % 2}")
    e.barrier()

    y_acc = p.sbuf("yacc", [128, 16, D], F32)
    w1h = [p.sbuf(f"w1h{i}", [128, 8, D], BF16) for i in range(2)]
    w3h = [p.sbuf(f"w3h{i}", [128, 8, D], BF16) for i in range(2)]
    w2h_t = p.sbuf("w2h", [128, 8, D], BF16)
    w2h = [w2h_t, w2h_t]
    act_buf = p.sbuf("actbuf", [128, 8, 512], BF16)
    s1_sb = p.sbuf("s1sb", [128, 1024], BF16)
    gwt_sb = p.sbuf("gwtsb", [128, 8, 8], BF16)
    comb = p.sbuf("comb", [128, 16, 9], F32)
    gz = p.sbuf("gz", [128, 8], F32)
    gm = p.sbuf("gm", [128, 8], F32)
    negm1 = p.sbuf("negm1", [128, 1], F32)
    gsc = p.sbuf("gsc", [128, 8], F32)
    gw_sb = act_buf[:].rearrange("p a b -> p (a b)")[0:8, 0:1024]
    e.op(GP, lambda g: g.memset(y_acc[:], 0.0), writes=("yacc",))

    if lvl < 2:
        return
    emit_ln_tiles(e, p, x3_d, u_fm,
                  dict(xt=xt, stat=stat, ubf=ubf, psT=psT, ident_bf=ident_bf), ukey="u",
                  src_key=[f"x3o{i}" for i in range(16)])

    if lvl < 3:
        return
    e.op(GP, lambda g: g.dma_start(out=gw_sb, in_=gw_d[:]), writes=("gw",), dma=True)
    for dc in range(8):
        e.op(TE, lambda t, dc=dc: t.transpose(psZ[:, 0:8].bitcast(BF16)[:, 0:8],
                                              gw_sb[:, dc * 128:(dc + 1) * 128],
                                              ident_bf[0:8, 0:8]),
             reads=("gw", "identbf"), writes=("psZ",))
        e.op(VE, lambda v, dc=dc: v.tensor_copy(out=gwt_sb[:, dc, :],
                                                in_=psZ[:, 0:8].bitcast(BF16)[:, 0:8]),
             reads=("psZ",), writes=("gwt",))
    for ti in range(NTILE):
        def fn_z(t, ti=ti):
            last = None
            for dc in range(8):
                last = t.matmul(psZ[:, 16:24], u_fm[:, dc, ti * 128:(ti + 1) * 128],
                                gwt_sb[:, dc, :], start=(dc == 0), stop=(dc == 7))
            return last
        e.op(TE, fn_z, reads=("ufm_u", "gwt"), writes=("psZ",))
        e.op(VE, lambda v: v.tensor_copy(out=gz[:], in_=psZ[:, 16:24]),
             reads=("psZ",), writes=("gz",))
        e.op(VE, lambda v: v.max(out=gm[:], in_=gz[:]), reads=("gz",), writes=("gm",))
        e.op(VE, lambda v: v.tensor_scalar(out=negm1[:], in0=gm[:, 0:1], scalar1=-1.0,
                                           scalar2=None, op0=ALU.mult),
             reads=("gm",), writes=("negm1",))
        e.op(SC, lambda s: s.activation(out=gsc[:], in_=gz[:], func=AF.Exp,
                                        bias=negm1[:], scale=1.0),
             reads=("gz", "negm1"), writes=("gsc",))
        e.op(SC, lambda s: s.activation(out=negm1[:], in_=gm[:, 1:2], func=AF.Exp,
                                        bias=negm1[:], scale=1.0),
             reads=("gm", "negm1"), writes=("negm1b",))
        e.op(VE, lambda v: v.tensor_scalar_add(negm1[:], negm1[:], 1.0),
             reads=("negm1b",), writes=("negm1b",))
        e.op(VE, lambda v: v.reciprocal(out=negm1[:], in_=negm1[:]),
             reads=("negm1b",), writes=("negm1b",))
        e.op(VE, lambda v: v.tensor_scalar(out=gsc[:], in0=gsc[:], scalar1=negm1[:],
                                           scalar2=None, op0=ALU.mult),
             reads=("gsc", "negm1b"), writes=("gsc",))
        e.op(VE, lambda v: v.tensor_scalar(out=gm[:], in0=gz[:], scalar1=gm[:, 1:2],
                                           scalar2=None, op0=ALU.is_ge),
             reads=("gz", "gm"), writes=("gmask",))
        e.op(VE, lambda v, ti=ti: v.tensor_mul(comb[:, ti, 0:8], gsc[:], gm[:]),
             reads=("gsc", "gmask"), writes=("comb",))
        e.op(VE, lambda v, ti=ti: v.memset(comb[:, ti, 8:9], 1.0),
             reads=(), writes=("comb",))

    e.op(VE, lambda v: v.engine_nop(), reads=(), writes=("gw",))

    if lvl < 4:
        return
    for eh in range(18):
        ex, half = eh // 2, eh % 2
        wb1, wb3, wb2 = w1h[eh % 2], w3h[eh % 2], w2h[eh % 2]
        wk1, wk3, wk2 = f"w1h{eh % 2}", f"w3h{eh % 2}", "w2h"
        if ex < 8:
            s1d = w1_d[ex][:, half * 1024:(half + 1) * 1024]
            s3d = w3_d[ex][:, half * 1024:(half + 1) * 1024]
            s2d = w2_d[ex][half * 1024:(half + 1) * 1024, :]
        else:
            s1d = sw1_d[0][:, half * 1024:(half + 1) * 1024]
            s3d = sw3_d[0][:, half * 1024:(half + 1) * 1024]
            s2d = sw2_d[0][half * 1024:(half + 1) * 1024, :]
        e.op(GP, lambda g, s1d=s1d, wb1=wb1: g.dma_start(
            out=wb1[:], in_=s1d.rearrange("(a p) f -> p a f", p=128)),
            writes=(wk1,), dma=True, dkey=wk1)
        e.op(GP, lambda g, s3d=s3d, wb3=wb3: g.dma_start(
            out=wb3[:], in_=s3d.rearrange("(a p) f -> p a f", p=128)),
            writes=(wk3,), dma=True, dkey=wk3)
        e.op(GP, lambda g, s2d=s2d, wb2=wb2: g.dma_start(
            out=wb2[:], in_=s2d.rearrange("(a p) f -> p a f", p=128)),
            writes=(wk2,), dma=True, dkey=wk2)

        for tq in range(4):
            for fc in range(8):
                par = fc % 2
                h1 = psH1[:, par * 512:(par + 1) * 512]
                h3 = psH3[:, par * 512:(par + 1) * 512]
                s1v = s1_sb[:, par * 512:(par + 1) * 512]

                def fn_h(t, tq=tq, fc=fc, wb1=wb1, wb3=wb3, h1=h1, h3=h3):
                    last = []
                    for dc in range(8):
                        t.matmul(h1, wb1[:, dc, fc * 128:(fc + 1) * 128],
                                 u_fm[:, dc, tq * 512:(tq + 1) * 512],
                                 start=(dc == 0), stop=(dc == 7))
                    for dc in range(8):
                        last.append(t.matmul(h3, wb3[:, dc, fc * 128:(fc + 1) * 128],
                                             u_fm[:, dc, tq * 512:(tq + 1) * 512],
                                             start=(dc == 0), stop=(dc == 7)))
                    return last[-1]
                e.op(TE, fn_h, reads=(wk1, wk3, "ufm_u"),
                     writes=(f"psH1{par}", f"psH3{par}"))
                e.op(SC, lambda s, h1=h1, s1v=s1v: s.activation(
                    out=s1v, in_=h1, func=AF.Silu),
                     reads=(f"psH1{par}",), writes=(f"s1sb{par}",))
                e.op(VE, lambda v, fc=fc, h3=h3, s1v=s1v: v.tensor_mul(
                    act_buf[:, fc, :], s1v, h3),
                    reads=(f"s1sb{par}", f"psH3{par}"), writes=("actbuf",))
            for ti in range(4):
                gi = tq * 4 + ti
                for ng in range(2):
                    def fn_y(t, ti=ti, ng=ng, wb2=wb2):
                        last = None
                        for fc in range(8):
                            last = t.matmul(psY[ng][:], act_buf[:, fc, ti * 128:(ti + 1) * 128],
                                            wb2[:, fc, ng * 512:(ng + 1) * 512],
                                            start=(fc == 0), stop=(fc == 7))
                        return last
                    e.op(TE, fn_y, reads=("actbuf", wk2), writes=(f"psY{ng}",))
                    e.op(VE, lambda v, gi=gi, ng=ng, ex=ex: v.scalar_tensor_tensor(
                        out=y_acc[:, gi, ng * 512:(ng + 1) * 512], in0=psY[ng][:],
                        scalar=comb[:, gi, ex:ex + 1],
                        in1=y_acc[:, gi, ng * 512:(ng + 1) * 512],
                        op0=ALU.mult, op1=ALU.add),
                        reads=(f"psY{ng}", "comb", "yacc"), writes=("yacc",))

    for i in range(NTILE):
        e.op(SY, lambda s, i=i: s.dma_start(out=xt[0][:], in_=x3_d[i * 128:(i + 1) * 128, :]),
             reads=(f"x3o{i}",), writes=("xt0",), dma=True, dkey="x3ld")
        e.op(VE, lambda v, i=i: v.tensor_add(out=xt[1][:], in0=xt[0][:], in1=y_acc[:, i, :]),
             reads=("xt0", "yacc"), writes=("xt1",))
        e.op(SY, lambda s, i=i: s.dma_start(out=x4_d[i * 128:(i + 1) * 128, :], in_=xt[1][:]),
             reads=("xt1",), dma=True, dkey="x4out")


def shard_x(x_full):
    out = []
    for c in range(NCORES):
        b, j = c // 4, c % 4
        out.append(np.ascontiguousarray(x_full[b, j * NT:(j + 1) * NT, :]))
    return out


RG2x4 = [[0, 1, 2, 3], [4, 5, 6, 7]]
MEGA_UPTO = 5
P2_KW = {}
P5_KW = {}
MEGA_DBG = False
MEGA_NOCC = False



def build_mega():
    p = Prog()
    nc = p.nc
    e = Emit(p)
    SY, TE, VE, SC, GP = ENGINES

    ext_in = {
        "x": ([NT, D], F32),
        "Wq": ([D, D], F32), "Wk": ([D, D], F32), "Wv": ([D, D], F32),
        "Wa": ([D, H], F32), "ba": ([H], F32), "AWo": ([D, D], F32),
        "Wp1": ([D, 128], F32), "bp1": ([128], F32),
        "Wp2": ([128, 8], F32), "bp2": ([8], F32),
        "W1": ([8, D, 2048], F32), "b1": ([8, 2048], F32),
        "W2": ([8, 2048, D], F32), "b2": ([8, D], F32),
        "FWo": ([D, D], F32), "bo": ([D], F32),
        "WqT": ([D, 256], F32), "WkT": ([D, 256], F32), "WvT": ([D, 256], F32),
        "WoT": ([D, D], F32),
        "gate_w": ([8, D], F32),
        "m_w1": ([8, D, 2048], F32), "m_w2": ([8, 2048, D], F32),
        "m_w3": ([8, D, 2048], F32),
        "sw1": ([1, D, 2048], F32), "sw2": ([1, 2048, D], F32),
        "sw3": ([1, D, 2048], F32),
        "pm": ([1, 8], F32), "cpos": ([1, 1], U32),
    }
    io = {k: nc.dram_tensor(k, sh, dt, kind="ExternalInput").ap()
          for k, (sh, dt) in ext_in.items()}
    io["x4"] = nc.dram_tensor("x4", [NT, D], F32, kind="ExternalOutput").ap()
    intern = {
        "chunkS": ([16, 128, 520], F32), "chunkA": ([16, H], F32),
        "stage1": ([128, 536], F32), "gath1": ([512, 536], F32),
        "stage2": ([1, D], F32), "gath2": ([4, D], F32),
        "stage3": ([128, 8 * NT], BF16), "gath3": ([512, 8 * NT], BF16),
        "stage4": ([128, 8 * NT], BF16), "gath4": ([512, 8 * NT], BF16),
    }
    for k, (sh, dt) in intern.items():
        io[k] = nc.dram_tensor(k, sh, dt, kind="Internal").ap()
    xkind = "ExternalOutput" if MEGA_DBG else "Internal"
    for k in ("x1", "x2", "x3"):
        io[k] = nc.dram_tensor(k, [NT, D], F32, kind=xkind).ap()

    C = {
        "identbf": p.sbuf("c_identbf", [128, 128], BF16),
        "identf": p.sbuf("c_identf", [128, 128], F32),
        "msl": p.sbuf("c_msl", [128, 128], F32),
        "mle": p.sbuf("c_mle", [128, 128], F32),
        "mneg": p.sbuf("c_mneg", [128, 128], F32),
        "onesrow": p.sbuf("c_onesrow", [1, 128], F32),
        "onesrowbf": p.sbuf("c_onesrowbf", [1, 128], BF16),
        "onescolf": p.sbuf("c_onescolf", [128, 1], F32),
        "onescolbf": p.sbuf("c_onescolbf", [128, 1], BF16),
    }
    make_const(e, GP, C["identbf"][:], "identbf", "ident")
    make_const(e, GP, C["identf"][:], "identf", "ident")
    make_const(e, GP, C["msl"][:], "msl", "strict_lower")
    make_const(e, GP, C["onesrow"][:], "onesrow", "ones")
    make_const(e, GP, C["onesrowbf"][:], "onesrowbf", "ones")
    make_const(e, GP, C["onescolf"][:], "onescol", "ones")
    make_const(e, GP, C["onescolbf"][:], "onescolbf", "ones")
    e.op(GP, lambda g: g.memset(C["mle"][:], 1.0), writes=("mle",))
    e.op(GP, lambda g: g.affine_select(out=C["mle"][:], in_=C["mle"][:],
                                       compare_op=ALU.is_ge, fill=0.0, base=0,
                                       pattern=[[1, 128]], channel_multiplier=-1),
         reads=("mle",), writes=("mle",))
    e.op(GP, lambda g: g.memset(C["mneg"][:], 0.0), writes=("mneg",))
    e.op(GP, lambda g: g.affine_select(out=C["mneg"][:], in_=C["mneg"][:],
                                       compare_op=ALU.is_ge, fill=-1e30, base=0,
                                       pattern=[[1, 128]], channel_multiplier=-1),
         reads=("mneg",), writes=("mneg",))

    def collective(name, src_key, dst_key, in_ap, out_ap):
        if MEGA_NOCC:
            return ()
        ctr = e.dma_sem(GP, name)

        def fn(g):
            return g.collective_compute(
                "AllGather", mybir.AluOpType.bypass, replica_groups=RG2x4,
                ins=[in_ap.opt()], outs=[out_ap.opt()]).then_inc(ctr.h)
        e.op(GP, fn, reads=(src_key,), writes=(dst_key,), self_inc=1, dkey=name)
        return (ctr,)

    x1 = x2 = x3 = x4 = ()
    with p.scope():
        car = (p.sbuf("qfm", [128, 8, NT], BF16),
               p.sbuf("kfm", [128, 8, NT], BF16),
               p.sbuf("vtm", [128, 16, 1040], BF16),
               p.sbuf("lnatm", [128, 16 * H], F32))
        with p.scope():
            emit_p1(p, e, C, io, car)
        x1 = collective("cc1", "stage1", "gath1", io["stage1"], io["gath1"])
        e.barrier(exclude=x1)
        if MEGA_UPTO >= 2:
            with p.scope():
                emit_p2(p, e, C, io, car, **P2_KW)
            x2 = collective("cc2", "stage2", "gath2", io["stage2"], io["gath2"])
    e.barrier(exclude=x1 + x2)
    if MEGA_UPTO >= 3:
        with p.scope():
            emit_p3(p, e, C, io)
            x3 = collective("cc3", "stage3", "gath3", io["stage3"], io["gath3"])
        e.barrier(exclude=x3)
    if MEGA_UPTO >= 4:
        with p.scope():
            emit_p4(p, e, C, io)
            x4 = collective("cc4", "stage4", "gath4", io["stage4"], io["gath4"])
        e.barrier(exclude=x4)
    if MEGA_UPTO >= 5:
        with p.scope():
            emit_p5(p, e, C, io, **P5_KW)
    e.flush()
    return p


_PROG_CACHE = {}


def _prog(name, builder):
    if name not in _PROG_CACHE:
        _PROG_CACHE[name] = builder()
    return _PROG_CACHE[name]


def kernel(**inputs):
    f32 = lambda a: np.ascontiguousarray(a, dtype=np.float32)
    x = f32(inputs["L0_attn_x"])
    shards = shard_x(x)
    p = _prog("mega", build_mega)

    def wslice(w, g):
        return np.ascontiguousarray(w[256 * g:256 * (g + 1), :].T)

    com = dict(
        Wq=f32(inputs["L0_attn_Wq"]), Wk=f32(inputs["L0_attn_Wk"]),
        Wv=f32(inputs["L0_attn_Wv"]), Wa=f32(inputs["L0_attn_Wa"]),
        ba=f32(inputs["L0_attn_ba"]), AWo=f32(inputs["L0_attn_Wo"]),
        Wp1=f32(inputs["L0_ffn_Wp1"]), bp1=f32(inputs["L0_ffn_bp1"]),
        Wp2=f32(inputs["L0_ffn_Wp2"]), bp2=f32(inputs["L0_ffn_bp2"]),
        W1=f32(inputs["L0_ffn_W1"]), b1=f32(inputs["L0_ffn_b1"]),
        W2=f32(inputs["L0_ffn_W2"]), b2=f32(inputs["L0_ffn_b2"]),
        FWo=f32(inputs["L0_ffn_Wo"]), bo=f32(inputs["L0_ffn_bo"]),
        WoT=np.ascontiguousarray(f32(inputs["L1_attn_Wo"]).T),
        gate_w=f32(inputs["L1_ffn_gate_w"]),
        m_w1=f32(inputs["L1_ffn_w1"]), m_w2=f32(inputs["L1_ffn_w2"]),
        m_w3=f32(inputs["L1_ffn_w3"]),
        sw1=f32(inputs["L1_ffn_sw1"]), sw2=f32(inputs["L1_ffn_sw2"]),
        sw3=f32(inputs["L1_ffn_sw3"]),
    )
    wq1, wk1, wv1 = (f32(inputs["L1_attn_Wq"]), f32(inputs["L1_attn_Wk"]),
                     f32(inputs["L1_attn_Wv"]))
    maps = []
    for c in range(NCORES):
        g = c % 4
        pmv = np.zeros((1, 8), np.float32)
        for k in range(3):
            pmv[0, k] = 1.0 if k < g else 0.0
            pmv[0, 4 + k] = 1.0 - pmv[0, k]
        maps.append(dict(x=shards[c],
                         WqT=wslice(wq1, g), WkT=wslice(wk1, g), WvT=wslice(wv1, g),
                         pm=pmv, cpos=np.array([[g]], np.uint32), **com))
    res = run_bass_kernel_spmd(p.nc, maps, core_ids=list(range(NCORES))).results
    out = np.empty((2, 8192, D), np.float32)
    for c in range(NCORES):
        b, jp = c // 4, c % 4
        out[b, jp * NT:(jp + 1) * NT, :] = res[c]["x4"]
    return out


# version 09
# speedup vs baseline: 1.0310x; 1.0103x over previous
import numpy as np
from contextlib import ExitStack, contextmanager

import concourse.bass as bass
import concourse.mybir as mybir
from concourse.bass_utils import run_bass_kernel_spmd

F32 = mybir.dt.float32
BF16 = mybir.dt.bfloat16
U32 = mybir.dt.uint32
AF = mybir.ActivationFunctionType
ALU = mybir.AluOpType
AX = mybir.AxisListType.X

D = 1024
NT = 2048
NTILE = 16
H = 16
DH = 64
NCORES = 8
LN_EPS = 1e-5
GLA_EPS = 1e-6

ENGINES = ("sync", "tensor", "vector", "scalar", "gpsimd")


class Ctr:
    __slots__ = ("h", "n", "name")

    def __init__(self, h, name):
        self.h = h
        self.n = 0
        self.name = name


class Prog:
    def __init__(self):
        self.nc = bass.Bass("TRN2", target_bir_lowering=False, debug=False)
        self.stack = ExitStack()
        self.root = self.stack
        self._sems = {}

    def sem(self, name):
        if name not in self._sems:
            h = self.root.enter_context(self.nc.semaphore(name))
            self._sems[name] = Ctr(h, name)
        return self._sems[name]

    def _uname(self, name):
        if not hasattr(self, "_names"):
            self._names = {}
        n = self._names.get(name, 0)
        self._names[name] = n + 1
        return name if n == 0 else f"{name}__{n}"

    def sbuf(self, name, shape, dtype):
        return self.stack.enter_context(
            self.nc.sbuf_tensor(self._uname(name), shape, dtype))

    def psum(self, name, shape, dtype):
        return self.stack.enter_context(
            self.nc.psum_tensor(self._uname(name), shape, dtype))

    @contextmanager
    def scope(self):
        outer = self.stack
        self.stack = ExitStack()
        try:
            yield
        finally:
            self.stack.close()
            self.stack = outer


class Emit:

    def __init__(self, p):
        self.p = p
        self.pend = {e: [] for e in ENGINES}
        self.eng_ctr = {e: p.sem("c_" + e) for e in ENGINES if e != "sync"}
        self.dma_sems = {}
        self.last_w = {}
        self.readers = {}
        self.hw = {e: {} for e in ENGINES}

    def dma_sem(self, eng, key):
        k = (eng, key)
        if k not in self.dma_sems:
            self.dma_sems[k] = self.p.sem(f"d{len(self.dma_sems)}_{key[:12]}")
        return self.dma_sems[k]

    def op(self, eng, fn, reads=(), writes=(), dma=False, n_dma=1, self_inc=0,
           dkey=None):
        if dma or self_inc:
            if dkey is None:
                dkey = writes[0] if writes else (reads[0] if reads else "anon")
            ctr = self.dma_sem(eng, dkey)
        else:
            ctr = self.eng_ctr[eng]
        waits = {}
        own_te = self.eng_ctr.get("tensor") if eng == "tensor" else None

        def need(dep):
            if dep is None:
                return
            c, n = dep
            if c is own_te:
                return
            if n <= 0 or self.hw[eng].get(c, 0) >= n:
                return
            waits[c] = max(waits.get(c, 0), n)

        for b in reads:
            need(self.last_w.get(b))
        for b in writes:
            need(self.last_w.get(b))
            for c, n in self.readers.get(b, {}).items():
                need((c, n))

        for c, n in waits.items():
            self.hw[eng][c] = max(self.hw[eng].get(c, 0), n)
        wl = [(c.h, n) for c, n in waits.items()]

        if self_inc:
            ctr.n += self_inc
        else:
            ctr.n += 16 * n_dma if dma else 1
        my_cnt = ctr.n
        inc_k = 16 if (dma or self_inc) else 1

        def closure(e, fn=fn, wl=wl, ctr=ctr, self_inc=self_inc, inc_k=inc_k):
            for (sh, n) in wl:
                e.wait_ge(sh, n)
            last = fn(e)
            if self_inc:
                return
            assert last is not None, "op fn must return its last instruction(s)"
            if not isinstance(last, (list, tuple)):
                last = [last]
            if inc_k == 16:
                for ins in last:
                    ins.then_inc(ctr.h, inc_k)
            else:
                last[-1].then_inc(ctr.h, 1)

        self.pend[eng].append(closure)
        for b in reads:
            self.readers.setdefault(b, {})[ctr] = my_cnt
        for b in writes:
            self.last_w[b] = (ctr, my_cnt)
            self.readers[b] = {}
        return (ctr, my_cnt)

    def barrier(self, exclude=()):
        snap = [(c, c.n) for c in
                list(self.eng_ctr.values()) + list(self.dma_sems.values())
                if c.n > 0 and c not in exclude]
        for eng in ENGINES:
            def cl(e, snap=snap):
                for c, n in snap:
                    e.wait_ge(c.h, n)
            self.pend[eng].append(cl)
            for c, n in snap:
                self.hw[eng][c] = max(self.hw[eng].get(c, 0), n)

    def flush(self):
        with self.p.nc.Block() as block:
            for e in ENGINES:
                ops = self.pend[e]
                if not ops:
                    continue

                def run_all(eng, ops=ops):
                    for cl in ops:
                        cl(eng)
                getattr(block, e)(run_all)
            final = [(c.h, c.n) for c in
                     list(self.eng_ctr.values()) + list(self.dma_sems.values()) if c.n > 0]

            def tail(eng):
                for (sh, n) in final:
                    eng.wait_ge(sh, n)
            block.sync(tail)
        self.pend = {e: [] for e in ENGINES}


def make_const(e, g_eng, ap, key, kind, sq=128):
    if kind == "ones":
        e.op(g_eng, lambda g: g.memset(ap, 1.0), writes=(key,))
        return
    fill0, cmp = (0.0, ALU.not_equal) if kind == "ident" else (1.0, ALU.is_gt)
    if kind == "ident":
        e.op(g_eng, lambda g: g.memset(ap, 0.0), writes=(key,))
        e.op(g_eng, lambda g: g.affine_select(out=ap, in_=ap, compare_op=ALU.not_equal,
                                              fill=1.0, base=0, pattern=[[-1, sq]],
                                              channel_multiplier=1),
             reads=(key,), writes=(key,))
    else:
        e.op(g_eng, lambda g: g.memset(ap, 1.0), writes=(key,))
        e.op(g_eng, lambda g: g.affine_select(out=ap, in_=ap, compare_op=ALU.is_gt,
                                              fill=0.0, base=0, pattern=[[-1, sq]],
                                              channel_multiplier=1),
             reads=(key,), writes=(key,))


def emit_ln_tiles(e, p, x_d, u_fm, bufs, n_tiles=NTILE, stat_out=None, ukey="u",
                  src_key=None):
    SY, TE, VE, SC = "sync", "tensor", "vector", "scalar"
    xt, stat, ubf, psT, ident_bf = (bufs["xt"], bufs["stat"], bufs["ubf"],
                                    bufs["psT"], bufs["ident_bf"])
    for i in range(n_tiles):
        xb, sb, ub = xt[i % 2], stat[i % 2], ubf[i % 2]
        xk, sk, uk = f"xt{i % 2}", f"stat{i % 2}", f"ubf{i % 2}"
        e.op(SY, lambda s, i=i, xb=xb: s.dma_start(
            out=xb[:], in_=x_d[i * 128:(i + 1) * 128, :]), writes=(xk,), dma=True,
            reads=(() if src_key is None else tuple(src_key)))
        e.op(VE, lambda v, xb=xb, sb=sb: v.bn_stats(out=sb[:, 0:6], in_=xb[:, 0:512]),
             reads=(xk,), writes=(sk,))
        e.op(VE, lambda v, xb=xb, sb=sb: v.bn_stats(out=sb[:, 6:12], in_=xb[:, 512:1024]),
             reads=(xk,), writes=(sk,))
        e.op(VE, lambda v, sb=sb: v.bn_aggr(out=sb[:, 12:14], in_=sb[:, 0:12]),
             reads=(sk,), writes=(sk,))
        e.op(VE, lambda v, sb=sb: v.tensor_scalar_add(sb[:, 13:14], sb[:, 13:14],
                                                      float(LN_EPS)),
             reads=(sk,), writes=(sk,))
        e.op(SC, lambda s, sb=sb: s.activation(out=sb[:, 14:15], in_=sb[:, 13:14],
                                               func=AF.Ln),
             reads=(sk,), writes=(sk,))
        e.op(SC, lambda s, sb=sb: s.activation(out=sb[:, 15:16], in_=sb[:, 14:15],
                                               func=AF.Exp, scale=-0.5),
             reads=(sk,), writes=(sk,))
        e.op(VE, lambda v, xb=xb, sb=sb, ub=ub: v.tensor_scalar(
            out=ub[:], in0=xb[:], scalar1=sb[:, 12:13], scalar2=sb[:, 15:16],
            op0=ALU.subtract, op1=ALU.mult), reads=(xk, sk), writes=(uk,))
        if stat_out is not None:
            e.op(VE, lambda v, sb=sb, i=i: v.tensor_copy(
                out=stat_out[:, 2 * i:2 * i + 2],
                in_=sb[:].rearrange("p (a b) -> p a b", b=1)[:, (12, 15), :].squeeze(2)
                if False else sb[:, 12:16:3]),
                reads=(sk,), writes=("lnstat_" + ukey,))

        def fn_tr(t, ub=ub):
            last = None
            for dc in range(8):
                last = t.transpose(psT[:, dc * 128:(dc + 1) * 128],
                                   ub[:, dc * 128:(dc + 1) * 128], ident_bf[:])
            return last
        e.op(TE, fn_tr, reads=(uk, "identbf"), writes=("psT",))
        e.op(VE, lambda v, i=i: v.tensor_copy(
            out=u_fm[:, :, i * 128:(i + 1) * 128],
            in_=psT[:].rearrange("p (a b) -> p a b", b=128)),
            reads=("psT",), writes=("ufm_" + ukey,))


def emit_proj_fm(e, dst, dk, u_fm, wsel, wkeys, psB, elu_bufs=None,
                 n_tg=4, psb_key="psB", ukey="ufm_u"):
    TE, VE, SC = "tensor", "vector", "scalar"
    for oc in range(8):
        for tg in range(n_tg):
            def fn_mm(t, oc=oc, tg=tg):
                last = None
                for dc in range(8):
                    last = t.matmul(psB[:], wsel(dc, oc),
                                    u_fm[:, dc, tg * 512:(tg + 1) * 512],
                                    start=(dc == 0), stop=(dc == 7))
                return last
            e.op(TE, fn_mm, reads=wkeys + (ukey,), writes=(psb_key,))
            if elu_bufs is not None:
                elu_sc, elu_sc2 = elu_bufs
                e.op(VE, lambda v: v.tensor_scalar(out=elu_sc[:], in0=psB[:], scalar1=0.0,
                                                   scalar2=None, op0=ALU.min),
                     reads=(psb_key,), writes=("elusc",))
                e.op(SC, lambda s: s.activation(out=elu_sc2[:], in_=elu_sc[:], func=AF.Exp),
                     reads=("elusc",), writes=("elusc2",))
                e.op(VE, lambda v, oc=oc, tg=tg: v.scalar_tensor_tensor(
                    out=dst[:, oc, tg * 512:(tg + 1) * 512], in0=psB[:], scalar=0.0,
                    in1=elu_sc2[:], op0=ALU.max, op1=ALU.add),
                    reads=(psb_key, "elusc2"), writes=(dk,))
            else:
                e.op(VE, lambda v, oc=oc, tg=tg: v.tensor_copy(
                    out=dst[:, oc, tg * 512:(tg + 1) * 512], in_=psB[:]),
                    reads=(psb_key,), writes=(dk,))


def emit_v_tm(e, v_tm, vk, u_fm, udk, wsel, wkeys, psB, psb_key="psB"):
    TE, VE = "tensor", "vector"
    for ti in range(NTILE):
        for ng in range(2):
            def fn_mm(t, ti=ti, ng=ng):
                last = None
                for dc in range(8):
                    last = t.matmul(psB[:], u_fm[:, dc, ti * 128:(ti + 1) * 128],
                                    wsel(dc, ng), start=(dc == 0), stop=(dc == 7))
                return last
            e.op(TE, fn_mm, reads=wkeys + ("ufm_" + udk,), writes=(psb_key,))
            e.op(VE, lambda v, ti=ti, ng=ng: v.tensor_copy(
                out=v_tm[:, ti].rearrange("p (h x) -> p h x", x=65)[:, ng * 8:(ng + 1) * 8, 0:64],
                in_=psB[:].rearrange("p (h d) -> p h d", d=64)),
                reads=(psb_key,), writes=(vk,))


def emit_p1(p, e, C, io, car):
    x_d, wq_d, wk_d, wv_d = io["x"], io["Wq"], io["Wk"], io["Wv"]
    wa_d, ba_d = io["Wa"], io["ba"]
    chs_d, cha_d, stage1 = io["chunkS"], io["chunkA"], io["stage1"]
    q_fm, k_fm, v_tm, lna_tm = car

    xt = [p.sbuf(f"xt{i}", [128, D], F32) for i in range(2)]
    stat = [p.sbuf(f"stat{i}", [128, 16], F32) for i in range(2)]
    ubf = [p.sbuf(f"ubf{i}", [128, D], BF16) for i in range(2)]
    u_fm = p.sbuf("ufm", [128, 8, NT], BF16)
    lna_fm = p.sbuf("lnafm", [16, NT], F32)
    a_fm = p.sbuf("afm", [16, NT], F32)
    wmat = p.sbuf("wmat", [128, 16, D], BF16)
    wa_sb = p.sbuf("wasb", [128, 8 * H], BF16)
    ba2_sb = p.sbuf("ba2sb", [16, 1], F32)
    elu_sc = p.sbuf("elusc", [128, 512], F32)
    elu_sc2 = p.sbuf("elusc2", [128, 512], F32)
    ident_bf, ident_f, msl = C["identbf"], C["identf"], C["msl"]
    ones_col, ones_row = C["onescolf"], C["onesrow"]
    k_tm = p.sbuf("ktm", [128, D], BF16)
    khat = p.sbuf("khat", [128, D], BF16)
    escale = p.sbuf("escale", [128, H], BF16)
    arow = p.sbuf("arow", [1, H], F32)
    s_run = p.sbuf("srun", [128, 520], F32)
    s_stash = p.sbuf("sstash", [128, 520], F32)

    psB = p.psum("psB", [128, 512], F32)
    psT = p.psum("psT", [128, 1024], BF16)
    psS = p.psum("psS", [128, 1024], F32)
    psR = p.psum("psR", [128, 128], F32)

    SY, TE, VE, SC, GP = ENGINES

    e.op(GP, lambda g: g.memset(
        v_tm[:].rearrange("p h (x y) -> p h x y", y=65)[:, :, :, 64:65], 1.0),
        writes=("vtm_ones",))

    def load_w(dst_half, w_d, tag):
        def fn(g):
            return [g.dma_start(out=wmat[:, dst_half * 8 + dc, :],
                                in_=w_d[dc * 128:(dc + 1) * 128, :]) for dc in range(8)]
        e.op(GP, fn, writes=(tag,), dma=True, n_dma=8)

    load_w(0, wk_d, "wmatA")
    load_w(1, wq_d, "wmatB")

    def fn_wa(g):
        ins = [g.dma_start(out=wa_sb[:, dc * H:(dc + 1) * H],
                           in_=wa_d[dc * 128:(dc + 1) * 128, :]) for dc in range(8)]
        ins.append(g.dma_start(out=ba2_sb[:], in_=ba_d[:].rearrange("(h x) -> h x", x=1)))
        return ins
    e.op(GP, fn_wa, writes=("wasb", "ba2"), dma=True, n_dma=9)
    e.op(VE, lambda v: v.tensor_scalar_add(ba2_sb[:], ba2_sb[:], 2.0),
         reads=("ba2",), writes=("ba2",))

    emit_ln_tiles(e, p, x_d, u_fm,
                  dict(xt=xt, stat=stat, ubf=ubf, psT=psT, ident_bf=ident_bf), ukey="u")

    emit_proj_fm(e, k_fm, "kfm", u_fm,
                 lambda dc, oc: wmat[:, dc, oc * 128:(oc + 1) * 128], ("wmatA",),
                 psB, elu_bufs=(elu_sc, elu_sc2))
    load_w(0, wv_d, "wmatA")
    emit_proj_fm(e, q_fm, "qfm", u_fm,
                 lambda dc, oc: wmat[:, 8 + dc, oc * 128:(oc + 1) * 128], ("wmatB",),
                 psB, elu_bufs=(elu_sc, elu_sc2))

    emit_v_tm(e, v_tm, "vtm", u_fm, "u",
              lambda dc, ng: wmat[:, dc, ng * 512:(ng + 1) * 512], ("wmatA",), psB)

    for tg in range(4):
        def fn_amm(t, tg=tg):
            last = None
            for dc in range(8):
                last = t.matmul(psB[0:16, :], wa_sb[:, dc * H:(dc + 1) * H],
                                u_fm[:, dc, tg * 512:(tg + 1) * 512],
                                start=(dc == 0), stop=(dc == 7))
            return last
        e.op(TE, fn_amm, reads=("wasb", "ufm_u"), writes=("psB",))
        e.op(SC, lambda s, tg=tg: s.activation(
            out=a_fm[:, tg * 512:(tg + 1) * 512], in_=psB[0:16, :], func=AF.Sigmoid,
            bias=ba2_sb[:], scale=1.0), reads=("psB", "ba2"), writes=("afm",))
    e.op(SC, lambda s: s.activation(out=lna_fm[:], in_=a_fm[:], func=AF.Ln),
         reads=("afm",), writes=("lnafm",))

    for grp in range(4):
        def fn_tr(t, grp=grp):
            last = None
            for cc in range(4):
                c = grp * 4 + cc
                last = t.transpose(psR[:, cc * 16:(cc + 1) * 16],
                                   lna_fm[:, c * 128:(c + 1) * 128], ident_f[0:16, 0:16])
            return last
        e.op(TE, fn_tr, reads=("lnafm", "identf"), writes=("psR",))
        e.op(VE, lambda v, grp=grp: v.tensor_copy(
            out=lna_tm[:, grp * 64:(grp + 1) * 64], in_=psR[:, 0:64]),
            reads=("psR",), writes=("lnatm",))

    e.op(VE, lambda v: v.memset(s_run[:], 0.0), writes=("srun",))
    for c in range(16):
        lna_c = lna_tm[:, c * 16:(c + 1) * 16]

        def fn_rev(t, lna_c=lna_c):
            t.matmul(psR[:, 0:16], msl[:], lna_c, start=True, stop=True)
            return t.matmul(psR[0:1, 16:32], ones_col[:], lna_c, start=True, stop=True)
        e.op(TE, fn_rev, reads=("lnatm", "msl", "onescol"), writes=("psR",))
        e.op(SC, lambda s: s.activation(out=escale[:], in_=psR[:, 0:16], func=AF.Exp),
             reads=("psR",), writes=("escale",))
        e.op(SC, lambda s: s.activation(out=arow[:], in_=psR[0:1, 16:32], func=AF.Exp),
             reads=("psR",), writes=("arow",))

        def fn_ktr(t, c=c):
            last = None
            for dc in range(8):
                last = t.transpose(psT[:, dc * 128:(dc + 1) * 128],
                                   k_fm[:, dc, c * 128:(c + 1) * 128], ident_bf[:])
            return last
        e.op(TE, fn_ktr, reads=("kfm", "identbf"), writes=("psT",))
        e.op(VE, lambda v: v.tensor_copy(out=k_tm[:], in_=psT[:]),
             reads=("psT",), writes=("ktm",))
        e.op(VE, lambda v: v.tensor_tensor(
            out=khat[:].rearrange("p (h d) -> p h d", d=64),
            in0=k_tm[:].rearrange("p (h d) -> p h d", d=64),
            in1=escale[:].unsqueeze(2).broadcast_to([128, 16, 64]), op=ALU.mult),
            reads=("ktm", "escale"), writes=("khat",))

        def fn_sadd(t, c=c):
            last = None
            for h in range(16):
                pb = 64 * (h % 2)
                j = h // 2
                off = (j // 4) * 512 + (j % 4) * 65
                last = t.matmul(psS[pb:pb + 64, off:off + 65],
                                khat[:, h * 64:(h + 1) * 64],
                                v_tm[:, c, h * 65:(h + 1) * 65],
                                start=True, stop=True)
            return last
        e.op(TE, fn_sadd, reads=("khat", "vtm", "vtm_ones"), writes=("psS",))

        def fn_abc(t):
            t.matmul(psR[0:64, 32:40], ones_row[0:1, 0:64], arow[:, 0:8],
                     start=True, stop=True)
            return t.matmul(psR[64:128, 32:40], ones_row[0:1, 0:64], arow[:, 8:16],
                            start=True, stop=True)
        e.op(TE, fn_abc, reads=("arow", "onesrow"), writes=("psR",))

        psS_v = psS[:].rearrange("p (b s) -> p b s", s=512)[:, :, 0:260]\
            .rearrange("p b (j x) -> p b j x", x=65)
        e.op(VE, lambda v, psS_v=psS_v: v.tensor_copy(
            out=s_stash[:].rearrange("p (b j x) -> p b j x", b=2, j=4), in_=psS_v),
             reads=("psS",), writes=("sstash",))
        e.op(VE, lambda v: v.tensor_tensor(
            out=s_run[:].rearrange("p (j x) -> p j x", x=65),
            in0=s_run[:].rearrange("p (j x) -> p j x", x=65),
            in1=psR[0:128, 32:40].unsqueeze(2).broadcast_to([128, 8, 65]),
            op=ALU.mult), reads=("srun", "psR"), writes=("srun",))
        e.op(VE, lambda v, psS_v=psS_v: v.tensor_tensor(
            out=s_run[:].rearrange("p (b j x) -> p b j x", b=2, j=4),
            in0=s_run[:].rearrange("p (b j x) -> p b j x", b=2, j=4),
            in1=psS_v, op=ALU.add),
             reads=("srun", "psS"), writes=("srun",))

        e.op(SY, lambda s, c=c: s.dma_start(out=chs_d[c], in_=s_stash[:]),
             reads=("sstash",), dma=True)
        e.op(SY, lambda s, c=c: s.dma_start(out=cha_d[c:c + 1, :], in_=arow[:]),
             reads=("arow",), dma=True)

    def fn_ac(t):
        last = None
        for c in range(16):
            last = t.matmul(psR[0:1, 48:64], ones_col[:], lna_tm[:, c * 16:(c + 1) * 16],
                            start=(c == 0), stop=(c == 15))
        return last
    e.op(TE, fn_ac, reads=("lnatm", "onescol"), writes=("psR",))
    e.op(SC, lambda s: s.activation(out=arow[:], in_=psR[0:1, 48:64], func=AF.Exp),
         reads=("psR",), writes=("arow",))

    z16 = p.sbuf("z16", [128, 16], F32)
    e.op(GP, lambda g: g.memset(z16[:], 0.0), writes=("z16",))
    e.op(VE, lambda v: v.tensor_copy(out=z16[0:1, :], in_=arow[:]),
         reads=("arow", "z16"), writes=("z16",))
    e.op(SY, lambda s: s.dma_start(out=stage1[:, 0:520], in_=s_run[:]),
         reads=("srun",), writes=("stage1",), dma=True, dkey="stage1")
    e.op(SY, lambda s: s.dma_start(out=stage1[:, 520:536], in_=z16[:]),
         reads=("z16",), writes=("stage1",), dma=True, dkey="stage1")


def emit_p2(p, e, C, io, car, upto=16, lvl=99):
    x_d, wo_d = io["x"], io["AWo"]
    chs_d, cha_d = io["chunkS"], io["chunkA"]
    gath1, pm_d = io["gath1"], io["pm"]
    x1_d, stage2 = io["x1"], io["stage2"]
    q_fm, k_fm, v_tm, lna_tm = car

    sprev = p.sbuf("sprev", [128, 3, 536], F32)
    aprev = p.sbuf("aprev", [1, 3 * H], F32)
    pm_sb = p.sbuf("pmsb", [1, 8], F32)
    pmb = p.sbuf("pmb", [128, 8], F32)
    chs = [p.sbuf(f"chs{i}", [128, 520], F32) for i in range(2)]
    cha = p.sbuf("cha", [1, 16 * H], F32)
    wo_sb = p.sbuf("wosb", [128, 8, D], BF16)
    s_run = p.sbuf("srun", [128, 520], F32)
    s_run_bf = p.sbuf("srunbf", [128, 520], BF16)
    ident_bf, ident_f = C["identbf"], C["identf"]
    mle, msl, mneg = C["mle"], C["msl"], C["mneg"]
    ones_row = C["onesrow"]
    lexp_tm = p.sbuf("lexptm", [128, H], F32)
    msl_sc = p.sbuf("mslsc", [128, 256], F32)
    dpair = p.sbuf("dpair", [128, 256], BF16)
    apair = p.sbuf("apair", [128, 256], BF16)
    apair_v = apair[:].rearrange("p (b s) -> p b s", s=128)
    cr_sb = p.sbuf("crsb", [128, 130], F32)
    n_sb = p.sbuf("nsb", [128, 16, 65], F32)
    dr_sb = p.sbuf("drsb", [128, H], F32)
    obuf = p.sbuf("obuf", [128, D], BF16)
    attn_fm = p.sbuf("attnfm", [128, 8, 128], BF16)
    xt = [p.sbuf(f"xt{i}", [128, D], F32) for i in range(2)]
    x1t = p.sbuf("x1t", [128, D], F32)
    u1t = p.sbuf("u1t", [128, D], F32)
    stat = p.sbuf("stat0", [128, 16], F32)

    psBD = p.psum("psBD", [128, 1024], F32)
    psN = p.psum("psN", [128, 1536], F32)
    psY = p.psum("psY", [128, 512], F32)
    psT = p.psum("psT", [128, 1024], BF16)
    psR = p.psum("psR", [128, 512], F32)

    SY, TE, VE, SC, GP = ENGINES

    def fn_ldprev(s):
        return [s.dma_start(out=sprev[:, k, :], in_=gath1[k * 128:(k + 1) * 128, :])
                for k in range(3)]
    e.op(SY, fn_ldprev, reads=("gath1",), writes=("sprev",), dma=True, n_dma=3)
    e.op(SY, lambda s: s.dma_start(out=pm_sb[:], in_=pm_d[:]),
         writes=("pm",), dma=True, dkey="pmld")
    e.op(SY, lambda s: s.dma_start(out=cha[:], in_=cha_d[:].rearrange("k h -> (k h)").rearrange("(o n) -> o n", o=1)), writes=("cha",), dma=True)
    e.op(GP, lambda g: g.dma_start(out=wo_sb[:],
                                   in_=wo_d[:].rearrange("(a p) b -> p a b", p=128)),
         writes=("wosb",), dma=True)

    def emit_abc(src_row, tag, extra=()):
        def fn(t):
            t.matmul(psR[0:64, 16:24], ones_row[0:1, 0:64], src_row[:, 0:16:2],
                     start=True, stop=True)
            return t.matmul(psR[64:128, 16:24], ones_row[0:1, 0:64], src_row[:, 1:16:2],
                            start=True, stop=True)
        e.op(TE, fn, reads=(tag, "onesrow") + tuple(extra), writes=("psR2k",))

    e.op(TE, lambda t: t.matmul(psR[:, 32:40], ones_row[0:1, :], pm_sb[0:1, :],
                                start=True, stop=True),
         reads=("pm", "onesrow"), writes=("psR2k",))
    e.op(VE, lambda v: v.tensor_copy(out=pmb[:], in_=psR[:, 32:40]),
         reads=("psR2k",), writes=("pmb",))
    def fn_aprime(v):
        return [v.tensor_scalar(out=aprev[0:1, k * 16:(k + 1) * 16],
                                in0=sprev[0:1, k, 520:536],
                                scalar1=pm_sb[0:1, k:k + 1],
                                scalar2=pm_sb[0:1, 4 + k:5 + k],
                                op0=ALU.mult, op1=ALU.add) for k in range(3)]
    e.op(VE, fn_aprime, reads=("sprev", "pm"), writes=("aprev",))
    e.op(VE, lambda v: v.tensor_scalar(out=s_run[:], in0=sprev[:, 0, 0:520],
                                       scalar1=pmb[:, 0:1], scalar2=None, op0=ALU.mult),
         reads=("sprev", "pmb"), writes=("srun",))
    for k in range(1, 3):
        emit_abc(aprev[0:1, k * 16:(k + 1) * 16], "aprev")
        e.op(VE, lambda v: v.tensor_tensor(
            out=s_run[:].rearrange("p (j x) -> p j x", x=65),
            in0=s_run[:].rearrange("p (j x) -> p j x", x=65),
            in1=psR[0:128, 16:24].unsqueeze(2).broadcast_to([128, 8, 65]),
            op=ALU.mult), reads=("srun", "psR2k"), writes=("srun",))
        e.op(VE, lambda v, k=k: v.scalar_tensor_tensor(
            out=s_run[:], in0=sprev[:, k, 0:520], scalar=pmb[:, k:k + 1],
            in1=s_run[:], op0=ALU.mult, op1=ALU.add),
             reads=("srun", "sprev", "pmb"), writes=("srun",))

    for c in range(upto):
        lna_c = lna_tm[:, c * 16:(c + 1) * 16]
        if lvl >= 0.2:
            e.op(SY, lambda s, c=c: s.dma_start(out=chs[c % 2][:], in_=chs_d[c]),
                 writes=(f"chs{c % 2}",), dma=True)

        if lvl >= 0.4:
            e.op(VE, lambda v: v.tensor_copy(out=s_run_bf[:], in_=s_run[:]),
                 reads=("srun",), writes=("srunbf",))

        if lvl >= 0.6:
            def fn_l(t, lna_c=lna_c):
                return t.matmul(psR[:, 0:16], mle[:], lna_c, start=True, stop=True)
            e.op(TE, fn_l, reads=("lnatm", "mle"), writes=("psR2k",))
        if 0.7 <= lvl < 0.72:
            e.op(VE, lambda v: v.tensor_copy(out=lexp_tm[:], in_=psR[:, 0:16]),
                 reads=("psR2k",), writes=("lexptm",))
        if 0.72 <= lvl < 0.74:
            e.op(SC, lambda s: s.copy(out=lexp_tm[:], in_=psR[:, 0:16]),
                 reads=("psR2k",), writes=("lexptm",))
        if 0.74 <= lvl < 0.76:
            e.op(SC, lambda s: s.activation(out=lexp_tm[:], in_=s_run[:, 0:16],
                                            func=AF.Exp),
                 reads=("srun",), writes=("lexptm",))
        if lvl >= 0.8:
            e.op(SC, lambda s: s.activation(out=lexp_tm[:], in_=psR[:, 0:16], func=AF.Exp),
                 reads=("psR2k",), writes=("lexptm",))

        for hp in range(8 if lvl >= 2 else 0):
            def fn_msc(v, hp=hp, lna_c=lna_c):
                return [v.tensor_scalar(out=msl_sc[:, j * 128:(j + 1) * 128], in0=msl[:],
                                        scalar1=lna_c[:, 2 * hp + j:2 * hp + j + 1],
                                        scalar2=None, op0=ALU.mult) for j in range(2)]
            e.op(VE, fn_msc, reads=("msl", "lnatm"), writes=("mslsc",))

            def fn_darg(t, hp=hp):
                last = []
                for j in range(2):
                    t.matmul(psBD[:, 256 + j * 512:384 + j * 512],
                             msl_sc[:, j * 128:(j + 1) * 128], mle[:],
                             start=True, stop=False)
                    last.append(t.matmul(psBD[:, 256 + j * 512:384 + j * 512],
                                         ident_f[:], mneg[:], start=False, stop=True))
                return last[-1]
            e.op(TE, fn_darg, reads=("mslsc", "mle", "mneg", "identf"), writes=("psBD",))

            def fn_dexp(s, hp=hp):
                return [s.activation(out=dpair[:, j * 128:(j + 1) * 128],
                                     in_=psBD[:, 256 + j * 512:384 + j * 512], func=AF.Exp)
                        for j in range(2)]
            e.op(SC, fn_dexp, reads=("psBD",), writes=("dpair",))

            def fn_qk(t, c=c, hp=hp):
                last = []
                for s in range(2):
                    pb = 64 * s
                    last.append(t.matmul(
                        psBD[:, s * 512:s * 512 + 128],
                        k_fm[pb:pb + 64, hp, c * 128:(c + 1) * 128],
                        q_fm[pb:pb + 64, hp, c * 128:(c + 1) * 128],
                        start=True, stop=True))
                return last
            e.op(TE, fn_qk, reads=("kfm", "qfm"), writes=("psBD",))
            psBD_qk = psBD[:].rearrange("p (b s) -> p b s", s=512)[:, :, 0:128]
            dpair_v = dpair[:].rearrange("p (b s) -> p b s", s=128)
            e.op(VE, lambda v: v.tensor_tensor(out=apair_v, in0=psBD_qk,
                                               in1=dpair_v, op=ALU.mult),
                 reads=("psBD", "dpair"), writes=("apair",))

            if lvl < 3:
                continue
            def fn_n(t, c=c, hp=hp):
                last = []
                for s in range(2):
                    h = 2 * hp + s
                    t.matmul(psN[:, s * 65:(s + 1) * 65], apair[:, s * 128:(s + 1) * 128],
                             v_tm[:, c, h * 65:(h + 1) * 65], start=True, stop=True)
                    pb = 64 * s
                    last.append(t.matmul(
                        psN[:, 512 * (s + 1):512 * (s + 1) + 65],
                        q_fm[pb:pb + 64, hp, c * 128:(c + 1) * 128],
                        s_run_bf[pb:pb + 64, hp * 65:(hp + 1) * 65],
                        start=True, stop=True))
                return last
            e.op(TE, fn_n, reads=("apair", "vtm", "qfm", "srunbf"), writes=("psN",))

            def fn_cr(s):
                return [s.copy(out=cr_sb[:, j * 65:(j + 1) * 65],
                               in_=psN[:, 512 * (j + 1):512 * (j + 1) + 65])
                        for j in range(2)]
            e.op(SC, fn_cr, reads=("psN",), writes=("crsb",))

            def fn_comb(v, hp=hp):
                return [v.scalar_tensor_tensor(
                    out=n_sb[:, 2 * hp + s, :], in0=cr_sb[:, s * 65:(s + 1) * 65],
                    scalar=lexp_tm[:, 2 * hp + s:2 * hp + s + 1],
                    in1=psN[:, s * 65:(s + 1) * 65], op0=ALU.mult, op1=ALU.add)
                    for s in range(2)]
            e.op(VE, fn_comb, reads=("crsb", "lexptm", "psN"), writes=("nsb",))

        if lvl < 4:
            continue
        e.op(VE, lambda v: v.tensor_scalar_add(dr_sb[:], n_sb[:, :, 64], float(GLA_EPS)),
             reads=("nsb",), writes=("drsb",))
        e.op(VE, lambda v: v.reciprocal(out=dr_sb[:], in_=dr_sb[:]),
             reads=("drsb",), writes=("drsb",))
        e.op(VE, lambda v: v.tensor_tensor(
            out=obuf[:].rearrange("p (h d) -> p h d", d=64),
            in0=n_sb[:, :, 0:64],
            in1=dr_sb[:].unsqueeze(2).broadcast_to([128, 16, 64]), op=ALU.mult),
            reads=("nsb", "drsb"), writes=("obuf",))

        emit_abc(cha[0:1, c * 16:(c + 1) * 16], "cha")
        e.op(VE, lambda v: v.tensor_tensor(
            out=s_run[:].rearrange("p (j x) -> p j x", x=65),
            in0=s_run[:].rearrange("p (j x) -> p j x", x=65),
            in1=psR[0:128, 16:24].unsqueeze(2).broadcast_to([128, 8, 65]),
            op=ALU.mult), reads=("srun", "psR2k"), writes=("srun",))
        e.op(VE, lambda v, c=c: v.tensor_tensor(out=s_run[:], in0=s_run[:],
                                                in1=chs[c % 2][:], op=ALU.add),
             reads=("srun", f"chs{c % 2}"), writes=("srun",))

        def fn_tr(t):
            last = None
            for dc in range(8):
                last = t.transpose(psT[:, dc * 128:(dc + 1) * 128],
                                   obuf[:, dc * 128:(dc + 1) * 128], ident_bf[:])
            return last
        e.op(TE, fn_tr, reads=("obuf", "identbf"), writes=("psT",))
        e.op(VE, lambda v: v.tensor_copy(
            out=attn_fm[:], in_=psT[:].rearrange("p (a b) -> p a b", b=128)),
            reads=("psT",), writes=("attnfm",))

        e.op(SY, lambda s, c=c: s.dma_start(out=xt[c % 2][:],
                                            in_=x_d[c * 128:(c + 1) * 128, :]),
             writes=(f"xt{c % 2}",), dma=True)

        for ng in range(2):
            def fn_wo(t, ng=ng):
                last = None
                for pc in range(8):
                    last = t.matmul(psY[:], attn_fm[:, pc, :],
                                    wo_sb[:, pc, ng * 512:(ng + 1) * 512],
                                    start=(pc == 0), stop=(pc == 7))
                return last
            e.op(TE, fn_wo, reads=("attnfm", "wosb"), writes=("psY",))
            e.op(VE, lambda v, c=c, ng=ng: v.tensor_tensor(
                out=x1t[:, ng * 512:(ng + 1) * 512],
                in0=xt[c % 2][:, ng * 512:(ng + 1) * 512],
                in1=psY[:], op=ALU.add),
                reads=(f"xt{c % 2}", "psY"), writes=("x1t",))
        e.op(SY, lambda s, c=c: s.dma_start(out=x1_d[c * 128:(c + 1) * 128, :],
                                            in_=x1t[:]),
             reads=("x1t",), dma=True, dkey="x1out")

        if c == 15:
            e.op(VE, lambda v: v.bn_stats(out=stat[:, 0:6], in_=x1t[:, 0:512]),
                 reads=("x1t",), writes=("stat0",))
            e.op(VE, lambda v: v.bn_stats(out=stat[:, 6:12], in_=x1t[:, 512:1024]),
                 reads=("x1t",), writes=("stat0",))
            e.op(VE, lambda v: v.bn_aggr(out=stat[:, 12:14], in_=stat[:, 0:12]),
                 reads=("stat0",), writes=("stat0",))
            e.op(VE, lambda v: v.tensor_scalar_add(stat[:, 13:14], stat[:, 13:14],
                                                   float(LN_EPS)),
                 reads=("stat0",), writes=("stat0",))
            e.op(SC, lambda s: s.activation(out=stat[:, 14:15], in_=stat[:, 13:14],
                                            func=AF.Ln), reads=("stat0",), writes=("stat0",))
            e.op(SC, lambda s: s.activation(out=stat[:, 15:16], in_=stat[:, 14:15],
                                            func=AF.Exp, scale=-0.5),
                 reads=("stat0",), writes=("stat0",))
            e.op(VE, lambda v: v.tensor_scalar(out=u1t[:], in0=x1t[:],
                                               scalar1=stat[:, 12:13],
                                               scalar2=stat[:, 15:16],
                                               op0=ALU.subtract, op1=ALU.mult),
                 reads=("x1t", "stat0"), writes=("u1t",))
            e.op(SY, lambda s: s.dma_start(out=stage2[:], in_=u1t[127:128, :]),
                 reads=("u1t",), writes=("stage2",), dma=True, dkey="u1out")


def emit_p3(p, e, C, io):
    x1_d, gath2 = io["x1"], io["gath2"]
    wp1_d, bp1_d, wp2_d, bp2_d = io["Wp1"], io["bp1"], io["Wp2"], io["bp2"]
    w1_d, b1_d, w2_d, b2_d = io["W1"], io["b1"], io["W2"], io["b2"]
    wo_d, bo_d = io["FWo"], io["bo"]
    x2_d, stage3 = io["x2"], io["stage3"]

    xt = [p.sbuf(f"xt{i}", [128, D], F32) for i in range(2)]
    stat = [p.sbuf(f"stat{i}", [128, 16], F32) for i in range(2)]
    ubf = [p.sbuf(f"ubf{i}", [128, D], BF16) for i in range(2)]
    u_fm = p.sbuf("ufm", [128, 8, NT], BF16)
    lnstat = p.sbuf("lnstat", [128, 2 * NTILE], F32)
    lrow = p.sbuf("lrow", [1, D], F32)
    wp1_sb = p.sbuf("wp1sb", [128, 8, 128], BF16)
    wp2_sb = p.sbuf("wp2sb", [128, 8], BF16)
    bp1_sb = p.sbuf("bp1sb", [1, 128], F32)
    bp2_sb = p.sbuf("bp2sb", [1, 8], F32)
    glog = p.sbuf("glog", [1, 8], F32)
    gmax = p.sbuf("gmax", [1, 8], F32)
    gsel = p.sbuf("gsel", [1, 8], U32)
    w1_sb = p.sbuf("w1sb", [128, 8, 2048], BF16)
    w2big = p.sbuf("w2big", [128, 16, 1024], BF16)
    w2_sb = w2big[:]
    b1r = p.sbuf("b1r", [1, 2048], F32)
    b2r = p.sbuf("b2r", [1, D], BF16)
    bor = p.sbuf("bor", [1, D], BF16)
    b1c = p.sbuf("b1c", [128, 16], F32)
    wo_sb = p.sbuf("wosb", [128, 8, D], BF16)
    ident_bf, ident_f = C["identbf"], C["identf"]
    ones_row, ones_row_bf = C["onesrow"], C["onesrowbf"]
    h_sb = p.sbuf("hsb", [128, 512], BF16)
    r_sb = p.sbuf("rsb", [128, D], BF16)
    r_fm = p.sbuf("rfm", [128, 8, 128], BF16)
    x2t = p.sbuf("x2t", [128, D], F32)
    u1t = p.sbuf("u1t", [128, D], F32)
    u2_fm = u_fm

    psB = p.psum("psB", [128, 512], F32)
    psT = p.psum("psT", [128, 1024], BF16)
    psH = p.psum("psH", [128, 1024], F32)
    psR2 = p.psum("psR2", [128, 2048], F32)

    SY, TE, VE, SC, GP = ENGINES

    e.op(SY, lambda s: s.dma_start(out=lrow[:], in_=gath2[3:4, :]),
         reads=("gath2",), writes=("lrow",), dma=True)
    e.op(GP, lambda g: g.dma_start(out=wp1_sb[:],
                                   in_=wp1_d[:].rearrange("(a p) f -> p a f", p=128)),
         writes=("wp1",), dma=True)
    e.op(GP, lambda g: g.dma_start(out=wp2_sb[:], in_=wp2_d[:]), writes=("wp2",), dma=True)
    e.op(SY, lambda s: s.dma_start(out=bp1_sb[:],
                                   in_=bp1_d[:].rearrange("(o n) -> o n", o=1)),
         writes=("bp1",), dma=True)
    e.op(SY, lambda s: s.dma_start(out=bp2_sb[:],
                                   in_=bp2_d[:].rearrange("(o n) -> o n", o=1)),
         writes=("bp2",), dma=True)

    def fn_lvt(t):
        last = None
        for dc in range(8):
            last = t.transpose(psB[:, dc:dc + 1], lrow[0:1, dc * 128:(dc + 1) * 128],
                               ident_f[0:1, 0:1])
        return last
    e.op(TE, fn_lvt, reads=("lrow", "identf"), writes=("psB",))
    lv_fm = p.sbuf("lvfm", [128, 8], BF16)
    e.op(VE, lambda v: v.tensor_copy(out=lv_fm[:], in_=psB[:, 0:8]),
         reads=("psB",), writes=("lvfm",))
    def fn_bp1(t):
        return t.transpose(psB[:, 8:9], bp1_sb[:], ident_f[0:1, 0:1])
    e.op(TE, fn_bp1, reads=("bp1", "identf"), writes=("psB",))
    bp1c = p.sbuf("bp1c", [128, 1], F32)
    e.op(VE, lambda v: v.tensor_copy(out=bp1c[:], in_=psB[:, 8:9]),
         reads=("psB",), writes=("bp1c",))

    def fn_z1(t):
        last = None
        for dc in range(8):
            last = t.matmul(psH[:, 0:1], wp1_sb[:, dc, :], lv_fm[:, dc:dc + 1],
                            start=(dc == 0), stop=(dc == 7))
        return last
    e.op(TE, fn_z1, reads=("wp1", "lvfm"), writes=("psH0",))
    z1g = p.sbuf("z1g", [128, 1], BF16)
    e.op(SC, lambda s: s.activation(out=z1g[:], in_=psH[:, 0:1], func=AF.Gelu,
                                    bias=bp1c[:], scale=1.0),
         reads=("psH0", "bp1c"), writes=("z1g",))
    def fn_z2(t):
        return t.matmul(psH[:, 4:12][0:1, :], z1g[:], wp2_sb[:], start=True, stop=True)
    e.op(TE, fn_z2, reads=("z1g", "wp2"), writes=("psH0",))
    e.op(VE, lambda v: v.tensor_add(out=glog[:], in0=psH[0:1, 4:12], in1=bp2_sb[:]),
         reads=("psH0", "bp2"), writes=("glog",))
    e.op(VE, lambda v: v.max(out=gmax[:], in_=glog[:]), reads=("glog",), writes=("gmax",))
    e.op(VE, lambda v: v.max_index(out=gsel[:], in_max=gmax[:], in_values=glog[:]),
         reads=("gmax", "glog"), writes=("gsel",))

    def fn_sel_dma(g):
        reg = g.alloc_register("selreg")
        g.load(reg, gsel[0:1, 0:1])
        dsem = e.dma_sem(GP, "expertw")
        for ex in range(8):
            with g.If_eq(reg, ex):
                g.dma_start(out=w1_sb[:],
                            in_=w1_d[ex].rearrange("(a p) f -> p a f", p=128)
                            ).then_inc(dsem.h, 16)
                g.dma_start(out=w2_sb,
                            in_=w2_d[ex].rearrange("(a p) f -> p a f", p=128)
                            ).then_inc(dsem.h, 16)
                g.dma_start(out=b1r[:],
                            in_=b1_d[ex].rearrange("(o n) -> o n", o=1)
                            ).then_inc(dsem.h, 16)
                g.dma_start(out=b2r[:],
                            in_=b2_d[ex].rearrange("(o n) -> o n", o=1)
                            ).then_inc(dsem.h, 16)
        return None
    e.op(GP, fn_sel_dma, reads=("gsel",), writes=("w1", "w2", "b1r", "b2r"),
         self_inc=64, dkey="expertw")

    for grp in range(2):
        def fn_b1(t, grp=grp):
            last = None
            for j in range(8):
                fc = grp * 8 + j
                last = t.transpose(psB[:, 16 + j:17 + j],
                                   b1r[0:1, fc * 128:(fc + 1) * 128], ident_f[0:1, 0:1])
            return last
        e.op(TE, fn_b1, reads=("b1r", "identf"), writes=("psB",))
        e.op(VE, lambda v, grp=grp: v.tensor_copy(out=b1c[:, grp * 8:(grp + 1) * 8],
                                                  in_=psB[:, 16:24]),
             reads=("psB",), writes=("b1c",))

    e.op(GP, lambda g: g.dma_start(out=wo_sb[:],
                                   in_=wo_d[:].rearrange("(a p) b -> p a b", p=128)),
         writes=("wosb",), dma=True)
    e.op(GP, lambda g: g.dma_start(out=bor[:], in_=bo_d[:].rearrange("(o n) -> o n", o=1)),
         writes=("bor",), dma=True)

    emit_ln_tiles(e, p, x1_d, u_fm,
                  dict(xt=xt, stat=stat, ubf=ubf, psT=psT, ident_bf=ident_bf),
                  stat_out=lnstat, ukey="u")

    for tp in range(8):
        for fc in range(16):
            par = fc % 2
            hps = psH[:, par * 512:par * 512 + 256]
            hsv = h_sb[:, par * 256:(par + 1) * 256]

            def fn_h(t, tp=tp, fc=fc, hps=hps):
                last = None
                for dc in range(8):
                    last = t.matmul(hps, w1_sb[:, dc, fc * 128:(fc + 1) * 128],
                                    u_fm[:, dc, tp * 256:(tp + 1) * 256],
                                    start=(dc == 0), stop=(dc == 7))
                return last
            e.op(TE, fn_h, reads=("w1", "ufm_u"), writes=(f"psH{par}",))
            e.op(SC, lambda s, fc=fc, hps=hps, hsv=hsv: s.activation(
                out=hsv, in_=hps, func=AF.Gelu,
                bias=b1c[:, fc:fc + 1], scale=1.0),
                 reads=(f"psH{par}", "b1c"), writes=(f"hsb{par}",))

            def fn_r(t, tp=tp, fc=fc, hsv=hsv):
                last = []
                for ti in range(2):
                    for ng in range(2):
                        last.append(t.matmul(
                            psR2[:, ti * 1024 + ng * 512:ti * 1024 + (ng + 1) * 512],
                            hsv[:, ti * 128:(ti + 1) * 128],
                            w2_sb[:, fc, ng * 512:(ng + 1) * 512],
                            start=(fc == 0), stop=False))
                return last[-1]
            e.op(TE, fn_r, reads=(f"hsb{par}", "w2"), writes=("psR2",))
        def fn_rb(t, tp=tp):
            last = []
            for ti in range(2):
                for ng in range(2):
                    last.append(t.matmul(
                        psR2[:, ti * 1024 + ng * 512:ti * 1024 + (ng + 1) * 512],
                        ones_row_bf[:], b2r[0:1, ng * 512:(ng + 1) * 512],
                        start=False, stop=True))
            return last[-1]
        e.op(TE, fn_rb, reads=("onesrowbf", "b2r"), writes=("psR2",))

        for ti in range(2):
            e.op(VE, lambda v, ti=ti: v.tensor_copy(out=r_sb[:],
                                                    in_=psR2[:, ti * 1024:(ti + 1) * 1024]),
                 reads=("psR2",), writes=("rsb",))

            def fn_tr(t):
                last = None
                for dc in range(8):
                    last = t.transpose(psT[:, dc * 128:(dc + 1) * 128],
                                       r_sb[:, dc * 128:(dc + 1) * 128], ident_bf[:])
                return last
            e.op(TE, fn_tr, reads=("rsb", "identbf"), writes=("psT",))
            e.op(VE, lambda v: v.tensor_copy(
                out=r_fm[:], in_=psT[:].rearrange("p (a b) -> p a b", b=128)),
                reads=("psT",), writes=("rfm",))

            i = tp * 2 + ti
            e.op(SY, lambda s, i=i: s.dma_start(out=xt[i % 2][:],
                                                in_=x1_d[i * 128:(i + 1) * 128, :]),
                 writes=(f"xt{i % 2}",), dma=True)
            for ng in range(2):
                def fn_o(t, ng=ng):
                    last = None
                    for pc in range(8):
                        last = t.matmul(psH[:, 0:512], r_fm[:, pc, :],
                                        wo_sb[:, pc, ng * 512:(ng + 1) * 512],
                                        start=(pc == 0), stop=False)
                    return t.matmul(psH[:, 0:512], ones_row_bf[:],
                                    bor[0:1, ng * 512:(ng + 1) * 512],
                                    start=False, stop=True)
                e.op(TE, fn_o, reads=("rfm", "wosb", "onesrowbf", "bor"), writes=("psH0",))
                e.op(VE, lambda v, i=i, ng=ng: v.tensor_scalar(
                    out=u1t[:, ng * 512:(ng + 1) * 512],
                    in0=xt[i % 2][:, ng * 512:(ng + 1) * 512],
                    scalar1=lnstat[:, 2 * i:2 * i + 1], scalar2=lnstat[:, 2 * i + 1:2 * i + 2],
                    op0=ALU.subtract, op1=ALU.mult),
                    reads=(f"xt{i % 2}", "lnstat_u"), writes=("u1t",))
                e.op(VE, lambda v, i=i, ng=ng: v.tensor_add(
                    out=u1t[:, ng * 512:(ng + 1) * 512],
                    in0=u1t[:, ng * 512:(ng + 1) * 512],
                    in1=xt[i % 2][:, ng * 512:(ng + 1) * 512]),
                    reads=(f"xt{i % 2}", "u1t"), writes=("u1t",))
                e.op(VE, lambda v, ng=ng: v.scalar_tensor_tensor(
                    out=x2t[:, ng * 512:(ng + 1) * 512], in0=psH[:, 0:512], scalar=0.3,
                    in1=u1t[:, ng * 512:(ng + 1) * 512], op0=ALU.mult, op1=ALU.add),
                    reads=("psH0", "u1t"), writes=("x2t",))
            e.op(SY, lambda s, i=i: s.dma_start(out=x2_d[i * 128:(i + 1) * 128, :],
                                                in_=x2t[:]),
                 reads=("x2t",), writes=(f"x2o{i}",), dma=True, dkey="x2out")

    e.op(VE, lambda v: v.engine_nop(), reads=(), writes=("ufm_u",))
    emit_ln_tiles(e, p, x2_d, u2_fm,
                  dict(xt=xt, stat=stat, ubf=ubf, psT=psT, ident_bf=ident_bf), ukey="u2",
                  src_key=[f"x2o{i}" for i in range(16)])
    e.op(SY, lambda s: s.dma_start(out=stage3[:],
                                   in_=u2_fm[:].rearrange("p a t -> p (a t)")),
         reads=("ufm_u2",), writes=("stage3",), dma=True, dkey="u2out")


def emit_p4(p, e, C, io):
    gath3 = io["gath3"]
    wq_d, wk_d, wv_d = io["WqT"], io["WkT"], io["WvT"]
    stage4 = io["stage4"]

    wq_sb = p.sbuf("wqsb", [128, 8, 256], BF16)
    wk_sb = p.sbuf("wksb", [128, 8, 256], BF16)
    wv_sb = p.sbuf("wvsb", [128, 8, 256], BF16)
    u2r = [p.sbuf(f"u2r{i}", [128, 8, NT], BF16) for i in range(2)]
    q_fm = p.sbuf("qfm", [128, 2, 4 * NT], BF16)
    k_fm = p.sbuf("kfm", [128, 2, 4 * NT], BF16)
    v_tm = p.sbuf("vtm", [128, 64, 260], BF16)
    pt_sb = p.sbuf("ptsb", [128, 2048], BF16)
    sq_sc = p.sbuf("sqsc", [128, 2048], BF16)
    bmax = p.sbuf("bmax", [1, 64], F32)
    negb2 = p.sbuf("negb2", [128, 2], F32)
    linvb = p.sbuf("linvb", [64, 512], F32)
    ot = [p.sbuf(f"ot{i}", [128, 512], BF16) for i in range(2)]
    nac = [[p.sbuf(f"nac{i}{s}", [65, 512], F32) for s in range(2)]
           for i in range(2)]
    linv2 = [[p.sbuf(f"linv{i}{s}", [1, 512], F32) for s in range(2)]
             for i in range(2)]
    ones_row, ones_col = C["onesrow"], C["onescolbf"]

    psS = p.psum("psS", [128, 2048], F32)
    psA = [p.psum("psA0", [128, 512], F32), p.psum("psA1", [128, 512], F32)]
    psB = p.psum("psB", [128, 512], F32)

    SY, TE, VE, SC, GP = ENGINES

    e.op(GP, lambda g: g.memset(
        v_tm[:].rearrange("p t (h x) -> p t h x", x=65)[:, :, :, 64:65], 1.0),
        writes=("vtm_ones",))
    e.op(GP, lambda g: g.memset(bmax[:], 0.0),
         writes=("bmax_q", "bmax_k", "bmax_t", "bmax_s"))

    for w_d, w_sb, tag in ((wq_d, wq_sb, "wq"), (wk_d, wk_sb, "wk"),
                           (wv_d, wv_sb, "wv")):
        e.op(GP, lambda g, w_d=w_d, w_sb=w_sb: g.dma_start(
            out=w_sb[:], in_=w_d.rearrange("(a p) f -> p a f", p=128)),
            writes=(tag,), dma=True, dkey=tag)

    for r in range(4):
        ub, ukk = u2r[r % 2], f"u2r{r % 2}"
        e.op(SY, lambda s, r=r, ub=ub: s.dma_start(
            out=ub[:],
            in_=gath3[r * 128:(r + 1) * 128, :].rearrange("p (a t) -> p a t", a=8)),
             reads=("gath3",), writes=(ukk,), dma=True, dkey=ukk)
        for dst, dk, tag in ((q_fm, "qfm", "wq"), (k_fm, "kfm", "wk")):
            w_sb = {"wq": wq_sb, "wk": wk_sb}[tag]
            for oc in range(2):
                for tg in range(4):
                    def fn_mm(t, oc=oc, tg=tg, ub=ub, w_sb=w_sb):
                        last = None
                        for dc in range(8):
                            last = t.matmul(psB[:], w_sb[:, dc, oc * 128:(oc + 1) * 128],
                                            ub[:, dc, tg * 512:(tg + 1) * 512],
                                            start=(dc == 0), stop=(dc == 7))
                        return last
                    e.op(TE, fn_mm, reads=(tag, ukk), writes=("psB",))
                    e.op(VE, lambda v, oc=oc, tg=tg, r=r, dst=dst, dk=dk: v.tensor_copy(
                        out=dst[:, oc, r * NT + tg * 512:r * NT + (tg + 1) * 512],
                        in_=psB[:]),
                        reads=("psB",), writes=(dk,))
        for ti in range(16):
            def fn_vm(t, ti=ti, ub=ub):
                last = None
                for dc in range(8):
                    last = t.matmul(psB[:, 0:256], ub[:, dc, ti * 128:(ti + 1) * 128],
                                    wv_sb[:, dc, :], start=(dc == 0), stop=(dc == 7))
                return last
            e.op(TE, fn_vm, reads=("wv", ukk), writes=("psB",))
            e.op(VE, lambda v, ti=ti, r=r: v.tensor_copy(
                out=v_tm[:, r * 16 + ti].rearrange("p (h x) -> p h x", x=65)[:, :, 0:64],
                in_=psB[:, 0:256].rearrange("p (h d) -> p h d", d=64)),
                reads=("psB",), writes=("vtm",))

    def emit_norms(src_ap, src_keys, out_off, tag):
        e.op(GP, lambda g: g.tensor_mul(sq_sc[:], src_ap, src_ap),
             reads=src_keys, writes=("sqsc",))

        def fn_nq(t):
            last = []
            for s in range(2):
                for g in range(4):
                    last.append(t.matmul(
                        psS[0:1, s * 1024 + (g % 2) * 512:s * 1024 + (g % 2) * 512 + 512],
                        ones_col[64 * s:64 * s + 64, :],
                        sq_sc[64 * s:64 * s + 64, g * 512:(g + 1) * 512],
                        start=(g < 2), stop=(g >= 2)))
            return last
        e.op(TE, fn_nq, reads=("sqsc", "onescol"), writes=("psS0", "psS1"))

        def fn_red(v):
            v.reduce_max(bmax[:, 30:31], psS[0:1, 0:1024], axis=AX)
            return v.reduce_max(bmax[:, 31:32], psS[0:1, 1024:2048], axis=AX)
        e.op(VE, fn_red, reads=("psS0", "psS1"), writes=("bmax_t",))
        e.op(VE, lambda v: v.tensor_max(bmax[:, out_off:out_off + 2],
                                        bmax[:, out_off:out_off + 2], bmax[:, 30:32]),
             reads=("bmax_t", tag), writes=(tag,))

    for hp in range(2):
        for r in range(4):
            emit_norms(q_fm[:, hp, r * NT:(r + 1) * NT], ("qfm",), 2 * hp, "bmax_q")
            emit_norms(k_fm[:, hp, r * NT:(r + 1) * NT], ("kfm",), 8 + 2 * hp, "bmax_k")

    for hp in range(2):
        e.op(VE, lambda v, hp=hp: v.tensor_mul(bmax[:, 24:26], bmax[:, 2 * hp:2 * hp + 2],
                                               bmax[:, 8 + 2 * hp:8 + 2 * hp + 2]),
             reads=("bmax_q", "bmax_k"), writes=("bmax_s",))
        e.op(VE, lambda v: v.tensor_max(bmax[:, 26:27], bmax[:, 24:25], bmax[:, 25:26]),
             reads=("bmax_s",), writes=("bmax_s",))
        e.op(SC, lambda s: s.activation(out=bmax[:, 27:28], in_=bmax[:, 26:27],
                                        func=AF.Sqrt), reads=("bmax_s",), writes=("bmax_s",))
        e.op(VE, lambda v: v.tensor_scalar(out=bmax[:, 28:29], in0=bmax[:, 27:28],
                                           scalar1=-0.1275, scalar2=None, op0=ALU.mult),
             reads=("bmax_s",), writes=("bmax_s",))
        e.op(TE, lambda t: t.matmul(psB[:, 0:1], ones_row[:], bmax[0:1, 28:29],
                                    start=True, stop=True),
             reads=("bmax_s", "onesrow"), writes=("psB",))
        e.op(SC, lambda s, hp=hp: s.copy(out=negb2[:, hp:hp + 1], in_=psB[:, 0:1]),
             reads=("psB",), writes=("negb",))

    dmneg = p.sbuf("dmneg", [128, 4, 512], BF16)
    identp4 = C["identbf"]

    def fn_dm(g):
        last = None
        for k in range(4):
            g.memset(dmneg[:, k, :], 0.0)
            last = g.affine_select(
                out=dmneg[:, k, :], in_=dmneg[:, k, :], compare_op=ALU.is_ge,
                fill=-1e30, base=-128 * k, pattern=[[1, 512]],
                channel_multiplier=-1)
        return last
    e.op(GP, fn_dm, writes=("dmneg",))

    pending_div = []
    for hp in range(2):
        for qt in range(16):
            ntt = 4 * (qt + 1)

            def emit_s(tt):
                h = tt % 2
                diag = tt >= 4 * qt

                def fn_s2(t, hp=hp, qt=qt, tt=tt, h=h, diag=diag):
                    last = []
                    for s in range(2):
                        seg = psS[:, h * 1024 + s * 512:h * 1024 + (s + 1) * 512]
                        mm = t.matmul(seg,
                                      k_fm[64 * s:64 * s + 64, hp, tt * 128:(tt + 1) * 128],
                                      q_fm[64 * s:64 * s + 64, hp, qt * 512:(qt + 1) * 512],
                                      start=True, stop=not diag)
                        if diag:
                            k = tt - 4 * qt
                            mm = t.matmul(seg, identp4[:], dmneg[:, k, :],
                                          start=False, stop=True)
                        last.append(mm)
                    return last
                e.op(TE, fn_s2, reads=("kfm", "qfm", "identbf", "dmneg"),
                     writes=(f"psS{h}",))
                e.op(SC, lambda s, tt=tt, h=h, hp=hp: s.activation(
                    out=pt_sb[:, h * 1024:(h + 1) * 1024],
                    in_=psS[:, h * 1024:(h + 1) * 1024], func=AF.Exp,
                    bias=negb2[:, hp:hp + 1], scale=0.125),
                    reads=(f"psS{h}", "negb"), writes=(f"pt{h}",))

            def emit_pv(tt):
                h = tt % 2

                def fn_pv(t, hp=hp, tt=tt, h=h, first=(tt == 0),
                          stop=(tt == ntt - 1)):
                    out = []
                    for s in range(2):
                        lh = 2 * hp + s
                        out.append(t.matmul(
                            psA[s][0:65, :],
                            v_tm[:, tt, lh * 65:(lh + 1) * 65],
                            pt_sb[:, h * 1024 + s * 512:h * 1024 + (s + 1) * 512],
                            start=first, stop=stop, skip_group_check=True))
                    return out
                e.op(TE, fn_pv, reads=("vtm", "vtm_ones", f"pt{h}"),
                     writes=("psA0", "psA1"))

            emit_s(0)
            if pending_div:
                pending_div.pop(0)()
            for tt in range(1, ntt):
                emit_s(tt)
                emit_pv(tt - 1)
            emit_pv(ntt - 1)

            par = qt % 2
            for s in range(2):
                e.op(VE, lambda v, s=s, par=par: v.tensor_copy(
                    out=nac[par][s][:], in_=psA[s][0:65, :]),
                    reads=(f"psA{s}",), writes=(f"nac{par}{s}",))
            for s in range(2):
                e.op(VE, lambda v, s=s, par=par: v.tensor_scalar(
                    out=linv2[par][s][:], in0=nac[par][s][64:65, :], scalar1=1e-35,
                    scalar2=None, op0=ALU.max),
                    reads=(f"nac{par}{s}",), writes=(f"linv{par}{s}",))
                e.op(VE, lambda v, s=s, par=par: v.reciprocal(
                    out=linv2[par][s][:], in_=linv2[par][s][:]),
                     reads=(f"linv{par}{s}",), writes=(f"linv{par}{s}",))

            def div_b(hp=hp, qt=qt, par=par):
                ob, obk = ot[par], f"ot{par}"
                for s in range(2):
                    e.op(TE, lambda t, s=s, par=par: t.matmul(
                        psB[0:64, :], ones_row[0:1, 0:64], linv2[par][s][:],
                        start=True, stop=True),
                         reads=(f"linv{par}{s}", "onesrow"), writes=("psB",))
                    e.op(SC, lambda s_: s_.copy(out=linvb[:], in_=psB[0:64, :]),
                         reads=("psB",), writes=("linvb",))
                    e.op(VE, lambda v, s=s, ob=ob, obk=obk, par=par: v.tensor_tensor(
                        out=ob[64 * s:64 * s + 64, :],
                        in0=nac[par][s][0:64, :], in1=linvb[:], op=ALU.mult),
                        reads=(f"nac{par}{s}", "linvb"), writes=(obk,))
                e.op(SY, lambda s, hp=hp, qt=qt, ob=ob, obk=obk: s.dma_start(
                    out=stage4[:, hp * 8192 + qt * 512:hp * 8192 + (qt + 1) * 512],
                    in_=ob[:]),
                    reads=(obk,), writes=("stage4",), dma=True, dkey="oout")
            pending_div.append(div_b)
    while pending_div:
        pending_div.pop(0)()


def emit_p5(p, e, C, io, lvl=99):
    x2_d, gath4, cpos_d = io["x2"], io["gath4"], io["cpos"]
    wot_d, gw_d = io["WoT"], io["gate_w"]
    w1_d, w2_d, w3_d = io["m_w1"], io["m_w2"], io["m_w3"]
    sw1_d, sw2_d, sw3_d = io["sw1"], io["sw2"], io["sw3"]
    x4_d, x3_d = io["x4"], io["x3"]

    xt = [p.sbuf(f"xt{i}", [128, D], F32) for i in range(2)]
    stat = [p.sbuf(f"stat{i}", [128, 16], F32) for i in range(2)]
    ubf = [p.sbuf(f"ubf{i}", [128, D], BF16) for i in range(2)]
    u_fm = p.sbuf("ufm", [128, 8, NT], BF16)
    o_sb = u_fm
    cp_sb = p.sbuf("cpsb", [1, 1], U32)
    ident_bf = C["identbf"]

    psH1 = p.psum("psH1", [128, 1024], F32)
    psH3 = p.psum("psH3", [128, 1024], F32)
    psY = [p.psum("psY0", [128, 512], F32), p.psum("psY1", [128, 512], F32)]
    psT = p.psum("psT", [128, 1024], BF16)
    psZ = p.psum("psZ", [128, 512], F32)

    SY, TE, VE, SC, GP = ENGINES

    e.op(SY, lambda s: s.dma_start(out=cp_sb[:], in_=cpos_d[:]),
         writes=("cpsb",), dma=True, dkey="cpld")
    ogat = e.dma_sem(GP, "ogat")

    def fn_ogather(g):
        reg = g.alloc_register("posreg")
        g.load(reg, cp_sb[0:1, 0:1])
        for j in range(4):
            with g.If_eq(reg, j):
                for gsrc in range(4):
                    for ch in range(2):
                        g.dma_start(
                            out=o_sb[:, 2 * gsrc + ch, :],
                            in_=gath4[gsrc * 128:(gsrc + 1) * 128,
                                      ch * 8192 + j * NT:ch * 8192 + (j + 1) * NT]
                        ).then_inc(ogat.h, 16)
        return None
    e.op(GP, fn_ogather, reads=("gath4", "cpsb"), writes=("ufm_u",),
         self_inc=128, dkey="ogat")

    with p.scope():
        wot_sb = p.sbuf("wotsb", [128, 8, D], BF16)
        x3t = p.sbuf("x3t", [128, D], F32)
        x4t = p.sbuf("x4t", [128, D], F32)
        e.op(GP, lambda g: g.dma_start(out=wot_sb[:],
                                       in_=wot_d[:].rearrange("(a p) b -> p a b", p=128)),
             writes=("wot",), dma=True, dkey="wotload")
        x3b = [x3t, x4t]
        for i in range(NTILE):
            e.op(SY, lambda s, i=i: s.dma_start(out=xt[i % 2][:],
                                                in_=x2_d[i * 128:(i + 1) * 128, :]),
                 writes=(f"xt{i % 2}",), dma=True)
            xb = x3b[i % 2]
            xbk = ("x3t", "x4t")[i % 2]
            for ng in range(2):
                def fn_y(t, i=i, ng=ng, wot_sb=wot_sb):
                    last = None
                    for mc in range(8):
                        last = t.matmul(psY[0][:], o_sb[:, mc, i * 128:(i + 1) * 128],
                                        wot_sb[:, mc, ng * 512:(ng + 1) * 512],
                                        start=(mc == 0), stop=(mc == 7))
                    return last
                e.op(TE, fn_y, reads=("ufm_u", "wot"), writes=("psY0",))
                e.op(VE, lambda v, i=i, ng=ng, xb=xb: v.tensor_add(
                    out=xb[:, ng * 512:(ng + 1) * 512],
                    in0=xt[i % 2][:, ng * 512:(ng + 1) * 512], in1=psY[0][:]),
                    reads=(f"xt{i % 2}", "psY0"), writes=(xbk,))
            e.op(SY, lambda s, i=i, xb=xb: s.dma_start(
                out=x3_d[i * 128:(i + 1) * 128, :], in_=xb[:]),
                 reads=(xbk,), writes=(f"x3o{i}",), dma=True, dkey=f"x3out{i % 2}")
    e.barrier()

    y_acc = p.sbuf("yacc", [128, 16, D], F32)
    w1h = [p.sbuf(f"w1h{i}", [128, 8, D], BF16) for i in range(2)]
    w3h = [p.sbuf(f"w3h{i}", [128, 8, D], BF16) for i in range(2)]
    w2h_t = p.sbuf("w2h", [128, 8, D], BF16)
    w2h = [w2h_t, w2h_t]
    act_buf = p.sbuf("actbuf", [128, 8, 512], BF16)
    s1_sb = p.sbuf("s1sb", [128, 1024], BF16)
    gwt_sb = p.sbuf("gwtsb", [128, 8, 8], BF16)
    comb = p.sbuf("comb", [128, 16, 9], F32)
    gz = p.sbuf("gz", [128, 8], F32)
    gm = p.sbuf("gm", [128, 8], F32)
    negm1 = p.sbuf("negm1", [128, 1], F32)
    gsc = p.sbuf("gsc", [128, 8], F32)
    gw_sb = act_buf[:].rearrange("p a b -> p (a b)")[0:8, 0:1024]
    e.op(GP, lambda g: g.memset(y_acc[:], 0.0), writes=("yacc",))

    if lvl < 2:
        return
    emit_ln_tiles(e, p, x3_d, u_fm,
                  dict(xt=xt, stat=stat, ubf=ubf, psT=psT, ident_bf=ident_bf), ukey="u",
                  src_key=[f"x3o{i}" for i in range(16)])

    if lvl < 3:
        return
    e.op(GP, lambda g: g.dma_start(out=gw_sb, in_=gw_d[:]), writes=("gw",), dma=True)
    for dc in range(8):
        e.op(TE, lambda t, dc=dc: t.transpose(psZ[:, 0:8].bitcast(BF16)[:, 0:8],
                                              gw_sb[:, dc * 128:(dc + 1) * 128],
                                              ident_bf[0:8, 0:8]),
             reads=("gw", "identbf"), writes=("psZ",))
        e.op(VE, lambda v, dc=dc: v.tensor_copy(out=gwt_sb[:, dc, :],
                                                in_=psZ[:, 0:8].bitcast(BF16)[:, 0:8]),
             reads=("psZ",), writes=("gwt",))
    for ti in range(NTILE):
        def fn_z(t, ti=ti):
            last = None
            for dc in range(8):
                last = t.matmul(psZ[:, 16:24], u_fm[:, dc, ti * 128:(ti + 1) * 128],
                                gwt_sb[:, dc, :], start=(dc == 0), stop=(dc == 7))
            return last
        e.op(TE, fn_z, reads=("ufm_u", "gwt"), writes=("psZ",))
        e.op(VE, lambda v: v.tensor_copy(out=gz[:], in_=psZ[:, 16:24]),
             reads=("psZ",), writes=("gz",))
        e.op(VE, lambda v: v.max(out=gm[:], in_=gz[:]), reads=("gz",), writes=("gm",))
        e.op(VE, lambda v: v.tensor_scalar(out=negm1[:], in0=gm[:, 0:1], scalar1=-1.0,
                                           scalar2=None, op0=ALU.mult),
             reads=("gm",), writes=("negm1",))
        e.op(SC, lambda s: s.activation(out=gsc[:], in_=gz[:], func=AF.Exp,
                                        bias=negm1[:], scale=1.0),
             reads=("gz", "negm1"), writes=("gsc",))
        e.op(SC, lambda s: s.activation(out=negm1[:], in_=gm[:, 1:2], func=AF.Exp,
                                        bias=negm1[:], scale=1.0),
             reads=("gm", "negm1"), writes=("negm1b",))
        e.op(VE, lambda v: v.tensor_scalar_add(negm1[:], negm1[:], 1.0),
             reads=("negm1b",), writes=("negm1b",))
        e.op(VE, lambda v: v.reciprocal(out=negm1[:], in_=negm1[:]),
             reads=("negm1b",), writes=("negm1b",))
        e.op(VE, lambda v: v.tensor_scalar(out=gsc[:], in0=gsc[:], scalar1=negm1[:],
                                           scalar2=None, op0=ALU.mult),
             reads=("gsc", "negm1b"), writes=("gsc",))
        e.op(VE, lambda v: v.tensor_scalar(out=gm[:], in0=gz[:], scalar1=gm[:, 1:2],
                                           scalar2=None, op0=ALU.is_ge),
             reads=("gz", "gm"), writes=("gmask",))
        e.op(VE, lambda v, ti=ti: v.tensor_mul(comb[:, ti, 0:8], gsc[:], gm[:]),
             reads=("gsc", "gmask"), writes=("comb",))
        e.op(VE, lambda v, ti=ti: v.memset(comb[:, ti, 8:9], 1.0),
             reads=(), writes=("comb",))

    e.op(VE, lambda v: v.engine_nop(), reads=(), writes=("gw",))

    if lvl < 4:
        return
    for eh in range(18):
        ex, half = eh // 2, eh % 2
        wb1, wb3, wb2 = w1h[eh % 2], w3h[eh % 2], w2h[eh % 2]
        wk1, wk3, wk2 = f"w1h{eh % 2}", f"w3h{eh % 2}", "w2h"
        if ex < 8:
            s1d = w1_d[ex][:, half * 1024:(half + 1) * 1024]
            s3d = w3_d[ex][:, half * 1024:(half + 1) * 1024]
            s2d = w2_d[ex][half * 1024:(half + 1) * 1024, :]
        else:
            s1d = sw1_d[0][:, half * 1024:(half + 1) * 1024]
            s3d = sw3_d[0][:, half * 1024:(half + 1) * 1024]
            s2d = sw2_d[0][half * 1024:(half + 1) * 1024, :]
        e.op(GP, lambda g, s1d=s1d, wb1=wb1: g.dma_start(
            out=wb1[:], in_=s1d.rearrange("(a p) f -> p a f", p=128)),
            writes=(wk1,), dma=True, dkey=wk1)
        e.op(GP, lambda g, s3d=s3d, wb3=wb3: g.dma_start(
            out=wb3[:], in_=s3d.rearrange("(a p) f -> p a f", p=128)),
            writes=(wk3,), dma=True, dkey=wk3)
        e.op(GP, lambda g, s2d=s2d, wb2=wb2: g.dma_start(
            out=wb2[:], in_=s2d.rearrange("(a p) f -> p a f", p=128)),
            writes=(wk2,), dma=True, dkey=wk2)

        for tq in range(4):
            for fc in range(8):
                par = fc % 2
                h1 = psH1[:, par * 512:(par + 1) * 512]
                h3 = psH3[:, par * 512:(par + 1) * 512]
                s1v = s1_sb[:, par * 512:(par + 1) * 512]

                def fn_h(t, tq=tq, fc=fc, wb1=wb1, wb3=wb3, h1=h1, h3=h3):
                    last = []
                    for dc in range(8):
                        t.matmul(h1, wb1[:, dc, fc * 128:(fc + 1) * 128],
                                 u_fm[:, dc, tq * 512:(tq + 1) * 512],
                                 start=(dc == 0), stop=(dc == 7))
                    for dc in range(8):
                        last.append(t.matmul(h3, wb3[:, dc, fc * 128:(fc + 1) * 128],
                                             u_fm[:, dc, tq * 512:(tq + 1) * 512],
                                             start=(dc == 0), stop=(dc == 7)))
                    return last[-1]
                e.op(TE, fn_h, reads=(wk1, wk3, "ufm_u"),
                     writes=(f"psH1{par}", f"psH3{par}"))
                e.op(SC, lambda s, h1=h1, s1v=s1v: s.activation(
                    out=s1v, in_=h1, func=AF.Silu),
                     reads=(f"psH1{par}",), writes=(f"s1sb{par}",))
                e.op(VE, lambda v, fc=fc, h3=h3, s1v=s1v: v.tensor_mul(
                    act_buf[:, fc, :], s1v, h3),
                    reads=(f"s1sb{par}", f"psH3{par}"), writes=("actbuf",))
            for ti in range(4):
                gi = tq * 4 + ti
                for ng in range(2):
                    def fn_y(t, ti=ti, ng=ng, wb2=wb2):
                        last = None
                        for fc in range(8):
                            last = t.matmul(psY[ng][:], act_buf[:, fc, ti * 128:(ti + 1) * 128],
                                            wb2[:, fc, ng * 512:(ng + 1) * 512],
                                            start=(fc == 0), stop=(fc == 7))
                        return last
                    e.op(TE, fn_y, reads=("actbuf", wk2), writes=(f"psY{ng}",))
                    e.op(VE, lambda v, gi=gi, ng=ng, ex=ex: v.scalar_tensor_tensor(
                        out=y_acc[:, gi, ng * 512:(ng + 1) * 512], in0=psY[ng][:],
                        scalar=comb[:, gi, ex:ex + 1],
                        in1=y_acc[:, gi, ng * 512:(ng + 1) * 512],
                        op0=ALU.mult, op1=ALU.add),
                        reads=(f"psY{ng}", "comb", "yacc"), writes=("yacc",))

    for i in range(NTILE):
        e.op(SY, lambda s, i=i: s.dma_start(out=xt[0][:], in_=x3_d[i * 128:(i + 1) * 128, :]),
             reads=(f"x3o{i}",), writes=("xt0",), dma=True, dkey="x3ld")
        e.op(VE, lambda v, i=i: v.tensor_add(out=xt[1][:], in0=xt[0][:], in1=y_acc[:, i, :]),
             reads=("xt0", "yacc"), writes=("xt1",))
        e.op(SY, lambda s, i=i: s.dma_start(out=x4_d[i * 128:(i + 1) * 128, :], in_=xt[1][:]),
             reads=("xt1",), dma=True, dkey="x4out")


def shard_x(x_full):
    out = []
    for c in range(NCORES):
        b, j = c // 4, c % 4
        out.append(np.ascontiguousarray(x_full[b, j * NT:(j + 1) * NT, :]))
    return out


RG2x4 = [[0, 1, 2, 3], [4, 5, 6, 7]]
MEGA_UPTO = 5
P2_KW = {}
P5_KW = {}
MEGA_DBG = False
MEGA_NOCC = False



def build_mega():
    p = Prog()
    nc = p.nc
    e = Emit(p)
    SY, TE, VE, SC, GP = ENGINES

    ext_in = {
        "x": ([NT, D], F32),
        "Wq": ([D, D], F32), "Wk": ([D, D], F32), "Wv": ([D, D], F32),
        "Wa": ([D, H], F32), "ba": ([H], F32), "AWo": ([D, D], F32),
        "Wp1": ([D, 128], F32), "bp1": ([128], F32),
        "Wp2": ([128, 8], F32), "bp2": ([8], F32),
        "W1": ([8, D, 2048], F32), "b1": ([8, 2048], F32),
        "W2": ([8, 2048, D], F32), "b2": ([8, D], F32),
        "FWo": ([D, D], F32), "bo": ([D], F32),
        "WqT": ([D, 256], F32), "WkT": ([D, 256], F32), "WvT": ([D, 256], F32),
        "WoT": ([D, D], F32),
        "gate_w": ([8, D], F32),
        "m_w1": ([8, D, 2048], F32), "m_w2": ([8, 2048, D], F32),
        "m_w3": ([8, D, 2048], F32),
        "sw1": ([1, D, 2048], F32), "sw2": ([1, 2048, D], F32),
        "sw3": ([1, D, 2048], F32),
        "pm": ([1, 8], F32), "cpos": ([1, 1], U32),
    }
    io = {k: nc.dram_tensor(k, sh, dt, kind="ExternalInput").ap()
          for k, (sh, dt) in ext_in.items()}
    io["x4"] = nc.dram_tensor("x4", [NT, D], F32, kind="ExternalOutput").ap()
    intern = {
        "chunkS": ([16, 128, 520], F32), "chunkA": ([16, H], F32),
        "stage1": ([128, 536], F32), "gath1": ([512, 536], F32),
        "stage2": ([1, D], F32), "gath2": ([4, D], F32),
        "stage3": ([128, 8 * NT], BF16), "gath3": ([512, 8 * NT], BF16),
        "stage4": ([128, 8 * NT], BF16), "gath4": ([512, 8 * NT], BF16),
    }
    for k, (sh, dt) in intern.items():
        io[k] = nc.dram_tensor(k, sh, dt, kind="Internal").ap()
    xkind = "ExternalOutput" if MEGA_DBG else "Internal"
    for k in ("x1", "x2", "x3"):
        io[k] = nc.dram_tensor(k, [NT, D], F32, kind=xkind).ap()

    C = {
        "identbf": p.sbuf("c_identbf", [128, 128], BF16),
        "identf": p.sbuf("c_identf", [128, 128], F32),
        "msl": p.sbuf("c_msl", [128, 128], F32),
        "mle": p.sbuf("c_mle", [128, 128], F32),
        "mneg": p.sbuf("c_mneg", [128, 128], F32),
        "onesrow": p.sbuf("c_onesrow", [1, 128], F32),
        "onesrowbf": p.sbuf("c_onesrowbf", [1, 128], BF16),
        "onescolf": p.sbuf("c_onescolf", [128, 1], F32),
        "onescolbf": p.sbuf("c_onescolbf", [128, 1], BF16),
    }
    make_const(e, GP, C["identbf"][:], "identbf", "ident")
    make_const(e, GP, C["identf"][:], "identf", "ident")
    make_const(e, GP, C["msl"][:], "msl", "strict_lower")
    make_const(e, GP, C["onesrow"][:], "onesrow", "ones")
    make_const(e, GP, C["onesrowbf"][:], "onesrowbf", "ones")
    make_const(e, GP, C["onescolf"][:], "onescol", "ones")
    make_const(e, GP, C["onescolbf"][:], "onescolbf", "ones")
    e.op(GP, lambda g: g.memset(C["mle"][:], 1.0), writes=("mle",))
    e.op(GP, lambda g: g.affine_select(out=C["mle"][:], in_=C["mle"][:],
                                       compare_op=ALU.is_ge, fill=0.0, base=0,
                                       pattern=[[1, 128]], channel_multiplier=-1),
         reads=("mle",), writes=("mle",))
    e.op(GP, lambda g: g.memset(C["mneg"][:], 0.0), writes=("mneg",))
    e.op(GP, lambda g: g.affine_select(out=C["mneg"][:], in_=C["mneg"][:],
                                       compare_op=ALU.is_ge, fill=-1e30, base=0,
                                       pattern=[[1, 128]], channel_multiplier=-1),
         reads=("mneg",), writes=("mneg",))

    def collective(name, src_key, dst_key, in_ap, out_ap):
        if MEGA_NOCC:
            return ()
        ctr = e.dma_sem(GP, name)

        def fn(g):
            return g.collective_compute(
                "AllGather", mybir.AluOpType.bypass, replica_groups=RG2x4,
                ins=[in_ap.opt()], outs=[out_ap.opt()]).then_inc(ctr.h)
        e.op(GP, fn, reads=(src_key,), writes=(dst_key,), self_inc=1, dkey=name)
        return (ctr,)

    x1 = x2 = x3 = x4 = ()
    with p.scope():
        car = (p.sbuf("qfm", [128, 8, NT], BF16),
               p.sbuf("kfm", [128, 8, NT], BF16),
               p.sbuf("vtm", [128, 16, 1040], BF16),
               p.sbuf("lnatm", [128, 16 * H], F32))
        with p.scope():
            emit_p1(p, e, C, io, car)
        x1 = collective("cc1", "stage1", "gath1", io["stage1"], io["gath1"])
        e.barrier(exclude=x1)
        if MEGA_UPTO >= 2:
            with p.scope():
                emit_p2(p, e, C, io, car, **P2_KW)
            x2 = collective("cc2", "stage2", "gath2", io["stage2"], io["gath2"])
    e.barrier(exclude=x1 + x2)
    if MEGA_UPTO >= 3:
        with p.scope():
            emit_p3(p, e, C, io)
            x3 = collective("cc3", "stage3", "gath3", io["stage3"], io["gath3"])
        e.barrier(exclude=x3)
    if MEGA_UPTO >= 4:
        with p.scope():
            emit_p4(p, e, C, io)
            x4 = collective("cc4", "stage4", "gath4", io["stage4"], io["gath4"])
        e.barrier(exclude=x4)
    if MEGA_UPTO >= 5:
        with p.scope():
            emit_p5(p, e, C, io, **P5_KW)
    e.flush()
    return p


_PROG_CACHE = {}


def _prog(name, builder):
    if name not in _PROG_CACHE:
        _PROG_CACHE[name] = builder()
    return _PROG_CACHE[name]


def kernel(**inputs):
    f32 = lambda a: np.ascontiguousarray(a, dtype=np.float32)
    x = f32(inputs["L0_attn_x"])
    shards = shard_x(x)
    p = _prog("mega", build_mega)

    def wslice(w, g):
        return np.ascontiguousarray(w[256 * g:256 * (g + 1), :].T)

    com = dict(
        Wq=f32(inputs["L0_attn_Wq"]), Wk=f32(inputs["L0_attn_Wk"]),
        Wv=f32(inputs["L0_attn_Wv"]), Wa=f32(inputs["L0_attn_Wa"]),
        ba=f32(inputs["L0_attn_ba"]), AWo=f32(inputs["L0_attn_Wo"]),
        Wp1=f32(inputs["L0_ffn_Wp1"]), bp1=f32(inputs["L0_ffn_bp1"]),
        Wp2=f32(inputs["L0_ffn_Wp2"]), bp2=f32(inputs["L0_ffn_bp2"]),
        W1=f32(inputs["L0_ffn_W1"]), b1=f32(inputs["L0_ffn_b1"]),
        W2=f32(inputs["L0_ffn_W2"]), b2=f32(inputs["L0_ffn_b2"]),
        FWo=f32(inputs["L0_ffn_Wo"]), bo=f32(inputs["L0_ffn_bo"]),
        WoT=np.ascontiguousarray(f32(inputs["L1_attn_Wo"]).T),
        gate_w=f32(inputs["L1_ffn_gate_w"]),
        m_w1=f32(inputs["L1_ffn_w1"]), m_w2=f32(inputs["L1_ffn_w2"]),
        m_w3=f32(inputs["L1_ffn_w3"]),
        sw1=f32(inputs["L1_ffn_sw1"]), sw2=f32(inputs["L1_ffn_sw2"]),
        sw3=f32(inputs["L1_ffn_sw3"]),
    )
    wq1, wk1, wv1 = (f32(inputs["L1_attn_Wq"]), f32(inputs["L1_attn_Wk"]),
                     f32(inputs["L1_attn_Wv"]))
    maps = []
    for c in range(NCORES):
        g = c % 4
        pmv = np.zeros((1, 8), np.float32)
        for k in range(3):
            pmv[0, k] = 1.0 if k < g else 0.0
            pmv[0, 4 + k] = 1.0 - pmv[0, k]
        maps.append(dict(x=shards[c],
                         WqT=wslice(wq1, g), WkT=wslice(wk1, g), WvT=wslice(wv1, g),
                         pm=pmv, cpos=np.array([[g]], np.uint32), **com))
    res = run_bass_kernel_spmd(p.nc, maps, core_ids=list(range(NCORES))).results
    out = np.empty((2, 8192, D), np.float32)
    for c in range(NCORES):
        b, jp = c // 4, c % 4
        out[b, jp * NT:(jp + 1) * NT, :] = res[c]["x4"]
    return out


# version 10
# speedup vs baseline: 1.0360x; 1.0049x over previous
import numpy as np
from contextlib import ExitStack, contextmanager

import concourse.bass as bass
import concourse.mybir as mybir
from concourse.bass_utils import run_bass_kernel_spmd

F32 = mybir.dt.float32
BF16 = mybir.dt.bfloat16
U32 = mybir.dt.uint32
AF = mybir.ActivationFunctionType
ALU = mybir.AluOpType
AX = mybir.AxisListType.X

D = 1024
NT = 2048
NTILE = 16
H = 16
DH = 64
NCORES = 8
LN_EPS = 1e-5
GLA_EPS = 1e-6

ENGINES = ("sync", "tensor", "vector", "scalar", "gpsimd")


class Ctr:
    __slots__ = ("h", "n", "name")

    def __init__(self, h, name):
        self.h = h
        self.n = 0
        self.name = name


class Prog:
    def __init__(self):
        self.nc = bass.Bass("TRN2", target_bir_lowering=False, debug=False)
        self.stack = ExitStack()
        self.root = self.stack
        self._sems = {}

    def sem(self, name):
        if name not in self._sems:
            h = self.root.enter_context(self.nc.semaphore(name))
            self._sems[name] = Ctr(h, name)
        return self._sems[name]

    def _uname(self, name):
        if not hasattr(self, "_names"):
            self._names = {}
        n = self._names.get(name, 0)
        self._names[name] = n + 1
        return name if n == 0 else f"{name}__{n}"

    def sbuf(self, name, shape, dtype):
        return self.stack.enter_context(
            self.nc.sbuf_tensor(self._uname(name), shape, dtype))

    def psum(self, name, shape, dtype):
        return self.stack.enter_context(
            self.nc.psum_tensor(self._uname(name), shape, dtype))

    @contextmanager
    def scope(self):
        outer = self.stack
        self.stack = ExitStack()
        try:
            yield
        finally:
            self.stack.close()
            self.stack = outer


class Emit:

    def __init__(self, p):
        self.p = p
        self.pend = {e: [] for e in ENGINES}
        self.eng_ctr = {e: p.sem("c_" + e) for e in ENGINES if e != "sync"}
        self.dma_sems = {}
        self.last_w = {}
        self.readers = {}
        self.hw = {e: {} for e in ENGINES}

    def dma_sem(self, eng, key):
        k = (eng, key)
        if k not in self.dma_sems:
            self.dma_sems[k] = self.p.sem(f"d{len(self.dma_sems)}_{key[:12]}")
        return self.dma_sems[k]

    def op(self, eng, fn, reads=(), writes=(), dma=False, n_dma=1, self_inc=0,
           dkey=None):
        if dma or self_inc:
            if dkey is None:
                dkey = writes[0] if writes else (reads[0] if reads else "anon")
            ctr = self.dma_sem(eng, dkey)
        else:
            ctr = self.eng_ctr[eng]
        waits = {}
        own_te = self.eng_ctr.get("tensor") if eng == "tensor" else None

        def need(dep):
            if dep is None:
                return
            c, n = dep
            if c is own_te:
                return
            if n <= 0 or self.hw[eng].get(c, 0) >= n:
                return
            waits[c] = max(waits.get(c, 0), n)

        for b in reads:
            need(self.last_w.get(b))
        for b in writes:
            need(self.last_w.get(b))
            for c, n in self.readers.get(b, {}).items():
                need((c, n))

        for c, n in waits.items():
            self.hw[eng][c] = max(self.hw[eng].get(c, 0), n)
        wl = [(c.h, n) for c, n in waits.items()]

        if self_inc:
            ctr.n += self_inc
        else:
            ctr.n += 16 * n_dma if dma else 1
        my_cnt = ctr.n
        inc_k = 16 if (dma or self_inc) else 1

        def closure(e, fn=fn, wl=wl, ctr=ctr, self_inc=self_inc, inc_k=inc_k):
            for (sh, n) in wl:
                e.wait_ge(sh, n)
            last = fn(e)
            if self_inc:
                return
            assert last is not None, "op fn must return its last instruction(s)"
            if not isinstance(last, (list, tuple)):
                last = [last]
            if inc_k == 16:
                for ins in last:
                    ins.then_inc(ctr.h, inc_k)
            else:
                last[-1].then_inc(ctr.h, 1)

        self.pend[eng].append(closure)
        for b in reads:
            self.readers.setdefault(b, {})[ctr] = my_cnt
        for b in writes:
            self.last_w[b] = (ctr, my_cnt)
            self.readers[b] = {}
        return (ctr, my_cnt)

    def barrier(self, exclude=()):
        snap = [(c, c.n) for c in
                list(self.eng_ctr.values()) + list(self.dma_sems.values())
                if c.n > 0 and c not in exclude]
        for eng in ENGINES:
            def cl(e, snap=snap):
                for c, n in snap:
                    e.wait_ge(c.h, n)
            self.pend[eng].append(cl)
            for c, n in snap:
                self.hw[eng][c] = max(self.hw[eng].get(c, 0), n)

    def flush(self):
        with self.p.nc.Block() as block:
            for e in ENGINES:
                ops = self.pend[e]
                if not ops:
                    continue

                def run_all(eng, ops=ops):
                    for cl in ops:
                        cl(eng)
                getattr(block, e)(run_all)
            final = [(c.h, c.n) for c in
                     list(self.eng_ctr.values()) + list(self.dma_sems.values()) if c.n > 0]

            def tail(eng):
                for (sh, n) in final:
                    eng.wait_ge(sh, n)
            block.sync(tail)
        self.pend = {e: [] for e in ENGINES}


def make_const(e, g_eng, ap, key, kind, sq=128):
    if kind == "ones":
        e.op(g_eng, lambda g: g.memset(ap, 1.0), writes=(key,))
        return
    fill0, cmp = (0.0, ALU.not_equal) if kind == "ident" else (1.0, ALU.is_gt)
    if kind == "ident":
        e.op(g_eng, lambda g: g.memset(ap, 0.0), writes=(key,))
        e.op(g_eng, lambda g: g.affine_select(out=ap, in_=ap, compare_op=ALU.not_equal,
                                              fill=1.0, base=0, pattern=[[-1, sq]],
                                              channel_multiplier=1),
             reads=(key,), writes=(key,))
    else:
        e.op(g_eng, lambda g: g.memset(ap, 1.0), writes=(key,))
        e.op(g_eng, lambda g: g.affine_select(out=ap, in_=ap, compare_op=ALU.is_gt,
                                              fill=0.0, base=0, pattern=[[-1, sq]],
                                              channel_multiplier=1),
             reads=(key,), writes=(key,))


def emit_ln_tiles(e, p, x_d, u_fm, bufs, n_tiles=NTILE, stat_out=None, ukey="u",
                  src_key=None):
    SY, TE, VE, SC = "sync", "tensor", "vector", "scalar"
    xt, stat, ubf, psT, ident_bf = (bufs["xt"], bufs["stat"], bufs["ubf"],
                                    bufs["psT"], bufs["ident_bf"])
    for i in range(n_tiles):
        xb, sb, ub = xt[i % 2], stat[i % 2], ubf[i % 2]
        xk, sk, uk = f"xt{i % 2}", f"stat{i % 2}", f"ubf{i % 2}"
        e.op(SY, lambda s, i=i, xb=xb: s.dma_start(
            out=xb[:], in_=x_d[i * 128:(i + 1) * 128, :]), writes=(xk,), dma=True,
            reads=(() if src_key is None else tuple(src_key)))
        e.op(VE, lambda v, xb=xb, sb=sb: v.bn_stats(out=sb[:, 0:6], in_=xb[:, 0:512]),
             reads=(xk,), writes=(sk,))
        e.op(VE, lambda v, xb=xb, sb=sb: v.bn_stats(out=sb[:, 6:12], in_=xb[:, 512:1024]),
             reads=(xk,), writes=(sk,))
        e.op(VE, lambda v, sb=sb: v.bn_aggr(out=sb[:, 12:14], in_=sb[:, 0:12]),
             reads=(sk,), writes=(sk,))
        e.op(VE, lambda v, sb=sb: v.tensor_scalar_add(sb[:, 13:14], sb[:, 13:14],
                                                      float(LN_EPS)),
             reads=(sk,), writes=(sk,))
        e.op(SC, lambda s, sb=sb: s.activation(out=sb[:, 14:15], in_=sb[:, 13:14],
                                               func=AF.Ln),
             reads=(sk,), writes=(sk,))
        e.op(SC, lambda s, sb=sb: s.activation(out=sb[:, 15:16], in_=sb[:, 14:15],
                                               func=AF.Exp, scale=-0.5),
             reads=(sk,), writes=(sk,))
        e.op(VE, lambda v, xb=xb, sb=sb, ub=ub: v.tensor_scalar(
            out=ub[:], in0=xb[:], scalar1=sb[:, 12:13], scalar2=sb[:, 15:16],
            op0=ALU.subtract, op1=ALU.mult), reads=(xk, sk), writes=(uk,))
        if stat_out is not None:
            e.op(VE, lambda v, sb=sb, i=i: v.tensor_copy(
                out=stat_out[:, 2 * i:2 * i + 2],
                in_=sb[:].rearrange("p (a b) -> p a b", b=1)[:, (12, 15), :].squeeze(2)
                if False else sb[:, 12:16:3]),
                reads=(sk,), writes=("lnstat_" + ukey,))

        def fn_tr(t, ub=ub):
            last = None
            for dc in range(8):
                last = t.transpose(psT[:, dc * 128:(dc + 1) * 128],
                                   ub[:, dc * 128:(dc + 1) * 128], ident_bf[:])
            return last
        e.op(TE, fn_tr, reads=(uk, "identbf"), writes=("psT",))
        e.op(VE, lambda v, i=i: v.tensor_copy(
            out=u_fm[:, :, i * 128:(i + 1) * 128],
            in_=psT[:].rearrange("p (a b) -> p a b", b=128)),
            reads=("psT",), writes=("ufm_" + ukey, f"ufm_{ukey}_g{i // 4}"))


def emit_proj_fm(e, dst, dk, u_fm, wsel, wkeys, psB, elu_bufs=None,
                 n_tg=4, psb_key="psB", ukey="ufm_u"):
    TE, VE, SC = "tensor", "vector", "scalar"
    for tg in range(n_tg):
        for oc in range(8):
            def fn_mm(t, oc=oc, tg=tg):
                last = None
                for dc in range(8):
                    last = t.matmul(psB[:], wsel(dc, oc),
                                    u_fm[:, dc, tg * 512:(tg + 1) * 512],
                                    start=(dc == 0), stop=(dc == 7))
                return last
            e.op(TE, fn_mm, reads=wkeys + (f"{ukey}_g{tg}",), writes=(psb_key,))
            if elu_bufs is not None:
                elu_sc, elu_sc2 = elu_bufs
                e.op(VE, lambda v: v.tensor_scalar(out=elu_sc[:], in0=psB[:], scalar1=0.0,
                                                   scalar2=None, op0=ALU.min),
                     reads=(psb_key,), writes=("elusc",))
                e.op(SC, lambda s: s.activation(out=elu_sc2[:], in_=elu_sc[:], func=AF.Exp),
                     reads=("elusc",), writes=("elusc2",))
                e.op(VE, lambda v, oc=oc, tg=tg: v.scalar_tensor_tensor(
                    out=dst[:, oc, tg * 512:(tg + 1) * 512], in0=psB[:], scalar=0.0,
                    in1=elu_sc2[:], op0=ALU.max, op1=ALU.add),
                    reads=(psb_key, "elusc2"), writes=(dk,))
            else:
                e.op(VE, lambda v, oc=oc, tg=tg: v.tensor_copy(
                    out=dst[:, oc, tg * 512:(tg + 1) * 512], in_=psB[:]),
                    reads=(psb_key,), writes=(dk,))


def emit_v_tm(e, v_tm, vk, u_fm, udk, wsel, wkeys, psB, psb_key="psB"):
    TE, VE = "tensor", "vector"
    for ti in range(NTILE):
        for ng in range(2):
            def fn_mm(t, ti=ti, ng=ng):
                last = None
                for dc in range(8):
                    last = t.matmul(psB[:], u_fm[:, dc, ti * 128:(ti + 1) * 128],
                                    wsel(dc, ng), start=(dc == 0), stop=(dc == 7))
                return last
            e.op(TE, fn_mm, reads=wkeys + (f"ufm_{udk}_g{ti // 4}",),
                 writes=(psb_key,))
            e.op(VE, lambda v, ti=ti, ng=ng: v.tensor_copy(
                out=v_tm[:, ti].rearrange("p (h x) -> p h x", x=65)[:, ng * 8:(ng + 1) * 8, 0:64],
                in_=psB[:].rearrange("p (h d) -> p h d", d=64)),
                reads=(psb_key,), writes=(vk,))


def emit_p1(p, e, C, io, car):
    x_d, wq_d, wk_d, wv_d = io["x"], io["Wq"], io["Wk"], io["Wv"]
    wa_d, ba_d = io["Wa"], io["ba"]
    chs_d, cha_d, stage1 = io["chunkS"], io["chunkA"], io["stage1"]
    q_fm, k_fm, v_tm, lna_tm = car

    xt = [p.sbuf(f"xt{i}", [128, D], F32) for i in range(2)]
    stat = [p.sbuf(f"stat{i}", [128, 16], F32) for i in range(2)]
    ubf = [p.sbuf(f"ubf{i}", [128, D], BF16) for i in range(2)]
    u_fm = p.sbuf("ufm", [128, 8, NT], BF16)
    lna_fm = p.sbuf("lnafm", [16, NT], F32)
    a_fm = p.sbuf("afm", [16, NT], F32)
    wmat = p.sbuf("wmat", [128, 16, D], BF16)
    wa_sb = p.sbuf("wasb", [128, 8 * H], BF16)
    ba2_sb = p.sbuf("ba2sb", [16, 1], F32)
    elu_sc = p.sbuf("elusc", [128, 512], F32)
    elu_sc2 = p.sbuf("elusc2", [128, 512], F32)
    ident_bf, ident_f, msl = C["identbf"], C["identf"], C["msl"]
    ones_col, ones_row = C["onescolf"], C["onesrow"]
    k_tm = p.sbuf("ktm", [128, D], BF16)
    khat = p.sbuf("khat", [128, D], BF16)
    escale = p.sbuf("escale", [128, H], BF16)
    arow = p.sbuf("arow", [1, H], F32)
    s_run = p.sbuf("srun", [128, 520], F32)
    s_stash = p.sbuf("sstash", [128, 520], F32)

    psB = p.psum("psB", [128, 512], F32)
    psT = p.psum("psT", [128, 1024], BF16)
    psS = p.psum("psS", [128, 1024], F32)
    psR = p.psum("psR", [128, 128], F32)

    SY, TE, VE, SC, GP = ENGINES

    e.op(GP, lambda g: g.memset(
        v_tm[:].rearrange("p h (x y) -> p h x y", y=65)[:, :, :, 64:65], 1.0),
        writes=("vtm_ones",))

    def load_w(dst_half, w_d, tag):
        def fn(g):
            return [g.dma_start(out=wmat[:, dst_half * 8 + dc, :],
                                in_=w_d[dc * 128:(dc + 1) * 128, :]) for dc in range(8)]
        e.op(GP, fn, writes=(tag,), dma=True, n_dma=8)

    load_w(0, wk_d, "wmatA")
    load_w(1, wq_d, "wmatB")

    def fn_wa(g):
        ins = [g.dma_start(out=wa_sb[:, dc * H:(dc + 1) * H],
                           in_=wa_d[dc * 128:(dc + 1) * 128, :]) for dc in range(8)]
        ins.append(g.dma_start(out=ba2_sb[:], in_=ba_d[:].rearrange("(h x) -> h x", x=1)))
        return ins
    e.op(GP, fn_wa, writes=("wasb", "ba2"), dma=True, n_dma=9)
    e.op(VE, lambda v: v.tensor_scalar_add(ba2_sb[:], ba2_sb[:], 2.0),
         reads=("ba2",), writes=("ba2",))

    emit_ln_tiles(e, p, x_d, u_fm,
                  dict(xt=xt, stat=stat, ubf=ubf, psT=psT, ident_bf=ident_bf), ukey="u")

    emit_proj_fm(e, k_fm, "kfm", u_fm,
                 lambda dc, oc: wmat[:, dc, oc * 128:(oc + 1) * 128], ("wmatA",),
                 psB, elu_bufs=(elu_sc, elu_sc2))
    load_w(0, wv_d, "wmatA")
    emit_proj_fm(e, q_fm, "qfm", u_fm,
                 lambda dc, oc: wmat[:, 8 + dc, oc * 128:(oc + 1) * 128], ("wmatB",),
                 psB, elu_bufs=(elu_sc, elu_sc2))

    emit_v_tm(e, v_tm, "vtm", u_fm, "u",
              lambda dc, ng: wmat[:, dc, ng * 512:(ng + 1) * 512], ("wmatA",), psB)

    for tg in range(4):
        def fn_amm(t, tg=tg):
            last = None
            for dc in range(8):
                last = t.matmul(psB[0:16, :], wa_sb[:, dc * H:(dc + 1) * H],
                                u_fm[:, dc, tg * 512:(tg + 1) * 512],
                                start=(dc == 0), stop=(dc == 7))
            return last
        e.op(TE, fn_amm, reads=("wasb", f"ufm_u_g{tg}"), writes=("psB",))
        e.op(SC, lambda s, tg=tg: s.activation(
            out=a_fm[:, tg * 512:(tg + 1) * 512], in_=psB[0:16, :], func=AF.Sigmoid,
            bias=ba2_sb[:], scale=1.0), reads=("psB", "ba2"), writes=("afm",))
    e.op(SC, lambda s: s.activation(out=lna_fm[:], in_=a_fm[:], func=AF.Ln),
         reads=("afm",), writes=("lnafm",))

    for grp in range(4):
        def fn_tr(t, grp=grp):
            last = None
            for cc in range(4):
                c = grp * 4 + cc
                last = t.transpose(psR[:, cc * 16:(cc + 1) * 16],
                                   lna_fm[:, c * 128:(c + 1) * 128], ident_f[0:16, 0:16])
            return last
        e.op(TE, fn_tr, reads=("lnafm", "identf"), writes=("psR",))
        e.op(VE, lambda v, grp=grp: v.tensor_copy(
            out=lna_tm[:, grp * 64:(grp + 1) * 64], in_=psR[:, 0:64]),
            reads=("psR",), writes=("lnatm",))

    e.op(VE, lambda v: v.memset(s_run[:], 0.0), writes=("srun",))
    for c in range(16):
        lna_c = lna_tm[:, c * 16:(c + 1) * 16]

        def fn_rev(t, lna_c=lna_c):
            t.matmul(psR[:, 0:16], msl[:], lna_c, start=True, stop=True)
            return t.matmul(psR[0:1, 16:32], ones_col[:], lna_c, start=True, stop=True)
        e.op(TE, fn_rev, reads=("lnatm", "msl", "onescol"), writes=("psR",))
        e.op(SC, lambda s: s.activation(out=escale[:], in_=psR[:, 0:16], func=AF.Exp),
             reads=("psR",), writes=("escale",))
        e.op(SC, lambda s: s.activation(out=arow[:], in_=psR[0:1, 16:32], func=AF.Exp),
             reads=("psR",), writes=("arow",))

        def fn_ktr(t, c=c):
            last = None
            for dc in range(8):
                last = t.transpose(psT[:, dc * 128:(dc + 1) * 128],
                                   k_fm[:, dc, c * 128:(c + 1) * 128], ident_bf[:])
            return last
        e.op(TE, fn_ktr, reads=("kfm", "identbf"), writes=("psT",))
        e.op(VE, lambda v: v.tensor_copy(out=k_tm[:], in_=psT[:]),
             reads=("psT",), writes=("ktm",))
        e.op(VE, lambda v: v.tensor_tensor(
            out=khat[:].rearrange("p (h d) -> p h d", d=64),
            in0=k_tm[:].rearrange("p (h d) -> p h d", d=64),
            in1=escale[:].unsqueeze(2).broadcast_to([128, 16, 64]), op=ALU.mult),
            reads=("ktm", "escale"), writes=("khat",))

        def fn_sadd(t, c=c):
            last = None
            for h in range(16):
                pb = 64 * (h % 2)
                j = h // 2
                off = (j // 4) * 512 + (j % 4) * 65
                last = t.matmul(psS[pb:pb + 64, off:off + 65],
                                khat[:, h * 64:(h + 1) * 64],
                                v_tm[:, c, h * 65:(h + 1) * 65],
                                start=True, stop=True)
            return last
        e.op(TE, fn_sadd, reads=("khat", "vtm", "vtm_ones"), writes=("psS",))

        def fn_abc(t):
            t.matmul(psR[0:64, 32:40], ones_row[0:1, 0:64], arow[:, 0:8],
                     start=True, stop=True)
            return t.matmul(psR[64:128, 32:40], ones_row[0:1, 0:64], arow[:, 8:16],
                            start=True, stop=True)
        e.op(TE, fn_abc, reads=("arow", "onesrow"), writes=("psR",))

        psS_v = psS[:].rearrange("p (b s) -> p b s", s=512)[:, :, 0:260]\
            .rearrange("p b (j x) -> p b j x", x=65)
        e.op(VE, lambda v, psS_v=psS_v: v.tensor_copy(
            out=s_stash[:].rearrange("p (b j x) -> p b j x", b=2, j=4), in_=psS_v),
             reads=("psS",), writes=("sstash",))
        e.op(VE, lambda v: v.tensor_tensor(
            out=s_run[:].rearrange("p (j x) -> p j x", x=65),
            in0=s_run[:].rearrange("p (j x) -> p j x", x=65),
            in1=psR[0:128, 32:40].unsqueeze(2).broadcast_to([128, 8, 65]),
            op=ALU.mult), reads=("srun", "psR"), writes=("srun",))
        e.op(VE, lambda v, psS_v=psS_v: v.tensor_tensor(
            out=s_run[:].rearrange("p (b j x) -> p b j x", b=2, j=4),
            in0=s_run[:].rearrange("p (b j x) -> p b j x", b=2, j=4),
            in1=psS_v, op=ALU.add),
             reads=("srun", "psS"), writes=("srun",))

        e.op(SY, lambda s, c=c: s.dma_start(out=chs_d[c], in_=s_stash[:]),
             reads=("sstash",), dma=True)
        e.op(SY, lambda s, c=c: s.dma_start(out=cha_d[c:c + 1, :], in_=arow[:]),
             reads=("arow",), dma=True)

    def fn_ac(t):
        last = None
        for c in range(16):
            last = t.matmul(psR[0:1, 48:64], ones_col[:], lna_tm[:, c * 16:(c + 1) * 16],
                            start=(c == 0), stop=(c == 15))
        return last
    e.op(TE, fn_ac, reads=("lnatm", "onescol"), writes=("psR",))
    e.op(SC, lambda s: s.activation(out=arow[:], in_=psR[0:1, 48:64], func=AF.Exp),
         reads=("psR",), writes=("arow",))

    z16 = p.sbuf("z16", [128, 16], F32)
    e.op(GP, lambda g: g.memset(z16[:], 0.0), writes=("z16",))
    e.op(VE, lambda v: v.tensor_copy(out=z16[0:1, :], in_=arow[:]),
         reads=("arow", "z16"), writes=("z16",))
    e.op(SY, lambda s: s.dma_start(out=stage1[:, 0:520], in_=s_run[:]),
         reads=("srun",), writes=("stage1",), dma=True, dkey="stage1")
    e.op(SY, lambda s: s.dma_start(out=stage1[:, 520:536], in_=z16[:]),
         reads=("z16",), writes=("stage1",), dma=True, dkey="stage1")


def emit_p2(p, e, C, io, car, upto=16, lvl=99):
    x_d, wo_d = io["x"], io["AWo"]
    chs_d, cha_d = io["chunkS"], io["chunkA"]
    gath1, pm_d = io["gath1"], io["pm"]
    x1_d, stage2 = io["x1"], io["stage2"]
    q_fm, k_fm, v_tm, lna_tm = car

    sprev = p.sbuf("sprev", [128, 3, 536], F32)
    aprev = p.sbuf("aprev", [1, 3 * H], F32)
    pm_sb = p.sbuf("pmsb", [1, 8], F32)
    pmb = p.sbuf("pmb", [128, 8], F32)
    chs = [p.sbuf(f"chs{i}", [128, 520], F32) for i in range(2)]
    cha = p.sbuf("cha", [1, 16 * H], F32)
    wo_sb = p.sbuf("wosb", [128, 8, D], BF16)
    s_run = p.sbuf("srun", [128, 520], F32)
    s_run_bf = p.sbuf("srunbf", [128, 520], BF16)
    ident_bf, ident_f = C["identbf"], C["identf"]
    mle, msl, mneg = C["mle"], C["msl"], C["mneg"]
    ones_row = C["onesrow"]
    lexp_tm = p.sbuf("lexptm", [128, H], F32)
    msl_sc = p.sbuf("mslsc", [128, 256], F32)
    dpair = p.sbuf("dpair", [128, 256], BF16)
    apair = p.sbuf("apair", [128, 256], BF16)
    apair_v = apair[:].rearrange("p (b s) -> p b s", s=128)
    cr_sb = p.sbuf("crsb", [128, 130], F32)
    n_sb = p.sbuf("nsb", [128, 16, 65], F32)
    dr_sb = p.sbuf("drsb", [128, H], F32)
    obuf = p.sbuf("obuf", [128, D], BF16)
    attn_fm = p.sbuf("attnfm", [128, 8, 128], BF16)
    xt = [p.sbuf(f"xt{i}", [128, D], F32) for i in range(2)]
    x1t = p.sbuf("x1t", [128, D], F32)
    u1t = p.sbuf("u1t", [128, D], F32)
    stat = p.sbuf("stat0", [128, 16], F32)

    psBD = p.psum("psBD", [128, 1024], F32)
    psN = p.psum("psN", [128, 1536], F32)
    psY = p.psum("psY", [128, 512], F32)
    psT = p.psum("psT", [128, 1024], BF16)
    psR = p.psum("psR", [128, 512], F32)

    SY, TE, VE, SC, GP = ENGINES

    def fn_ldprev(s):
        return [s.dma_start(out=sprev[:, k, :], in_=gath1[k * 128:(k + 1) * 128, :])
                for k in range(3)]
    e.op(SY, fn_ldprev, reads=("gath1",), writes=("sprev",), dma=True, n_dma=3)
    e.op(SY, lambda s: s.dma_start(out=pm_sb[:], in_=pm_d[:]),
         writes=("pm",), dma=True, dkey="pmld")
    e.op(SY, lambda s: s.dma_start(out=cha[:], in_=cha_d[:].rearrange("k h -> (k h)").rearrange("(o n) -> o n", o=1)), writes=("cha",), dma=True)
    e.op(GP, lambda g: g.dma_start(out=wo_sb[:],
                                   in_=wo_d[:].rearrange("(a p) b -> p a b", p=128)),
         writes=("wosb",), dma=True)

    def emit_abc(src_row, tag, extra=()):
        def fn(t):
            t.matmul(psR[0:64, 16:24], ones_row[0:1, 0:64], src_row[:, 0:16:2],
                     start=True, stop=True)
            return t.matmul(psR[64:128, 16:24], ones_row[0:1, 0:64], src_row[:, 1:16:2],
                            start=True, stop=True)
        e.op(TE, fn, reads=(tag, "onesrow") + tuple(extra), writes=("psR2k",))

    e.op(TE, lambda t: t.matmul(psR[:, 32:40], ones_row[0:1, :], pm_sb[0:1, :],
                                start=True, stop=True),
         reads=("pm", "onesrow"), writes=("psR2k",))
    e.op(VE, lambda v: v.tensor_copy(out=pmb[:], in_=psR[:, 32:40]),
         reads=("psR2k",), writes=("pmb",))
    def fn_aprime(v):
        return [v.tensor_scalar(out=aprev[0:1, k * 16:(k + 1) * 16],
                                in0=sprev[0:1, k, 520:536],
                                scalar1=pm_sb[0:1, k:k + 1],
                                scalar2=pm_sb[0:1, 4 + k:5 + k],
                                op0=ALU.mult, op1=ALU.add) for k in range(3)]
    e.op(VE, fn_aprime, reads=("sprev", "pm"), writes=("aprev",))
    e.op(VE, lambda v: v.tensor_scalar(out=s_run[:], in0=sprev[:, 0, 0:520],
                                       scalar1=pmb[:, 0:1], scalar2=None, op0=ALU.mult),
         reads=("sprev", "pmb"), writes=("srun",))
    for k in range(1, 3):
        emit_abc(aprev[0:1, k * 16:(k + 1) * 16], "aprev")
        e.op(VE, lambda v: v.tensor_tensor(
            out=s_run[:].rearrange("p (j x) -> p j x", x=65),
            in0=s_run[:].rearrange("p (j x) -> p j x", x=65),
            in1=psR[0:128, 16:24].unsqueeze(2).broadcast_to([128, 8, 65]),
            op=ALU.mult), reads=("srun", "psR2k"), writes=("srun",))
        e.op(VE, lambda v, k=k: v.scalar_tensor_tensor(
            out=s_run[:], in0=sprev[:, k, 0:520], scalar=pmb[:, k:k + 1],
            in1=s_run[:], op0=ALU.mult, op1=ALU.add),
             reads=("srun", "sprev", "pmb"), writes=("srun",))

    for c in range(upto):
        lna_c = lna_tm[:, c * 16:(c + 1) * 16]
        if lvl >= 0.2:
            e.op(SY, lambda s, c=c: s.dma_start(out=chs[c % 2][:], in_=chs_d[c]),
                 writes=(f"chs{c % 2}",), dma=True)

        if lvl >= 0.4:
            e.op(VE, lambda v: v.tensor_copy(out=s_run_bf[:], in_=s_run[:]),
                 reads=("srun",), writes=("srunbf",))

        if lvl >= 0.6:
            def fn_l(t, lna_c=lna_c):
                return t.matmul(psR[:, 0:16], mle[:], lna_c, start=True, stop=True)
            e.op(TE, fn_l, reads=("lnatm", "mle"), writes=("psR2k",))
        if 0.7 <= lvl < 0.72:
            e.op(VE, lambda v: v.tensor_copy(out=lexp_tm[:], in_=psR[:, 0:16]),
                 reads=("psR2k",), writes=("lexptm",))
        if 0.72 <= lvl < 0.74:
            e.op(SC, lambda s: s.copy(out=lexp_tm[:], in_=psR[:, 0:16]),
                 reads=("psR2k",), writes=("lexptm",))
        if 0.74 <= lvl < 0.76:
            e.op(SC, lambda s: s.activation(out=lexp_tm[:], in_=s_run[:, 0:16],
                                            func=AF.Exp),
                 reads=("srun",), writes=("lexptm",))
        if lvl >= 0.8:
            e.op(SC, lambda s: s.activation(out=lexp_tm[:], in_=psR[:, 0:16], func=AF.Exp),
                 reads=("psR2k",), writes=("lexptm",))

        for hp in range(8 if lvl >= 2 else 0):
            def fn_msc(v, hp=hp, lna_c=lna_c):
                return [v.tensor_scalar(out=msl_sc[:, j * 128:(j + 1) * 128], in0=msl[:],
                                        scalar1=lna_c[:, 2 * hp + j:2 * hp + j + 1],
                                        scalar2=None, op0=ALU.mult) for j in range(2)]
            e.op(VE, fn_msc, reads=("msl", "lnatm"), writes=("mslsc",))

            def fn_darg(t, hp=hp):
                last = []
                for j in range(2):
                    t.matmul(psBD[:, 256 + j * 512:384 + j * 512],
                             msl_sc[:, j * 128:(j + 1) * 128], mle[:],
                             start=True, stop=False)
                    last.append(t.matmul(psBD[:, 256 + j * 512:384 + j * 512],
                                         ident_f[:], mneg[:], start=False, stop=True))
                return last[-1]
            e.op(TE, fn_darg, reads=("mslsc", "mle", "mneg", "identf"), writes=("psBD",))

            def fn_dexp(s, hp=hp):
                return [s.activation(out=dpair[:, j * 128:(j + 1) * 128],
                                     in_=psBD[:, 256 + j * 512:384 + j * 512], func=AF.Exp)
                        for j in range(2)]
            e.op(SC, fn_dexp, reads=("psBD",), writes=("dpair",))

            def fn_qk(t, c=c, hp=hp):
                last = []
                for s in range(2):
                    pb = 64 * s
                    last.append(t.matmul(
                        psBD[:, s * 512:s * 512 + 128],
                        k_fm[pb:pb + 64, hp, c * 128:(c + 1) * 128],
                        q_fm[pb:pb + 64, hp, c * 128:(c + 1) * 128],
                        start=True, stop=True))
                return last
            e.op(TE, fn_qk, reads=("kfm", "qfm"), writes=("psBD",))
            psBD_qk = psBD[:].rearrange("p (b s) -> p b s", s=512)[:, :, 0:128]
            dpair_v = dpair[:].rearrange("p (b s) -> p b s", s=128)
            e.op(VE, lambda v: v.tensor_tensor(out=apair_v, in0=psBD_qk,
                                               in1=dpair_v, op=ALU.mult),
                 reads=("psBD", "dpair"), writes=("apair",))

            if lvl < 3:
                continue
            def fn_n(t, c=c, hp=hp):
                last = []
                for s in range(2):
                    h = 2 * hp + s
                    t.matmul(psN[:, s * 65:(s + 1) * 65], apair[:, s * 128:(s + 1) * 128],
                             v_tm[:, c, h * 65:(h + 1) * 65], start=True, stop=True)
                    pb = 64 * s
                    last.append(t.matmul(
                        psN[:, 512 * (s + 1):512 * (s + 1) + 65],
                        q_fm[pb:pb + 64, hp, c * 128:(c + 1) * 128],
                        s_run_bf[pb:pb + 64, hp * 65:(hp + 1) * 65],
                        start=True, stop=True))
                return last
            e.op(TE, fn_n, reads=("apair", "vtm", "qfm", "srunbf"), writes=("psN",))

            def fn_cr(s):
                return [s.copy(out=cr_sb[:, j * 65:(j + 1) * 65],
                               in_=psN[:, 512 * (j + 1):512 * (j + 1) + 65])
                        for j in range(2)]
            e.op(SC, fn_cr, reads=("psN",), writes=("crsb",))

            def fn_comb(v, hp=hp):
                return [v.scalar_tensor_tensor(
                    out=n_sb[:, 2 * hp + s, :], in0=cr_sb[:, s * 65:(s + 1) * 65],
                    scalar=lexp_tm[:, 2 * hp + s:2 * hp + s + 1],
                    in1=psN[:, s * 65:(s + 1) * 65], op0=ALU.mult, op1=ALU.add)
                    for s in range(2)]
            e.op(VE, fn_comb, reads=("crsb", "lexptm", "psN"), writes=("nsb",))

        if lvl < 4:
            continue
        e.op(VE, lambda v: v.tensor_scalar_add(dr_sb[:], n_sb[:, :, 64], float(GLA_EPS)),
             reads=("nsb",), writes=("drsb",))
        e.op(VE, lambda v: v.reciprocal(out=dr_sb[:], in_=dr_sb[:]),
             reads=("drsb",), writes=("drsb",))
        e.op(VE, lambda v: v.tensor_tensor(
            out=obuf[:].rearrange("p (h d) -> p h d", d=64),
            in0=n_sb[:, :, 0:64],
            in1=dr_sb[:].unsqueeze(2).broadcast_to([128, 16, 64]), op=ALU.mult),
            reads=("nsb", "drsb"), writes=("obuf",))

        emit_abc(cha[0:1, c * 16:(c + 1) * 16], "cha")
        e.op(VE, lambda v: v.tensor_tensor(
            out=s_run[:].rearrange("p (j x) -> p j x", x=65),
            in0=s_run[:].rearrange("p (j x) -> p j x", x=65),
            in1=psR[0:128, 16:24].unsqueeze(2).broadcast_to([128, 8, 65]),
            op=ALU.mult), reads=("srun", "psR2k"), writes=("srun",))
        e.op(VE, lambda v, c=c: v.tensor_tensor(out=s_run[:], in0=s_run[:],
                                                in1=chs[c % 2][:], op=ALU.add),
             reads=("srun", f"chs{c % 2}"), writes=("srun",))

        def fn_tr(t):
            last = None
            for dc in range(8):
                last = t.transpose(psT[:, dc * 128:(dc + 1) * 128],
                                   obuf[:, dc * 128:(dc + 1) * 128], ident_bf[:])
            return last
        e.op(TE, fn_tr, reads=("obuf", "identbf"), writes=("psT",))
        e.op(VE, lambda v: v.tensor_copy(
            out=attn_fm[:], in_=psT[:].rearrange("p (a b) -> p a b", b=128)),
            reads=("psT",), writes=("attnfm",))

        e.op(SY, lambda s, c=c: s.dma_start(out=xt[c % 2][:],
                                            in_=x_d[c * 128:(c + 1) * 128, :]),
             writes=(f"xt{c % 2}",), dma=True)

        for ng in range(2):
            def fn_wo(t, ng=ng):
                last = None
                for pc in range(8):
                    last = t.matmul(psY[:], attn_fm[:, pc, :],
                                    wo_sb[:, pc, ng * 512:(ng + 1) * 512],
                                    start=(pc == 0), stop=(pc == 7))
                return last
            e.op(TE, fn_wo, reads=("attnfm", "wosb"), writes=("psY",))
            e.op(VE, lambda v, c=c, ng=ng: v.tensor_tensor(
                out=x1t[:, ng * 512:(ng + 1) * 512],
                in0=xt[c % 2][:, ng * 512:(ng + 1) * 512],
                in1=psY[:], op=ALU.add),
                reads=(f"xt{c % 2}", "psY"), writes=("x1t",))
        e.op(SY, lambda s, c=c: s.dma_start(out=x1_d[c * 128:(c + 1) * 128, :],
                                            in_=x1t[:]),
             reads=("x1t",), dma=True, dkey="x1out")

        if c == 15:
            e.op(VE, lambda v: v.bn_stats(out=stat[:, 0:6], in_=x1t[:, 0:512]),
                 reads=("x1t",), writes=("stat0",))
            e.op(VE, lambda v: v.bn_stats(out=stat[:, 6:12], in_=x1t[:, 512:1024]),
                 reads=("x1t",), writes=("stat0",))
            e.op(VE, lambda v: v.bn_aggr(out=stat[:, 12:14], in_=stat[:, 0:12]),
                 reads=("stat0",), writes=("stat0",))
            e.op(VE, lambda v: v.tensor_scalar_add(stat[:, 13:14], stat[:, 13:14],
                                                   float(LN_EPS)),
                 reads=("stat0",), writes=("stat0",))
            e.op(SC, lambda s: s.activation(out=stat[:, 14:15], in_=stat[:, 13:14],
                                            func=AF.Ln), reads=("stat0",), writes=("stat0",))
            e.op(SC, lambda s: s.activation(out=stat[:, 15:16], in_=stat[:, 14:15],
                                            func=AF.Exp, scale=-0.5),
                 reads=("stat0",), writes=("stat0",))
            e.op(VE, lambda v: v.tensor_scalar(out=u1t[:], in0=x1t[:],
                                               scalar1=stat[:, 12:13],
                                               scalar2=stat[:, 15:16],
                                               op0=ALU.subtract, op1=ALU.mult),
                 reads=("x1t", "stat0"), writes=("u1t",))
            e.op(SY, lambda s: s.dma_start(out=stage2[:], in_=u1t[127:128, :]),
                 reads=("u1t",), writes=("stage2",), dma=True, dkey="u1out")


def emit_p3(p, e, C, io, ccfn):
    x1_d, gath2 = io["x1"], io["gath2"]
    wp1_d, bp1_d, wp2_d, bp2_d = io["Wp1"], io["bp1"], io["Wp2"], io["bp2"]
    w1_d, b1_d, w2_d, b2_d = io["W1"], io["b1"], io["W2"], io["b2"]
    wo_d, bo_d = io["FWo"], io["bo"]
    x2_d = io["x2"]
    stage3a, stage3b = io["stage3a"], io["stage3b"]

    xt = [p.sbuf(f"xt{i}", [128, D], F32) for i in range(2)]
    stat = [p.sbuf(f"stat{i}", [128, 16], F32) for i in range(2)]
    ubf = [p.sbuf(f"ubf{i}", [128, D], BF16) for i in range(2)]
    u_fm = p.sbuf("ufm", [128, 8, NT], BF16)
    lnstat = p.sbuf("lnstat", [128, 2 * NTILE], F32)
    lrow = p.sbuf("lrow", [1, D], F32)
    wp1_sb = p.sbuf("wp1sb", [128, 8, 128], BF16)
    wp2_sb = p.sbuf("wp2sb", [128, 8], BF16)
    bp1_sb = p.sbuf("bp1sb", [1, 128], F32)
    bp2_sb = p.sbuf("bp2sb", [1, 8], F32)
    glog = p.sbuf("glog", [1, 8], F32)
    gmax = p.sbuf("gmax", [1, 8], F32)
    gsel = p.sbuf("gsel", [1, 8], U32)
    w1_sb = p.sbuf("w1sb", [128, 8, 2048], BF16)
    w2big = p.sbuf("w2big", [128, 16, 1024], BF16)
    w2_sb = w2big[:]
    b1r = p.sbuf("b1r", [1, 2048], F32)
    b2r = p.sbuf("b2r", [1, D], BF16)
    bor = p.sbuf("bor", [1, D], BF16)
    b1c = p.sbuf("b1c", [128, 16], F32)
    wo_sb = p.sbuf("wosb", [128, 8, D], BF16)
    ident_bf, ident_f = C["identbf"], C["identf"]
    ones_row, ones_row_bf = C["onesrow"], C["onesrowbf"]
    h_sb = p.sbuf("hsb", [128, 512], BF16)
    r_sb = p.sbuf("rsb", [128, D], BF16)
    r_fm = p.sbuf("rfm", [128, 8, 128], BF16)
    x2t = p.sbuf("x2t", [128, D], F32)
    u1t = p.sbuf("u1t", [128, D], F32)
    u2_fm = u_fm

    psB = p.psum("psB", [128, 512], F32)
    psT = p.psum("psT", [128, 1024], BF16)
    psH = p.psum("psH", [128, 1024], F32)
    psR2 = p.psum("psR2", [128, 2048], F32)

    SY, TE, VE, SC, GP = ENGINES

    e.op(SY, lambda s: s.dma_start(out=lrow[:], in_=gath2[3:4, :]),
         reads=("gath2",), writes=("lrow",), dma=True)
    e.op(GP, lambda g: g.dma_start(out=wp1_sb[:],
                                   in_=wp1_d[:].rearrange("(a p) f -> p a f", p=128)),
         writes=("wp1",), dma=True)
    e.op(GP, lambda g: g.dma_start(out=wp2_sb[:], in_=wp2_d[:]), writes=("wp2",), dma=True)
    e.op(SY, lambda s: s.dma_start(out=bp1_sb[:],
                                   in_=bp1_d[:].rearrange("(o n) -> o n", o=1)),
         writes=("bp1",), dma=True)
    e.op(SY, lambda s: s.dma_start(out=bp2_sb[:],
                                   in_=bp2_d[:].rearrange("(o n) -> o n", o=1)),
         writes=("bp2",), dma=True)

    def fn_lvt(t):
        last = None
        for dc in range(8):
            last = t.transpose(psB[:, dc:dc + 1], lrow[0:1, dc * 128:(dc + 1) * 128],
                               ident_f[0:1, 0:1])
        return last
    e.op(TE, fn_lvt, reads=("lrow", "identf"), writes=("psB",))
    lv_fm = p.sbuf("lvfm", [128, 8], BF16)
    e.op(VE, lambda v: v.tensor_copy(out=lv_fm[:], in_=psB[:, 0:8]),
         reads=("psB",), writes=("lvfm",))
    def fn_bp1(t):
        return t.transpose(psB[:, 8:9], bp1_sb[:], ident_f[0:1, 0:1])
    e.op(TE, fn_bp1, reads=("bp1", "identf"), writes=("psB",))
    bp1c = p.sbuf("bp1c", [128, 1], F32)
    e.op(VE, lambda v: v.tensor_copy(out=bp1c[:], in_=psB[:, 8:9]),
         reads=("psB",), writes=("bp1c",))

    def fn_z1(t):
        last = None
        for dc in range(8):
            last = t.matmul(psH[:, 0:1], wp1_sb[:, dc, :], lv_fm[:, dc:dc + 1],
                            start=(dc == 0), stop=(dc == 7))
        return last
    e.op(TE, fn_z1, reads=("wp1", "lvfm"), writes=("psH0",))
    z1g = p.sbuf("z1g", [128, 1], BF16)
    e.op(SC, lambda s: s.activation(out=z1g[:], in_=psH[:, 0:1], func=AF.Gelu,
                                    bias=bp1c[:], scale=1.0),
         reads=("psH0", "bp1c"), writes=("z1g",))
    def fn_z2(t):
        return t.matmul(psH[:, 4:12][0:1, :], z1g[:], wp2_sb[:], start=True, stop=True)
    e.op(TE, fn_z2, reads=("z1g", "wp2"), writes=("psH0",))
    e.op(VE, lambda v: v.tensor_add(out=glog[:], in0=psH[0:1, 4:12], in1=bp2_sb[:]),
         reads=("psH0", "bp2"), writes=("glog",))
    e.op(VE, lambda v: v.max(out=gmax[:], in_=glog[:]), reads=("glog",), writes=("gmax",))
    e.op(VE, lambda v: v.max_index(out=gsel[:], in_max=gmax[:], in_values=glog[:]),
         reads=("gmax", "glog"), writes=("gsel",))

    def fn_sel_dma(g):
        reg = g.alloc_register("selreg")
        g.load(reg, gsel[0:1, 0:1])
        dsem = e.dma_sem(GP, "expertw")
        for ex in range(8):
            with g.If_eq(reg, ex):
                g.dma_start(out=w1_sb[:],
                            in_=w1_d[ex].rearrange("(a p) f -> p a f", p=128)
                            ).then_inc(dsem.h, 16)
                g.dma_start(out=w2_sb,
                            in_=w2_d[ex].rearrange("(a p) f -> p a f", p=128)
                            ).then_inc(dsem.h, 16)
                g.dma_start(out=b1r[:],
                            in_=b1_d[ex].rearrange("(o n) -> o n", o=1)
                            ).then_inc(dsem.h, 16)
                g.dma_start(out=b2r[:],
                            in_=b2_d[ex].rearrange("(o n) -> o n", o=1)
                            ).then_inc(dsem.h, 16)
        return None
    e.op(GP, fn_sel_dma, reads=("gsel",), writes=("w1", "w2", "b1r", "b2r"),
         self_inc=64, dkey="expertw")

    for grp in range(2):
        def fn_b1(t, grp=grp):
            last = None
            for j in range(8):
                fc = grp * 8 + j
                last = t.transpose(psB[:, 16 + j:17 + j],
                                   b1r[0:1, fc * 128:(fc + 1) * 128], ident_f[0:1, 0:1])
            return last
        e.op(TE, fn_b1, reads=("b1r", "identf"), writes=("psB",))
        e.op(VE, lambda v, grp=grp: v.tensor_copy(out=b1c[:, grp * 8:(grp + 1) * 8],
                                                  in_=psB[:, 16:24]),
             reads=("psB",), writes=("b1c",))

    e.op(GP, lambda g: g.dma_start(out=wo_sb[:],
                                   in_=wo_d[:].rearrange("(a p) b -> p a b", p=128)),
         writes=("wosb",), dma=True)
    e.op(GP, lambda g: g.dma_start(out=bor[:], in_=bo_d[:].rearrange("(o n) -> o n", o=1)),
         writes=("bor",), dma=True)

    emit_ln_tiles(e, p, x1_d, u_fm,
                  dict(xt=xt, stat=stat, ubf=ubf, psT=psT, ident_bf=ident_bf),
                  stat_out=lnstat, ukey="u")

    for tp in range(8):
        for fc in range(16):
            par = fc % 2
            hps = psH[:, par * 512:par * 512 + 256]
            hsv = h_sb[:, par * 256:(par + 1) * 256]

            def fn_h(t, tp=tp, fc=fc, hps=hps):
                last = None
                for dc in range(8):
                    last = t.matmul(hps, w1_sb[:, dc, fc * 128:(fc + 1) * 128],
                                    u_fm[:, dc, tp * 256:(tp + 1) * 256],
                                    start=(dc == 0), stop=(dc == 7))
                return last
            e.op(TE, fn_h, reads=("w1", f"ufm_u_g{tp // 2}"), writes=(f"psH{par}",))
            e.op(SC, lambda s, fc=fc, hps=hps, hsv=hsv: s.activation(
                out=hsv, in_=hps, func=AF.Gelu,
                bias=b1c[:, fc:fc + 1], scale=1.0),
                 reads=(f"psH{par}", "b1c"), writes=(f"hsb{par}",))

            def fn_r(t, tp=tp, fc=fc, hsv=hsv):
                last = []
                for ti in range(2):
                    for ng in range(2):
                        last.append(t.matmul(
                            psR2[:, ti * 1024 + ng * 512:ti * 1024 + (ng + 1) * 512],
                            hsv[:, ti * 128:(ti + 1) * 128],
                            w2_sb[:, fc, ng * 512:(ng + 1) * 512],
                            start=(fc == 0), stop=False))
                return last[-1]
            e.op(TE, fn_r, reads=(f"hsb{par}", "w2"), writes=("psR2",))
        def fn_rb(t, tp=tp):
            last = []
            for ti in range(2):
                for ng in range(2):
                    last.append(t.matmul(
                        psR2[:, ti * 1024 + ng * 512:ti * 1024 + (ng + 1) * 512],
                        ones_row_bf[:], b2r[0:1, ng * 512:(ng + 1) * 512],
                        start=False, stop=True))
            return last[-1]
        e.op(TE, fn_rb, reads=("onesrowbf", "b2r"), writes=("psR2",))

        for ti in range(2):
            e.op(VE, lambda v, ti=ti: v.tensor_copy(out=r_sb[:],
                                                    in_=psR2[:, ti * 1024:(ti + 1) * 1024]),
                 reads=("psR2",), writes=("rsb",))

            def fn_tr(t):
                last = None
                for dc in range(8):
                    last = t.transpose(psT[:, dc * 128:(dc + 1) * 128],
                                       r_sb[:, dc * 128:(dc + 1) * 128], ident_bf[:])
                return last
            e.op(TE, fn_tr, reads=("rsb", "identbf"), writes=("psT",))
            e.op(VE, lambda v: v.tensor_copy(
                out=r_fm[:], in_=psT[:].rearrange("p (a b) -> p a b", b=128)),
                reads=("psT",), writes=("rfm",))

            i = tp * 2 + ti
            e.op(SY, lambda s, i=i: s.dma_start(out=xt[i % 2][:],
                                                in_=x1_d[i * 128:(i + 1) * 128, :]),
                 writes=(f"xt{i % 2}",), dma=True)
            for ng in range(2):
                def fn_o(t, ng=ng):
                    last = None
                    for pc in range(8):
                        last = t.matmul(psH[:, 0:512], r_fm[:, pc, :],
                                        wo_sb[:, pc, ng * 512:(ng + 1) * 512],
                                        start=(pc == 0), stop=False)
                    return t.matmul(psH[:, 0:512], ones_row_bf[:],
                                    bor[0:1, ng * 512:(ng + 1) * 512],
                                    start=False, stop=True)
                e.op(TE, fn_o, reads=("rfm", "wosb", "onesrowbf", "bor"), writes=("psH0",))
                e.op(VE, lambda v, i=i, ng=ng: v.tensor_scalar(
                    out=u1t[:, ng * 512:(ng + 1) * 512],
                    in0=xt[i % 2][:, ng * 512:(ng + 1) * 512],
                    scalar1=lnstat[:, 2 * i:2 * i + 1], scalar2=lnstat[:, 2 * i + 1:2 * i + 2],
                    op0=ALU.subtract, op1=ALU.mult),
                    reads=(f"xt{i % 2}", "lnstat_u"), writes=("u1t",))
                e.op(VE, lambda v, i=i, ng=ng: v.tensor_add(
                    out=u1t[:, ng * 512:(ng + 1) * 512],
                    in0=u1t[:, ng * 512:(ng + 1) * 512],
                    in1=xt[i % 2][:, ng * 512:(ng + 1) * 512]),
                    reads=(f"xt{i % 2}", "u1t"), writes=("u1t",))
                e.op(VE, lambda v, ng=ng: v.scalar_tensor_tensor(
                    out=x2t[:, ng * 512:(ng + 1) * 512], in0=psH[:, 0:512], scalar=0.3,
                    in1=u1t[:, ng * 512:(ng + 1) * 512], op0=ALU.mult, op1=ALU.add),
                    reads=("psH0", "u1t"), writes=("x2t",))
            e.op(SY, lambda s, i=i: s.dma_start(out=x2_d[i * 128:(i + 1) * 128, :],
                                                in_=x2t[:]),
                 reads=("x2t",), writes=(f"x2o{i}",), dma=True, dkey="x2out")

    e.op(VE, lambda v: v.engine_nop(), reads=(),
         writes=("ufm_u",) + tuple(f"ufm_u_g{i}" for i in range(4)))
    emit_ln_tiles(e, p, x2_d, u2_fm,
                  dict(xt=xt, stat=stat, ubf=ubf, psT=psT, ident_bf=ident_bf), ukey="u2",
                  src_key=[f"x2o{i}" for i in range(16)])
    e.op(SY, lambda s: s.dma_start(
        out=stage3a[:].rearrange("p (a t) -> p a t", t=1024),
        in_=u2_fm[:, :, 0:1024]),
         reads=("ufm_u2_g0", "ufm_u2_g1"), writes=("stage3a",), dma=True,
         dkey="u2outa")
    ccfn("cc3a", "stage3a", "gath3a", io["gath3a"] if False else stage3a, io["gath3a"])
    e.op(SY, lambda s: s.dma_start(
        out=stage3b[:].rearrange("p (a t) -> p a t", t=1024),
        in_=u2_fm[:, :, 1024:2048]),
         reads=("ufm_u2_g2", "ufm_u2_g3"), writes=("stage3b",), dma=True,
         dkey="u2outb")
    ccfn("cc3b", "stage3b", "gath3b", stage3b, io["gath3b"])


def emit_p4(p, e, C, io, ccfn):
    gath3a, gath3b = io["gath3a"], io["gath3b"]
    wq_d, wk_d, wv_d = io["WqT"], io["WkT"], io["WvT"]
    stage4 = [io["stage4a"], io["stage4b"]]

    wq_sb = p.sbuf("wqsb", [128, 8, 256], BF16)
    wk_sb = p.sbuf("wksb", [128, 8, 256], BF16)
    wv_sb = p.sbuf("wvsb", [128, 8, 256], BF16)
    u2r = [p.sbuf(f"u2r{i}", [128, 8, NT], BF16) for i in range(2)]
    q_fm = p.sbuf("qfm", [128, 2, 4 * NT], BF16)
    k_fm = p.sbuf("kfm", [128, 2, 4 * NT], BF16)
    v_tm = p.sbuf("vtm", [128, 64, 260], BF16)
    pt_sb = p.sbuf("ptsb", [128, 2048], BF16)
    sq_sc = p.sbuf("sqsc", [128, 2048], BF16)
    bmax = p.sbuf("bmax", [1, 64], F32)
    negb2 = p.sbuf("negb2", [128, 2], F32)
    linvb = p.sbuf("linvb", [64, 512], F32)
    ot = [p.sbuf(f"ot{i}", [128, 512], BF16) for i in range(2)]
    nac = [[p.sbuf(f"nac{i}{s}", [65, 512], F32) for s in range(2)]
           for i in range(2)]
    linv2 = [[p.sbuf(f"linv{i}{s}", [1, 512], F32) for s in range(2)]
             for i in range(2)]
    ones_row, ones_col = C["onesrow"], C["onescolbf"]

    psS = p.psum("psS", [128, 2048], F32)
    psA = [p.psum("psA0", [128, 512], F32), p.psum("psA1", [128, 512], F32)]
    psB = p.psum("psB", [128, 512], F32)

    SY, TE, VE, SC, GP = ENGINES

    e.op(GP, lambda g: g.memset(
        v_tm[:].rearrange("p t (h x) -> p t h x", x=65)[:, :, :, 64:65], 1.0),
        writes=("vtm_ones",))
    e.op(GP, lambda g: g.memset(bmax[:], 0.0),
         writes=("bmax_q", "bmax_k", "bmax_t", "bmax_s"))

    for w_d, w_sb, tag in ((wq_d, wq_sb, "wq"), (wk_d, wk_sb, "wk"),
                           (wv_d, wv_sb, "wv")):
        e.op(GP, lambda g, w_d=w_d, w_sb=w_sb: g.dma_start(
            out=w_sb[:], in_=w_d.rearrange("(a p) f -> p a f", p=128)),
            writes=(tag,), dma=True, dkey=tag)

    for r in range(4):
        ub, ukk = u2r[r % 2], f"u2r{r % 2}"
        def fn_ldu(s, r=r, ub=ub):
            return [
                s.dma_start(out=ub[:, :, 0:1024],
                            in_=gath3a[r * 128:(r + 1) * 128, :]
                            .rearrange("p (a t) -> p a t", a=8)),
                s.dma_start(out=ub[:, :, 1024:2048],
                            in_=gath3b[r * 128:(r + 1) * 128, :]
                            .rearrange("p (a t) -> p a t", a=8))]
        e.op(SY, fn_ldu, reads=("gath3a", "gath3b"), writes=(ukk,), dma=True,
             n_dma=2, dkey=ukk)
        for dst, dk, tag in ((q_fm, "qfm", "wq"), (k_fm, "kfm", "wk")):
            w_sb = {"wq": wq_sb, "wk": wk_sb}[tag]
            for oc in range(2):
                for tg in range(4):
                    def fn_mm(t, oc=oc, tg=tg, ub=ub, w_sb=w_sb):
                        last = None
                        for dc in range(8):
                            last = t.matmul(psB[:], w_sb[:, dc, oc * 128:(oc + 1) * 128],
                                            ub[:, dc, tg * 512:(tg + 1) * 512],
                                            start=(dc == 0), stop=(dc == 7))
                        return last
                    e.op(TE, fn_mm, reads=(tag, ukk), writes=("psB",))
                    e.op(VE, lambda v, oc=oc, tg=tg, r=r, dst=dst, dk=dk: v.tensor_copy(
                        out=dst[:, oc, r * NT + tg * 512:r * NT + (tg + 1) * 512],
                        in_=psB[:]),
                        reads=("psB",), writes=(dk,))
        for ti in range(16):
            def fn_vm(t, ti=ti, ub=ub):
                last = None
                for dc in range(8):
                    last = t.matmul(psB[:, 0:256], ub[:, dc, ti * 128:(ti + 1) * 128],
                                    wv_sb[:, dc, :], start=(dc == 0), stop=(dc == 7))
                return last
            e.op(TE, fn_vm, reads=("wv", ukk), writes=("psB",))
            e.op(VE, lambda v, ti=ti, r=r: v.tensor_copy(
                out=v_tm[:, r * 16 + ti].rearrange("p (h x) -> p h x", x=65)[:, :, 0:64],
                in_=psB[:, 0:256].rearrange("p (h d) -> p h d", d=64)),
                reads=("psB",), writes=("vtm",))

    def emit_norms(src_ap, src_keys, out_off, tag):
        e.op(GP, lambda g: g.tensor_mul(sq_sc[:], src_ap, src_ap),
             reads=src_keys, writes=("sqsc",))

        def fn_nq(t):
            last = []
            for s in range(2):
                for g in range(4):
                    last.append(t.matmul(
                        psS[0:1, s * 1024 + (g % 2) * 512:s * 1024 + (g % 2) * 512 + 512],
                        ones_col[64 * s:64 * s + 64, :],
                        sq_sc[64 * s:64 * s + 64, g * 512:(g + 1) * 512],
                        start=(g < 2), stop=(g >= 2)))
            return last
        e.op(TE, fn_nq, reads=("sqsc", "onescol"), writes=("psS0", "psS1"))

        def fn_red(v):
            v.reduce_max(bmax[:, 30:31], psS[0:1, 0:1024], axis=AX)
            return v.reduce_max(bmax[:, 31:32], psS[0:1, 1024:2048], axis=AX)
        e.op(VE, fn_red, reads=("psS0", "psS1"), writes=("bmax_t",))
        e.op(VE, lambda v: v.tensor_max(bmax[:, out_off:out_off + 2],
                                        bmax[:, out_off:out_off + 2], bmax[:, 30:32]),
             reads=("bmax_t", tag), writes=(tag,))

    for hp in range(2):
        for r in range(4):
            emit_norms(q_fm[:, hp, r * NT:(r + 1) * NT], ("qfm",), 2 * hp, "bmax_q")
            emit_norms(k_fm[:, hp, r * NT:(r + 1) * NT], ("kfm",), 8 + 2 * hp, "bmax_k")

    for hp in range(2):
        e.op(VE, lambda v, hp=hp: v.tensor_mul(bmax[:, 24:26], bmax[:, 2 * hp:2 * hp + 2],
                                               bmax[:, 8 + 2 * hp:8 + 2 * hp + 2]),
             reads=("bmax_q", "bmax_k"), writes=("bmax_s",))
        e.op(VE, lambda v: v.tensor_max(bmax[:, 26:27], bmax[:, 24:25], bmax[:, 25:26]),
             reads=("bmax_s",), writes=("bmax_s",))
        e.op(SC, lambda s: s.activation(out=bmax[:, 27:28], in_=bmax[:, 26:27],
                                        func=AF.Sqrt), reads=("bmax_s",), writes=("bmax_s",))
        e.op(VE, lambda v: v.tensor_scalar(out=bmax[:, 28:29], in0=bmax[:, 27:28],
                                           scalar1=-0.1275, scalar2=None, op0=ALU.mult),
             reads=("bmax_s",), writes=("bmax_s",))
        e.op(TE, lambda t: t.matmul(psB[:, 0:1], ones_row[:], bmax[0:1, 28:29],
                                    start=True, stop=True),
             reads=("bmax_s", "onesrow"), writes=("psB",))
        e.op(SC, lambda s, hp=hp: s.copy(out=negb2[:, hp:hp + 1], in_=psB[:, 0:1]),
             reads=("psB",), writes=("negb",))

    dmneg = p.sbuf("dmneg", [128, 4, 512], BF16)
    identp4 = C["identbf"]

    def fn_dm(g):
        last = None
        for k in range(4):
            g.memset(dmneg[:, k, :], 0.0)
            last = g.affine_select(
                out=dmneg[:, k, :], in_=dmneg[:, k, :], compare_op=ALU.is_ge,
                fill=-1e30, base=-128 * k, pattern=[[1, 512]],
                channel_multiplier=-1)
        return last
    e.op(GP, fn_dm, writes=("dmneg",))

    pending_div = []
    for hp in range(2):
        for qt in range(16):
            ntt = 4 * (qt + 1)

            def emit_s(tt):
                h = tt % 2
                diag = tt >= 4 * qt

                def fn_s2(t, hp=hp, qt=qt, tt=tt, h=h, diag=diag):
                    last = []
                    for s in range(2):
                        seg = psS[:, h * 1024 + s * 512:h * 1024 + (s + 1) * 512]
                        mm = t.matmul(seg,
                                      k_fm[64 * s:64 * s + 64, hp, tt * 128:(tt + 1) * 128],
                                      q_fm[64 * s:64 * s + 64, hp, qt * 512:(qt + 1) * 512],
                                      start=True, stop=not diag)
                        if diag:
                            k = tt - 4 * qt
                            mm = t.matmul(seg, identp4[:], dmneg[:, k, :],
                                          start=False, stop=True)
                        last.append(mm)
                    return last
                e.op(TE, fn_s2, reads=("kfm", "qfm", "identbf", "dmneg"),
                     writes=(f"psS{h}",))
                e.op(SC, lambda s, tt=tt, h=h, hp=hp: s.activation(
                    out=pt_sb[:, h * 1024:(h + 1) * 1024],
                    in_=psS[:, h * 1024:(h + 1) * 1024], func=AF.Exp,
                    bias=negb2[:, hp:hp + 1], scale=0.125),
                    reads=(f"psS{h}", "negb"), writes=(f"pt{h}",))

            def emit_pv(tt):
                h = tt % 2

                def fn_pv(t, hp=hp, tt=tt, h=h, first=(tt == 0),
                          stop=(tt == ntt - 1)):
                    out = []
                    for s in range(2):
                        lh = 2 * hp + s
                        out.append(t.matmul(
                            psA[s][0:65, :],
                            v_tm[:, tt, lh * 65:(lh + 1) * 65],
                            pt_sb[:, h * 1024 + s * 512:h * 1024 + (s + 1) * 512],
                            start=first, stop=stop, skip_group_check=True))
                    return out
                e.op(TE, fn_pv, reads=("vtm", "vtm_ones", f"pt{h}"),
                     writes=("psA0", "psA1"))

            emit_s(0)
            if pending_div:
                pending_div.pop(0)()
            for tt in range(1, ntt):
                emit_s(tt)
                emit_pv(tt - 1)
            emit_pv(ntt - 1)

            par = qt % 2
            for s in range(2):
                e.op(VE, lambda v, s=s, par=par: v.tensor_copy(
                    out=nac[par][s][:], in_=psA[s][0:65, :]),
                    reads=(f"psA{s}",), writes=(f"nac{par}{s}",))
            for s in range(2):
                e.op(VE, lambda v, s=s, par=par: v.tensor_scalar(
                    out=linv2[par][s][:], in0=nac[par][s][64:65, :], scalar1=1e-35,
                    scalar2=None, op0=ALU.max),
                    reads=(f"nac{par}{s}",), writes=(f"linv{par}{s}",))
                e.op(VE, lambda v, s=s, par=par: v.reciprocal(
                    out=linv2[par][s][:], in_=linv2[par][s][:]),
                     reads=(f"linv{par}{s}",), writes=(f"linv{par}{s}",))

            def div_b(hp=hp, qt=qt, par=par):
                ob, obk = ot[par], f"ot{par}"
                for s in range(2):
                    e.op(TE, lambda t, s=s, par=par: t.matmul(
                        psB[0:64, :], ones_row[0:1, 0:64], linv2[par][s][:],
                        start=True, stop=True),
                         reads=(f"linv{par}{s}", "onesrow"), writes=("psB",))
                    e.op(SC, lambda s_: s_.copy(out=linvb[:], in_=psB[0:64, :]),
                         reads=("psB",), writes=("linvb",))
                    e.op(VE, lambda v, s=s, ob=ob, obk=obk, par=par: v.tensor_tensor(
                        out=ob[64 * s:64 * s + 64, :],
                        in0=nac[par][s][0:64, :], in1=linvb[:], op=ALU.mult),
                        reads=(f"nac{par}{s}", "linvb"), writes=(obk,))
                e.op(SY, lambda s, hp=hp, qt=qt, ob=ob, obk=obk: s.dma_start(
                    out=stage4[hp][:, qt * 512:(qt + 1) * 512], in_=ob[:]),
                    reads=(obk,), writes=(f"stage4{hp}",), dma=True, dkey="oout")
            pending_div.append(div_b)
        if hp == 0:
            while pending_div:
                pending_div.pop(0)()
            ccfn("cc4a", "stage40", "gath4a", stage4[0], io["gath4a"])
    while pending_div:
        pending_div.pop(0)()
    ccfn("cc4b", "stage41", "gath4b", stage4[1], io["gath4b"])


def emit_p5(p, e, C, io, lvl=99):
    x2_d, cpos_d = io["x2"], io["cpos"]
    gath4h = [io["gath4a"], io["gath4b"]]
    wot_d, gw_d = io["WoT"], io["gate_w"]
    w1_d, w2_d, w3_d = io["m_w1"], io["m_w2"], io["m_w3"]
    sw1_d, sw2_d, sw3_d = io["sw1"], io["sw2"], io["sw3"]
    x4_d, x3_d = io["x4"], io["x3"]

    xt = [p.sbuf(f"xt{i}", [128, D], F32) for i in range(2)]
    stat = [p.sbuf(f"stat{i}", [128, 16], F32) for i in range(2)]
    ubf = [p.sbuf(f"ubf{i}", [128, D], BF16) for i in range(2)]
    u_fm = p.sbuf("ufm", [128, 8, NT], BF16)
    o_sb = u_fm
    cp_sb = p.sbuf("cpsb", [1, 1], U32)
    ident_bf = C["identbf"]

    psH1 = p.psum("psH1", [128, 1024], F32)
    psH3 = p.psum("psH3", [128, 1024], F32)
    psY = [p.psum("psY0", [128, 512], F32), p.psum("psY1", [128, 512], F32)]
    psT = p.psum("psT", [128, 1024], BF16)
    psZ = p.psum("psZ", [128, 512], F32)

    SY, TE, VE, SC, GP = ENGINES

    e.op(SY, lambda s: s.dma_start(out=cp_sb[:], in_=cpos_d[:]),
         writes=("cpsb",), dma=True, dkey="cpld")
    ogat = e.dma_sem(GP, "ogat")

    def fn_ogather(g):
        reg = g.alloc_register("posreg")
        g.load(reg, cp_sb[0:1, 0:1])
        for j in range(4):
            with g.If_eq(reg, j):
                for gsrc in range(4):
                    for ch in range(2):
                        g.dma_start(
                            out=o_sb[:, 2 * gsrc + ch, :],
                            in_=gath4h[ch][gsrc * 128:(gsrc + 1) * 128,
                                           j * NT:(j + 1) * NT]
                        ).then_inc(ogat.h, 16)
        return None
    e.op(GP, fn_ogather, reads=("gath4a", "gath4b", "cpsb"), writes=("ufm_u",),
         self_inc=128, dkey="ogat")

    with p.scope():
        wot_sb = p.sbuf("wotsb", [128, 8, D], BF16)
        x3t = p.sbuf("x3t", [128, D], F32)
        x4t = p.sbuf("x4t", [128, D], F32)
        e.op(GP, lambda g: g.dma_start(out=wot_sb[:],
                                       in_=wot_d[:].rearrange("(a p) b -> p a b", p=128)),
             writes=("wot",), dma=True, dkey="wotload")
        x3b = [x3t, x4t]
        for i in range(NTILE):
            e.op(SY, lambda s, i=i: s.dma_start(out=xt[i % 2][:],
                                                in_=x2_d[i * 128:(i + 1) * 128, :]),
                 writes=(f"xt{i % 2}",), dma=True)
            xb = x3b[i % 2]
            xbk = ("x3t", "x4t")[i % 2]
            for ng in range(2):
                def fn_y(t, i=i, ng=ng, wot_sb=wot_sb):
                    last = None
                    for mc in range(8):
                        last = t.matmul(psY[0][:], o_sb[:, mc, i * 128:(i + 1) * 128],
                                        wot_sb[:, mc, ng * 512:(ng + 1) * 512],
                                        start=(mc == 0), stop=(mc == 7))
                    return last
                e.op(TE, fn_y, reads=("ufm_u", "wot"), writes=("psY0",))
                e.op(VE, lambda v, i=i, ng=ng, xb=xb: v.tensor_add(
                    out=xb[:, ng * 512:(ng + 1) * 512],
                    in0=xt[i % 2][:, ng * 512:(ng + 1) * 512], in1=psY[0][:]),
                    reads=(f"xt{i % 2}", "psY0"), writes=(xbk,))
            e.op(SY, lambda s, i=i, xb=xb: s.dma_start(
                out=x3_d[i * 128:(i + 1) * 128, :], in_=xb[:]),
                 reads=(xbk,), writes=(f"x3o{i}",), dma=True, dkey=f"x3out{i % 2}")
    e.barrier()

    y_acc = p.sbuf("yacc", [128, 16, D], F32)
    w1h = [p.sbuf(f"w1h{i}", [128, 8, D], BF16) for i in range(2)]
    w3h = [p.sbuf(f"w3h{i}", [128, 8, D], BF16) for i in range(2)]
    w2h_t = p.sbuf("w2h", [128, 8, D], BF16)
    w2h = [w2h_t, w2h_t]
    act_buf = p.sbuf("actbuf", [128, 8, 512], BF16)
    s1_sb = p.sbuf("s1sb", [128, 1024], BF16)
    gwt_sb = p.sbuf("gwtsb", [128, 8, 8], BF16)
    comb = p.sbuf("comb", [128, 16, 9], F32)
    gz = p.sbuf("gz", [128, 8], F32)
    gm = p.sbuf("gm", [128, 8], F32)
    negm1 = p.sbuf("negm1", [128, 1], F32)
    gsc = p.sbuf("gsc", [128, 8], F32)
    gw_sb = act_buf[:].rearrange("p a b -> p (a b)")[0:8, 0:1024]
    e.op(GP, lambda g: g.memset(y_acc[:], 0.0), writes=("yacc",))

    if lvl < 2:
        return
    emit_ln_tiles(e, p, x3_d, u_fm,
                  dict(xt=xt, stat=stat, ubf=ubf, psT=psT, ident_bf=ident_bf), ukey="u",
                  src_key=[f"x3o{i}" for i in range(16)])

    if lvl < 3:
        return
    e.op(GP, lambda g: g.dma_start(out=gw_sb, in_=gw_d[:]), writes=("gw",), dma=True)
    for dc in range(8):
        e.op(TE, lambda t, dc=dc: t.transpose(psZ[:, 0:8].bitcast(BF16)[:, 0:8],
                                              gw_sb[:, dc * 128:(dc + 1) * 128],
                                              ident_bf[0:8, 0:8]),
             reads=("gw", "identbf"), writes=("psZ",))
        e.op(VE, lambda v, dc=dc: v.tensor_copy(out=gwt_sb[:, dc, :],
                                                in_=psZ[:, 0:8].bitcast(BF16)[:, 0:8]),
             reads=("psZ",), writes=("gwt",))
    for ti in range(NTILE):
        def fn_z(t, ti=ti):
            last = None
            for dc in range(8):
                last = t.matmul(psZ[:, 16:24], u_fm[:, dc, ti * 128:(ti + 1) * 128],
                                gwt_sb[:, dc, :], start=(dc == 0), stop=(dc == 7))
            return last
        e.op(TE, fn_z, reads=(f"ufm_u_g{ti // 4}", "gwt"), writes=("psZ",))
        e.op(VE, lambda v: v.tensor_copy(out=gz[:], in_=psZ[:, 16:24]),
             reads=("psZ",), writes=("gz",))
        e.op(VE, lambda v: v.max(out=gm[:], in_=gz[:]), reads=("gz",), writes=("gm",))
        e.op(VE, lambda v: v.tensor_scalar(out=negm1[:], in0=gm[:, 0:1], scalar1=-1.0,
                                           scalar2=None, op0=ALU.mult),
             reads=("gm",), writes=("negm1",))
        e.op(SC, lambda s: s.activation(out=gsc[:], in_=gz[:], func=AF.Exp,
                                        bias=negm1[:], scale=1.0),
             reads=("gz", "negm1"), writes=("gsc",))
        e.op(SC, lambda s: s.activation(out=negm1[:], in_=gm[:, 1:2], func=AF.Exp,
                                        bias=negm1[:], scale=1.0),
             reads=("gm", "negm1"), writes=("negm1b",))
        e.op(VE, lambda v: v.tensor_scalar_add(negm1[:], negm1[:], 1.0),
             reads=("negm1b",), writes=("negm1b",))
        e.op(VE, lambda v: v.reciprocal(out=negm1[:], in_=negm1[:]),
             reads=("negm1b",), writes=("negm1b",))
        e.op(VE, lambda v: v.tensor_scalar(out=gsc[:], in0=gsc[:], scalar1=negm1[:],
                                           scalar2=None, op0=ALU.mult),
             reads=("gsc", "negm1b"), writes=("gsc",))
        e.op(VE, lambda v: v.tensor_scalar(out=gm[:], in0=gz[:], scalar1=gm[:, 1:2],
                                           scalar2=None, op0=ALU.is_ge),
             reads=("gz", "gm"), writes=("gmask",))
        e.op(VE, lambda v, ti=ti: v.tensor_mul(comb[:, ti, 0:8], gsc[:], gm[:]),
             reads=("gsc", "gmask"), writes=("comb",))
        e.op(VE, lambda v, ti=ti: v.memset(comb[:, ti, 8:9], 1.0),
             reads=(), writes=("comb",))

    e.op(VE, lambda v: v.engine_nop(), reads=(), writes=("gw",))

    if lvl < 4:
        return
    for eh in range(18):
        ex, half = eh // 2, eh % 2
        wb1, wb3, wb2 = w1h[eh % 2], w3h[eh % 2], w2h[eh % 2]
        wk1, wk3, wk2 = f"w1h{eh % 2}", f"w3h{eh % 2}", "w2h"
        if ex < 8:
            s1d = w1_d[ex][:, half * 1024:(half + 1) * 1024]
            s3d = w3_d[ex][:, half * 1024:(half + 1) * 1024]
            s2d = w2_d[ex][half * 1024:(half + 1) * 1024, :]
        else:
            s1d = sw1_d[0][:, half * 1024:(half + 1) * 1024]
            s3d = sw3_d[0][:, half * 1024:(half + 1) * 1024]
            s2d = sw2_d[0][half * 1024:(half + 1) * 1024, :]
        e.op(GP, lambda g, s1d=s1d, wb1=wb1: g.dma_start(
            out=wb1[:], in_=s1d.rearrange("(a p) f -> p a f", p=128)),
            writes=(wk1,), dma=True, dkey=wk1)
        e.op(GP, lambda g, s3d=s3d, wb3=wb3: g.dma_start(
            out=wb3[:], in_=s3d.rearrange("(a p) f -> p a f", p=128)),
            writes=(wk3,), dma=True, dkey=wk3)
        e.op(GP, lambda g, s2d=s2d, wb2=wb2: g.dma_start(
            out=wb2[:], in_=s2d.rearrange("(a p) f -> p a f", p=128)),
            writes=(wk2,), dma=True, dkey=wk2)

        for tq in range(4):
            for fc in range(8):
                par = fc % 2
                h1 = psH1[:, par * 512:(par + 1) * 512]
                h3 = psH3[:, par * 512:(par + 1) * 512]
                s1v = s1_sb[:, par * 512:(par + 1) * 512]

                def fn_h(t, tq=tq, fc=fc, wb1=wb1, wb3=wb3, h1=h1, h3=h3):
                    last = []
                    for dc in range(8):
                        t.matmul(h1, wb1[:, dc, fc * 128:(fc + 1) * 128],
                                 u_fm[:, dc, tq * 512:(tq + 1) * 512],
                                 start=(dc == 0), stop=(dc == 7))
                    for dc in range(8):
                        last.append(t.matmul(h3, wb3[:, dc, fc * 128:(fc + 1) * 128],
                                             u_fm[:, dc, tq * 512:(tq + 1) * 512],
                                             start=(dc == 0), stop=(dc == 7)))
                    return last[-1]
                e.op(TE, fn_h, reads=(wk1, wk3, f"ufm_u_g{tq}"),
                     writes=(f"psH1{par}", f"psH3{par}"))
                e.op(SC, lambda s, h1=h1, s1v=s1v: s.activation(
                    out=s1v, in_=h1, func=AF.Silu),
                     reads=(f"psH1{par}",), writes=(f"s1sb{par}",))
                e.op(VE, lambda v, fc=fc, h3=h3, s1v=s1v: v.tensor_mul(
                    act_buf[:, fc, :], s1v, h3),
                    reads=(f"s1sb{par}", f"psH3{par}"), writes=("actbuf",))
            for ti in range(4):
                gi = tq * 4 + ti
                for ng in range(2):
                    def fn_y(t, ti=ti, ng=ng, wb2=wb2):
                        last = None
                        for fc in range(8):
                            last = t.matmul(psY[ng][:], act_buf[:, fc, ti * 128:(ti + 1) * 128],
                                            wb2[:, fc, ng * 512:(ng + 1) * 512],
                                            start=(fc == 0), stop=(fc == 7))
                        return last
                    e.op(TE, fn_y, reads=("actbuf", wk2), writes=(f"psY{ng}",))
                    e.op(VE, lambda v, gi=gi, ng=ng, ex=ex: v.scalar_tensor_tensor(
                        out=y_acc[:, gi, ng * 512:(ng + 1) * 512], in0=psY[ng][:],
                        scalar=comb[:, gi, ex:ex + 1],
                        in1=y_acc[:, gi, ng * 512:(ng + 1) * 512],
                        op0=ALU.mult, op1=ALU.add),
                        reads=(f"psY{ng}", "comb", "yacc"), writes=("yacc",))

    for i in range(NTILE):
        e.op(SY, lambda s, i=i: s.dma_start(out=xt[0][:], in_=x3_d[i * 128:(i + 1) * 128, :]),
             reads=(f"x3o{i}",), writes=("xt0",), dma=True, dkey="x3ld")
        e.op(VE, lambda v, i=i: v.tensor_add(out=xt[1][:], in0=xt[0][:], in1=y_acc[:, i, :]),
             reads=("xt0", "yacc"), writes=("xt1",))
        e.op(SY, lambda s, i=i: s.dma_start(out=x4_d[i * 128:(i + 1) * 128, :], in_=xt[1][:]),
             reads=("xt1",), dma=True, dkey="x4out")


def shard_x(x_full):
    out = []
    for c in range(NCORES):
        b, j = c // 4, c % 4
        out.append(np.ascontiguousarray(x_full[b, j * NT:(j + 1) * NT, :]))
    return out


RG2x4 = [[0, 1, 2, 3], [4, 5, 6, 7]]
MEGA_UPTO = 5
P2_KW = {}
P5_KW = {}
MEGA_DBG = False
MEGA_NOCC = False



def build_mega():
    p = Prog()
    nc = p.nc
    e = Emit(p)
    SY, TE, VE, SC, GP = ENGINES

    ext_in = {
        "x": ([NT, D], F32),
        "Wq": ([D, D], F32), "Wk": ([D, D], F32), "Wv": ([D, D], F32),
        "Wa": ([D, H], F32), "ba": ([H], F32), "AWo": ([D, D], F32),
        "Wp1": ([D, 128], F32), "bp1": ([128], F32),
        "Wp2": ([128, 8], F32), "bp2": ([8], F32),
        "W1": ([8, D, 2048], F32), "b1": ([8, 2048], F32),
        "W2": ([8, 2048, D], F32), "b2": ([8, D], F32),
        "FWo": ([D, D], F32), "bo": ([D], F32),
        "WqT": ([D, 256], F32), "WkT": ([D, 256], F32), "WvT": ([D, 256], F32),
        "WoT": ([D, D], F32),
        "gate_w": ([8, D], F32),
        "m_w1": ([8, D, 2048], F32), "m_w2": ([8, 2048, D], F32),
        "m_w3": ([8, D, 2048], F32),
        "sw1": ([1, D, 2048], F32), "sw2": ([1, 2048, D], F32),
        "sw3": ([1, D, 2048], F32),
        "pm": ([1, 8], F32), "cpos": ([1, 1], U32),
    }
    io = {k: nc.dram_tensor(k, sh, dt, kind="ExternalInput").ap()
          for k, (sh, dt) in ext_in.items()}
    io["x4"] = nc.dram_tensor("x4", [NT, D], F32, kind="ExternalOutput").ap()
    intern = {
        "chunkS": ([16, 128, 520], F32), "chunkA": ([16, H], F32),
        "stage1": ([128, 536], F32), "gath1": ([512, 536], F32),
        "stage2": ([1, D], F32), "gath2": ([4, D], F32),
        "stage3a": ([128, 4 * NT], BF16), "gath3a": ([512, 4 * NT], BF16),
        "stage3b": ([128, 4 * NT], BF16), "gath3b": ([512, 4 * NT], BF16),
        "stage4a": ([128, 4 * NT], BF16), "gath4a": ([512, 4 * NT], BF16),
        "stage4b": ([128, 4 * NT], BF16), "gath4b": ([512, 4 * NT], BF16),
    }
    for k, (sh, dt) in intern.items():
        io[k] = nc.dram_tensor(k, sh, dt, kind="Internal").ap()
    xkind = "ExternalOutput" if MEGA_DBG else "Internal"
    for k in ("x1", "x2", "x3"):
        io[k] = nc.dram_tensor(k, [NT, D], F32, kind=xkind).ap()

    C = {
        "identbf": p.sbuf("c_identbf", [128, 128], BF16),
        "identf": p.sbuf("c_identf", [128, 128], F32),
        "msl": p.sbuf("c_msl", [128, 128], F32),
        "mle": p.sbuf("c_mle", [128, 128], F32),
        "mneg": p.sbuf("c_mneg", [128, 128], F32),
        "onesrow": p.sbuf("c_onesrow", [1, 128], F32),
        "onesrowbf": p.sbuf("c_onesrowbf", [1, 128], BF16),
        "onescolf": p.sbuf("c_onescolf", [128, 1], F32),
        "onescolbf": p.sbuf("c_onescolbf", [128, 1], BF16),
    }
    make_const(e, GP, C["identbf"][:], "identbf", "ident")
    make_const(e, GP, C["identf"][:], "identf", "ident")
    make_const(e, GP, C["msl"][:], "msl", "strict_lower")
    make_const(e, GP, C["onesrow"][:], "onesrow", "ones")
    make_const(e, GP, C["onesrowbf"][:], "onesrowbf", "ones")
    make_const(e, GP, C["onescolf"][:], "onescol", "ones")
    make_const(e, GP, C["onescolbf"][:], "onescolbf", "ones")
    e.op(GP, lambda g: g.memset(C["mle"][:], 1.0), writes=("mle",))
    e.op(GP, lambda g: g.affine_select(out=C["mle"][:], in_=C["mle"][:],
                                       compare_op=ALU.is_ge, fill=0.0, base=0,
                                       pattern=[[1, 128]], channel_multiplier=-1),
         reads=("mle",), writes=("mle",))
    e.op(GP, lambda g: g.memset(C["mneg"][:], 0.0), writes=("mneg",))
    e.op(GP, lambda g: g.affine_select(out=C["mneg"][:], in_=C["mneg"][:],
                                       compare_op=ALU.is_ge, fill=-1e30, base=0,
                                       pattern=[[1, 128]], channel_multiplier=-1),
         reads=("mneg",), writes=("mneg",))

    cc_ctrs = []

    def collective(name, src_key, dst_key, in_ap, out_ap):
        if MEGA_NOCC:
            return ()
        ctr = e.dma_sem(GP, name)

        def fn(g):
            return g.collective_compute(
                "AllGather", mybir.AluOpType.bypass, replica_groups=RG2x4,
                ins=[in_ap.opt()], outs=[out_ap.opt()]).then_inc(ctr.h)
        e.op(GP, fn, reads=(src_key,), writes=(dst_key,), self_inc=1, dkey=name)
        cc_ctrs.append(ctr)
        return (ctr,)

    x1 = x2 = x3 = x4 = ()
    with p.scope():
        car = (p.sbuf("qfm", [128, 8, NT], BF16),
               p.sbuf("kfm", [128, 8, NT], BF16),
               p.sbuf("vtm", [128, 16, 1040], BF16),
               p.sbuf("lnatm", [128, 16 * H], F32))
        with p.scope():
            emit_p1(p, e, C, io, car)
        x1 = collective("cc1", "stage1", "gath1", io["stage1"], io["gath1"])
        e.barrier(exclude=x1)
        if MEGA_UPTO >= 2:
            with p.scope():
                emit_p2(p, e, C, io, car, **P2_KW)
            x2 = collective("cc2", "stage2", "gath2", io["stage2"], io["gath2"])
    e.barrier(exclude=x1 + x2)
    if MEGA_UPTO >= 3:
        with p.scope():
            emit_p3(p, e, C, io, collective)
        e.barrier(exclude=tuple(cc_ctrs))
    if MEGA_UPTO >= 4:
        with p.scope():
            emit_p4(p, e, C, io, collective)
        e.barrier(exclude=tuple(cc_ctrs))
    if MEGA_UPTO >= 5:
        with p.scope():
            emit_p5(p, e, C, io, **P5_KW)
    e.flush()
    return p


_PROG_CACHE = {}


def _prog(name, builder):
    if name not in _PROG_CACHE:
        _PROG_CACHE[name] = builder()
    return _PROG_CACHE[name]


def kernel(**inputs):
    f32 = lambda a: np.ascontiguousarray(a, dtype=np.float32)
    x = f32(inputs["L0_attn_x"])
    shards = shard_x(x)
    p = _prog("mega", build_mega)

    def wslice(w, g):
        return np.ascontiguousarray(w[256 * g:256 * (g + 1), :].T)

    com = dict(
        Wq=f32(inputs["L0_attn_Wq"]), Wk=f32(inputs["L0_attn_Wk"]),
        Wv=f32(inputs["L0_attn_Wv"]), Wa=f32(inputs["L0_attn_Wa"]),
        ba=f32(inputs["L0_attn_ba"]), AWo=f32(inputs["L0_attn_Wo"]),
        Wp1=f32(inputs["L0_ffn_Wp1"]), bp1=f32(inputs["L0_ffn_bp1"]),
        Wp2=f32(inputs["L0_ffn_Wp2"]), bp2=f32(inputs["L0_ffn_bp2"]),
        W1=f32(inputs["L0_ffn_W1"]), b1=f32(inputs["L0_ffn_b1"]),
        W2=f32(inputs["L0_ffn_W2"]), b2=f32(inputs["L0_ffn_b2"]),
        FWo=f32(inputs["L0_ffn_Wo"]), bo=f32(inputs["L0_ffn_bo"]),
        WoT=np.ascontiguousarray(f32(inputs["L1_attn_Wo"]).T),
        gate_w=f32(inputs["L1_ffn_gate_w"]),
        m_w1=f32(inputs["L1_ffn_w1"]), m_w2=f32(inputs["L1_ffn_w2"]),
        m_w3=f32(inputs["L1_ffn_w3"]),
        sw1=f32(inputs["L1_ffn_sw1"]), sw2=f32(inputs["L1_ffn_sw2"]),
        sw3=f32(inputs["L1_ffn_sw3"]),
    )
    wq1, wk1, wv1 = (f32(inputs["L1_attn_Wq"]), f32(inputs["L1_attn_Wk"]),
                     f32(inputs["L1_attn_Wv"]))
    maps = []
    for c in range(NCORES):
        g = c % 4
        pmv = np.zeros((1, 8), np.float32)
        for k in range(3):
            pmv[0, k] = 1.0 if k < g else 0.0
            pmv[0, 4 + k] = 1.0 - pmv[0, k]
        maps.append(dict(x=shards[c],
                         WqT=wslice(wq1, g), WkT=wslice(wk1, g), WvT=wslice(wv1, g),
                         pm=pmv, cpos=np.array([[g]], np.uint32), **com))
    res = run_bass_kernel_spmd(p.nc, maps, core_ids=list(range(NCORES))).results
    out = np.empty((2, 8192, D), np.float32)
    for c in range(NCORES):
        b, jp = c // 4, c % 4
        out[b, jp * NT:(jp + 1) * NT, :] = res[c]["x4"]
    return out


# version 11
# speedup vs baseline: 1.0426x; 1.0064x over previous
import numpy as np
from contextlib import ExitStack, contextmanager

import concourse.bass as bass
import concourse.mybir as mybir
from concourse.bass_utils import run_bass_kernel_spmd

F32 = mybir.dt.float32
BF16 = mybir.dt.bfloat16
U32 = mybir.dt.uint32
AF = mybir.ActivationFunctionType
ALU = mybir.AluOpType
AX = mybir.AxisListType.X

D = 1024
NT = 2048
NTILE = 16
H = 16
DH = 64
NCORES = 8
LN_EPS = 1e-5
GLA_EPS = 1e-6

ENGINES = ("sync", "tensor", "vector", "scalar", "gpsimd")


class Ctr:
    __slots__ = ("h", "n", "name")

    def __init__(self, h, name):
        self.h = h
        self.n = 0
        self.name = name


class Prog:
    def __init__(self):
        self.nc = bass.Bass("TRN2", target_bir_lowering=False, debug=False)
        self.stack = ExitStack()
        self.root = self.stack
        self._sems = {}

    def sem(self, name):
        if name not in self._sems:
            h = self.root.enter_context(self.nc.semaphore(name))
            self._sems[name] = Ctr(h, name)
        return self._sems[name]

    def _uname(self, name):
        if not hasattr(self, "_names"):
            self._names = {}
        n = self._names.get(name, 0)
        self._names[name] = n + 1
        return name if n == 0 else f"{name}__{n}"

    def sbuf(self, name, shape, dtype):
        return self.stack.enter_context(
            self.nc.sbuf_tensor(self._uname(name), shape, dtype))

    def psum(self, name, shape, dtype):
        return self.stack.enter_context(
            self.nc.psum_tensor(self._uname(name), shape, dtype))

    @contextmanager
    def scope(self):
        outer = self.stack
        self.stack = ExitStack()
        try:
            yield
        finally:
            self.stack.close()
            self.stack = outer


class Emit:

    def __init__(self, p):
        self.p = p
        self.pend = {e: [] for e in ENGINES}
        self.eng_ctr = {e: p.sem("c_" + e) for e in ENGINES if e != "sync"}
        self.dma_sems = {}
        self.last_w = {}
        self.readers = {}
        self.hw = {e: {} for e in ENGINES}

    def dma_sem(self, eng, key):
        k = (eng, key)
        if k not in self.dma_sems:
            self.dma_sems[k] = self.p.sem(f"d{len(self.dma_sems)}_{key[:12]}")
        return self.dma_sems[k]

    def op(self, eng, fn, reads=(), writes=(), dma=False, n_dma=1, self_inc=0,
           dkey=None):
        if dma or self_inc:
            if dkey is None:
                dkey = writes[0] if writes else (reads[0] if reads else "anon")
            ctr = self.dma_sem(eng, dkey)
        else:
            ctr = self.eng_ctr[eng]
        waits = {}
        own_te = self.eng_ctr.get("tensor") if eng == "tensor" else None

        def need(dep):
            if dep is None:
                return
            c, n = dep
            if c is own_te:
                return
            if n <= 0 or self.hw[eng].get(c, 0) >= n:
                return
            waits[c] = max(waits.get(c, 0), n)

        for b in reads:
            need(self.last_w.get(b))
        for b in writes:
            need(self.last_w.get(b))
            for c, n in self.readers.get(b, {}).items():
                need((c, n))

        for c, n in waits.items():
            self.hw[eng][c] = max(self.hw[eng].get(c, 0), n)
        wl = [(c.h, n) for c, n in waits.items()]

        if self_inc:
            ctr.n += self_inc
        else:
            ctr.n += 16 * n_dma if dma else 1
        my_cnt = ctr.n
        inc_k = 16 if (dma or self_inc) else 1

        def closure(e, fn=fn, wl=wl, ctr=ctr, self_inc=self_inc, inc_k=inc_k):
            for (sh, n) in wl:
                e.wait_ge(sh, n)
            last = fn(e)
            if self_inc:
                return
            assert last is not None, "op fn must return its last instruction(s)"
            if not isinstance(last, (list, tuple)):
                last = [last]
            if inc_k == 16:
                for ins in last:
                    ins.then_inc(ctr.h, inc_k)
            else:
                last[-1].then_inc(ctr.h, 1)

        self.pend[eng].append(closure)
        for b in reads:
            self.readers.setdefault(b, {})[ctr] = my_cnt
        for b in writes:
            self.last_w[b] = (ctr, my_cnt)
            self.readers[b] = {}
        return (ctr, my_cnt)

    def barrier(self, exclude=()):
        snap = [(c, c.n) for c in
                list(self.eng_ctr.values()) + list(self.dma_sems.values())
                if c.n > 0 and c not in exclude]
        for eng in ENGINES:
            def cl(e, snap=snap):
                for c, n in snap:
                    e.wait_ge(c.h, n)
            self.pend[eng].append(cl)
            for c, n in snap:
                self.hw[eng][c] = max(self.hw[eng].get(c, 0), n)

    def flush(self):
        with self.p.nc.Block() as block:
            for e in ENGINES:
                ops = self.pend[e]
                if not ops:
                    continue

                def run_all(eng, ops=ops):
                    for cl in ops:
                        cl(eng)
                getattr(block, e)(run_all)
            final = [(c.h, c.n) for c in
                     list(self.eng_ctr.values()) + list(self.dma_sems.values()) if c.n > 0]

            def tail(eng):
                for (sh, n) in final:
                    eng.wait_ge(sh, n)
            block.sync(tail)
        self.pend = {e: [] for e in ENGINES}


def make_const(e, g_eng, ap, key, kind, sq=128):
    if kind == "ones":
        e.op(g_eng, lambda g: g.memset(ap, 1.0), writes=(key,))
        return
    fill0, cmp = (0.0, ALU.not_equal) if kind == "ident" else (1.0, ALU.is_gt)
    if kind == "ident":
        e.op(g_eng, lambda g: g.memset(ap, 0.0), writes=(key,))
        e.op(g_eng, lambda g: g.affine_select(out=ap, in_=ap, compare_op=ALU.not_equal,
                                              fill=1.0, base=0, pattern=[[-1, sq]],
                                              channel_multiplier=1),
             reads=(key,), writes=(key,))
    else:
        e.op(g_eng, lambda g: g.memset(ap, 1.0), writes=(key,))
        e.op(g_eng, lambda g: g.affine_select(out=ap, in_=ap, compare_op=ALU.is_gt,
                                              fill=0.0, base=0, pattern=[[-1, sq]],
                                              channel_multiplier=1),
             reads=(key,), writes=(key,))


def emit_ln_tiles(e, p, x_d, u_fm, bufs, n_tiles=NTILE, stat_out=None, ukey="u",
                  src_key=None):
    SY, TE, VE, SC = "sync", "tensor", "vector", "scalar"
    xt, stat, ubf, psT, ident_bf = (bufs["xt"], bufs["stat"], bufs["ubf"],
                                    bufs["psT"], bufs["ident_bf"])
    for i in range(n_tiles):
        xb, sb, ub = xt[i % 2], stat[i % 2], ubf[i % 2]
        xk, sk, uk = f"xt{i % 2}", f"stat{i % 2}", f"ubf{i % 2}"
        e.op(SY, lambda s, i=i, xb=xb: s.dma_start(
            out=xb[:], in_=x_d[i * 128:(i + 1) * 128, :]), writes=(xk,), dma=True,
            reads=(() if src_key is None else tuple(src_key)))
        e.op(VE, lambda v, xb=xb, sb=sb: v.bn_stats(out=sb[:, 0:6], in_=xb[:, 0:512]),
             reads=(xk,), writes=(sk,))
        e.op(VE, lambda v, xb=xb, sb=sb: v.bn_stats(out=sb[:, 6:12], in_=xb[:, 512:1024]),
             reads=(xk,), writes=(sk,))
        e.op(VE, lambda v, sb=sb: v.bn_aggr(out=sb[:, 12:14], in_=sb[:, 0:12]),
             reads=(sk,), writes=(sk,))
        e.op(VE, lambda v, sb=sb: v.tensor_scalar_add(sb[:, 13:14], sb[:, 13:14],
                                                      float(LN_EPS)),
             reads=(sk,), writes=(sk,))
        e.op(SC, lambda s, sb=sb: s.activation(out=sb[:, 14:15], in_=sb[:, 13:14],
                                               func=AF.Ln),
             reads=(sk,), writes=(sk,))
        e.op(SC, lambda s, sb=sb: s.activation(out=sb[:, 15:16], in_=sb[:, 14:15],
                                               func=AF.Exp, scale=-0.5),
             reads=(sk,), writes=(sk,))
        e.op(VE, lambda v, xb=xb, sb=sb, ub=ub: v.tensor_scalar(
            out=ub[:], in0=xb[:], scalar1=sb[:, 12:13], scalar2=sb[:, 15:16],
            op0=ALU.subtract, op1=ALU.mult), reads=(xk, sk), writes=(uk,))
        if stat_out is not None:
            e.op(VE, lambda v, sb=sb, i=i: v.tensor_copy(
                out=stat_out[:, 2 * i:2 * i + 2],
                in_=sb[:].rearrange("p (a b) -> p a b", b=1)[:, (12, 15), :].squeeze(2)
                if False else sb[:, 12:16:3]),
                reads=(sk,), writes=("lnstat_" + ukey,))

        def fn_tr(t, ub=ub):
            last = None
            for dc in range(8):
                last = t.transpose(psT[:, dc * 128:(dc + 1) * 128],
                                   ub[:, dc * 128:(dc + 1) * 128], ident_bf[:])
            return last
        e.op(TE, fn_tr, reads=(uk, "identbf"), writes=("psT",))
        e.op(VE, lambda v, i=i: v.tensor_copy(
            out=u_fm[:, :, i * 128:(i + 1) * 128],
            in_=psT[:].rearrange("p (a b) -> p a b", b=128)),
            reads=("psT",), writes=("ufm_" + ukey, f"ufm_{ukey}_g{i // 4}"))


def emit_proj_fm(e, dst, dk, u_fm, wsel, wkeys, psB, elu_bufs=None,
                 n_tg=4, psb_key="psB", ukey="ufm_u"):
    TE, VE, SC = "tensor", "vector", "scalar"
    for tg in range(n_tg):
        for oc in range(8):
            def fn_mm(t, oc=oc, tg=tg):
                last = None
                for dc in range(8):
                    last = t.matmul(psB[:], wsel(dc, oc),
                                    u_fm[:, dc, tg * 512:(tg + 1) * 512],
                                    start=(dc == 0), stop=(dc == 7))
                return last
            e.op(TE, fn_mm, reads=wkeys + (f"{ukey}_g{tg}",), writes=(psb_key,))
            if elu_bufs is not None:
                elu_sc, elu_sc2 = elu_bufs
                e.op(VE, lambda v: v.tensor_scalar(out=elu_sc[:], in0=psB[:], scalar1=0.0,
                                                   scalar2=None, op0=ALU.min),
                     reads=(psb_key,), writes=("elusc",))
                e.op(SC, lambda s: s.activation(out=elu_sc2[:], in_=elu_sc[:], func=AF.Exp),
                     reads=("elusc",), writes=("elusc2",))
                e.op(VE, lambda v, oc=oc, tg=tg: v.scalar_tensor_tensor(
                    out=dst[:, oc, tg * 512:(tg + 1) * 512], in0=psB[:], scalar=0.0,
                    in1=elu_sc2[:], op0=ALU.max, op1=ALU.add),
                    reads=(psb_key, "elusc2"), writes=(dk,))
            else:
                e.op(VE, lambda v, oc=oc, tg=tg: v.tensor_copy(
                    out=dst[:, oc, tg * 512:(tg + 1) * 512], in_=psB[:]),
                    reads=(psb_key,), writes=(dk,))


def emit_v_tm(e, v_tm, vk, u_fm, udk, wsel, wkeys, psB, psb_key="psB"):
    TE, VE = "tensor", "vector"
    for ti in range(NTILE):
        for ng in range(2):
            def fn_mm(t, ti=ti, ng=ng):
                last = None
                for dc in range(8):
                    last = t.matmul(psB[:], u_fm[:, dc, ti * 128:(ti + 1) * 128],
                                    wsel(dc, ng), start=(dc == 0), stop=(dc == 7))
                return last
            e.op(TE, fn_mm, reads=wkeys + (f"ufm_{udk}_g{ti // 4}",),
                 writes=(psb_key,))
            e.op(VE, lambda v, ti=ti, ng=ng: v.tensor_copy(
                out=v_tm[:, ti].rearrange("p (h x) -> p h x", x=65)[:, ng * 8:(ng + 1) * 8, 0:64],
                in_=psB[:].rearrange("p (h d) -> p h d", d=64)),
                reads=(psb_key,), writes=(vk,))


def emit_p1(p, e, C, io, car):
    x_d, wq_d, wk_d, wv_d = io["x"], io["Wq"], io["Wk"], io["Wv"]
    wa_d, ba_d = io["Wa"], io["ba"]
    chs_d, cha_d, stage1 = io["chunkS"], io["chunkA"], io["stage1"]
    q_fm, k_fm, v_tm, lna_tm = car

    xt = [p.sbuf(f"xt{i}", [128, D], F32) for i in range(2)]
    stat = [p.sbuf(f"stat{i}", [128, 16], F32) for i in range(2)]
    ubf = [p.sbuf(f"ubf{i}", [128, D], BF16) for i in range(2)]
    u_fm = p.sbuf("ufm", [128, 8, NT], BF16)
    lna_fm = p.sbuf("lnafm", [16, NT], F32)
    a_fm = p.sbuf("afm", [16, NT], F32)
    wmat = p.sbuf("wmat", [128, 16, D], BF16)
    wa_sb = p.sbuf("wasb", [128, 8 * H], BF16)
    ba2_sb = p.sbuf("ba2sb", [16, 1], F32)
    elu_sc = p.sbuf("elusc", [128, 512], F32)
    elu_sc2 = p.sbuf("elusc2", [128, 512], F32)
    ident_bf, ident_f, msl = C["identbf"], C["identf"], C["msl"]
    ones_col, ones_row = C["onescolf"], C["onesrow"]
    k_tm = p.sbuf("ktm", [128, D], BF16)
    khat = p.sbuf("khat", [128, D], BF16)
    escale = p.sbuf("escale", [128, H], BF16)
    arow = p.sbuf("arow", [1, H], F32)
    s_run = p.sbuf("srun", [128, 520], F32)
    s_stash = p.sbuf("sstash", [128, 520], F32)

    psB = p.psum("psB", [128, 512], F32)
    psT = p.psum("psT", [128, 1024], BF16)
    psS = p.psum("psS", [128, 1024], F32)
    psR = p.psum("psR", [128, 128], F32)

    SY, TE, VE, SC, GP = ENGINES

    e.op(GP, lambda g: g.memset(
        v_tm[:].rearrange("p h (x y) -> p h x y", y=65)[:, :, :, 64:65], 1.0),
        writes=("vtm_ones",))

    def load_w(dst_half, w_d, tag):
        def fn(g):
            return [g.dma_start(out=wmat[:, dst_half * 8 + dc, :],
                                in_=w_d[dc * 128:(dc + 1) * 128, :]) for dc in range(8)]
        e.op(GP, fn, writes=(tag,), dma=True, n_dma=8)

    load_w(0, wk_d, "wmatA")
    load_w(1, wq_d, "wmatB")

    def fn_wa(g):
        ins = [g.dma_start(out=wa_sb[:, dc * H:(dc + 1) * H],
                           in_=wa_d[dc * 128:(dc + 1) * 128, :]) for dc in range(8)]
        ins.append(g.dma_start(out=ba2_sb[:], in_=ba_d[:].rearrange("(h x) -> h x", x=1)))
        return ins
    e.op(GP, fn_wa, writes=("wasb", "ba2"), dma=True, n_dma=9)
    e.op(VE, lambda v: v.tensor_scalar_add(ba2_sb[:], ba2_sb[:], 2.0),
         reads=("ba2",), writes=("ba2",))

    emit_ln_tiles(e, p, x_d, u_fm,
                  dict(xt=xt, stat=stat, ubf=ubf, psT=psT, ident_bf=ident_bf), ukey="u")

    emit_proj_fm(e, k_fm, "kfm", u_fm,
                 lambda dc, oc: wmat[:, dc, oc * 128:(oc + 1) * 128], ("wmatA",),
                 psB, elu_bufs=(elu_sc, elu_sc2))
    load_w(0, wv_d, "wmatA")
    emit_proj_fm(e, q_fm, "qfm", u_fm,
                 lambda dc, oc: wmat[:, 8 + dc, oc * 128:(oc + 1) * 128], ("wmatB",),
                 psB, elu_bufs=(elu_sc, elu_sc2))

    emit_v_tm(e, v_tm, "vtm", u_fm, "u",
              lambda dc, ng: wmat[:, dc, ng * 512:(ng + 1) * 512], ("wmatA",), psB)

    for tg in range(4):
        def fn_amm(t, tg=tg):
            last = None
            for dc in range(8):
                last = t.matmul(psB[0:16, :], wa_sb[:, dc * H:(dc + 1) * H],
                                u_fm[:, dc, tg * 512:(tg + 1) * 512],
                                start=(dc == 0), stop=(dc == 7))
            return last
        e.op(TE, fn_amm, reads=("wasb", f"ufm_u_g{tg}"), writes=("psB",))
        e.op(SC, lambda s, tg=tg: s.activation(
            out=a_fm[:, tg * 512:(tg + 1) * 512], in_=psB[0:16, :], func=AF.Sigmoid,
            bias=ba2_sb[:], scale=1.0), reads=("psB", "ba2"), writes=("afm",))
    e.op(SC, lambda s: s.activation(out=lna_fm[:], in_=a_fm[:], func=AF.Ln),
         reads=("afm",), writes=("lnafm",))

    for grp in range(4):
        def fn_tr(t, grp=grp):
            last = None
            for cc in range(4):
                c = grp * 4 + cc
                last = t.transpose(psR[:, cc * 16:(cc + 1) * 16],
                                   lna_fm[:, c * 128:(c + 1) * 128], ident_f[0:16, 0:16])
            return last
        e.op(TE, fn_tr, reads=("lnafm", "identf"), writes=("psR",))
        e.op(VE, lambda v, grp=grp: v.tensor_copy(
            out=lna_tm[:, grp * 64:(grp + 1) * 64], in_=psR[:, 0:64]),
            reads=("psR",), writes=("lnatm",))

    e.op(VE, lambda v: v.memset(s_run[:], 0.0), writes=("srun",))
    for c in range(16):
        lna_c = lna_tm[:, c * 16:(c + 1) * 16]

        def fn_rev(t, lna_c=lna_c):
            t.matmul(psR[:, 0:16], msl[:], lna_c, start=True, stop=True)
            return t.matmul(psR[0:1, 16:32], ones_col[:], lna_c, start=True, stop=True)
        e.op(TE, fn_rev, reads=("lnatm", "msl", "onescol"), writes=("psR",))
        e.op(SC, lambda s: s.activation(out=escale[:], in_=psR[:, 0:16], func=AF.Exp),
             reads=("psR",), writes=("escale",))
        e.op(SC, lambda s: s.activation(out=arow[:], in_=psR[0:1, 16:32], func=AF.Exp),
             reads=("psR",), writes=("arow",))

        def fn_ktr(t, c=c):
            last = None
            for dc in range(8):
                last = t.transpose(psT[:, dc * 128:(dc + 1) * 128],
                                   k_fm[:, dc, c * 128:(c + 1) * 128], ident_bf[:])
            return last
        e.op(TE, fn_ktr, reads=("kfm", "identbf"), writes=("psT",))
        e.op(VE, lambda v: v.tensor_copy(out=k_tm[:], in_=psT[:]),
             reads=("psT",), writes=("ktm",))
        e.op(VE, lambda v: v.tensor_tensor(
            out=khat[:].rearrange("p (h d) -> p h d", d=64),
            in0=k_tm[:].rearrange("p (h d) -> p h d", d=64),
            in1=escale[:].unsqueeze(2).broadcast_to([128, 16, 64]), op=ALU.mult),
            reads=("ktm", "escale"), writes=("khat",))

        def fn_sadd(t, c=c):
            last = None
            for h in range(16):
                pb = 64 * (h % 2)
                j = h // 2
                off = (j // 4) * 512 + (j % 4) * 65
                last = t.matmul(psS[pb:pb + 64, off:off + 65],
                                khat[:, h * 64:(h + 1) * 64],
                                v_tm[:, c, h * 65:(h + 1) * 65],
                                start=True, stop=True)
            return last
        e.op(TE, fn_sadd, reads=("khat", "vtm", "vtm_ones"), writes=("psS",))

        def fn_abc(t):
            t.matmul(psR[0:64, 32:40], ones_row[0:1, 0:64], arow[:, 0:8],
                     start=True, stop=True)
            return t.matmul(psR[64:128, 32:40], ones_row[0:1, 0:64], arow[:, 8:16],
                            start=True, stop=True)
        e.op(TE, fn_abc, reads=("arow", "onesrow"), writes=("psR",))

        psS_v = psS[:].rearrange("p (b s) -> p b s", s=512)[:, :, 0:260]\
            .rearrange("p b (j x) -> p b j x", x=65)
        e.op(VE, lambda v, psS_v=psS_v: v.tensor_copy(
            out=s_stash[:].rearrange("p (b j x) -> p b j x", b=2, j=4), in_=psS_v),
             reads=("psS",), writes=("sstash",))
        e.op(VE, lambda v: v.tensor_tensor(
            out=s_run[:].rearrange("p (j x) -> p j x", x=65),
            in0=s_run[:].rearrange("p (j x) -> p j x", x=65),
            in1=psR[0:128, 32:40].unsqueeze(2).broadcast_to([128, 8, 65]),
            op=ALU.mult), reads=("srun", "psR"), writes=("srun",))
        e.op(VE, lambda v, psS_v=psS_v: v.tensor_tensor(
            out=s_run[:].rearrange("p (b j x) -> p b j x", b=2, j=4),
            in0=s_run[:].rearrange("p (b j x) -> p b j x", b=2, j=4),
            in1=psS_v, op=ALU.add),
             reads=("srun", "psS"), writes=("srun",))

        e.op(SY, lambda s, c=c: s.dma_start(out=chs_d[c], in_=s_stash[:]),
             reads=("sstash",), dma=True)
        e.op(SY, lambda s, c=c: s.dma_start(out=cha_d[c:c + 1, :], in_=arow[:]),
             reads=("arow",), dma=True)

    def fn_ac(t):
        last = None
        for c in range(16):
            last = t.matmul(psR[0:1, 48:64], ones_col[:], lna_tm[:, c * 16:(c + 1) * 16],
                            start=(c == 0), stop=(c == 15))
        return last
    e.op(TE, fn_ac, reads=("lnatm", "onescol"), writes=("psR",))
    e.op(SC, lambda s: s.activation(out=arow[:], in_=psR[0:1, 48:64], func=AF.Exp),
         reads=("psR",), writes=("arow",))

    z16 = p.sbuf("z16", [128, 16], F32)
    e.op(GP, lambda g: g.memset(z16[:], 0.0), writes=("z16",))
    e.op(VE, lambda v: v.tensor_copy(out=z16[0:1, :], in_=arow[:]),
         reads=("arow", "z16"), writes=("z16",))
    e.op(SY, lambda s: s.dma_start(out=stage1[:, 0:520], in_=s_run[:]),
         reads=("srun",), writes=("stage1",), dma=True, dkey="stage1")
    e.op(SY, lambda s: s.dma_start(out=stage1[:, 520:536], in_=z16[:]),
         reads=("z16",), writes=("stage1",), dma=True, dkey="stage1")


def emit_p2(p, e, C, io, car, upto=16, lvl=99):
    x_d, wo_d = io["x"], io["AWo"]
    chs_d, cha_d = io["chunkS"], io["chunkA"]
    gath1, pm_d = io["gath1"], io["pm"]
    x1_d, stage2 = io["x1"], io["stage2"]
    q_fm, k_fm, v_tm, lna_tm = car

    sprev = p.sbuf("sprev", [128, 3, 536], F32)
    aprev = p.sbuf("aprev", [1, 3 * H], F32)
    pm_sb = p.sbuf("pmsb", [1, 8], F32)
    pmb = p.sbuf("pmb", [128, 8], F32)
    chs = [p.sbuf(f"chs{i}", [128, 520], F32) for i in range(2)]
    cha = p.sbuf("cha", [1, 16 * H], F32)
    wo_sb = p.sbuf("wosb", [128, 8, D], BF16)
    s_run = p.sbuf("srun", [128, 520], F32)
    s_run_bf = p.sbuf("srunbf", [128, 520], BF16)
    ident_bf, ident_f = C["identbf"], C["identf"]
    mle, msl, mneg = C["mle"], C["msl"], C["mneg"]
    ones_row = C["onesrow"]
    lexp_tm = p.sbuf("lexptm", [128, H], F32)
    msl_sc = p.sbuf("mslsc", [128, 256], F32)
    dpair = p.sbuf("dpair", [128, 256], BF16)
    apair = p.sbuf("apair", [128, 256], BF16)
    apair_v = apair[:].rearrange("p (b s) -> p b s", s=128)
    cr_sb = p.sbuf("crsb", [128, 130], F32)
    n_sb = p.sbuf("nsb", [128, 16, 65], F32)
    dr_sb = p.sbuf("drsb", [128, H], F32)
    obuf = p.sbuf("obuf", [128, D], BF16)
    attn_fm = p.sbuf("attnfm", [128, 8, 128], BF16)
    xt = [p.sbuf(f"xt{i}", [128, D], F32) for i in range(2)]
    x1t = p.sbuf("x1t", [128, D], F32)
    u1t = p.sbuf("u1t", [128, D], F32)
    stat = p.sbuf("stat0", [128, 16], F32)

    psBD = p.psum("psBD", [128, 1024], F32)
    psN = p.psum("psN", [128, 1536], F32)
    psY = p.psum("psY", [128, 512], F32)
    psT = p.psum("psT", [128, 1024], BF16)
    psR = p.psum("psR", [128, 512], F32)

    SY, TE, VE, SC, GP = ENGINES

    def fn_ldprev(s):
        return [s.dma_start(out=sprev[:, k, :], in_=gath1[k * 128:(k + 1) * 128, :])
                for k in range(3)]
    e.op(SY, fn_ldprev, reads=("gath1",), writes=("sprev",), dma=True, n_dma=3)
    e.op(SY, lambda s: s.dma_start(out=pm_sb[:], in_=pm_d[:]),
         writes=("pm",), dma=True, dkey="pmld")
    e.op(SY, lambda s: s.dma_start(out=cha[:], in_=cha_d[:].rearrange("k h -> (k h)").rearrange("(o n) -> o n", o=1)), writes=("cha",), dma=True)
    e.op(GP, lambda g: g.dma_start(out=wo_sb[:],
                                   in_=wo_d[:].rearrange("(a p) b -> p a b", p=128)),
         writes=("wosb",), dma=True)

    def emit_abc(src_row, tag, extra=()):
        def fn(t):
            t.matmul(psR[0:64, 16:24], ones_row[0:1, 0:64], src_row[:, 0:16:2],
                     start=True, stop=True)
            return t.matmul(psR[64:128, 16:24], ones_row[0:1, 0:64], src_row[:, 1:16:2],
                            start=True, stop=True)
        e.op(TE, fn, reads=(tag, "onesrow") + tuple(extra), writes=("psR2k",))

    e.op(TE, lambda t: t.matmul(psR[:, 32:40], ones_row[0:1, :], pm_sb[0:1, :],
                                start=True, stop=True),
         reads=("pm", "onesrow"), writes=("psR2k",))
    e.op(VE, lambda v: v.tensor_copy(out=pmb[:], in_=psR[:, 32:40]),
         reads=("psR2k",), writes=("pmb",))
    def fn_aprime(v):
        return [v.tensor_scalar(out=aprev[0:1, k * 16:(k + 1) * 16],
                                in0=sprev[0:1, k, 520:536],
                                scalar1=pm_sb[0:1, k:k + 1],
                                scalar2=pm_sb[0:1, 4 + k:5 + k],
                                op0=ALU.mult, op1=ALU.add) for k in range(3)]
    e.op(VE, fn_aprime, reads=("sprev", "pm"), writes=("aprev",))
    e.op(VE, lambda v: v.tensor_scalar(out=s_run[:], in0=sprev[:, 0, 0:520],
                                       scalar1=pmb[:, 0:1], scalar2=None, op0=ALU.mult),
         reads=("sprev", "pmb"), writes=("srun",))
    for k in range(1, 3):
        emit_abc(aprev[0:1, k * 16:(k + 1) * 16], "aprev")
        e.op(VE, lambda v: v.tensor_tensor(
            out=s_run[:].rearrange("p (j x) -> p j x", x=65),
            in0=s_run[:].rearrange("p (j x) -> p j x", x=65),
            in1=psR[0:128, 16:24].unsqueeze(2).broadcast_to([128, 8, 65]),
            op=ALU.mult), reads=("srun", "psR2k"), writes=("srun",))
        e.op(VE, lambda v, k=k: v.scalar_tensor_tensor(
            out=s_run[:], in0=sprev[:, k, 0:520], scalar=pmb[:, k:k + 1],
            in1=s_run[:], op0=ALU.mult, op1=ALU.add),
             reads=("srun", "sprev", "pmb"), writes=("srun",))

    for c in range(upto):
        lna_c = lna_tm[:, c * 16:(c + 1) * 16]
        if lvl >= 0.2:
            e.op(SY, lambda s, c=c: s.dma_start(out=chs[c % 2][:], in_=chs_d[c]),
                 writes=(f"chs{c % 2}",), dma=True)

        if lvl >= 0.4:
            e.op(VE, lambda v: v.tensor_copy(out=s_run_bf[:], in_=s_run[:]),
                 reads=("srun",), writes=("srunbf",))

        if lvl >= 0.6:
            def fn_l(t, lna_c=lna_c):
                return t.matmul(psR[:, 0:16], mle[:], lna_c, start=True, stop=True)
            e.op(TE, fn_l, reads=("lnatm", "mle"), writes=("psR2k",))
        if 0.7 <= lvl < 0.72:
            e.op(VE, lambda v: v.tensor_copy(out=lexp_tm[:], in_=psR[:, 0:16]),
                 reads=("psR2k",), writes=("lexptm",))
        if 0.72 <= lvl < 0.74:
            e.op(SC, lambda s: s.copy(out=lexp_tm[:], in_=psR[:, 0:16]),
                 reads=("psR2k",), writes=("lexptm",))
        if 0.74 <= lvl < 0.76:
            e.op(SC, lambda s: s.activation(out=lexp_tm[:], in_=s_run[:, 0:16],
                                            func=AF.Exp),
                 reads=("srun",), writes=("lexptm",))
        if lvl >= 0.8:
            e.op(SC, lambda s: s.activation(out=lexp_tm[:], in_=psR[:, 0:16], func=AF.Exp),
                 reads=("psR2k",), writes=("lexptm",))

        for hp in range(8 if lvl >= 2 else 0):
            def fn_msc(v, hp=hp, lna_c=lna_c):
                return [v.tensor_scalar(out=msl_sc[:, j * 128:(j + 1) * 128], in0=msl[:],
                                        scalar1=lna_c[:, 2 * hp + j:2 * hp + j + 1],
                                        scalar2=None, op0=ALU.mult) for j in range(2)]
            e.op(VE, fn_msc, reads=("msl", "lnatm"), writes=("mslsc",))

            def fn_darg(t, hp=hp):
                last = []
                for j in range(2):
                    t.matmul(psBD[:, 256 + j * 512:384 + j * 512],
                             msl_sc[:, j * 128:(j + 1) * 128], mle[:],
                             start=True, stop=False)
                    last.append(t.matmul(psBD[:, 256 + j * 512:384 + j * 512],
                                         ident_f[:], mneg[:], start=False, stop=True))
                return last[-1]
            e.op(TE, fn_darg, reads=("mslsc", "mle", "mneg", "identf"), writes=("psBD",))

            def fn_dexp(s, hp=hp):
                return [s.activation(out=dpair[:, j * 128:(j + 1) * 128],
                                     in_=psBD[:, 256 + j * 512:384 + j * 512], func=AF.Exp)
                        for j in range(2)]
            e.op(SC, fn_dexp, reads=("psBD",), writes=("dpair",))

            def fn_qk(t, c=c, hp=hp):
                last = []
                for s in range(2):
                    pb = 64 * s
                    last.append(t.matmul(
                        psBD[:, s * 512:s * 512 + 128],
                        k_fm[pb:pb + 64, hp, c * 128:(c + 1) * 128],
                        q_fm[pb:pb + 64, hp, c * 128:(c + 1) * 128],
                        start=True, stop=True))
                return last
            e.op(TE, fn_qk, reads=("kfm", "qfm"), writes=("psBD",))
            psBD_qk = psBD[:].rearrange("p (b s) -> p b s", s=512)[:, :, 0:128]
            dpair_v = dpair[:].rearrange("p (b s) -> p b s", s=128)
            e.op(VE, lambda v: v.tensor_tensor(out=apair_v, in0=psBD_qk,
                                               in1=dpair_v, op=ALU.mult),
                 reads=("psBD", "dpair"), writes=("apair",))

            if lvl < 3:
                continue
            def fn_n(t, c=c, hp=hp):
                last = []
                for s in range(2):
                    h = 2 * hp + s
                    t.matmul(psN[:, s * 65:(s + 1) * 65], apair[:, s * 128:(s + 1) * 128],
                             v_tm[:, c, h * 65:(h + 1) * 65], start=True, stop=True)
                    pb = 64 * s
                    last.append(t.matmul(
                        psN[:, 512 * (s + 1):512 * (s + 1) + 65],
                        q_fm[pb:pb + 64, hp, c * 128:(c + 1) * 128],
                        s_run_bf[pb:pb + 64, hp * 65:(hp + 1) * 65],
                        start=True, stop=True))
                return last
            e.op(TE, fn_n, reads=("apair", "vtm", "qfm", "srunbf"), writes=("psN",))

            def fn_cr(s):
                return [s.copy(out=cr_sb[:, j * 65:(j + 1) * 65],
                               in_=psN[:, 512 * (j + 1):512 * (j + 1) + 65])
                        for j in range(2)]
            e.op(SC, fn_cr, reads=("psN",), writes=("crsb",))

            def fn_comb(v, hp=hp):
                return [v.scalar_tensor_tensor(
                    out=n_sb[:, 2 * hp + s, :], in0=cr_sb[:, s * 65:(s + 1) * 65],
                    scalar=lexp_tm[:, 2 * hp + s:2 * hp + s + 1],
                    in1=psN[:, s * 65:(s + 1) * 65], op0=ALU.mult, op1=ALU.add)
                    for s in range(2)]
            e.op(VE, fn_comb, reads=("crsb", "lexptm", "psN"), writes=("nsb",))

        if lvl < 4:
            continue
        e.op(VE, lambda v: v.tensor_scalar_add(dr_sb[:], n_sb[:, :, 64], float(GLA_EPS)),
             reads=("nsb",), writes=("drsb",))
        e.op(VE, lambda v: v.reciprocal(out=dr_sb[:], in_=dr_sb[:]),
             reads=("drsb",), writes=("drsb",))
        e.op(VE, lambda v: v.tensor_tensor(
            out=obuf[:].rearrange("p (h d) -> p h d", d=64),
            in0=n_sb[:, :, 0:64],
            in1=dr_sb[:].unsqueeze(2).broadcast_to([128, 16, 64]), op=ALU.mult),
            reads=("nsb", "drsb"), writes=("obuf",))

        emit_abc(cha[0:1, c * 16:(c + 1) * 16], "cha")
        e.op(VE, lambda v: v.tensor_tensor(
            out=s_run[:].rearrange("p (j x) -> p j x", x=65),
            in0=s_run[:].rearrange("p (j x) -> p j x", x=65),
            in1=psR[0:128, 16:24].unsqueeze(2).broadcast_to([128, 8, 65]),
            op=ALU.mult), reads=("srun", "psR2k"), writes=("srun",))
        e.op(VE, lambda v, c=c: v.tensor_tensor(out=s_run[:], in0=s_run[:],
                                                in1=chs[c % 2][:], op=ALU.add),
             reads=("srun", f"chs{c % 2}"), writes=("srun",))

        def fn_tr(t):
            last = None
            for dc in range(8):
                last = t.transpose(psT[:, dc * 128:(dc + 1) * 128],
                                   obuf[:, dc * 128:(dc + 1) * 128], ident_bf[:])
            return last
        e.op(TE, fn_tr, reads=("obuf", "identbf"), writes=("psT",))
        e.op(VE, lambda v: v.tensor_copy(
            out=attn_fm[:], in_=psT[:].rearrange("p (a b) -> p a b", b=128)),
            reads=("psT",), writes=("attnfm",))

        e.op(SY, lambda s, c=c: s.dma_start(out=xt[c % 2][:],
                                            in_=x_d[c * 128:(c + 1) * 128, :]),
             writes=(f"xt{c % 2}",), dma=True)

        for ng in range(2):
            def fn_wo(t, ng=ng):
                last = None
                for pc in range(8):
                    last = t.matmul(psY[:], attn_fm[:, pc, :],
                                    wo_sb[:, pc, ng * 512:(ng + 1) * 512],
                                    start=(pc == 0), stop=(pc == 7))
                return last
            e.op(TE, fn_wo, reads=("attnfm", "wosb"), writes=("psY",))
            e.op(VE, lambda v, c=c, ng=ng: v.tensor_tensor(
                out=x1t[:, ng * 512:(ng + 1) * 512],
                in0=xt[c % 2][:, ng * 512:(ng + 1) * 512],
                in1=psY[:], op=ALU.add),
                reads=(f"xt{c % 2}", "psY"), writes=("x1t",))
        e.op(SY, lambda s, c=c: s.dma_start(out=x1_d[c * 128:(c + 1) * 128, :],
                                            in_=x1t[:]),
             reads=("x1t",), dma=True, dkey="x1out")

        if c == 15:
            e.op(VE, lambda v: v.bn_stats(out=stat[:, 0:6], in_=x1t[:, 0:512]),
                 reads=("x1t",), writes=("stat0",))
            e.op(VE, lambda v: v.bn_stats(out=stat[:, 6:12], in_=x1t[:, 512:1024]),
                 reads=("x1t",), writes=("stat0",))
            e.op(VE, lambda v: v.bn_aggr(out=stat[:, 12:14], in_=stat[:, 0:12]),
                 reads=("stat0",), writes=("stat0",))
            e.op(VE, lambda v: v.tensor_scalar_add(stat[:, 13:14], stat[:, 13:14],
                                                   float(LN_EPS)),
                 reads=("stat0",), writes=("stat0",))
            e.op(SC, lambda s: s.activation(out=stat[:, 14:15], in_=stat[:, 13:14],
                                            func=AF.Ln), reads=("stat0",), writes=("stat0",))
            e.op(SC, lambda s: s.activation(out=stat[:, 15:16], in_=stat[:, 14:15],
                                            func=AF.Exp, scale=-0.5),
                 reads=("stat0",), writes=("stat0",))
            e.op(VE, lambda v: v.tensor_scalar(out=u1t[:], in0=x1t[:],
                                               scalar1=stat[:, 12:13],
                                               scalar2=stat[:, 15:16],
                                               op0=ALU.subtract, op1=ALU.mult),
                 reads=("x1t", "stat0"), writes=("u1t",))
            e.op(SY, lambda s: s.dma_start(out=stage2[:], in_=u1t[127:128, :]),
                 reads=("u1t",), writes=("stage2",), dma=True, dkey="u1out")


def emit_p3(p, e, C, io, ccfn):
    x1_d, gath2 = io["x1"], io["gath2"]
    wp1_d, bp1_d, wp2_d, bp2_d = io["Wp1"], io["bp1"], io["Wp2"], io["bp2"]
    w1_d, b1_d, w2_d, b2_d = io["W1"], io["b1"], io["W2"], io["b2"]
    wo_d, bo_d = io["FWo"], io["bo"]
    x2_d = io["x2"]
    stage3a, stage3b = io["stage3a"], io["stage3b"]

    xt = [p.sbuf(f"xt{i}", [128, D], F32) for i in range(2)]
    stat = [p.sbuf(f"stat{i}", [128, 16], F32) for i in range(2)]
    ubf = [p.sbuf(f"ubf{i}", [128, D], BF16) for i in range(2)]
    u_fm = p.sbuf("ufm", [128, 8, NT], BF16)
    lnstat = p.sbuf("lnstat", [128, 2 * NTILE], F32)
    lrow = p.sbuf("lrow", [1, D], F32)
    wp1_sb = p.sbuf("wp1sb", [128, 8, 128], BF16)
    wp2_sb = p.sbuf("wp2sb", [128, 8], BF16)
    bp1_sb = p.sbuf("bp1sb", [1, 128], F32)
    bp2_sb = p.sbuf("bp2sb", [1, 8], F32)
    glog = p.sbuf("glog", [1, 8], F32)
    gmax = p.sbuf("gmax", [1, 8], F32)
    gsel = p.sbuf("gsel", [1, 8], U32)
    w1_sb = p.sbuf("w1sb", [128, 8, 2048], BF16)
    w2big = p.sbuf("w2big", [128, 16, 1024], BF16)
    w2_sb = w2big[:]
    b1r = p.sbuf("b1r", [1, 2048], F32)
    b2r = p.sbuf("b2r", [1, D], BF16)
    bor = p.sbuf("bor", [1, D], BF16)
    b1c = p.sbuf("b1c", [128, 16], F32)
    wo_sb = p.sbuf("wosb", [128, 8, D], BF16)
    ident_bf, ident_f = C["identbf"], C["identf"]
    ones_row, ones_row_bf = C["onesrow"], C["onesrowbf"]
    h_sb = p.sbuf("hsb", [128, 512], BF16)
    r_sb = p.sbuf("rsb", [128, D], BF16)
    r_fm = p.sbuf("rfm", [128, 8, 128], BF16)
    x2t = p.sbuf("x2t", [128, D], F32)
    u1t = p.sbuf("u1t", [128, D], F32)
    u2_fm = u_fm

    psB = p.psum("psB", [128, 512], F32)
    psT = p.psum("psT", [128, 1024], BF16)
    psH = p.psum("psH", [128, 1024], F32)
    psR2 = p.psum("psR2", [128, 2048], F32)

    SY, TE, VE, SC, GP = ENGINES

    e.op(SY, lambda s: s.dma_start(out=lrow[:], in_=gath2[3:4, :]),
         reads=("gath2",), writes=("lrow",), dma=True)
    e.op(GP, lambda g: g.dma_start(out=wp1_sb[:],
                                   in_=wp1_d[:].rearrange("(a p) f -> p a f", p=128)),
         writes=("wp1",), dma=True)
    e.op(GP, lambda g: g.dma_start(out=wp2_sb[:], in_=wp2_d[:]), writes=("wp2",), dma=True)
    e.op(SY, lambda s: s.dma_start(out=bp1_sb[:],
                                   in_=bp1_d[:].rearrange("(o n) -> o n", o=1)),
         writes=("bp1",), dma=True)
    e.op(SY, lambda s: s.dma_start(out=bp2_sb[:],
                                   in_=bp2_d[:].rearrange("(o n) -> o n", o=1)),
         writes=("bp2",), dma=True)

    def fn_lvt(t):
        last = None
        for dc in range(8):
            last = t.transpose(psB[:, dc:dc + 1], lrow[0:1, dc * 128:(dc + 1) * 128],
                               ident_f[0:1, 0:1])
        return last
    e.op(TE, fn_lvt, reads=("lrow", "identf"), writes=("psB",))
    lv_fm = p.sbuf("lvfm", [128, 8], BF16)
    e.op(VE, lambda v: v.tensor_copy(out=lv_fm[:], in_=psB[:, 0:8]),
         reads=("psB",), writes=("lvfm",))
    def fn_bp1(t):
        return t.transpose(psB[:, 8:9], bp1_sb[:], ident_f[0:1, 0:1])
    e.op(TE, fn_bp1, reads=("bp1", "identf"), writes=("psB",))
    bp1c = p.sbuf("bp1c", [128, 1], F32)
    e.op(VE, lambda v: v.tensor_copy(out=bp1c[:], in_=psB[:, 8:9]),
         reads=("psB",), writes=("bp1c",))

    def fn_z1(t):
        last = None
        for dc in range(8):
            last = t.matmul(psH[:, 0:1], wp1_sb[:, dc, :], lv_fm[:, dc:dc + 1],
                            start=(dc == 0), stop=(dc == 7))
        return last
    e.op(TE, fn_z1, reads=("wp1", "lvfm"), writes=("psH0",))
    z1g = p.sbuf("z1g", [128, 1], BF16)
    e.op(SC, lambda s: s.activation(out=z1g[:], in_=psH[:, 0:1], func=AF.Gelu,
                                    bias=bp1c[:], scale=1.0),
         reads=("psH0", "bp1c"), writes=("z1g",))
    def fn_z2(t):
        return t.matmul(psH[:, 4:12][0:1, :], z1g[:], wp2_sb[:], start=True, stop=True)
    e.op(TE, fn_z2, reads=("z1g", "wp2"), writes=("psH0",))
    e.op(VE, lambda v: v.tensor_add(out=glog[:], in0=psH[0:1, 4:12], in1=bp2_sb[:]),
         reads=("psH0", "bp2"), writes=("glog",))
    e.op(VE, lambda v: v.max(out=gmax[:], in_=glog[:]), reads=("glog",), writes=("gmax",))
    e.op(VE, lambda v: v.max_index(out=gsel[:], in_max=gmax[:], in_values=glog[:]),
         reads=("gmax", "glog"), writes=("gsel",))

    def fn_sel_dma(g):
        reg = g.alloc_register("selreg")
        g.load(reg, gsel[0:1, 0:1])
        dsem = e.dma_sem(GP, "expertw")
        for ex in range(8):
            with g.If_eq(reg, ex):
                g.dma_start(out=w1_sb[:],
                            in_=w1_d[ex].rearrange("(a p) f -> p a f", p=128)
                            ).then_inc(dsem.h, 16)
                g.dma_start(out=w2_sb,
                            in_=w2_d[ex].rearrange("(a p) f -> p a f", p=128)
                            ).then_inc(dsem.h, 16)
                g.dma_start(out=b1r[:],
                            in_=b1_d[ex].rearrange("(o n) -> o n", o=1)
                            ).then_inc(dsem.h, 16)
                g.dma_start(out=b2r[:],
                            in_=b2_d[ex].rearrange("(o n) -> o n", o=1)
                            ).then_inc(dsem.h, 16)
        return None
    e.op(GP, fn_sel_dma, reads=("gsel",), writes=("w1", "w2", "b1r", "b2r"),
         self_inc=64, dkey="expertw")

    for grp in range(2):
        def fn_b1(t, grp=grp):
            last = None
            for j in range(8):
                fc = grp * 8 + j
                last = t.transpose(psB[:, 16 + j:17 + j],
                                   b1r[0:1, fc * 128:(fc + 1) * 128], ident_f[0:1, 0:1])
            return last
        e.op(TE, fn_b1, reads=("b1r", "identf"), writes=("psB",))
        e.op(VE, lambda v, grp=grp: v.tensor_copy(out=b1c[:, grp * 8:(grp + 1) * 8],
                                                  in_=psB[:, 16:24]),
             reads=("psB",), writes=("b1c",))

    e.op(GP, lambda g: g.dma_start(out=wo_sb[:],
                                   in_=wo_d[:].rearrange("(a p) b -> p a b", p=128)),
         writes=("wosb",), dma=True)
    e.op(GP, lambda g: g.dma_start(out=bor[:], in_=bo_d[:].rearrange("(o n) -> o n", o=1)),
         writes=("bor",), dma=True)

    emit_ln_tiles(e, p, x1_d, u_fm,
                  dict(xt=xt, stat=stat, ubf=ubf, psT=psT, ident_bf=ident_bf),
                  stat_out=lnstat, ukey="u")

    for tp in range(8):
        for fc in range(16):
            par = fc % 2
            hps = psH[:, par * 512:par * 512 + 256]
            hsv = h_sb[:, par * 256:(par + 1) * 256]

            def fn_h(t, tp=tp, fc=fc, hps=hps):
                last = None
                for dc in range(8):
                    last = t.matmul(hps, w1_sb[:, dc, fc * 128:(fc + 1) * 128],
                                    u_fm[:, dc, tp * 256:(tp + 1) * 256],
                                    start=(dc == 0), stop=(dc == 7))
                return last
            e.op(TE, fn_h, reads=("w1", f"ufm_u_g{tp // 2}"), writes=(f"psH{par}",))
            e.op(SC, lambda s, fc=fc, hps=hps, hsv=hsv: s.activation(
                out=hsv, in_=hps, func=AF.Gelu,
                bias=b1c[:, fc:fc + 1], scale=1.0),
                 reads=(f"psH{par}", "b1c"), writes=(f"hsb{par}",))

            def fn_r(t, tp=tp, fc=fc, hsv=hsv):
                last = []
                for ti in range(2):
                    for ng in range(2):
                        last.append(t.matmul(
                            psR2[:, ti * 1024 + ng * 512:ti * 1024 + (ng + 1) * 512],
                            hsv[:, ti * 128:(ti + 1) * 128],
                            w2_sb[:, fc, ng * 512:(ng + 1) * 512],
                            start=(fc == 0), stop=False))
                return last[-1]
            e.op(TE, fn_r, reads=(f"hsb{par}", "w2"), writes=("psR2",))
        def fn_rb(t, tp=tp):
            last = []
            for ti in range(2):
                for ng in range(2):
                    last.append(t.matmul(
                        psR2[:, ti * 1024 + ng * 512:ti * 1024 + (ng + 1) * 512],
                        ones_row_bf[:], b2r[0:1, ng * 512:(ng + 1) * 512],
                        start=False, stop=True))
            return last[-1]
        e.op(TE, fn_rb, reads=("onesrowbf", "b2r"), writes=("psR2",))

        for ti in range(2):
            e.op(VE, lambda v, ti=ti: v.tensor_copy(out=r_sb[:],
                                                    in_=psR2[:, ti * 1024:(ti + 1) * 1024]),
                 reads=("psR2",), writes=("rsb",))

            def fn_tr(t):
                last = None
                for dc in range(8):
                    last = t.transpose(psT[:, dc * 128:(dc + 1) * 128],
                                       r_sb[:, dc * 128:(dc + 1) * 128], ident_bf[:])
                return last
            e.op(TE, fn_tr, reads=("rsb", "identbf"), writes=("psT",))
            e.op(VE, lambda v: v.tensor_copy(
                out=r_fm[:], in_=psT[:].rearrange("p (a b) -> p a b", b=128)),
                reads=("psT",), writes=("rfm",))

            i = tp * 2 + ti
            e.op(SY, lambda s, i=i: s.dma_start(out=xt[i % 2][:],
                                                in_=x1_d[i * 128:(i + 1) * 128, :]),
                 writes=(f"xt{i % 2}",), dma=True)
            for ng in range(2):
                def fn_o(t, ng=ng):
                    last = None
                    for pc in range(8):
                        last = t.matmul(psH[:, 0:512], r_fm[:, pc, :],
                                        wo_sb[:, pc, ng * 512:(ng + 1) * 512],
                                        start=(pc == 0), stop=False)
                    return t.matmul(psH[:, 0:512], ones_row_bf[:],
                                    bor[0:1, ng * 512:(ng + 1) * 512],
                                    start=False, stop=True)
                e.op(TE, fn_o, reads=("rfm", "wosb", "onesrowbf", "bor"), writes=("psH0",))
                e.op(VE, lambda v, i=i, ng=ng: v.tensor_scalar(
                    out=u1t[:, ng * 512:(ng + 1) * 512],
                    in0=xt[i % 2][:, ng * 512:(ng + 1) * 512],
                    scalar1=lnstat[:, 2 * i:2 * i + 1], scalar2=lnstat[:, 2 * i + 1:2 * i + 2],
                    op0=ALU.subtract, op1=ALU.mult),
                    reads=(f"xt{i % 2}", "lnstat_u"), writes=("u1t",))
                e.op(VE, lambda v, i=i, ng=ng: v.tensor_add(
                    out=u1t[:, ng * 512:(ng + 1) * 512],
                    in0=u1t[:, ng * 512:(ng + 1) * 512],
                    in1=xt[i % 2][:, ng * 512:(ng + 1) * 512]),
                    reads=(f"xt{i % 2}", "u1t"), writes=("u1t",))
                e.op(VE, lambda v, ng=ng: v.scalar_tensor_tensor(
                    out=x2t[:, ng * 512:(ng + 1) * 512], in0=psH[:, 0:512], scalar=0.3,
                    in1=u1t[:, ng * 512:(ng + 1) * 512], op0=ALU.mult, op1=ALU.add),
                    reads=("psH0", "u1t"), writes=("x2t",))
            e.op(SY, lambda s, i=i: s.dma_start(out=x2_d[i * 128:(i + 1) * 128, :],
                                                in_=x2t[:]),
                 reads=("x2t",), writes=(f"x2o{i}",), dma=True, dkey="x2out")

    e.op(VE, lambda v: v.engine_nop(), reads=(),
         writes=("ufm_u",) + tuple(f"ufm_u_g{i}" for i in range(4)))
    emit_ln_tiles(e, p, x2_d, u2_fm,
                  dict(xt=xt, stat=stat, ubf=ubf, psT=psT, ident_bf=ident_bf), ukey="u2",
                  src_key=[f"x2o{i}" for i in range(16)])
    e.op(SY, lambda s: s.dma_start(
        out=stage3a[:].rearrange("p (a t) -> p a t", t=1024),
        in_=u2_fm[:, :, 0:1024]),
         reads=("ufm_u2_g0", "ufm_u2_g1"), writes=("stage3a",), dma=True,
         dkey="u2outa")
    ccfn("cc3a", "stage3a", "gath3a", io["gath3a"] if False else stage3a, io["gath3a"])
    e.op(SY, lambda s: s.dma_start(
        out=stage3b[:].rearrange("p (a t) -> p a t", t=1024),
        in_=u2_fm[:, :, 1024:2048]),
         reads=("ufm_u2_g2", "ufm_u2_g3"), writes=("stage3b",), dma=True,
         dkey="u2outb")
    ccfn("cc3b", "stage3b", "gath3b", stage3b, io["gath3b"])


def emit_p4(p, e, C, io, ccfn):
    gath3a, gath3b = io["gath3a"], io["gath3b"]
    wq_d, wk_d, wv_d = io["WqT"], io["WkT"], io["WvT"]
    stage4 = [io["stage4a"], io["stage4b"]]

    wq_sb = p.sbuf("wqsb", [128, 8, 256], BF16)
    wk_sb = p.sbuf("wksb", [128, 8, 256], BF16)
    wv_sb = p.sbuf("wvsb", [128, 8, 256], BF16)
    u2r = [p.sbuf(f"u2r{i}", [128, 8, NT], BF16) for i in range(2)]
    q_fm = p.sbuf("qfm", [128, 2, 4 * NT], BF16)
    k_fm = p.sbuf("kfm", [128, 2, 4 * NT], BF16)
    v_tm = p.sbuf("vtm", [128, 64, 260], BF16)
    pt_sb = p.sbuf("ptsb", [128, 2048], BF16)
    sq_sc = p.sbuf("sqsc", [128, 2048], BF16)
    bmax = p.sbuf("bmax", [1, 64], F32)
    negb2 = p.sbuf("negb2", [128, 2], F32)
    linvb = p.sbuf("linvb", [64, 512], F32)
    ot = [p.sbuf(f"ot{i}", [128, 512], BF16) for i in range(2)]
    nac = [[p.sbuf(f"nac{i}{s}", [65, 512], F32) for s in range(2)]
           for i in range(2)]
    linv2 = [[p.sbuf(f"linv{i}{s}", [1, 512], F32) for s in range(2)]
             for i in range(2)]
    ones_row, ones_col = C["onesrow"], C["onescolbf"]

    psS = p.psum("psS", [128, 2048], F32)
    psA = [p.psum("psA0", [128, 512], F32), p.psum("psA1", [128, 512], F32)]
    psB = p.psum("psB", [128, 512], F32)

    SY, TE, VE, SC, GP = ENGINES

    e.op(GP, lambda g: g.memset(
        v_tm[:].rearrange("p t (h x) -> p t h x", x=65)[:, :, :, 64:65], 1.0),
        writes=("vtm_ones",))
    e.op(GP, lambda g: g.memset(bmax[:], 0.0),
         writes=("bmax_q", "bmax_k", "bmax_t", "bmax_s"))

    for w_d, w_sb, tag in ((wq_d, wq_sb, "wq"), (wk_d, wk_sb, "wk"),
                           (wv_d, wv_sb, "wv")):
        e.op(GP, lambda g, w_d=w_d, w_sb=w_sb: g.dma_start(
            out=w_sb[:], in_=w_d.rearrange("(a p) f -> p a f", p=128)),
            writes=(tag,), dma=True, dkey=tag)

    for r in range(4):
        ub, ukk = u2r[r % 2], f"u2r{r % 2}"
        def fn_ldu(s, r=r, ub=ub):
            return [
                s.dma_start(out=ub[:, :, 0:1024],
                            in_=gath3a[r * 128:(r + 1) * 128, :]
                            .rearrange("p (a t) -> p a t", a=8)),
                s.dma_start(out=ub[:, :, 1024:2048],
                            in_=gath3b[r * 128:(r + 1) * 128, :]
                            .rearrange("p (a t) -> p a t", a=8))]
        e.op(SY, fn_ldu, reads=("gath3a", "gath3b"), writes=(ukk,), dma=True,
             n_dma=2, dkey=ukk)
        for dst, dk, tag in ((q_fm, "qfm", "wq"), (k_fm, "kfm", "wk")):
            w_sb = {"wq": wq_sb, "wk": wk_sb}[tag]
            for oc in range(2):
                for tg in range(4):
                    def fn_mm(t, oc=oc, tg=tg, ub=ub, w_sb=w_sb):
                        last = None
                        for dc in range(8):
                            last = t.matmul(psB[:], w_sb[:, dc, oc * 128:(oc + 1) * 128],
                                            ub[:, dc, tg * 512:(tg + 1) * 512],
                                            start=(dc == 0), stop=(dc == 7))
                        return last
                    e.op(TE, fn_mm, reads=(tag, ukk), writes=("psB",))
                    e.op(VE, lambda v, oc=oc, tg=tg, r=r, dst=dst, dk=dk: v.tensor_copy(
                        out=dst[:, oc, r * NT + tg * 512:r * NT + (tg + 1) * 512],
                        in_=psB[:]),
                        reads=("psB",), writes=(dk,))
        for ti in range(16):
            def fn_vm(t, ti=ti, ub=ub):
                last = None
                for dc in range(8):
                    last = t.matmul(psB[:, 0:256], ub[:, dc, ti * 128:(ti + 1) * 128],
                                    wv_sb[:, dc, :], start=(dc == 0), stop=(dc == 7))
                return last
            e.op(TE, fn_vm, reads=("wv", ukk), writes=("psB",))
            e.op(VE, lambda v, ti=ti, r=r: v.tensor_copy(
                out=v_tm[:, r * 16 + ti].rearrange("p (h x) -> p h x", x=65)[:, :, 0:64],
                in_=psB[:, 0:256].rearrange("p (h d) -> p h d", d=64)),
                reads=("psB",), writes=("vtm",))

    def emit_norms(src_ap, src_keys, out_off, tag):
        e.op(GP, lambda g: g.tensor_mul(sq_sc[:], src_ap, src_ap),
             reads=src_keys, writes=("sqsc",))

        def fn_nq(t):
            last = []
            for s in range(2):
                for g in range(4):
                    last.append(t.matmul(
                        psS[0:1, s * 1024 + (g % 2) * 512:s * 1024 + (g % 2) * 512 + 512],
                        ones_col[64 * s:64 * s + 64, :],
                        sq_sc[64 * s:64 * s + 64, g * 512:(g + 1) * 512],
                        start=(g < 2), stop=(g >= 2)))
            return last
        e.op(TE, fn_nq, reads=("sqsc", "onescol"), writes=("psS0", "psS1"))

        def fn_red(v):
            v.reduce_max(bmax[:, 30:31], psS[0:1, 0:1024], axis=AX)
            return v.reduce_max(bmax[:, 31:32], psS[0:1, 1024:2048], axis=AX)
        e.op(VE, fn_red, reads=("psS0", "psS1"), writes=("bmax_t",))
        e.op(VE, lambda v: v.tensor_max(bmax[:, out_off:out_off + 2],
                                        bmax[:, out_off:out_off + 2], bmax[:, 30:32]),
             reads=("bmax_t", tag), writes=(tag,))

    for hp in range(2):
        for r in range(4):
            emit_norms(q_fm[:, hp, r * NT:(r + 1) * NT], ("qfm",), 2 * hp, "bmax_q")
            emit_norms(k_fm[:, hp, r * NT:(r + 1) * NT], ("kfm",), 8 + 2 * hp, "bmax_k")

    for hp in range(2):
        e.op(VE, lambda v, hp=hp: v.tensor_mul(bmax[:, 24:26], bmax[:, 2 * hp:2 * hp + 2],
                                               bmax[:, 8 + 2 * hp:8 + 2 * hp + 2]),
             reads=("bmax_q", "bmax_k"), writes=("bmax_s",))
        e.op(VE, lambda v: v.tensor_max(bmax[:, 26:27], bmax[:, 24:25], bmax[:, 25:26]),
             reads=("bmax_s",), writes=("bmax_s",))
        e.op(SC, lambda s: s.activation(out=bmax[:, 27:28], in_=bmax[:, 26:27],
                                        func=AF.Sqrt), reads=("bmax_s",), writes=("bmax_s",))
        e.op(VE, lambda v: v.tensor_scalar(out=bmax[:, 28:29], in0=bmax[:, 27:28],
                                           scalar1=-0.1275, scalar2=None, op0=ALU.mult),
             reads=("bmax_s",), writes=("bmax_s",))
        e.op(TE, lambda t: t.matmul(psB[:, 0:1], ones_row[:], bmax[0:1, 28:29],
                                    start=True, stop=True),
             reads=("bmax_s", "onesrow"), writes=("psB",))
        e.op(SC, lambda s, hp=hp: s.copy(out=negb2[:, hp:hp + 1], in_=psB[:, 0:1]),
             reads=("psB",), writes=("negb",))

    dmneg = p.sbuf("dmneg", [128, 4, 512], BF16)
    identp4 = C["identbf"]

    for k in range(4):
        e.op(GP, lambda g, k=k: g.memset(dmneg[:, k, :], 0.0), writes=("dmneg",))
        e.op(GP, lambda g, k=k: g.affine_select(
            out=dmneg[:, k, :], in_=dmneg[:, k, :], compare_op=ALU.is_ge,
            fill=-1e30, base=-128 * k, pattern=[[1, 512]],
            channel_multiplier=-1), reads=("dmneg",), writes=("dmneg",))

    pending_div = []
    for hp in range(2):
        for qt in range(16):
            ntt = 4 * (qt + 1)

            def emit_s(tt):
                h = tt % 2
                diag = tt >= 4 * qt

                def fn_s2(t, hp=hp, qt=qt, tt=tt, h=h, diag=diag):
                    last = []
                    for s in range(2):
                        seg = psS[:, h * 1024 + s * 512:h * 1024 + (s + 1) * 512]
                        mm = t.matmul(seg,
                                      k_fm[64 * s:64 * s + 64, hp, tt * 128:(tt + 1) * 128],
                                      q_fm[64 * s:64 * s + 64, hp, qt * 512:(qt + 1) * 512],
                                      start=True, stop=not diag)
                        if diag:
                            k = tt - 4 * qt
                            mm = t.matmul(seg, identp4[:], dmneg[:, k, :],
                                          start=False, stop=True)
                        last.append(mm)
                    return last
                e.op(TE, fn_s2, reads=("kfm", "qfm", "identbf", "dmneg"),
                     writes=(f"psS{h}",))
                e.op(SC, lambda s, tt=tt, h=h, hp=hp: s.activation(
                    out=pt_sb[:, h * 1024:(h + 1) * 1024],
                    in_=psS[:, h * 1024:(h + 1) * 1024], func=AF.Exp,
                    bias=negb2[:, hp:hp + 1], scale=0.125),
                    reads=(f"psS{h}", "negb"), writes=(f"pt{h}",))

            def emit_pv(tt):
                h = tt % 2

                def fn_pv(t, hp=hp, tt=tt, h=h, first=(tt == 0),
                          stop=(tt == ntt - 1)):
                    out = []
                    for s in range(2):
                        lh = 2 * hp + s
                        out.append(t.matmul(
                            psA[s][0:65, :],
                            v_tm[:, tt, lh * 65:(lh + 1) * 65],
                            pt_sb[:, h * 1024 + s * 512:h * 1024 + (s + 1) * 512],
                            start=first, stop=stop, skip_group_check=True))
                    return out
                e.op(TE, fn_pv, reads=("vtm", "vtm_ones", f"pt{h}"),
                     writes=("psA0", "psA1"))

            emit_s(0)
            if pending_div:
                pending_div.pop(0)()
            for tt in range(1, ntt):
                emit_s(tt)
                emit_pv(tt - 1)
            emit_pv(ntt - 1)

            par = qt % 2
            for s in range(2):
                e.op(VE, lambda v, s=s, par=par: v.tensor_copy(
                    out=nac[par][s][:], in_=psA[s][0:65, :]),
                    reads=(f"psA{s}",), writes=(f"nac{par}{s}",))
            for s in range(2):
                e.op(VE, lambda v, s=s, par=par: v.tensor_scalar(
                    out=linv2[par][s][:], in0=nac[par][s][64:65, :], scalar1=1e-35,
                    scalar2=None, op0=ALU.max),
                    reads=(f"nac{par}{s}",), writes=(f"linv{par}{s}",))
                e.op(VE, lambda v, s=s, par=par: v.reciprocal(
                    out=linv2[par][s][:], in_=linv2[par][s][:]),
                     reads=(f"linv{par}{s}",), writes=(f"linv{par}{s}",))

            def div_b(hp=hp, qt=qt, par=par):
                ob, obk = ot[par], f"ot{par}"
                for s in range(2):
                    e.op(TE, lambda t, s=s, par=par: t.matmul(
                        psB[0:64, :], ones_row[0:1, 0:64], linv2[par][s][:],
                        start=True, stop=True),
                         reads=(f"linv{par}{s}", "onesrow"), writes=("psB",))
                    e.op(SC, lambda s_: s_.copy(out=linvb[:], in_=psB[0:64, :]),
                         reads=("psB",), writes=("linvb",))
                    e.op(VE, lambda v, s=s, ob=ob, obk=obk, par=par: v.tensor_tensor(
                        out=ob[64 * s:64 * s + 64, :],
                        in0=nac[par][s][0:64, :], in1=linvb[:], op=ALU.mult),
                        reads=(f"nac{par}{s}", "linvb"), writes=(obk,))
                e.op(SY, lambda s, hp=hp, qt=qt, ob=ob, obk=obk: s.dma_start(
                    out=stage4[hp][:, qt * 512:(qt + 1) * 512], in_=ob[:]),
                    reads=(obk,), writes=(f"stage4{hp}",), dma=True, dkey="oout")
            pending_div.append(div_b)
        if hp == 0:
            while pending_div:
                pending_div.pop(0)()
            ccfn("cc4a", "stage40", "gath4a", stage4[0], io["gath4a"])
    while pending_div:
        pending_div.pop(0)()
    ccfn("cc4b", "stage41", "gath4b", stage4[1], io["gath4b"])


def emit_p5(p, e, C, io, lvl=99):
    x2_d, cpos_d = io["x2"], io["cpos"]
    gath4h = [io["gath4a"], io["gath4b"]]
    wot_d, gw_d = io["WoT"], io["gate_w"]
    w1_d, w2_d, w3_d = io["m_w1"], io["m_w2"], io["m_w3"]
    sw1_d, sw2_d, sw3_d = io["sw1"], io["sw2"], io["sw3"]
    x4_d, x3_d = io["x4"], io["x3"]

    xt = [p.sbuf(f"xt{i}", [128, D], F32) for i in range(2)]
    stat = [p.sbuf(f"stat{i}", [128, 16], F32) for i in range(2)]
    ubf = [p.sbuf(f"ubf{i}", [128, D], BF16) for i in range(2)]
    u_fm = p.sbuf("ufm", [128, 8, NT], BF16)
    o_sb = u_fm
    cp_sb = p.sbuf("cpsb", [1, 1], U32)
    ident_bf = C["identbf"]

    psH1 = p.psum("psH1", [128, 1024], F32)
    psH3 = p.psum("psH3", [128, 1024], F32)
    psY = [p.psum("psY0", [128, 512], F32), p.psum("psY1", [128, 512], F32)]
    psT = p.psum("psT", [128, 1024], BF16)
    psZ = p.psum("psZ", [128, 512], F32)

    SY, TE, VE, SC, GP = ENGINES

    e.op(SY, lambda s: s.dma_start(out=cp_sb[:], in_=cpos_d[:]),
         writes=("cpsb",), dma=True, dkey="cpld")
    ogat = e.dma_sem(GP, "ogat")

    def fn_ogather(g):
        reg = g.alloc_register("posreg")
        g.load(reg, cp_sb[0:1, 0:1])
        for j in range(4):
            with g.If_eq(reg, j):
                for gsrc in range(4):
                    for ch in range(2):
                        g.dma_start(
                            out=o_sb[:, 2 * gsrc + ch, :],
                            in_=gath4h[ch][gsrc * 128:(gsrc + 1) * 128,
                                           j * NT:(j + 1) * NT]
                        ).then_inc(ogat.h, 16)
        return None
    e.op(GP, fn_ogather, reads=("gath4a", "gath4b", "cpsb"), writes=("ufm_u",),
         self_inc=128, dkey="ogat")

    with p.scope():
        wot_sb = p.sbuf("wotsb", [128, 8, D], BF16)
        x3t = p.sbuf("x3t", [128, D], F32)
        x4t = p.sbuf("x4t", [128, D], F32)
        e.op(GP, lambda g: g.dma_start(out=wot_sb[:],
                                       in_=wot_d[:].rearrange("(a p) b -> p a b", p=128)),
             writes=("wot",), dma=True, dkey="wotload")
        x3b = [x3t, x4t]
        for i in range(NTILE):
            e.op(SY, lambda s, i=i: s.dma_start(out=xt[i % 2][:],
                                                in_=x2_d[i * 128:(i + 1) * 128, :]),
                 writes=(f"xt{i % 2}",), dma=True)
            xb = x3b[i % 2]
            xbk = ("x3t", "x4t")[i % 2]
            for ng in range(2):
                def fn_y(t, i=i, ng=ng, wot_sb=wot_sb):
                    last = None
                    for mc in range(8):
                        last = t.matmul(psY[0][:], o_sb[:, mc, i * 128:(i + 1) * 128],
                                        wot_sb[:, mc, ng * 512:(ng + 1) * 512],
                                        start=(mc == 0), stop=(mc == 7))
                    return last
                e.op(TE, fn_y, reads=("ufm_u", "wot"), writes=("psY0",))
                e.op(VE, lambda v, i=i, ng=ng, xb=xb: v.tensor_add(
                    out=xb[:, ng * 512:(ng + 1) * 512],
                    in0=xt[i % 2][:, ng * 512:(ng + 1) * 512], in1=psY[0][:]),
                    reads=(f"xt{i % 2}", "psY0"), writes=(xbk,))
            e.op(SY, lambda s, i=i, xb=xb: s.dma_start(
                out=x3_d[i * 128:(i + 1) * 128, :], in_=xb[:]),
                 reads=(xbk,), writes=(f"x3o{i}",), dma=True, dkey=f"x3out{i % 2}")
    e.barrier()

    y_acc = p.sbuf("yacc", [128, 16, D], F32)
    w1h = [p.sbuf(f"w1h{i}", [128, 8, D], BF16) for i in range(2)]
    w3h = [p.sbuf(f"w3h{i}", [128, 8, D], BF16) for i in range(2)]
    w2h_t = p.sbuf("w2h", [128, 8, D], BF16)
    w2h = [w2h_t, w2h_t]
    act_buf = p.sbuf("actbuf", [128, 8, 512], BF16)
    s1_sb = p.sbuf("s1sb", [128, 1024], BF16)
    gwt_sb = p.sbuf("gwtsb", [128, 8, 8], BF16)
    comb = p.sbuf("comb", [128, 16, 9], F32)
    gz = p.sbuf("gz", [128, 8], F32)
    gm = p.sbuf("gm", [128, 8], F32)
    negm1 = p.sbuf("negm1", [128, 1], F32)
    gsc = p.sbuf("gsc", [128, 8], F32)
    gw_sb = act_buf[:].rearrange("p a b -> p (a b)")[0:8, 0:1024]
    e.op(GP, lambda g: g.memset(y_acc[:], 0.0), writes=("yacc",))

    if lvl < 2:
        return
    emit_ln_tiles(e, p, x3_d, u_fm,
                  dict(xt=xt, stat=stat, ubf=ubf, psT=psT, ident_bf=ident_bf), ukey="u",
                  src_key=[f"x3o{i}" for i in range(16)])

    if lvl < 3:
        return
    e.op(GP, lambda g: g.dma_start(out=gw_sb, in_=gw_d[:]), writes=("gw",), dma=True)
    for dc in range(8):
        e.op(TE, lambda t, dc=dc: t.transpose(psZ[:, 0:8].bitcast(BF16)[:, 0:8],
                                              gw_sb[:, dc * 128:(dc + 1) * 128],
                                              ident_bf[0:8, 0:8]),
             reads=("gw", "identbf"), writes=("psZ",))
        e.op(VE, lambda v, dc=dc: v.tensor_copy(out=gwt_sb[:, dc, :],
                                                in_=psZ[:, 0:8].bitcast(BF16)[:, 0:8]),
             reads=("psZ",), writes=("gwt",))
    for ti in range(NTILE):
        def fn_z(t, ti=ti):
            last = None
            for dc in range(8):
                last = t.matmul(psZ[:, 16:24], u_fm[:, dc, ti * 128:(ti + 1) * 128],
                                gwt_sb[:, dc, :], start=(dc == 0), stop=(dc == 7))
            return last
        e.op(TE, fn_z, reads=(f"ufm_u_g{ti // 4}", "gwt"), writes=("psZ",))
        e.op(VE, lambda v: v.tensor_copy(out=gz[:], in_=psZ[:, 16:24]),
             reads=("psZ",), writes=("gz",))
        e.op(VE, lambda v: v.max(out=gm[:], in_=gz[:]), reads=("gz",), writes=("gm",))
        e.op(VE, lambda v: v.tensor_scalar(out=negm1[:], in0=gm[:, 0:1], scalar1=-1.0,
                                           scalar2=None, op0=ALU.mult),
             reads=("gm",), writes=("negm1",))
        e.op(SC, lambda s: s.activation(out=gsc[:], in_=gz[:], func=AF.Exp,
                                        bias=negm1[:], scale=1.0),
             reads=("gz", "negm1"), writes=("gsc",))
        e.op(SC, lambda s: s.activation(out=negm1[:], in_=gm[:, 1:2], func=AF.Exp,
                                        bias=negm1[:], scale=1.0),
             reads=("gm", "negm1"), writes=("negm1b",))
        e.op(VE, lambda v: v.tensor_scalar_add(negm1[:], negm1[:], 1.0),
             reads=("negm1b",), writes=("negm1b",))
        e.op(VE, lambda v: v.reciprocal(out=negm1[:], in_=negm1[:]),
             reads=("negm1b",), writes=("negm1b",))
        e.op(VE, lambda v: v.tensor_scalar(out=gsc[:], in0=gsc[:], scalar1=negm1[:],
                                           scalar2=None, op0=ALU.mult),
             reads=("gsc", "negm1b"), writes=("gsc",))
        e.op(VE, lambda v: v.tensor_scalar(out=gm[:], in0=gz[:], scalar1=gm[:, 1:2],
                                           scalar2=None, op0=ALU.is_ge),
             reads=("gz", "gm"), writes=("gmask",))
        e.op(VE, lambda v, ti=ti: v.tensor_mul(comb[:, ti, 0:8], gsc[:], gm[:]),
             reads=("gsc", "gmask"), writes=("comb",))
        e.op(VE, lambda v, ti=ti: v.memset(comb[:, ti, 8:9], 1.0),
             reads=(), writes=("comb",))

    e.op(VE, lambda v: v.engine_nop(), reads=(), writes=("gw",))

    if lvl < 4:
        return
    for eh in range(18):
        ex, half = eh // 2, eh % 2
        wb1, wb3, wb2 = w1h[eh % 2], w3h[eh % 2], w2h[eh % 2]
        wk1, wk3, wk2 = f"w1h{eh % 2}", f"w3h{eh % 2}", "w2h"
        if ex < 8:
            s1d = w1_d[ex][:, half * 1024:(half + 1) * 1024]
            s3d = w3_d[ex][:, half * 1024:(half + 1) * 1024]
            s2d = w2_d[ex][half * 1024:(half + 1) * 1024, :]
        else:
            s1d = sw1_d[0][:, half * 1024:(half + 1) * 1024]
            s3d = sw3_d[0][:, half * 1024:(half + 1) * 1024]
            s2d = sw2_d[0][half * 1024:(half + 1) * 1024, :]
        e.op(GP, lambda g, s1d=s1d, wb1=wb1: g.dma_start(
            out=wb1[:], in_=s1d.rearrange("(a p) f -> p a f", p=128)),
            writes=(wk1,), dma=True, dkey=wk1)
        e.op(GP, lambda g, s3d=s3d, wb3=wb3: g.dma_start(
            out=wb3[:], in_=s3d.rearrange("(a p) f -> p a f", p=128)),
            writes=(wk3,), dma=True, dkey=wk3)
        e.op(GP, lambda g, s2d=s2d, wb2=wb2: g.dma_start(
            out=wb2[:], in_=s2d.rearrange("(a p) f -> p a f", p=128)),
            writes=(wk2,), dma=True, dkey=wk2)

        for tq in range(4):
            for fc in range(8):
                par = fc % 2
                h1 = psH1[:, par * 512:(par + 1) * 512]
                h3 = psH3[:, par * 512:(par + 1) * 512]
                s1v = s1_sb[:, par * 512:(par + 1) * 512]

                def fn_h(t, tq=tq, fc=fc, wb1=wb1, wb3=wb3, h1=h1, h3=h3):
                    last = []
                    for dc in range(8):
                        t.matmul(h1, wb1[:, dc, fc * 128:(fc + 1) * 128],
                                 u_fm[:, dc, tq * 512:(tq + 1) * 512],
                                 start=(dc == 0), stop=(dc == 7))
                    for dc in range(8):
                        last.append(t.matmul(h3, wb3[:, dc, fc * 128:(fc + 1) * 128],
                                             u_fm[:, dc, tq * 512:(tq + 1) * 512],
                                             start=(dc == 0), stop=(dc == 7)))
                    return last[-1]
                e.op(TE, fn_h, reads=(wk1, wk3, f"ufm_u_g{tq}"),
                     writes=(f"psH1{par}", f"psH3{par}"))
                e.op(SC, lambda s, h1=h1, s1v=s1v: s.activation(
                    out=s1v, in_=h1, func=AF.Silu),
                     reads=(f"psH1{par}",), writes=(f"s1sb{par}",))
                e.op(VE, lambda v, fc=fc, h3=h3, s1v=s1v: v.tensor_mul(
                    act_buf[:, fc, :], s1v, h3),
                    reads=(f"s1sb{par}", f"psH3{par}"), writes=("actbuf",))
            for ti in range(4):
                gi = tq * 4 + ti
                for ng in range(2):
                    def fn_y(t, ti=ti, ng=ng, wb2=wb2):
                        last = None
                        for fc in range(8):
                            last = t.matmul(psY[ng][:], act_buf[:, fc, ti * 128:(ti + 1) * 128],
                                            wb2[:, fc, ng * 512:(ng + 1) * 512],
                                            start=(fc == 0), stop=(fc == 7))
                        return last
                    e.op(TE, fn_y, reads=("actbuf", wk2), writes=(f"psY{ng}",))
                    e.op(VE, lambda v, gi=gi, ng=ng, ex=ex: v.scalar_tensor_tensor(
                        out=y_acc[:, gi, ng * 512:(ng + 1) * 512], in0=psY[ng][:],
                        scalar=comb[:, gi, ex:ex + 1],
                        in1=y_acc[:, gi, ng * 512:(ng + 1) * 512],
                        op0=ALU.mult, op1=ALU.add),
                        reads=(f"psY{ng}", "comb", "yacc"), writes=("yacc",))

    for i in range(NTILE):
        e.op(SY, lambda s, i=i: s.dma_start(out=xt[0][:], in_=x3_d[i * 128:(i + 1) * 128, :]),
             reads=(f"x3o{i}",), writes=("xt0",), dma=True, dkey="x3ld")
        e.op(VE, lambda v, i=i: v.tensor_add(out=xt[1][:], in0=xt[0][:], in1=y_acc[:, i, :]),
             reads=("xt0", "yacc"), writes=("xt1",))
        e.op(SY, lambda s, i=i: s.dma_start(out=x4_d[i * 128:(i + 1) * 128, :], in_=xt[1][:]),
             reads=("xt1",), dma=True, dkey="x4out")


def shard_x(x_full):
    out = []
    for c in range(NCORES):
        b, j = c // 4, c % 4
        out.append(np.ascontiguousarray(x_full[b, j * NT:(j + 1) * NT, :]))
    return out


RG2x4 = [[0, 1, 2, 3], [4, 5, 6, 7]]
MEGA_UPTO = 5
P2_KW = {}
P5_KW = {}
MEGA_DBG = False
MEGA_NOCC = False



def build_mega():
    p = Prog()
    nc = p.nc
    e = Emit(p)
    SY, TE, VE, SC, GP = ENGINES

    ext_in = {
        "x": ([NT, D], F32),
        "Wq": ([D, D], F32), "Wk": ([D, D], F32), "Wv": ([D, D], F32),
        "Wa": ([D, H], F32), "ba": ([H], F32), "AWo": ([D, D], F32),
        "Wp1": ([D, 128], F32), "bp1": ([128], F32),
        "Wp2": ([128, 8], F32), "bp2": ([8], F32),
        "W1": ([8, D, 2048], F32), "b1": ([8, 2048], F32),
        "W2": ([8, 2048, D], F32), "b2": ([8, D], F32),
        "FWo": ([D, D], F32), "bo": ([D], F32),
        "WqT": ([D, 256], F32), "WkT": ([D, 256], F32), "WvT": ([D, 256], F32),
        "WoT": ([D, D], F32),
        "gate_w": ([8, D], F32),
        "m_w1": ([8, D, 2048], F32), "m_w2": ([8, 2048, D], F32),
        "m_w3": ([8, D, 2048], F32),
        "sw1": ([1, D, 2048], F32), "sw2": ([1, 2048, D], F32),
        "sw3": ([1, D, 2048], F32),
        "pm": ([1, 8], F32), "cpos": ([1, 1], U32),
    }
    io = {k: nc.dram_tensor(k, sh, dt, kind="ExternalInput").ap()
          for k, (sh, dt) in ext_in.items()}
    io["x4"] = nc.dram_tensor("x4", [NT, D], F32, kind="ExternalOutput").ap()
    intern = {
        "chunkS": ([16, 128, 520], F32), "chunkA": ([16, H], F32),
        "stage1": ([128, 536], F32), "gath1": ([512, 536], F32),
        "stage2": ([1, D], F32), "gath2": ([4, D], F32),
        "stage3a": ([128, 4 * NT], BF16), "gath3a": ([512, 4 * NT], BF16),
        "stage3b": ([128, 4 * NT], BF16), "gath3b": ([512, 4 * NT], BF16),
        "stage4a": ([128, 4 * NT], BF16), "gath4a": ([512, 4 * NT], BF16),
        "stage4b": ([128, 4 * NT], BF16), "gath4b": ([512, 4 * NT], BF16),
    }
    for k, (sh, dt) in intern.items():
        io[k] = nc.dram_tensor(k, sh, dt, kind="Internal").ap()
    xkind = "ExternalOutput" if MEGA_DBG else "Internal"
    for k in ("x1", "x2", "x3"):
        io[k] = nc.dram_tensor(k, [NT, D], F32, kind=xkind).ap()

    C = {
        "identbf": p.sbuf("c_identbf", [128, 128], BF16),
        "identf": p.sbuf("c_identf", [128, 128], F32),
        "msl": p.sbuf("c_msl", [128, 128], F32),
        "mle": p.sbuf("c_mle", [128, 128], F32),
        "mneg": p.sbuf("c_mneg", [128, 128], F32),
        "onesrow": p.sbuf("c_onesrow", [1, 128], F32),
        "onesrowbf": p.sbuf("c_onesrowbf", [1, 128], BF16),
        "onescolf": p.sbuf("c_onescolf", [128, 1], F32),
        "onescolbf": p.sbuf("c_onescolbf", [128, 1], BF16),
    }
    make_const(e, GP, C["identbf"][:], "identbf", "ident")
    make_const(e, GP, C["identf"][:], "identf", "ident")
    make_const(e, GP, C["msl"][:], "msl", "strict_lower")
    make_const(e, GP, C["onesrow"][:], "onesrow", "ones")
    make_const(e, GP, C["onesrowbf"][:], "onesrowbf", "ones")
    make_const(e, GP, C["onescolf"][:], "onescol", "ones")
    make_const(e, GP, C["onescolbf"][:], "onescolbf", "ones")
    e.op(GP, lambda g: g.memset(C["mle"][:], 1.0), writes=("mle",))
    e.op(GP, lambda g: g.affine_select(out=C["mle"][:], in_=C["mle"][:],
                                       compare_op=ALU.is_ge, fill=0.0, base=0,
                                       pattern=[[1, 128]], channel_multiplier=-1),
         reads=("mle",), writes=("mle",))
    e.op(GP, lambda g: g.memset(C["mneg"][:], 0.0), writes=("mneg",))
    e.op(GP, lambda g: g.affine_select(out=C["mneg"][:], in_=C["mneg"][:],
                                       compare_op=ALU.is_ge, fill=-1e30, base=0,
                                       pattern=[[1, 128]], channel_multiplier=-1),
         reads=("mneg",), writes=("mneg",))

    cc_ctrs = []

    def collective(name, src_key, dst_key, in_ap, out_ap):
        if MEGA_NOCC:
            return ()
        ctr = e.dma_sem(GP, name)

        def fn(g):
            return g.collective_compute(
                "AllGather", mybir.AluOpType.bypass, replica_groups=RG2x4,
                ins=[in_ap.opt()], outs=[out_ap.opt()]).then_inc(ctr.h)
        e.op(GP, fn, reads=(src_key,), writes=(dst_key,), self_inc=1, dkey=name)
        cc_ctrs.append(ctr)
        return (ctr,)

    x1 = x2 = x3 = x4 = ()
    with p.scope():
        car = (p.sbuf("qfm", [128, 8, NT], BF16),
               p.sbuf("kfm", [128, 8, NT], BF16),
               p.sbuf("vtm", [128, 16, 1040], BF16),
               p.sbuf("lnatm", [128, 16 * H], F32))
        with p.scope():
            emit_p1(p, e, C, io, car)
        x1 = collective("cc1", "stage1", "gath1", io["stage1"], io["gath1"])
        e.barrier(exclude=x1)
        if MEGA_UPTO >= 2:
            with p.scope():
                emit_p2(p, e, C, io, car, **P2_KW)
            x2 = collective("cc2", "stage2", "gath2", io["stage2"], io["gath2"])
    e.barrier(exclude=x1 + x2)
    if MEGA_UPTO >= 3:
        with p.scope():
            emit_p3(p, e, C, io, collective)
        e.barrier(exclude=tuple(cc_ctrs))
    if MEGA_UPTO >= 4:
        with p.scope():
            emit_p4(p, e, C, io, collective)
        e.barrier(exclude=tuple(cc_ctrs))
    if MEGA_UPTO >= 5:
        with p.scope():
            emit_p5(p, e, C, io, **P5_KW)
    e.flush()
    return p


_PROG_CACHE = {}


def _prog(name, builder):
    if name not in _PROG_CACHE:
        _PROG_CACHE[name] = builder()
    return _PROG_CACHE[name]


def kernel(**inputs):
    f32 = lambda a: np.ascontiguousarray(a, dtype=np.float32)
    x = f32(inputs["L0_attn_x"])
    shards = shard_x(x)
    p = _prog("mega", build_mega)

    def wslice(w, g):
        return np.ascontiguousarray(w[256 * g:256 * (g + 1), :].T)

    com = dict(
        Wq=f32(inputs["L0_attn_Wq"]), Wk=f32(inputs["L0_attn_Wk"]),
        Wv=f32(inputs["L0_attn_Wv"]), Wa=f32(inputs["L0_attn_Wa"]),
        ba=f32(inputs["L0_attn_ba"]), AWo=f32(inputs["L0_attn_Wo"]),
        Wp1=f32(inputs["L0_ffn_Wp1"]), bp1=f32(inputs["L0_ffn_bp1"]),
        Wp2=f32(inputs["L0_ffn_Wp2"]), bp2=f32(inputs["L0_ffn_bp2"]),
        W1=f32(inputs["L0_ffn_W1"]), b1=f32(inputs["L0_ffn_b1"]),
        W2=f32(inputs["L0_ffn_W2"]), b2=f32(inputs["L0_ffn_b2"]),
        FWo=f32(inputs["L0_ffn_Wo"]), bo=f32(inputs["L0_ffn_bo"]),
        WoT=np.ascontiguousarray(f32(inputs["L1_attn_Wo"]).T),
        gate_w=f32(inputs["L1_ffn_gate_w"]),
        m_w1=f32(inputs["L1_ffn_w1"]), m_w2=f32(inputs["L1_ffn_w2"]),
        m_w3=f32(inputs["L1_ffn_w3"]),
        sw1=f32(inputs["L1_ffn_sw1"]), sw2=f32(inputs["L1_ffn_sw2"]),
        sw3=f32(inputs["L1_ffn_sw3"]),
    )
    wq1, wk1, wv1 = (f32(inputs["L1_attn_Wq"]), f32(inputs["L1_attn_Wk"]),
                     f32(inputs["L1_attn_Wv"]))
    maps = []
    for c in range(NCORES):
        g = c % 4
        pmv = np.zeros((1, 8), np.float32)
        for k in range(3):
            pmv[0, k] = 1.0 if k < g else 0.0
            pmv[0, 4 + k] = 1.0 - pmv[0, k]
        maps.append(dict(x=shards[c],
                         WqT=wslice(wq1, g), WkT=wslice(wk1, g), WvT=wslice(wv1, g),
                         pm=pmv, cpos=np.array([[g]], np.uint32), **com))
    res = run_bass_kernel_spmd(p.nc, maps, core_ids=list(range(NCORES))).results
    out = np.empty((2, 8192, D), np.float32)
    for c in range(NCORES):
        b, jp = c // 4, c % 4
        out[b, jp * NT:(jp + 1) * NT, :] = res[c]["x4"]
    return out
